# version 05
# baseline (speedup 1.0000x reference)
import contextlib
import numpy as np
import ml_dtypes
import concourse.bass as bass
import concourse.mybir as mybir
from concourse.bass_utils import run_bass_kernel_spmd

F32 = mybir.dt.float32
BF16 = mybir.dt.bfloat16
U8 = mybir.dt.uint8
I32 = mybir.dt.int32
F16 = mybir.dt.float16
AF = mybir.ActivationFunctionType
ALU = mybir.AluOpType
BF = ml_dtypes.bfloat16

B, S, D = 4, 4096, 2048
NCORE = 8
T = 2048
NCH = D // 128
NH, HD = 16, 128
KVH0 = 4
TT = 512
NTT = T // TT
CHUNKS_OF = {0: [0, 3, 4, 7], 1: [1, 2, 5, 6]}
EXT = [2, 4, 6, 8]
_SLOT = {}
for _h in (0, 1):
    for _i, _g in enumerate(CHUNKS_OF[_h]):
        _SLOT[_g] = _h * 4 + _i


def _core_pos(c):
    return np.concatenate([np.arange(g * TT, (g + 1) * TT) for g in CHUNKS_OF[c % 2]])
ROPE_BASE = 10000.0
NEG = -1e30
EPS_LN = 1e-5


def _tile_lhsT(w):
    o, i = w.shape
    mo, k = o // 128, i // 128
    t = w.reshape(mo, 128, k, 128)
    return np.ascontiguousarray(t.transpose(0, 3, 2, 1)).astype(np.float32)


def _rope_tables():
    inv = 1.0 / ROPE_BASE ** (np.arange(0, HD, 2, dtype=np.float64) / HD)
    pos = np.arange(S, dtype=np.float64)
    fr = pos[:, None] * inv[None, :]
    cos = np.repeat(np.cos(fr), 2, axis=1).T
    sin = np.repeat(np.sin(fr), 2, axis=1).T
    return (np.ascontiguousarray(cos.astype(np.float32)),
            np.ascontiguousarray(sin.astype(np.float32)))


def _rope_P():
    P = np.zeros((HD, HD), np.float32)
    for i in range(HD // 2):
        P[2 * i, 2 * i + 1] = -1.0
        P[2 * i + 1, 2 * i] = 1.0
    return np.ascontiguousarray(P.T.astype(BF))


def _diag_masks():
    m = np.zeros((4, 128, 512), np.float32)
    for j in range(4):
        for r in range(128):
            m[j, r, j * 128 + r + 1:] = NEG
    return np.ascontiguousarray(m)


class Sem:
    def __init__(self, handle, base=0):
        self.h = handle
        self.base = base
        self.n = 0

    def inc(self, instr, k=1):
        instr.then_inc(self.h, k)
        self.n += k
        return instr

    def wge(self, eng, v):
        return eng.wait_ge(self.h, self.base + v)

    def pairs(self):
        return [(self.h, self.base + self.n)]


class SemGroup:
    def __init__(self, sems):
        self.sems = sems

    def pairs(self):
        return [pp for s in self.sems for pp in s.pairs()]


def _wpairs(waits):
    return [pp for w in waits for pp in w.pairs()]


class Prog:
    def __init__(self, name="mk"):
        self.nc = bass.Bass("TRN2", target_bir_lowering=False, debug=False)
        self.stack = contextlib.ExitStack()
        self.sb_stacks = [self.stack]
        self._n = 0
        self.block = None

    def sem(self, name=None, kind="eng"):
        if not hasattr(self, "_sem_free"):
            self._sem_free = {"eng": [], "sw": [], "hw": [], "col": []}
            self._sem_scopes = [[]]
        pool = self._sem_free[kind]
        if pool:
            h, total = pool.pop()
            s = Sem(h, base=total)
        else:
            name = name or f"s{self._n}"
            self._n += 1
            s = Sem(self.stack.enter_context(self.nc.semaphore(name)))
        s.kind = kind
        self._sem_scopes[-1].append(s)
        return s

    def sb(self, name, shape, dtype):
        return self.sb_stacks[-1].enter_context(self.nc.sbuf_tensor(name, shape, dtype))

    @contextlib.contextmanager
    def scope(self):
        st = contextlib.ExitStack()
        self.sb_stacks.append(st)
        if not hasattr(self, "_sem_free"):
            self._sem_free = {"eng": [], "sw": [], "hw": [], "col": []}
            self._sem_scopes = [[]]
        self._sem_scopes.append([])
        try:
            yield
        finally:
            self.sb_stacks.pop()
            st.close()
            freed = self._sem_scopes.pop()
            drain = [(s.h, s.base + s.n) for s in freed
                     if getattr(s, "kind", "eng") in ("sw", "hw", "col") and s.n > 0]
            if drain:
                @self.block.gpsimd
                def _(g):
                    for h, n in drain:
                        g.wait_ge(h, n)

                @self.block.sync
                def _(g):
                    for h, n in drain:
                        g.wait_ge(h, n)
            for s in freed:
                self._sem_free[getattr(s, "kind", "eng")].append((s.h, s.base + s.n))

    def ps(self, name, shape, dtype=F32):
        return self.stack.enter_context(self.nc.psum_tensor(name, shape, dtype))

    def din(self, name, shape, dtype):
        return self.nc.dram_tensor(name, shape, dtype, kind="ExternalInput").ap()

    def dout(self, name, shape, dtype):
        return self.nc.dram_tensor(name, shape, dtype, kind="ExternalOutput").ap()

    def dint(self, name, shape, dtype):
        return self.nc.dram_tensor(name, shape, dtype, kind="Internal").ap()

    def open(self):
        self.pf = [self.ps(f"pf{i}", [128, 512], F32) for i in range(6)]
        self.pb = [self.ps(f"pb{i}", [128, 1024], BF16) for i in range(2)]
        self.block = self.stack.enter_context(self.nc.Block())
        return self

    def close(self):
        self.stack.close()


def load_consts(p, reg_vals=()):
    cid = p.din("c_id", [128, 128], BF16)
    cidf = p.din("c_idf", [128, 128], F32)
    ident = p.sb("ident", [128, 128], BF16)
    identf = p.sb("identf", [128, 128], F32)
    ones_col = p.sb("ones_col", [128, 1], BF16)
    ones_row = p.sb("ones_row", [1, 128], BF16)
    ones_row_f = p.sb("ones_row_f", [1, 128], F32)
    ctiles = []
    for i, v in enumerate(reg_vals):
        t = p.sb(f"creg{i}", [128, 1], F32)
        p.nc.const_aps.register(F32, float(v), t[:, 0:1])
        ctiles.append((t, float(v)))
    s_hw = p.sem("s_const", kind="hw")
    s_en = p.sem()

    @p.block.sync
    def _(sync):
        s_hw.inc(sync.dma_start(out=ident[:, :], in_=cid[:, :]), 16)
        s_hw.inc(sync.dma_start(out=identf[:, :], in_=cidf[:, :]), 16)

    @p.block.vector
    def _(vector):
        vector.memset(ones_col[:, :], 1.0)
        vector.memset(ones_row[:, :], 1.0)
        for t, v in ctiles:
            vector.memset(t[:, :], v)
        s_en.inc(vector.memset(ones_row_f[:, :], 1.0))
    return {"ident": ident, "identf": identf, "ones_col": ones_col,
            "ones_row": ones_row, "ones_row_f": ones_row_f}, SemGroup([s_hw, s_en])


def emit_load_xbf(p, xsrc, xbf):
    s_x = p.sem(kind="sw")

    @p.block.gpsimd
    def _(g):
        s_x.inc(g.dma_start(out=xbf[:, :].rearrange("p (c t) -> p c t", c=NCH),
                            in_=xsrc.rearrange("(c p) t -> p c t", p=128)), 16)
    return s_x


def ensure_proj_scratch(p):
    if not hasattr(p, "proj_scratch"):
        p.proj_scratch = {
            "wbuf": [p.sb(f"pj_w{i}", [128, NCH * 128], BF16) for i in range(2)],
            "ev": [p.sb(f"pj_ev{i}", [128, TT], BF16) for i in range(2)],
            "qf": [p.sb(f"pj_qf{i}", [128, TT], BF16) for i in range(2)],
            "rt1": p.sb("pj_rt1", [128, TT], F32),
            "rt2": p.sb("pj_rt2", [128, TT], F32),
        }
    return p.proj_scratch


def emit_ln(p, xbf, xln, consts, waits, tag, ntok=T):
    ntt = ntok // TT
    oc = consts["ones_col"]
    orow = consts["ones_row"]
    xsq = p.sb(f"xsq_{tag}", [128, NCH * TT], BF16)
    srow = p.sb(f"srow_{tag}", [1, 6 * TT], F32)
    mrow = p.sb(f"mrow_{tag}", [1, 2 * TT], BF16)
    tmp = p.sb(f"lntmp_{tag}", [128, 4 * TT], BF16)
    st_sum, st_sq = p.pf[4], p.pf[5]
    bc0, bc1 = p.pf[2], p.pf[3]
    s_sq, s_stat, s_row, s_sd, s_rr, s_bc, s_va = (p.sem() for _ in range(7))
    s_ln = p.sem()
    wvals = _wpairs(waits)

    def xs(c, t0):
        return slice(c * ntok + t0, c * ntok + t0 + TT)

    @p.block.scalar
    def _(scalar):
        for tt in range(ntt):
            t0 = tt * TT
            if tt == 0:
                for h, n in wvals:
                    scalar.wait_ge(h, n)
            for c in range(NCH - 1):
                scalar.activation(xsq[:, c * TT:(c + 1) * TT], xbf[:, xs(c, t0)], AF.Square)
            s_sq.inc(scalar.activation(xsq[:, (NCH - 1) * TT:], xbf[:, xs(NCH - 1, t0)], AF.Square))
            s_row.wge(scalar, tt + 1)
            s_sd.inc(scalar.activation(srow[:, 4 * TT:5 * TT], srow[:, 3 * TT:4 * TT], AF.Sqrt))

    @p.block.tensor
    def _(tensor):
        for tt in range(ntt):
            t0 = tt * TT
            if tt == 0:
                for h, n in wvals:
                    tensor.wait_ge(h, n)
            s_sq.wge(tensor, tt + 1)
            for c in range(NCH):
                tensor.matmul(st_sum[0:1, :], oc[:, :], xbf[:, xs(c, t0)],
                              start=(c == 0), stop=(c == NCH - 1))
            for c in range(NCH):
                i = tensor.matmul(st_sq[0:1, :], oc[:, :], xsq[:, c * TT:(c + 1) * TT],
                                  start=(c == 0), stop=(c == NCH - 1))
            s_stat.inc(i)
            s_rr.wge(tensor, tt + 1)
            if tt > 0:
                s_ln.wge(tensor, tt)
            tensor.matmul(bc0[:, :], orow[:, :], mrow[0:1, 0:TT], start=True, stop=True)
            s_bc.inc(tensor.matmul(bc1[:, :], orow[:, :], mrow[0:1, TT:2 * TT], start=True, stop=True))

    @p.block.vector
    def _(vector):
        for tt in range(ntt):
            t0 = tt * TT
            s_stat.wge(vector, tt + 1)
            vector.tensor_scalar(srow[:, 0:TT], st_sum[0:1, :], 1.0 / D, None, ALU.mult)
            vector.tensor_scalar(mrow[:, 0:TT], st_sum[0:1, :], 1.0 / D, None, ALU.mult)
            s_va.inc(vector.tensor_scalar(srow[:, TT:2 * TT], st_sq[0:1, :], 1.0 / D, None, ALU.mult))
            s_va.wge(vector, s_va.n)
            s_va.inc(vector.tensor_tensor(srow[:, 2 * TT:3 * TT], srow[:, 0:TT], srow[:, 0:TT], ALU.mult))
            s_va.wge(vector, s_va.n)
            s_row.inc(vector.scalar_tensor_tensor(srow[:, 3 * TT:4 * TT], srow[:, TT:2 * TT],
                                                  EPS_LN, srow[:, 2 * TT:3 * TT],
                                                  ALU.add, ALU.subtract))
            s_sd.wge(vector, tt + 1)
            s_va.inc(vector.reciprocal(srow[:, 5 * TT:6 * TT], srow[:, 4 * TT:5 * TT]))
            s_va.wge(vector, s_va.n)
            s_rr.inc(vector.tensor_copy(mrow[:, TT:2 * TT], srow[:, 5 * TT:6 * TT]))
            s_bc.wge(vector, tt + 1)
            for g in range(NCH // 4):
                if g > 0:
                    s_va.wge(vector, s_va.n)
                elif tt > 0:
                    s_ln.wge(vector, tt)
                for c4 in range(4):
                    c = g * 4 + c4
                    i = vector.tensor_tensor(tmp[:, c4 * TT:(c4 + 1) * TT], xbf[:, xs(c, t0)],
                                             bc0[:, :], ALU.subtract)
                s_va.inc(i)
                s_va.wge(vector, s_va.n)
                for c4 in range(4):
                    c = g * 4 + c4
                    i = vector.tensor_tensor(xln[:, xs(c, t0)], tmp[:, c4 * TT:(c4 + 1) * TT],
                                             bc1[:, :], ALU.mult)
                if g < NCH // 4 - 1:
                    s_va.inc(i)
            s_ln.inc(i)
    return s_ln


def emit_proj(p, xln, wdram, nmo, consts, waits, tag,
              bias_sb=None, bias_off=0, mode="plain", rope=None, out_dram=None,
              ntok=T, ev_consumer=None, sel=None):
    ntt = ntok // TT
    sc = ensure_proj_scratch(p)
    wbuf, ev = sc["wbuf"], sc["ev"]
    pp = [p.pf[0], p.pf[1]]
    s_wld = [p.sem(kind="sw"), p.sem(kind="sw")]
    s_done2 = [p.sem(kind="hw"), p.sem(kind="hw")]
    s_mm, s_ev, s_rr2 = p.sem(), p.sem(), p.sem()
    wvals = _wpairs(waits)

    if mode in ("rope", "vtm"):
        qf = sc["qf"]
        rt1, rt2 = sc["rt1"], sc["rt2"]
        rot_ps = p.pf[2]
        tp_ps = p.pb[0]
        s_rot, s_qf = p.sem(), p.sem()
    s_free = p.sem()

    def xs(c, t0):
        return slice(c * ntok + t0, c * ntok + t0 + TT)

    @p.block.gpsimd
    def _(g):
        for m in range(nmo):
            if m == 0:
                for h, n in wvals:
                    g.wait_ge(h, n)
            if m >= 2:
                s_mm.wge(g, (m - 1) * ntt)
            s_wld[m % 2].inc(g.dma_start(out=wbuf[m % 2][:, :],
                                         in_=wdram[m, :, :, :].rearrange("p k mm -> p (k mm)")), 16)

    @p.block.tensor
    def _(tensor):
        for m in range(nmo):
            s_wld[m % 2].wge(tensor, 16 * (m // 2 + 1))
            if m == 0:
                for h, n in wvals:
                    tensor.wait_ge(h, n)
            for tt in range(ntt):
                t0 = tt * TT
                it = m * ntt + tt
                if it >= 2:
                    s_ev.wge(tensor, it - 1)
                for c in range(NCH):
                    i = tensor.matmul(pp[it % 2][:, :], wbuf[m % 2][:, c * 128:(c + 1) * 128],
                                      xln[:, xs(c, t0)], start=(c == 0), stop=(c == NCH - 1))
                s_mm.inc(i)
                if mode == "rope":
                    s_ev.wge(tensor, it + 1)
                    if it >= 1:
                        s_qf.wge(tensor, it)
                    s_rot.inc(tensor.matmul(rot_ps[:, :], rope[2][:, :], ev[it % 2][:, :],
                                            start=True, stop=True))
                elif mode == "vtm":
                    s_ev.wge(tensor, it + 1)
                    if it >= 1:
                        s_qf.wge(tensor, it)
                    for sg in range(4):
                        i = tensor.transpose(tp_ps[:, sg * 128:(sg + 1) * 128],
                                             ev[it % 2][:, sg * 128:(sg + 1) * 128], consts["ident"][:, :])
                    s_rot.inc(i)

    @p.block.vector
    def _(vector):
        for m in range(nmo):
            for tt in range(ntt):
                it = m * ntt + tt
                s_mm.wge(vector, it + 1)
                if it >= 2 and mode != "sel":
                    if mode in ("rope", "vtm"):
                        s_qf.wge(vector, it - 1)
                    elif mode == "cb":
                        s_free.wge(vector, it - 1)
                    else:
                        s_done2[it % 2].wge(vector, 16 * (it // 2))
                if mode == "sel":
                    mask_all, acc = sel
                    i = vector.copy_predicated(
                        acc[:, tt * TT:(tt + 1) * TT],
                        mask_all[:, m * T + tt * TT:m * T + (tt + 1) * TT],
                        pp[it % 2][:, :])
                elif bias_sb is not None:
                    i = vector.tensor_scalar(ev[it % 2][:, :], pp[it % 2][:, :],
                                             bias_sb[:, bias_off + m:bias_off + m + 1], None, ALU.add)
                else:
                    i = vector.tensor_copy(ev[it % 2][:, :], pp[it % 2][:, :])
                s_ev.inc(i)
                if mode == "rope":
                    s_rot.wge(vector, it + 1)
                    if it >= 2:
                        s_done2[it % 2].wge(vector, 16 * (it // 2))
                    cos_sb, sin_sb, _ = rope
                    t0 = tt * TT
                    s_ev.wge(vector, it + 1)
                    vector.tensor_tensor(rt1[:, :], ev[it % 2][:, :], cos_sb[:, t0:t0 + TT], ALU.mult)
                    s_rr2.inc(vector.tensor_tensor(rt2[:, :], rot_ps[:, :], sin_sb[:, t0:t0 + TT], ALU.mult))
                    s_rr2.wge(vector, s_rr2.n)
                    s_qf.inc(vector.tensor_tensor(qf[it % 2][:, :], rt1[:, :], rt2[:, :], ALU.add))
                elif mode == "vtm":
                    s_rot.wge(vector, it + 1)
                    if it >= 2:
                        s_done2[it % 2].wge(vector, 16 * (it // 2))
                    s_qf.inc(vector.tensor_copy(qf[it % 2][:, :], tp_ps[:, 0:512]))

    if mode == "sel":
        return s_ev
    if mode == "cb":
        ev_consumer(s_ev, s_free, ev)
        return s_ev

    @p.block.sync
    def _(g):
        for m in range(nmo):
            for tt in range(ntt):
                it = m * ntt + tt
                t0 = tt * TT
                if mode in ("rope", "vtm"):
                    s_qf.wge(g, it + 1)
                    src = qf[it % 2][:, :]
                else:
                    s_ev.wge(g, it + 1)
                    src = ev[it % 2][:, :]
                if mode == "vtm":
                    oap = out_dram[m, t0:t0 + TT, :].rearrange("(sg pp) d -> pp sg d", pp=128)
                    src = src.rearrange("p (sg d) -> p sg d", sg=4)
                else:
                    oap = out_dram[m, :, t0:t0 + TT]
                i = g.dma_start(out=oap, in_=src)
                s_done2[it % 2].inc(i, 16)
    return SemGroup(s_done2)


def emit_resadd_ln(p, x_dram, add_dram, x1_dram, xbf, consts, waits, tag,
                   wo_dram=None, bias_sb=None, bias_off=0):
    xt = p.sb(f"rx_{tag}", [128, NCH * TT], F32)
    at = [p.sb(f"ra0_{tag}", [128, NCH * TT], BF16)]
    x1c = [p.sb(f"rx1{i}_{tag}", [128, TT], F32) for i in range(2)]
    if wo_dram is not None:
        wbuf = ensure_proj_scratch(p)["wbuf"]
    pp = [p.pf[0], p.pf[1]]
    ybuf = ensure_proj_scratch(p)["rt1"]
    s_mm, s_ev, s_yb = (p.sem() for _ in range(3))
    s_xl = p.sem(kind="sw")
    s_al = p.sem(kind="sw")
    s_wl = [p.sem(kind="sw"), p.sem(kind="sw")]
    s_st = [p.sem(kind="hw"), p.sem(kind="hw")]
    wvals = _wpairs(waits)

    @p.block.gpsimd
    def _(g):
        for hh, n in wvals:
            g.wait_ge(hh, n)
        for tt in range(NTT):
            if tt >= 1:
                s_ev.wge(g, 2 * NCH * tt)
            s_xl.inc(g.dma_start(out=xt[:, :].rearrange("p (c t) -> p c t", c=NCH),
                                 in_=x_dram[:, :, tt * TT:(tt + 1) * TT].rearrange("c p t -> p c t")), 16)
            s_al.inc(g.dma_start(out=at[0][:, :].rearrange("p (c t) -> p c t", c=NCH),
                                 in_=add_dram[:, :, tt * TT:(tt + 1) * TT].rearrange("c p t -> p c t")), 16)
            if wo_dram is not None:
                for m in range(NCH):
                    it = tt * NCH + m
                    if it >= 2:
                        s_mm.wge(g, it - 1)
                    s_wl[it % 2].inc(g.dma_start(out=wbuf[it % 2][:, :],
                                                 in_=wo_dram[m, :, :, :].rearrange("p k mm -> p (k mm)")), 16)

    if wo_dram is not None:
        @p.block.tensor
        def _(tensor):
            for tt in range(NTT):
                s_al.wge(tensor, 16 * (tt + 1))
                if tt == 0:
                    for hh, n in wvals:
                        tensor.wait_ge(hh, n)
                for m in range(NCH):
                    it = tt * NCH + m
                    s_wl[it % 2].wge(tensor, 16 * (it // 2 + 1))
                    if it >= 2:
                        s_ev.wge(tensor, 2 * (it - 1))
                    for c in range(NCH):
                        i = tensor.matmul(pp[it % 2][:, :], wbuf[it % 2][:, c * 128:(c + 1) * 128],
                                          at[0][:, c * TT:(c + 1) * TT],
                                          start=(c == 0), stop=(c == NCH - 1))
                    s_mm.inc(i)

    @p.block.vector
    def _(vector):
        for tt in range(NTT):
            s_xl.wge(vector, 16 * (tt + 1))
            for m in range(NCH):
                it = tt * NCH + m
                xc = xt[:, m * TT:(m + 1) * TT]
                if wo_dram is not None:
                    s_mm.wge(vector, it + 1)
                    if it >= 2:
                        s_st[it % 2].wge(vector, 16 * (it // 2))
                    if it >= 1:
                        s_ev.wge(vector, 2 * it)
                    s_yb.inc(vector.tensor_scalar(ybuf[:, :], pp[it % 2][:, :],
                                                  bias_sb[:, bias_off + m:bias_off + m + 1],
                                                  None, ALU.add))
                    s_yb.wge(vector, s_yb.n)
                    s_ev.inc(vector.tensor_tensor(x1c[it % 2][:, :], ybuf[:, :], xc, ALU.add))
                    i = vector.tensor_tensor(xbf[:, m * T + tt * TT:m * T + tt * TT + TT],
                                             ybuf[:, :], xc, ALU.add)
                    s_ev.inc(i)
                else:
                    s_al.wge(vector, 16 * (tt + 1))
                    if it >= 2:
                        s_st[it % 2].wge(vector, 16 * (it // 2))
                    ac = at[0][:, m * TT:(m + 1) * TT]
                    s_ev.inc(vector.tensor_tensor(x1c[it % 2][:, :], xc, ac, ALU.add))
                    i = vector.tensor_tensor(xbf[:, m * T + tt * TT:m * T + tt * TT + TT],
                                             xc, ac, ALU.add)
                    s_ev.inc(i)

    @p.block.sync
    def _(g):
        for tt in range(NTT):
            for m in range(NCH):
                it = tt * NCH + m
                s_ev.wge(g, 2 * (it + 1))
                s_st[it % 2].inc(g.dma_start(out=x1_dram[m, :, tt * TT:(tt + 1) * TT],
                                             in_=x1c[it % 2][:, :]), 16)
    return s_ev, SemGroup(s_st)


def emit_l0f_gating(p, xln, wg_sb, consts, waits, tag):
    identf = consts["identf"]
    orow_f = consts["ones_row_f"]
    g_sb = p.sb(f"g_{tag}", [16, TT], F32)
    gtm = p.sb(f"gtm_{tag}", [128, 4 * 16], F32)
    gm = p.sb(f"gm_{tag}", [128, 4], F32)
    gnm = p.sb(f"gnm_{tag}", [128, 4], F32)
    ge = p.sb(f"ge_{tag}", [128, 64], F32)
    gs = p.sb(f"gs_{tag}", [128, 4], F32)
    gt1 = p.sb(f"gt1_{tag}", [128, 4], F32)
    trow = p.sb(f"trow_{tag}", [1, TT], F32)
    ind_fm = p.sb(f"ind_{tag}", [16, NCH * 128], F32)
    top1row = p.sb(f"t1r_{tag}", [1, NCH * 128], F32)
    g_ps = p.pf[0]
    t_ps = p.pf[1]
    mb_ps = p.pf[2]
    t1_ps = p.pf[3]
    (s_gmm, s_gev, s_gt, s_gst, s_gnm, s_ge, s_gr, s_grt, s_gre, s_mb,
     s_ind) = (p.sem() for _ in range(11))
    wvals = _wpairs(waits)

    @p.block.tensor
    def _(tensor):
        for tt in range(NTT):
            if tt == 0:
                for hh, n in wvals:
                    tensor.wait_ge(hh, n)
            if tt > 0:
                s_gev.wge(tensor, tt)
            for c in range(NCH):
                i = tensor.matmul(g_ps[0:16, 0:TT], wg_sb[:, c * 16:(c + 1) * 16],
                                  xln[:, c * T + tt * TT:c * T + tt * TT + TT],
                                  start=(c == 0), stop=(c == NCH - 1))
            s_gmm.inc(i)
            s_gev.wge(tensor, tt + 1)
            if tt > 0:
                s_gst.wge(tensor, tt)
                s_gre.wge(tensor, tt)
            for gr in range(4):
                i = tensor.transpose(t_ps[0:128, gr * 16:(gr + 1) * 16],
                                     g_sb[0:16, gr * 128:(gr + 1) * 128], identf[0:16, 0:16])
            s_gt.inc(i)
            s_gr.wge(tensor, tt + 1)
            for gr in range(4):
                tensor.transpose(t_ps[0:1, gr * 128:(gr + 1) * 128], gm[:, gr:gr + 1],
                                 identf[:, :])
                i = tensor.transpose(t1_ps[0:1, gr * 128:(gr + 1) * 128], gt1[:, gr:gr + 1],
                                     identf[:, :])
            s_grt.inc(i)
            s_gre.wge(tensor, tt + 1)
            if tt > 0:
                s_ind.wge(tensor, tt)
            s_mb.inc(tensor.matmul(mb_ps[0:16, 0:TT], orow_f[0:1, 0:16], trow[0:1, 0:TT],
                                   start=True, stop=True))

    @p.block.vector
    def _(vector):
        for tt in range(NTT):
            if tt == 0:
                for hh, n in wvals:
                    vector.wait_ge(hh, n)
            s_gmm.wge(vector, tt + 1)
            if tt > 0:
                s_ind.wge(vector, tt)
            s_gev.inc(vector.tensor_copy(g_sb[0:16, :], g_ps[0:16, 0:TT]))
            s_gt.wge(vector, tt + 1)
            s_gst.inc(vector.tensor_copy(gtm[:, :], t_ps[0:128, 0:64]))
            s_gst.wge(vector, tt + 1)
            if tt > 0:
                s_grt.wge(vector, tt)
            for gr in range(4):
                i = vector.reduce_max(gm[:, gr:gr + 1], gtm[:, gr * 16:(gr + 1) * 16],
                                      axis=mybir.AxisListType.X)
            s_gnm.inc(i)
            s_gnm.wge(vector, 2 * tt + 1)
            for gr in range(4):
                i = vector.tensor_scalar(gnm[:, gr:gr + 1], gm[:, gr:gr + 1], -1.0, None, ALU.mult)
            s_gnm.inc(i)
            s_ge.wge(vector, 4 * (tt + 1))
            for gr in range(4):
                i = vector.reciprocal(gt1[:, gr:gr + 1], gs[:, gr:gr + 1])
            s_gr.inc(i)
            s_grt.wge(vector, tt + 1)
            vector.tensor_copy(trow[0:1, 0:TT], t_ps[0:1, 0:TT])
            s_gre.inc(vector.tensor_copy(top1row[0:1, tt * TT:(tt + 1) * TT], t1_ps[0:1, 0:TT]))
            s_mb.wge(vector, tt + 1)
            s_ind.inc(vector.tensor_tensor(ind_fm[0:16, tt * TT:(tt + 1) * TT], g_sb[0:16, :],
                                           mb_ps[0:16, 0:TT], ALU.is_equal))

    @p.block.scalar
    def _(scalar):
        for tt in range(NTT):
            s_gnm.wge(scalar, 2 * (tt + 1))
            for gr in range(4):
                i = scalar.activation(ge[:, gr * 16:(gr + 1) * 16], gtm[:, gr * 16:(gr + 1) * 16],
                                      AF.Exp, bias=gnm[:, gr:gr + 1], accum_out=gs[:, gr:gr + 1])
                s_ge.inc(i)
    return ind_fm, top1row, s_ind, s_gre


def emit_attn_tok(p, consts, q_dram, kg_dram, vg_dram, att_dram, nhd, kvmap, scale,
                  ptq_sb, cpos_sb, waits, tag, gate=None):
    NBQ = NTT
    q_sb = [p.sb(f"aq{i}_{tag}", [128, T], BF16) for i in range(2)]
    k_sb = [p.sb(f"ak{i}_{tag}", [128, S], BF16) for i in range(2)]
    v_sb = [p.sb(f"av{i}_{tag}", [128, S], BF16) for i in range(2)]
    p_sb = p.sb(f"apb_{tag}", [128, 4 * 512], BF16)
    pt_sb = [p.sb(f"apt{i}_{tag}", [128, 512], BF16) for i in range(2)]
    bsq = p.sb(f"absq_{tag}", [128, 512], BF16)
    NBK = S // 512
    brow = p.sb(f"abrow_{tag}", [1, NTT + NBK + 4], F32)
    nbrow = p.sb(f"anbr_{tag}", [1, 1], F32)
    nb_sb = p.sb(f"anb_{tag}", [128, 1], F32)
    lsl = p.sb(f"alsl_{tag}", [128, 4 * 8], F32)
    ls = p.sb(f"als_{tag}", [128, 4], F32)
    rj_sb = p.sb(f"arj_{tag}", [128, 4], F32)
    rt_sb = p.sb(f"art_{tag}", [1, 512], F32)
    r_sb = p.sb(f"arr_{tag}", [128, 512], F32)
    att_sb = [p.sb(f"ao{i}_{tag}", [128, 512], BF16) for i in range(2)]
    gate_sb = p.sb(f"agt_{tag}", [128, 512], BF16)
    msk_sb = p.sb(f"amsk_{tag}", [128, 4 * 2 * 4 * 512], BF16)
    att_ps = p.pf[0]
    sc_ps = [p.pf[1], p.pf[2]]
    r_ps = p.pf[3]
    b_ps = p.pf[4]
    g_ps = p.pf[5]
    pt_ps = [p.pb[0], p.pb[1]]
    ident = consts["ident"]
    oc, orow_f = consts["ones_col"], consts["ones_row_f"]
    wvals = _wpairs(waits)

    (s_bsq, s_bmm, s_bred, s_brow, s_bsd, s_bneg, s_nb, s_nbe, s_sc, s_mk,
     s_exp, s_tp, s_pte, s_av, s_rj, s_rt, s_rte, s_rbc, s_gt, s_gs, s_fin,
     s_cmb, s_aux, s_mgb, s_mgen) = (p.sem() for _ in range(25))
    s_ldq = [p.sem(kind="sw"), p.sem(kind="sw")]
    s_ldkv = [p.sem(kind="sw"), p.sem(kind="sw")]
    s_stp = [p.sem(kind="hw"), p.sem(kind="hw")]

    def mcol(qc, b2, j):
        return ((qc * 2 + b2) * 4 + j) * 512

    @p.block.tensor
    def _(tensor):
        for hh, n in wvals:
            tensor.wait_ge(hh, n)
        for qc in range(NBQ):
            for b2 in range(2):
                kb = EXT[qc] - 2 + b2
                if (qc, b2) != (0, 0):
                    s_mgen.wge(tensor, (qc * 2 + b2) * 4)
                i = tensor.matmul(b_ps[:, :], orow_f[0:1, :],
                                  cpos_sb[0:1, kb * 512:(kb + 1) * 512],
                                  start=True, stop=True)
                s_mgb.inc(i)

    @p.block.vector
    def _(vector):
        for hh, n in wvals:
            vector.wait_ge(hh, n)
        for qc in range(NBQ):
            for b2 in range(2):
                s_mgb.wge(vector, qc * 2 + b2 + 1)
                for j in range(4):
                    s_mgen.inc(vector.tensor_scalar(
                        msk_sb[:, mcol(qc, b2, j):mcol(qc, b2, j) + 512],
                        b_ps[:, :], ptq_sb[:, qc * 4 + j:qc * 4 + j + 1], NEG,
                        ALU.is_gt, ALU.mult))

    kv_fresh = {}
    slot_of = {}
    kv_tgt = {}
    nslot = 0
    slot_uses = [0, 0]
    for h in range(nhd):
        kv = kvmap(h)
        if kv not in slot_of:
            slot_of[kv] = nslot % 2
            nslot += 1
            kv_fresh[h] = kv
            slot_uses[slot_of[kv]] += 1
        kv_tgt[h] = 16 * 16 * slot_uses[slot_of[kv]]

    sc_ord, mk_ord, exp_ord, tp_ord, pte_ord, av_ord, fin_ord, gt_ord = ({} for _ in range(8))
    nsc = nmk = nexp = ntp = npte = nav = nfin = ngt = 0
    slice_base = {}
    nbs = 0
    for h in range(nhd):
        ns = NTT + (NBK if h in kv_fresh else 0)
        slice_base[h] = (nbs, ns)
        nbs += ns
        for qc in range(NBQ):
            E = EXT[qc]
            if gate is not None:
                ngt += 1
                gt_ord[(h, qc)] = ngt
            for kb in range(E):
                for j in range(4):
                    nsc += 1
                    sc_ord[(h, qc, kb, j)] = nsc
                    if kb >= E - 2:
                        nmk += 1
                        mk_ord[(h, qc, kb, j)] = nmk
                    nexp += 1
                    exp_ord[(h, qc, kb, j)] = nexp
                for c in range(4):
                    ntp += 1
                    tp_ord[(h, qc, kb, c)] = ntp
                    npte += 1
                    pte_ord[(h, qc, kb, c)] = npte
                    nav += 1
                    av_ord[(h, qc, kb, c)] = nav
            nfin += 1
            fin_ord[(h, qc)] = nfin

    @p.block.gpsimd
    def _(g):
        for h in range(nhd):
            if h == 0:
                for hh, n in wvals:
                    g.wait_ge(hh, n)
            if h >= 2:
                s_stp[0].wge(g, 16 * 2 * (h - 1))
                s_stp[1].wge(g, 16 * 2 * (h - 1))
            if h in kv_fresh:
                kv = kv_fresh[h]
                sl = slot_of[kv]
                for gch in range(8):
                    src, pos = divmod(_SLOT[gch], 4)
                    g.dma_start(out=k_sb[sl][:, gch * 512:(gch + 1) * 512],
                                in_=kg_dram[src, kv, :, pos * 512:(pos + 1) * 512]).then_inc(s_ldkv[sl].h, 16)
                    g.dma_start(
                        out=v_sb[sl][:, gch * 512:(gch + 1) * 512].rearrange("pp (cc d) -> pp cc d", d=128),
                        in_=vg_dram[src, kv, pos * 512:(pos + 1) * 512, :].rearrange("(cc pp) d -> pp cc d", pp=128)
                    ).then_inc(s_ldkv[sl].h, 16)
                s_ldkv[sl].n += 16 * 16
            s_ldq[h % 2].inc(g.dma_start(out=q_sb[h % 2][:, :], in_=q_dram[h, :, :]), 16)

    @p.block.tensor
    def _(tensor):
        for h in range(nhd):
            kv = kvmap(h)
            ksl = slot_of[kv]
            s_ldq[h % 2].wge(tensor, 16 * (h // 2 + 1))
            s_ldkv[ksl].wge(tensor, kv_tgt[h])
            sbase, ns = slice_base[h]
            if h == 0:
                s_mgen.wge(tensor, 32)
            if h >= 1:
                s_nbe.wge(tensor, h)
            for sl in range(ns):
                s_bsq.wge(tensor, sbase + sl + 1)
                if sbase + sl > 0:
                    s_bred.wge(tensor, sbase + sl)
                s_bmm.inc(tensor.matmul(b_ps[0:1, 0:512], oc[:, :], bsq[:, :],
                                        start=True, stop=True))
            s_bneg.wge(tensor, h + 1)
            s_nb.inc(tensor.matmul(b_ps[0:128, 0:1], orow_f[:, 0:128], nbrow[0:1, 0:1],
                                   start=True, stop=True))
            for qc in range(NBQ):
                E = EXT[qc]
                if gate is not None:
                    go = gt_ord[(h, qc)]
                    if go > 1:
                        s_gs.wge(tensor, go - 1)
                    s_gt.inc(tensor.matmul(g_ps[:, :], gate[0][:, :],
                                           q_sb[h % 2][:, qc * 512:(qc + 1) * 512],
                                           start=True, stop=True))
                for kb in range(E):
                    for j in range(4):
                        o = sc_ord[(h, qc, kb, j)]
                        if o > 2:
                            s_exp.wge(tensor, o - 2)
                        tq0 = (4 * qc + j) * 128
                        s_sc.inc(tensor.matmul(sc_ps[j % 2][:, :],
                                               q_sb[h % 2][:, tq0:tq0 + 128],
                                               k_sb[ksl][:, kb * 512:(kb + 1) * 512],
                                               start=True, stop=True))
                    for c in range(4):
                        o = tp_ord[(h, qc, kb, c)]
                        s_exp.wge(tensor, exp_ord[(h, qc, kb, 3)])
                        if o > 2:
                            s_av.wge(tensor, o - 2)
                        for j in range(4):
                            i = tensor.transpose(pt_ps[c % 2][:, j * 128:(j + 1) * 128],
                                                 p_sb[:, j * 512 + c * 128:j * 512 + (c + 1) * 128],
                                                 ident[:, :])
                        s_tp.inc(i)
                        s_pte.wge(tensor, pte_ord[(h, qc, kb, c)])
                        first = (kb == 0 and c == 0)
                        if first and fin_ord[(h, qc)] > 1:
                            s_fin.wge(tensor, fin_ord[(h, qc)] - 1)
                        cc = kb * 4 + c
                        s_av.inc(tensor.matmul(att_ps[:, :],
                                               v_sb[ksl][:, cc * 128:(cc + 1) * 128],
                                               pt_sb[c % 2][:, :],
                                               start=first, stop=(kb == E - 1 and c == 3)))
                fo = fin_ord[(h, qc)]
                s_rj.wge(tensor, fo)
                for j in range(4):
                    i = tensor.transpose(r_ps[0:1, j * 128:(j + 1) * 128],
                                         rj_sb[:, j:j + 1], consts["identf"][:, :])
                s_rt.inc(i)
                s_rte.wge(tensor, fo)
                for j in range(4):
                    i = tensor.matmul(r_ps[:, j * 128:(j + 1) * 128], orow_f[:, 0:128],
                                      rt_sb[0:1, j * 128:(j + 1) * 128], start=True, stop=True)
                s_rbc.inc(i)

    @p.block.scalar
    def _(scalar):
        for h in range(nhd):
            sbase, ns = slice_base[h]
            fresh = h in kv_fresh
            if h == 0:
                for hh, n in wvals:
                    scalar.wait_ge(hh, n)
            s_ldq[h % 2].wge(scalar, 16 * (h // 2 + 1))
            s_ldkv[slot_of[kvmap(h)]].wge(scalar, kv_tgt[h])
            si = 0
            for buf, nsl in ([(q_sb[h % 2], NTT)] +
                             ([(k_sb[slot_of[kvmap(h)]], NBK)] if fresh else [])):
                for sl in range(nsl):
                    if sbase + si > 0:
                        s_bmm.wge(scalar, sbase + si)
                    s_bsq.inc(scalar.activation(bsq[:, :], buf[:, sl * 512:(sl + 1) * 512],
                                                AF.Square))
                    si += 1
            s_brow.wge(scalar, h + 1)
            s_bsd.inc(scalar.activation(brow[0:1, NTT + NBK + 1:NTT + NBK + 2],
                                        brow[0:1, NTT + NBK:NTT + NBK + 1], AF.Sqrt))
            for qc in range(NBQ):
                E = EXT[qc]
                if gate is not None:
                    go = gt_ord[(h, qc)]
                    s_gt.wge(scalar, go)
                    if go > 1:
                        s_fin.wge(scalar, go - 1)
                    s_gs.inc(scalar.activation(gate_sb[:, :], g_ps[:, :], AF.Sigmoid,
                                               bias=gate[1][:, 0:1]))
                for kb in range(E):
                    for j in range(4):
                        if kb >= E - 2:
                            s_mk.wge(scalar, mk_ord[(h, qc, kb, j)])
                        else:
                            s_sc.wge(scalar, sc_ord[(h, qc, kb, j)])
                        if kb > 0 and j == 0:
                            s_tp.wge(scalar, tp_ord[(h, qc, kb - 1, 3)])
                        if kb == 0 and j == 0:
                            fo = fin_ord[(h, qc)]
                            if fo > 1:
                                s_fin.wge(scalar, fo - 1)
                            if qc == 0:
                                s_nbe.wge(scalar, h + 1)
                        s_exp.inc(scalar.activation(
                            p_sb[:, j * 512:(j + 1) * 512], sc_ps[j % 2][:, :], AF.Exp,
                            bias=nb_sb[:, 0:1], scale=scale,
                            accum_out=lsl[:, j * 8 + kb:j * 8 + kb + 1]))

    @p.block.vector
    def _(vector):
        for h in range(nhd):
            sbase, ns = slice_base[h]
            if h == 0:
                for hh, n in wvals:
                    vector.wait_ge(hh, n)
            for si in range(ns):
                s_bmm.wge(vector, sbase + si + 1)
                s_bred.inc(vector.reduce_max(brow[0:1, si:si + 1], b_ps[0:1, 0:512],
                                             axis=mybir.AxisListType.X))
            s_bred.wge(vector, sbase + ns)
            vector.reduce_max(brow[0:1, NTT + NBK + 2:NTT + NBK + 3], brow[0:1, 0:NTT],
                              axis=mybir.AxisListType.X)
            s_cmb.inc(vector.reduce_max(brow[0:1, NTT + NBK + 3:NTT + NBK + 4],
                                        brow[0:1, NTT:NTT + NBK], axis=mybir.AxisListType.X))
            s_cmb.wge(vector, s_cmb.n)
            s_brow.inc(vector.tensor_tensor(brow[0:1, NTT + NBK:NTT + NBK + 1],
                                            brow[0:1, NTT + NBK + 2:NTT + NBK + 3],
                                            brow[0:1, NTT + NBK + 3:NTT + NBK + 4], ALU.mult))
            s_bsd.wge(vector, h + 1)
            s_bneg.inc(vector.tensor_scalar(nbrow[0:1, 0:1],
                                            brow[0:1, NTT + NBK + 1:NTT + NBK + 2],
                                            -scale, None, ALU.mult))
            s_nb.wge(vector, h + 1)
            s_nbe.inc(vector.tensor_copy(nb_sb[:, 0:1], b_ps[0:128, 0:1]))
            for qc in range(NBQ):
                E = EXT[qc]
                for kb in range(E):
                    if kb >= E - 2:
                        b2 = kb - (E - 2)
                        for j in range(4):
                            s_sc.wge(vector, sc_ord[(h, qc, kb, j)])
                            s_mk.inc(vector.tensor_tensor(
                                sc_ps[j % 2][:, :], sc_ps[j % 2][:, :],
                                msk_sb[:, mcol(qc, b2, j):mcol(qc, b2, j) + 512], ALU.add))
                    for c in range(4):
                        o = pte_ord[(h, qc, kb, c)]
                        s_tp.wge(vector, tp_ord[(h, qc, kb, c)])
                        if o > 2:
                            s_av.wge(vector, o - 2)
                        s_pte.inc(vector.tensor_copy(pt_sb[c % 2][:, :], pt_ps[c % 2][:, 0:512]))
                fo = fin_ord[(h, qc)]
                s_av.wge(vector, av_ord[(h, qc, E - 1, 3)])
                for j in range(4):
                    i = vector.reduce_sum(ls[:, j:j + 1], lsl[:, j * 8:j * 8 + E],
                                          axis=mybir.AxisListType.X)
                s_aux.inc(i)
                s_aux.wge(vector, s_aux.n)
                for j in range(4):
                    i = vector.reciprocal(rj_sb[:, j:j + 1], ls[:, j:j + 1])
                s_rj.inc(i)
                s_rt.wge(vector, fo)
                s_rte.inc(vector.tensor_copy(rt_sb[0:1, :], r_ps[0:1, 0:512]))
                s_rbc.wge(vector, fo)
                if fo > 2:
                    s_stp[(fo - 1) % 2].wge(vector, 16 * ((fo - 3) // 2 + 1))
                s_aux.inc(vector.tensor_copy(r_sb[:, :], r_ps[:, 0:512]))
                s_aux.wge(vector, s_aux.n)
                i = vector.tensor_tensor(att_sb[(fo - 1) % 2][:, :], att_ps[:, :], r_sb[:, :], ALU.mult)
                if gate is not None:
                    s_aux.inc(i)
                    s_aux.wge(vector, s_aux.n)
                    s_gs.wge(vector, gt_ord[(h, qc)])
                    i = vector.tensor_tensor(att_sb[(fo - 1) % 2][:, :], att_sb[(fo - 1) % 2][:, :],
                                             gate_sb[:, :], ALU.mult)
                s_fin.inc(i)

    @p.block.sync
    def _(g):
        for h in range(nhd):
            for qc in range(NBQ):
                fo = fin_ord[(h, qc)]
                s_fin.wge(g, fo)
                s_stp[(fo - 1) % 2].inc(g.dma_start(out=att_dram[h, :, qc * 512:(qc + 1) * 512],
                                                    in_=att_sb[(fo - 1) % 2][:, :]), 16)
    return SemGroup(s_stp)


def emit_final_wait(p, sems):
    @p.block.gpsimd
    def _(g):
        for s in sems:
            for h, nn in s.pairs():
                g.wait_ge(h, nn)


def build_p1():
    p = Prog().open()
    x = p.din("x", [D, T], F32)
    wq = p.din("wq", [NH, 128, NCH, 128], F32)
    wk = p.din("wk", [KVH0, 128, NCH, 128], F32)
    wv = p.din("wv", [KVH0, 128, NCH, 128], F32)
    bqkv = p.din("bqkv", [128, NH + 2 * KVH0], F32)
    ctab = p.din("ctab", [128, T], F32)
    stab = p.din("stab", [128, T], F32)
    cpt = p.din("c_pt", [128, 128], BF16)
    q_out = p.dout("q_out", [NH, 128, T], BF16)
    k_out = p.dout("k_out", [KVH0, 128, T], BF16)
    v_out = p.dout("v_out", [KVH0, T, 128], BF16)

    consts, s_const = load_consts(p)
    xln = p.sb("xln", [128, NCH * T], BF16)
    xbf = xln
    cos_sb = p.sb("cos", [128, T], F32)
    sin_sb = p.sb("sin", [128, T], F32)
    pt_sb = p.sb("pt", [128, 128], BF16)
    bias_sb = p.sb("bias", [128, NH + 2 * KVH0], F32)
    s_misc = p.sem(kind="hw")

    @p.block.sync
    def _(sync):
        sync.dma_start(out=cos_sb[:, :], in_=ctab[:, :]).then_inc(s_misc.h, 16)
        sync.dma_start(out=sin_sb[:, :], in_=stab[:, :]).then_inc(s_misc.h, 16)
        sync.dma_start(out=pt_sb[:, :], in_=cpt[:, :]).then_inc(s_misc.h, 16)
        sync.dma_start(out=bias_sb[:, :], in_=bqkv[:, :]).then_inc(s_misc.h, 16)
    s_misc.n = 64

    s_x = emit_load_xbf(p, x, xbf)
    s_ln = emit_ln(p, xbf, xln, consts, [s_x, s_const], "l0a")
    rope = (cos_sb, sin_sb, pt_sb)
    sq = emit_proj(p, xln, wq, NH, consts, [s_ln, s_misc], "q",
                   bias_sb=bias_sb, bias_off=0, mode="rope", rope=rope, out_dram=q_out)
    sk = emit_proj(p, xln, wk, KVH0, consts, [s_ln, s_misc, sq], "k",
                   bias_sb=bias_sb, bias_off=NH, mode="rope", rope=rope, out_dram=k_out)
    sv = emit_proj(p, xln, wv, KVH0, consts, [s_ln, s_misc, sk], "v",
                   bias_sb=bias_sb, bias_off=NH + KVH0, mode="vtm", out_dram=v_out)
    emit_final_wait(p, [sq, sk, sv])
    return p


def build_attn_tok(nkv, gate=False, tag="a"):
    p = Prog().open()
    q = p.din("q", [NH, 128, T], BF16)
    kg = p.din("kg", [2, nkv, 128, T], BF16)
    vg = p.din("vg", [2, nkv, T, 128], BF16)
    ptq = p.din("ptq", [128, 16], F32)
    cpos = p.din("cpos", [1, S], F32)
    att = p.dout("att", [NH, 128, T], BF16)
    scale = HD ** -0.5
    consts, s_const = load_consts(p, reg_vals=[scale])
    ptq_sb = p.sb("ptq_sb", [128, 16], F32)
    cpos_sb = p.sb("cpos_sb", [1, S], F32)
    s_mh = p.sem(kind="hw")
    s_msw = p.sem(kind="sw")
    s_misc = SemGroup([s_mh, s_msw])
    gt = None
    if gate:
        gwt = p.din("gwt", [128, 128], F32)
        gbt = p.din("gbt", [128, 1], F32)
        gw_sb = p.sb("gw", [128, 128], BF16)
        gb_sb = p.sb("gb", [128, 1], F32)
        gt = (gw_sb, gb_sb)

    @p.block.sync
    def _(sync):
        s_mh.inc(sync.dma_start(out=ptq_sb[:, :], in_=ptq[:, :]), 16)
        s_mh.inc(sync.dma_start(out=cpos_sb[:, :], in_=cpos[:, :]), 16)

    if gate:
        @p.block.gpsimd
        def _(g):
            s_msw.inc(g.dma_start(out=gw_sb[:, :], in_=gwt[:, :]), 16)
            s_msw.inc(g.dma_start(out=gb_sb[:, :], in_=gbt[:, :]), 16)

    kvmap = (lambda h: h * nkv // NH) if nkv > 1 else (lambda h: 0)
    s_done = emit_attn_tok(p, consts, q, kg, vg, att, NH, kvmap, scale,
                           ptq_sb, cpos_sb, [s_const, s_misc], tag, gate=gt)
    emit_final_wait(p, [s_done])
    return p


def _attn_host_consts(c):
    pos = _core_pos(c).astype(np.float32)
    ptq = np.ascontiguousarray(pos.reshape(16, 128).T)
    cpos = np.ascontiguousarray(np.arange(S, dtype=np.float32).reshape(1, S))
    return ptq, cpos


def _run_attn(builder_args, q_key, k_key, v_key, results, extra=None):
    nkv, gate, tag = builder_args
    p = build_attn_tok(nkv, gate=gate, tag=tag)
    cid = np.ascontiguousarray(np.eye(128, dtype=BF))
    cidf = np.ascontiguousarray(np.eye(128, dtype=np.float32))
    in_maps = []
    for c in range(NCORE):
        pair = (c // 2) * 2
        kgv = np.stack([results[pair][k_key], results[pair + 1][k_key]])
        vgv = np.stack([results[pair][v_key], results[pair + 1][v_key]])
        if kgv.ndim == 3:
            kgv = kgv[:, None]
            vgv = vgv[:, None]
        ptq, cpos = _attn_host_consts(c)
        m = {"q": results[c][q_key], "kg": np.ascontiguousarray(kgv),
             "vg": np.ascontiguousarray(vgv), "ptq": ptq, "cpos": cpos,
             "c_id": cid, "c_idf": cidf}
        if extra:
            m.update(extra)
        in_maps.append(m)
    res = run_bass_kernel_spmd(p.nc, in_maps, core_ids=list(range(NCORE)))
    p.close()
    return res.results


def run_p2(p1_results, masks=None):
    return _run_attn((KVH0, False, "a"), "q_out", "k_out", "v_out", p1_results)


def emit_mask_all(p, consts, ind_dram, s_indd, tag):
    orow_f = consts["ones_row_f"]
    mk_ps = p.pf[2]
    mask_all = p.sb(f"mska_{tag}", [128, 16 * T], U8)
    ind_t = p.sb(f"indt_{tag}", [1, 2 * TT], F32)
    s_mld = [p.sem(kind="hw"), p.sem(kind="hw")]
    s_mb2, s_mev = p.sem(), p.sem()
    indd_n = s_indd.n

    @p.block.sync
    def _(sync):
        s_indd.wge(sync, indd_n)
        for m in range(16):
            for tt in range(NTT):
                it = m * NTT + tt
                if it >= 2:
                    s_mb2.wge(sync, it - 1)
                s_mld[it % 2].inc(sync.dma_start(
                    out=ind_t[0:1, (it % 2) * TT:(it % 2 + 1) * TT],
                    in_=ind_dram[m, tt * TT:(tt + 1) * TT]), 16)

    @p.block.tensor
    def _(tensor):
        for m in range(16):
            for tt in range(NTT):
                it = m * NTT + tt
                s_mld[it % 2].wge(tensor, 16 * (it // 2 + 1))
                if it >= 1:
                    s_mev.wge(tensor, it)
                s_mb2.inc(tensor.matmul(
                    mk_ps[:, :], orow_f[0:1, :],
                    ind_t[0:1, (it % 2) * TT:(it % 2 + 1) * TT],
                    start=True, stop=True))

    @p.block.vector
    def _(vector):
        for m in range(16):
            for tt in range(NTT):
                it = m * NTT + tt
                s_mb2.wge(vector, it + 1)
                s_mev.inc(vector.tensor_copy(
                    mask_all[:, m * T + tt * TT:m * T + (tt + 1) * TT], mk_ps[:, :]))
    return mask_all, s_mev


def build_p3():
    p = Prog().open()
    x = p.din("x", [NCH, 128, T], F32)
    attd = p.din("attd", [NCH, 128, T], BF16)
    wo = p.din("wo", [NCH, 128, NCH, 128], F32)
    bo = p.din("bo", [128, NCH], F32)
    wg = p.din("wg", [128, NCH * 16], F32)
    wqkv = p.din("wqkv", [48, 128, NCH, 128], F32)
    x1_out = p.dout("x1_out", [NCH, 128, T], F32)
    q_out = p.dout("q_out", [16, 128, T], BF16)
    ksh_out = p.dout("ksh_out", [1, 128, T], BF16)
    vsh_out = p.dout("vsh_out", [1, T, 128], BF16)
    ind_dram = p.dint("ind_d", [16, T], F32)

    consts, s_const = load_consts(p)
    ensure_proj_scratch(p)
    bias_sb = p.sb("bias", [128, NCH], F32)
    wg_sb = p.sb("wg_sb", [128, NCH * 16], BF16)
    xln = p.sb("xln", [128, NCH * T], BF16)
    ksh_acc = p.sb("ksh_acc", [128, T], BF16)
    vsh_acc = p.sb("vsh_acc", [128, T], BF16)
    vtm_sb = p.sb("vtm_sb", [128, TT], BF16)
    s_mh = p.sem(kind="hw")
    s_msw = p.sem(kind="sw")
    s_misc = SemGroup([s_mh, s_msw])

    @p.block.sync
    def _(sync):
        s_mh.inc(sync.dma_start(out=bias_sb[:, :], in_=bo[:, :]), 16)

    @p.block.gpsimd
    def _(g):
        s_msw.inc(g.dma_start(out=wg_sb[:, :], in_=wg[:, :]), 16)

    with p.scope():
        s_x1, s_st1 = emit_resadd_ln(p, x, attd, x1_out, xln, consts, [s_const, s_misc],
                                     "p3", wo_dram=wo, bias_sb=bias_sb, bias_off=0)
        s_ln = emit_ln(p, xln, xln, consts, [s_x1, s_st1], "p3")
    ind_fm, top1row, s_ind, s_gre = emit_l0f_gating(p, xln, wg_sb, consts,
                                                    [s_ln, s_misc], "p3")
    s_indf = p.sem(kind="hw")

    @p.block.sync
    def _(sync):
        s_ind.wge(sync, s_ind.n)
        s_indf.inc(sync.dma_start(out=ind_dram[:, :], in_=ind_fm[0:16, :]), 16)

    sq = emit_proj(p, xln, wqkv[0:16], 16, consts, [s_ln, s_ind], "q",
                   mode="plain", out_dram=q_out)
    mask_all, s_mask = emit_mask_all(p, consts, ind_dram, s_indf, "p3")
    s_kacc = emit_proj(p, xln, wqkv[16:32], 16, consts, [s_ln, sq, s_mask], "k",
                       mode="sel", sel=(mask_all, ksh_acc))
    s_vacc = emit_proj(p, xln, wqkv[32:48], 16, consts, [s_ln, s_kacc], "v",
                       mode="sel", sel=(mask_all, vsh_acc))

    s_kst = p.sem(kind="sw")

    @p.block.gpsimd
    def _(g):
        s_kacc.wge(g, s_kacc.n)
        s_kst.inc(g.dma_start(out=ksh_out[0, :, :], in_=ksh_acc[:, :]), 16)

    t1_ps = p.pf[3]
    vt_ps = p.pb[0]
    s_t1b, s_t1m, s_vt, s_vte = (p.sem() for _ in range(4))
    s_vst = p.sem(kind="hw")
    orow_f = consts["ones_row_f"]
    ident = consts["ident"]

    @p.block.tensor
    def _(tensor):
        s_vacc.wge(tensor, s_vacc.n)
        s_gre.wge(tensor, s_gre.n)
        for tt in range(NTT):
            if tt > 0:
                s_t1m.wge(tensor, tt)
            s_t1b.inc(tensor.matmul(t1_ps[:, :], orow_f[0:1, :],
                                    top1row[0:1, tt * TT:(tt + 1) * TT], start=True, stop=True))
            s_t1m.wge(tensor, tt + 1)
            if tt > 0:
                s_vte.wge(tensor, 4 * tt)
            for sg in range(4):
                i = tensor.transpose(vt_ps[:, sg * 128:(sg + 1) * 128],
                                     vsh_acc[:, tt * TT + sg * 128:tt * TT + (sg + 1) * 128],
                                     ident[:, :])
            s_vt.inc(i)

    @p.block.vector
    def _(vector):
        for tt in range(NTT):
            s_t1b.wge(vector, tt + 1)
            s_t1m.inc(vector.tensor_tensor(vsh_acc[:, tt * TT:(tt + 1) * TT],
                                           vsh_acc[:, tt * TT:(tt + 1) * TT],
                                           t1_ps[:, :], ALU.mult))
            s_vt.wge(vector, tt + 1)
            if tt > 0:
                s_vst.wge(vector, 16 * tt)
            for sg in range(4):
                i = vector.tensor_copy(vtm_sb[:, sg * 128:(sg + 1) * 128],
                                       vt_ps[:, sg * 128:(sg + 1) * 128])
                s_vte.inc(i)

    @p.block.sync
    def _(g):
        for tt in range(NTT):
            s_vte.wge(g, 4 * (tt + 1))
            s_vst.inc(g.dma_start(
                out=vsh_out[0, tt * TT:(tt + 1) * TT, :].rearrange("(sg pp) d -> pp sg d", pp=128),
                in_=vtm_sb[:, :].rearrange("p (sg d) -> p sg d", sg=4)), 16)

    emit_final_wait(p, [s_st1, sq, s_kst, s_vst])
    return p


def run_p3(inputs, p2_results, xs_list):
    p = build_p3()
    Wo = np.asarray(inputs["L0_attn_Wo"])
    bo = np.asarray(inputs["L0_attn_bo"]).reshape(NCH, 128).T
    Wg = np.asarray(inputs["L0_ffn_W_gate"])
    wg_host = np.ascontiguousarray(Wg.reshape(16, 16, 128).transpose(2, 1, 0).reshape(128, 256).astype(np.float32))
    wo_t = _tile_lhsT(Wo)
    wqkv_t = _tile_lhsT(np.asarray(inputs["L0_ffn_W_qkv"]))
    cid = np.ascontiguousarray(np.eye(128, dtype=BF))
    cidf = np.ascontiguousarray(np.eye(128, dtype=np.float32))
    in_maps = []
    for c in range(NCORE):
        attd = p2_results[c]["att"]
        in_maps.append({
            "x": np.ascontiguousarray(xs_list[c].reshape(NCH, 128, T)),
            "attd": attd, "wo": wo_t, "bo": np.ascontiguousarray(bo.astype(np.float32)),
            "wg": wg_host, "wqkv": wqkv_t, "c_id": cid, "c_idf": cidf,
        })
    res = run_bass_kernel_spmd(p.nc, in_maps, core_ids=list(range(NCORE)))
    p.close()
    return res.results



def build_p5():
    p = Prog().open()
    x = p.din("x", [NCH, 128, T], F32)
    attd = p.din("attd", [NCH, 128, T], BF16)
    wq = p.din("wq", [NH, 128, NCH, 128], F32)
    wk = p.din("wk", [NH, 128, NCH, 128], F32)
    wv = p.din("wv", [NH, 128, NCH, 128], F32)
    x2_out = p.dout("x2_out", [NCH, 128, T], F32)
    q_out = p.dout("q_out", [NH, 128, T], BF16)
    k_out = p.dout("k_out", [NH, 128, T], BF16)
    v_out = p.dout("v_out", [NH, T, 128], BF16)

    consts, s_const = load_consts(p)
    ensure_proj_scratch(p)
    xln = p.sb("xln", [128, NCH * T], BF16)
    with p.scope():
        s_x1, s_st1 = emit_resadd_ln(p, x, attd, x2_out, xln, consts, [s_const], "p5")
        s_ln = emit_ln(p, xln, xln, consts, [s_x1, s_st1], "p5")
    sq = emit_proj(p, xln, wq, NH, consts, [s_ln], "q", mode="plain", out_dram=q_out)
    sk = emit_proj(p, xln, wk, NH, consts, [s_ln, sq], "k", mode="plain", out_dram=k_out)
    sv = emit_proj(p, xln, wv, NH, consts, [s_ln, sk], "v", mode="vtm", out_dram=v_out)
    emit_final_wait(p, [s_st1, sq, sk, sv])
    return p


def run_p5(inputs, p4_results, p3_results):
    p = build_p5()
    wq_t = _tile_lhsT(np.asarray(inputs["L1_attn_q_w"]))
    wk_t = _tile_lhsT(np.asarray(inputs["L1_attn_k_w"]))
    wv_t = _tile_lhsT(np.asarray(inputs["L1_attn_v_w"]))
    cid = np.ascontiguousarray(np.eye(128, dtype=BF))
    cidf = np.ascontiguousarray(np.eye(128, dtype=np.float32))
    in_maps = []
    for c in range(NCORE):
        attd = p4_results[c]["att"]
        in_maps.append({
            "x": p3_results[c]["x1_out"], "attd": attd,
            "wq": wq_t, "wk": wk_t, "wv": wv_t, "c_id": cid, "c_idf": cidf,
        })
    res = run_bass_kernel_spmd(p.nc, in_maps, core_ids=list(range(NCORE)))
    p.close()
    return res.results


def run_p4(p3_results):
    out = []
    for c in range(NCORE):
        r = dict(p3_results[c])
        r["ksh3"] = r["ksh_out"][0]
        r["vsh3"] = r["vsh_out"][0]
        out.append(r)
    return _run_attn((1, False, "m"), "q_out", "ksh3", "vsh3", out)


def run_p6(inputs, p5_results):
    gwt = np.ascontiguousarray(np.asarray(inputs["L1_attn_gate_w"]).T.astype(np.float32))
    gbt = np.ascontiguousarray(np.asarray(inputs["L1_attn_gate_b"]).reshape(128, 1).astype(np.float32))
    return _run_attn((NH, True, "g"), "q_out", "k_out", "v_out", p5_results,
                     extra={"gwt": gwt, "gbt": gbt})


def emit_dequant(p, consts, pk_dram, sc_dram, out_dram, nlm, nw, tag, waits=(), bank=0):
    ngrp = nw * 8 // 128
    SPW = 256 if nw % 256 == 0 else 176
    assert nw % SPW == 0 and SPW % 16 == 0
    nsp = nw // SPW
    gps = SPW * 8 // 128
    identf = consts["identf"]
    pk = p.sb(f"dqp_{tag}", [128, nw], I32)
    an = p.sb(f"dqa_{tag}", [128, 8 * SPW], I32)
    b2 = p.sb(f"dqb_{tag}", [128, 8 * SPW], I32)
    f1 = p.sb(f"dqf1_{tag}", [128, 8 * SPW], F32)
    f2 = p.sb(f"dqf2_{tag}", [128, 8 * SPW], F32)
    f3 = p.sb(f"dqf3_{tag}", [128, 8 * SPW], F32)
    f4 = p.sb(f"dqf4_{tag}", [128, 8 * SPW], F32)
    valh = p.sb(f"dqv_{tag}", [128, 8 * SPW], F32)
    sc16 = p.sb(f"dqs_{tag}", [128, ngrp], F16)
    sc32 = p.sb(f"dqs3_{tag}", [128, ngrp], F32)
    asm = p.sb(f"dqo_{tag}", [128, ngrp * 128], BF16)
    tp_ps = p.pf[bank]
    s_dec, s_val, s_scl, s_tp, s_ev, s_sc32 = (p.sem() for _ in range(6))
    s_ld = p.sem(kind="hw")
    s_st = p.sem(kind="sw")
    wvals = _wpairs(waits)

    @p.block.sync
    def _(sync):
        for lm in range(nlm):
            if lm >= 1:
                s_dec.wge(sync, lm * nsp)
            s_ld.inc(sync.dma_start(out=pk[:, :], in_=pk_dram[lm, :, :]), 16)
            if lm >= 1:
                s_sc32.wge(sync, lm)
            s_ld.inc(sync.dma_start(out=sc16[:, :], in_=sc_dram[lm, :, :]), 16)

    @p.block.vector
    def _(vector):
        for lm in range(nlm):
            if lm == 0:
                for hh, n in wvals:
                    vector.wait_ge(hh, n)
            s_ld.wge(vector, 32 * (lm + 1))
            s_sc32.inc(vector.tensor_copy(sc32[:, :], sc16[:, :]))
            for sp in range(nsp):
                if lm + sp > 0:
                    s_tp.wge(vector, (lm * nsp + sp - 1) * gps + gps)
                w0 = sp * SPW
                J = SPW

                def _stage(emit1):
                    for j in range(8):
                        i = emit1(j)
                    s_val.inc(i)
                    s_val.wge(vector, s_val.n)

                _stage(lambda j: vector.tensor_scalar(an[:, j * J:(j + 1) * J], pk[:, w0:w0 + SPW],
                                                      4 * j, 15, ALU.logical_shift_right, ALU.bitwise_and))
                _stage(lambda j: vector.tensor_scalar(b2[:, j * J:(j + 1) * J], an[:, j * J:(j + 1) * J],
                                                      7, None, ALU.bitwise_and))
                _stage(lambda j: vector.tensor_copy(f1[:, j * J:(j + 1) * J], b2[:, j * J:(j + 1) * J]))
                _stage(lambda j: vector.tensor_scalar(b2[:, j * J:(j + 1) * J], an[:, j * J:(j + 1) * J],
                                                      3, None, ALU.logical_shift_right))
                _stage(lambda j: vector.tensor_copy(f2[:, j * J:(j + 1) * J], b2[:, j * J:(j + 1) * J]))
                _stage(lambda j: vector.tensor_scalar(f2[:, j * J:(j + 1) * J], f2[:, j * J:(j + 1) * J],
                                                      -2.0, 1.0, ALU.mult, ALU.add))
                _stage(lambda j: vector.tensor_scalar(f3[:, j * J:(j + 1) * J], f1[:, j * J:(j + 1) * J],
                                                      -4.0, 0.0, ALU.add, ALU.max))
                _stage(lambda j: vector.tensor_scalar(f4[:, j * J:(j + 1) * J], f1[:, j * J:(j + 1) * J],
                                                      -6.0, 0.0, ALU.add, ALU.max))
                _stage(lambda j: vector.tensor_tensor(f3[:, j * J:(j + 1) * J], f1[:, j * J:(j + 1) * J],
                                                      f3[:, j * J:(j + 1) * J], ALU.add))
                _stage(lambda j: vector.scalar_tensor_tensor(f3[:, j * J:(j + 1) * J], f3[:, j * J:(j + 1) * J],
                                                             0.5, f4[:, j * J:(j + 1) * J], ALU.mult, ALU.add))
                vh = valh[:, :].rearrange("p (w j) -> p j w", j=8)
                for j in range(8):
                    i = vector.tensor_tensor(vh[:, j, :], f3[:, j * J:(j + 1) * J],
                                             f2[:, j * J:(j + 1) * J], ALU.mult)
                s_dec.inc(i)
                s_dec.wge(vector, s_dec.n)
                for gl in range(gps):
                    g = sp * gps + gl
                    i = vector.tensor_scalar(valh[:, gl * 128:(gl + 1) * 128],
                                             valh[:, gl * 128:(gl + 1) * 128],
                                             sc32[:, g:g + 1], None, ALU.mult)
                s_scl.inc(i)
                for gl in range(gps):
                    g = sp * gps + gl
                    if lm >= 1 and sp == 0 and gl == 0:
                        s_st.wge(vector, 16 * lm)
                    s_tp.wge(vector, (lm * nsp + sp) * gps + gl + 1)
                    s_ev.inc(vector.tensor_copy(asm[:, g * 128:(g + 1) * 128], tp_ps[:, 0:128]))

    @p.block.tensor
    def _(tensor):
        for lm in range(nlm):
            if lm == 0:
                for hh, n in wvals:
                    tensor.wait_ge(hh, n)
            for sp in range(nsp):
                s_scl.wge(tensor, lm * nsp + sp + 1)
                for gl in range(gps):
                    o = (lm * nsp + sp) * gps + gl
                    if o >= 1:
                        s_ev.wge(tensor, o)
                    s_tp.inc(tensor.transpose(tp_ps[:, 0:128],
                                              valh[:, gl * 128:(gl + 1) * 128], identf[:, :]))

    @p.block.gpsimd
    def _(g):
        for lm in range(nlm):
            s_ev.wge(g, lm * ngrp + ngrp)
            s_st.inc(g.dma_start(out=out_dram[lm, :, :], in_=asm[:, :]), 16)
    return s_st


def build_p7a():
    p = Prog().open()
    gup = p.din("gup", [11, 128, 256], I32)
    gus = p.din("gus", [11, 128, 16], F16)
    dnp = p.din("dnp", [2, 128, 704], I32)
    dns = p.din("dns", [2, 128, 44], F16)
    wgu_mine = p.dout("wgu_mine", [11, 128, 16 * 128], BF16)
    down_mine = p.dout("down_mine", [2, 128, 44 * 128], BF16)
    consts, s_const = load_consts(p)

    s1 = emit_dequant(p, consts, gup, gus, wgu_mine, 11, 256, "gu", waits=[s_const], bank=0)
    s2 = emit_dequant(p, consts, dnp, dns, down_mine, 2, 704, "dn", waits=[s_const], bank=1)
    emit_final_wait(p, [s1, s2])
    return p


def run_p7a(inputs):
    p = build_p7a()
    gup = np.asarray(inputs["L1_ffn_gate_up_packed"])
    gus = np.asarray(inputs["L1_ffn_gate_up_scales"])
    dnp = np.asarray(inputs["L1_ffn_down_packed"])
    dns = np.asarray(inputs["L1_ffn_down_scales"])
    cid = np.ascontiguousarray(np.eye(128, dtype=BF))
    cidf = np.ascontiguousarray(np.eye(128, dtype=np.float32))
    in_maps = []
    for c in range(NCORE):
        r0, r1 = c * 1408, (c + 1) * 1408
        d0, d1 = c * 256, (c + 1) * 256
        in_maps.append({
            "gup": np.ascontiguousarray(gup[r0:r1].reshape(11, 128, 256)),
            "gus": np.ascontiguousarray(gus.T[r0:r1].reshape(11, 128, 16)),
            "dnp": np.ascontiguousarray(dnp[d0:d1].reshape(2, 128, 704)),
            "dns": np.ascontiguousarray(dns.T[d0:d1].reshape(2, 128, 44)),
            "c_id": cid, "c_idf": cidf,
        })
    res = run_bass_kernel_spmd(p.nc, in_maps, core_ids=list(range(NCORE)))
    p.close()
    return res.results


def emit_l1ffn(p, consts, xln, wgu, wdn, h_dram, xout, s_st1, s_ln, extra_waits):
    oc = consts["ones_col"]
    orow_f = consts["ones_row_f"]

    wbA = [p.sb(f"wA{i}", [128, NCH * 128], BF16) for i in range(4)]
    sil_sb = p.sb("sil_sb", [128, TT], F32)
    hb = [p.sb(f"hb{i}", [128, TT], BF16) for i in range(2)]
    wdb = [p.sb(f"wd{i}", [128, 5 * 128], BF16) for i in range(2)]
    hsb = [p.sb(f"hs{i}", [128, TT], BF16) for i in range(2)]
    y_sb = p.sb("y_sb", [128, NCH * TT], BF16)
    ysq = p.sb("ysq", [128, TT], BF16)
    x3t = p.sb("x3t", [128, NCH * TT], F32)
    rrow = p.sb("rrow", [1, 3 * TT], F32)
    ytmp = p.sb("ytmp", [128, TT], F32)
    xoc = [p.sb(f"xoc{i}", [128, TT], F32) for i in range(2)]
    OG = [(0, 5), (5, 10), (10, 15), (15, 16)]
    yev_cum = []
    tot = 0
    for g in range(4 * NTT):
        yev_cum.append(tot)
        tot += OG[g % 4][1] - OG[g % 4][0]

    (s_gu, s_sil, s_h, s_ymm, s_yev, s_ysq, s_rms,
     s_msr, s_sd, s_rr, s_rb, s_fin, s_aux) = (p.sem() for _ in range(13))
    s_x3 = p.sem(kind="sw")
    s_wA = [p.sem(kind="sw"), p.sem(kind="sw")]
    s_hld = [p.sem(kind="sw"), p.sem(kind="sw")]
    s_wd = [p.sem(kind="sw"), p.sem(kind="sw")]
    s_hst = [p.sem(kind="hw"), p.sem(kind="hw")]
    s_fst = [p.sem(kind="hw"), p.sem(kind="hw")]

    @p.block.gpsimd
    def _(g):
        for hh, nn in _wpairs(extra_waits):
            g.wait_ge(hh, nn)
        for tt in range(NTT):
            for ip in range(44):
                it = tt * 44 + ip
                if it >= 2:
                    s_gu.wge(g, it - 1)
                s_wA[it % 2].inc(g.dma_start(out=wbA[(it % 2) * 2][:, :], in_=wgu[ip, :, :]), 16)
                s_wA[it % 2].inc(g.dma_start(out=wbA[(it % 2) * 2 + 1][:, :], in_=wgu[44 + ip, :, :]), 16)
            for ogi, (o0, o1) in enumerate(OG):
                if ogi == 0:
                    s_hst[0].wge(g, 16 * 22 * (tt + 1))
                    s_hst[1].wge(g, 16 * 22 * (tt + 1))
                for i in range(44):
                    ib = (tt * 4 + ogi) * 44 + i
                    if ib >= 2:
                        s_ymm.wge(g, ib - 1)
                    s_hld[ib % 2].inc(g.dma_start(out=hsb[ib % 2][:, :],
                                                  in_=h_dram[i, :, tt * TT:(tt + 1) * TT]), 16)
                    s_wd[ib % 2].inc(g.dma_start(
                        out=wdb[ib % 2][:, 0:(o1 - o0) * 128].rearrange("p (o mm) -> p o mm", o=o1 - o0),
                        in_=wdn[o0:o1, :, i * 128:(i + 1) * 128].rearrange("o p mm -> p o mm")), 16)
            if tt >= 1:
                s_fin.wge(g, tt * NCH)
            for hh, nn in s_st1.pairs():
                g.wait_ge(hh, nn)
            s_x3.inc(g.dma_start(out=x3t[:, :].rearrange("p (c t) -> p c t", c=NCH),
                                 in_=xout[0:NCH, :, tt * TT:(tt + 1) * TT].rearrange("c p t -> p c t")), 16)

    @p.block.tensor
    def _(tensor):
        for tt in range(NTT):
            if tt == 0:
                s_ln.wge(tensor, s_ln.n)
            for ip in range(44):
                it = tt * 44 + ip
                s_wA[it % 2].wge(tensor, 32 * (it // 2 + 1))
                if it >= 1:
                    s_sil.wge(tensor, it)
                    s_h.wge(tensor, it)
                if ip == 0 and tt >= 1:
                    s_aux.wge(tensor, tt * NCH)
                for c in range(NCH):
                    tensor.matmul(p.pf[0][:, :], wbA[(it % 2) * 2][:, c * 128:(c + 1) * 128],
                                  xln[:, c * T + tt * TT:c * T + tt * TT + TT],
                                  start=(c == 0), stop=(c == NCH - 1))
                for c in range(NCH):
                    i = tensor.matmul(p.pf[1][:, :], wbA[(it % 2) * 2 + 1][:, c * 128:(c + 1) * 128],
                                      xln[:, c * T + tt * TT:c * T + tt * TT + TT],
                                      start=(c == 0), stop=(c == NCH - 1))
                s_gu.inc(i)
            s_h.wge(tensor, 44 * (tt + 1))
            for ogi, (o0, o1) in enumerate(OG):
                gG = tt * 4 + ogi
                for i in range(44):
                    ib = gG * 44 + i
                    s_hld[ib % 2].wge(tensor, 16 * (ib // 2 + 1))
                    s_wd[ib % 2].wge(tensor, 16 * (ib // 2 + 1))
                    if i == 0 and gG >= 1:
                        s_yev.wge(tensor, yev_cum[gG])
                    for ol in range(o1 - o0):
                        ii = tensor.matmul(p.pf[ol][:, :], wdb[ib % 2][:, ol * 128:(ol + 1) * 128],
                                           hsb[ib % 2][:, :], start=(i == 0), stop=(i == 43))
                    s_ymm.inc(ii)
                for ol in range(o1 - o0):
                    o = o0 + ol
                    s_ysq.wge(tensor, tt * NCH + o + 1)
                    i = tensor.matmul(p.pf[5][0:1, 0:TT], oc[:, :], ysq[:, :],
                                      start=(o == 0), stop=(o == NCH - 1))
                    s_rms.inc(i)
            s_rr.wge(tensor, tt + 1)
            s_rb.inc(tensor.matmul(p.pf[0][:, :], orow_f[0:1, :], rrow[0:1, 2 * TT:3 * TT],
                                   start=True, stop=True))

    @p.block.scalar
    def _(scalar):
        for tt in range(NTT):
            for ip in range(44):
                it = tt * 44 + ip
                s_gu.wge(scalar, it + 1)
                if it >= 1:
                    s_h.wge(scalar, it)
                s_sil.inc(scalar.activation(sil_sb[:, :], p.pf[0][:, :], AF.Silu))
            for o in range(NCH):
                s_yev.wge(scalar, tt * NCH + o + 1)
                if tt * NCH + o >= 1:
                    s_rms.wge(scalar, tt * NCH + o)
                s_ysq.inc(scalar.activation(ysq[:, :], y_sb[:, o * TT:(o + 1) * TT], AF.Square))
            s_msr.wge(scalar, tt + 1)
            s_sd.inc(scalar.activation(rrow[0:1, TT:2 * TT], rrow[0:1, 0:TT], AF.Sqrt))

    @p.block.vector
    def _(vector):
        for tt in range(NTT):
            for ip in range(44):
                it = tt * 44 + ip
                s_sil.wge(vector, it + 1)
                if it >= 2:
                    s_hst[it % 2].wge(vector, 16 * (it // 2))
                s_h.inc(vector.tensor_tensor(hb[it % 2][:, :], sil_sb[:, :], p.pf[1][:, :], ALU.mult))
            for ogi, (o0, o1) in enumerate(OG):
                gG = tt * 4 + ogi
                s_ymm.wge(vector, gG * 44 + 44)
                for ol in range(o1 - o0):
                    o = o0 + ol
                    if tt >= 1 and o == 0:
                        s_ysq.wge(vector, tt * NCH)
                    s_yev.inc(vector.tensor_copy(y_sb[:, o * TT:(o + 1) * TT], p.pf[ol][:, :]))
            s_rms.wge(vector, tt * NCH + NCH)
            s_msr.inc(vector.tensor_scalar(rrow[0:1, 0:TT], p.pf[5][0:1, 0:TT],
                                           1.0 / D, 1e-6, ALU.mult, ALU.add))
            s_sd.wge(vector, tt + 1)
            s_rr.inc(vector.reciprocal(rrow[0:1, 2 * TT:3 * TT], rrow[0:1, TT:2 * TT]))
            s_rb.wge(vector, tt + 1)
            s_x3.wge(vector, 16 * (tt + 1))
            for o in range(NCH):
                if tt * NCH + o >= 2:
                    s_fst[(tt * NCH + o) % 2].wge(vector, 16 * ((tt * NCH + o) // 2))
                if tt * NCH + o >= 1:
                    s_fin.wge(vector, tt * NCH + o)
                s_aux.inc(vector.tensor_tensor(ytmp[:, :], y_sb[:, o * TT:(o + 1) * TT],
                                               p.pf[0][:, :], ALU.mult))
                s_aux.wge(vector, s_aux.n)
                s_fin.inc(vector.tensor_tensor(xoc[(tt * NCH + o) % 2][:, :], ytmp[:, :],
                                               x3t[:, o * TT:(o + 1) * TT], ALU.add))

    @p.block.sync
    def _(g):
        for tt in range(NTT):
            for ip in range(44):
                it = tt * 44 + ip
                s_h.wge(g, it + 1)
                s_hst[it % 2].inc(g.dma_start(out=h_dram[ip, :, tt * TT:(tt + 1) * TT],
                                              in_=hb[it % 2][:, :]), 16)
            for o in range(NCH):
                s_fin.wge(g, tt * NCH + o + 1)
                s_fst[(tt * NCH + o) % 2].inc(g.dma_start(out=xout[o, :, tt * TT:(tt + 1) * TT],
                                                          in_=xoc[(tt * NCH + o) % 2][:, :]), 16)

    return SemGroup(s_fst)


def build_p7b():
    p = Prog().open()
    x = p.din("x", [NCH, 128, T], F32)
    attd = p.din("attd", [NCH, 128, T], BF16)
    wo = p.din("wo", [NCH, 128, NCH, 128], F32)
    bo = p.din("bo", [128, NCH], F32)
    wgu = p.din("wgu", [88, 128, 16 * 128], BF16)
    wdn = p.din("wdn", [16, 128, 44 * 128], BF16)
    xout = p.dout("xout", [NCH, 128, T], F32)
    h_dram = p.dint("h_d", [44, 128, T], BF16)

    consts, s_const = load_consts(p)
    ensure_proj_scratch(p)
    oc = consts["ones_col"]
    orow_f = consts["ones_row_f"]
    bias_sb = p.sb("bias", [128, NCH], F32)
    xln = p.sb("xln", [128, NCH * T], BF16)
    s_misc = p.sem(kind="hw")

    @p.block.sync
    def _(sync):
        s_misc.inc(sync.dma_start(out=bias_sb[:, :], in_=bo[:, :]), 16)

    with p.scope():
        s_x1, s_st1 = emit_resadd_ln(p, x, attd, xout, xln, consts, [s_const, s_misc],
                                     "p7", wo_dram=wo, bias_sb=bias_sb, bias_off=0)
        s_ln = emit_ln(p, xln, xln, consts, [s_x1, s_st1], "p7")
    s_fst = emit_l1ffn(p, consts, xln, wgu, wdn, h_dram, xout, s_st1, s_ln, [])
    emit_final_wait(p, [SemGroup(s_fst)])
    return p


def run_p7b(inputs, p6_results, p5_results, p7a_results):
    p = build_p7b()
    wo_t = _tile_lhsT(np.asarray(inputs["L1_attn_o_w"]))
    bo = np.zeros((128, NCH), np.float32)
    wgu_all = np.concatenate([p7a_results[c]["wgu_mine"] for c in range(NCORE)], axis=0)
    wdn_all = np.concatenate([p7a_results[c]["down_mine"] for c in range(NCORE)], axis=0)
    cid = np.ascontiguousarray(np.eye(128, dtype=BF))
    cidf = np.ascontiguousarray(np.eye(128, dtype=np.float32))
    in_maps = []
    for c in range(NCORE):
        attd = p6_results[c]["att"]
        in_maps.append({
            "x": p5_results[c]["x2_out"], "attd": attd, "wo": wo_t, "bo": bo,
            "wgu": wgu_all, "wdn": wdn_all, "c_id": cid, "c_idf": cidf,
        })
    res = run_bass_kernel_spmd(p.nc, in_maps, core_ids=list(range(NCORE)))
    p.close()
    return res.results


def _prep_shared(inputs):
    d = {}
    d["c_id"] = np.ascontiguousarray(np.eye(128, dtype=BF))
    d["c_idf"] = np.ascontiguousarray(np.eye(128, dtype=np.float32))
    d["c_pt"] = _rope_P()
    cos, sin = _rope_tables()
    d["rope_cos"], d["rope_sin"] = cos, sin
    d["masks"] = _diag_masks()
    d["wq0"] = _tile_lhsT(np.asarray(inputs["L0_attn_Wq"]))
    d["wk0"] = _tile_lhsT(np.asarray(inputs["L0_attn_Wk"]))
    d["wv0"] = _tile_lhsT(np.asarray(inputs["L0_attn_Wv"]))
    bq = np.asarray(inputs["L0_attn_bq"]).reshape(NH, 128).T
    bk = np.asarray(inputs["L0_attn_bk"]).reshape(KVH0, 128).T
    bv = np.asarray(inputs["L0_attn_bv"]).reshape(KVH0, 128).T
    d["bqkv0"] = np.ascontiguousarray(np.concatenate([bq, bk, bv], axis=1).astype(np.float32))
    return d


def _x_shards(x):
    out = []
    for c in range(NCORE):
        s = c // 2
        out.append(np.ascontiguousarray(x[s, _core_pos(c), :].T))
    return out


def run_p1(inputs):
    sh = _prep_shared(inputs)
    xs = _x_shards(np.asarray(inputs["L0_attn_x"]))
    p = build_p1()
    in_maps = []
    for c in range(NCORE):
        h = c % 2
        in_maps.append({
            "x": xs[c],
            "wq": sh["wq0"], "wk": sh["wk0"], "wv": sh["wv0"],
            "bqkv": sh["bqkv0"],
            "ctab": np.ascontiguousarray(sh["rope_cos"][:, _core_pos(c)]),
            "stab": np.ascontiguousarray(sh["rope_sin"][:, _core_pos(c)]),
            "c_pt": sh["c_pt"], "c_id": sh["c_id"], "c_idf": sh["c_idf"],
        })
    res = run_bass_kernel_spmd(p.nc, in_maps, core_ids=list(range(NCORE)))
    p.close()
    return res.results


PAIRS = [[0, 1], [2, 3], [4, 5], [6, 7]]
ALL8 = [list(range(NCORE))]


def emit_collective(p, ins_ap, outs_ap, groups, wait_sems, tag):
    s_c = p.sem(kind="col")
    wv = _wpairs(wait_sems)

    @p.block.gpsimd
    def _(g):
        for h, n in wv:
            g.wait_ge(h, n)
        s_c.inc(g.collective_compute("AllGather", ALU.bypass, replica_groups=groups,
                                     ins=[ins_ap], outs=[outs_ap]))
    return s_c


def build_fused(upto=8, nhd26=8):
    global UPTO26_NHD
    UPTO26_NHD = nhd26
    p = Prog().open()
    x = p.din("x", [NCH, 128, T], F32)
    wq0 = p.din("wq0", [NH, 128, NCH, 128], F32)
    wk0 = p.din("wk0", [KVH0, 128, NCH, 128], F32)
    wv0 = p.din("wv0", [KVH0, 128, NCH, 128], F32)
    bqkv = p.din("bqkv", [128, NH + 2 * KVH0], F32)
    ctab = p.din("ctab", [128, T], F32)
    stab = p.din("stab", [128, T], F32)
    cpt = p.din("c_pt", [128, 128], BF16)
    wo0 = p.din("wo0", [NCH, 128, NCH, 128], F32)
    bo0 = p.din("bo0", [128, NCH], F32)
    wg = p.din("wg", [128, NCH * 16], F32)
    wqkv = p.din("wqkv", [48, 128, NCH, 128], F32)
    wq1 = p.din("wq1", [NH, 128, NCH, 128], F32)
    wk1 = p.din("wk1", [NH, 128, NCH, 128], F32)
    wv1 = p.din("wv1", [NH, 128, NCH, 128], F32)
    gwt = p.din("gwt", [128, 128], F32)
    gbt = p.din("gbt", [128, 1], F32)
    ow1 = p.din("ow1", [NCH, 128, NCH, 128], F32)
    bz = p.din("bz", [128, NCH], F32)
    gup = p.din("gup", [11, 128, 256], I32)
    gus = p.din("gus", [11, 128, 16], F16)
    dnp = p.din("dnp", [2, 128, 704], I32)
    dns = p.din("dns", [2, 128, 44], F16)
    ptq = p.din("ptq", [128, 16], F32)
    cpos = p.din("cpos", [1, S], F32)
    xout = p.dout("xout", [NCH, 128, T], F32)

    q0 = p.dint("q0", [NH, 128, T], BF16)
    k0c = p.dint("k0c", [1, KVH0 * 128 * T], BF16)
    v0c = p.dint("v0c", [1, KVH0 * T * 128], BF16)
    k0g = p.dint("k0g", [2, KVH0 * 128 * T], BF16)
    v0g = p.dint("v0g", [2, KVH0 * T * 128], BF16)
    att0 = p.dint("att0", [NH, 128, T], BF16)
    x1 = p.dint("x1", [NCH, 128, T], F32)
    ind_dram = p.dint("ind_d", [16, T], F32)
    qf0 = p.dint("qf0", [NH, 128, T], BF16)
    kshc = p.dint("kshc", [1, 128 * T], BF16)
    vshc = p.dint("vshc", [1, T * 128], BF16)
    kshg = p.dint("kshg", [2, 128 * T], BF16)
    vshg = p.dint("vshg", [2, T * 128], BF16)
    attm = p.dint("attm", [NH, 128, T], BF16)
    x2 = p.dint("x2", [NCH, 128, T], F32)
    q1 = p.dint("q1", [NH, 128, T], BF16)
    k1c = p.dint("k1c", [1, NH * 128 * T], BF16)
    v1c = p.dint("v1c", [1, NH * T * 128], BF16)
    k1g = p.dint("k1g", [2, NH * 128 * T], BF16)
    v1g = p.dint("v1g", [2, NH * T * 128], BF16)
    att1 = p.dint("att1", [NH, 128, T], BF16)
    wgu_mine = p.dint("wgu_mine", [11, 128 * 16 * 128], BF16)
    down_mine = p.dint("down_mine", [2, 128 * 44 * 128], BF16)
    wgu_all = p.dint("wgu_all", [88, 128 * 16 * 128], BF16)
    down_all = p.dint("down_all", [16, 128 * 44 * 128], BF16)
    h_dram = p.dint("h_d", [44, 128, T], BF16)

    scale = HD ** -0.5
    consts, s_const = load_consts(p, reg_vals=[scale])
    ensure_proj_scratch(p)
    ptq_sb = p.sb("ptq_sb", [128, 16], F32)
    cpos_sb = p.sb("cpos_sb", [1, S], F32)
    bias_sb = p.sb("bias_all", [128, NH + 2 * KVH0], F32)
    bo_sb = p.sb("bo_sb", [128, NCH], F32)
    bz_sb = p.sb("bz_sb", [128, NCH], F32)
    wg_sb = p.sb("wg_sb", [128, NCH * 16], BF16)
    gw_sb = p.sb("gw_sb", [128, 128], BF16)
    gb_sb = p.sb("gb_sb", [128, 1], F32)
    s_mh = p.sem(kind="hw")
    s_msw = p.sem(kind="sw")
    s_misc = SemGroup([s_mh, s_msw])

    @p.block.sync
    def _(sync):
        s_mh.inc(sync.dma_start(out=ptq_sb[:, :], in_=ptq[:, :]), 16)
        s_mh.inc(sync.dma_start(out=cpos_sb[:, :], in_=cpos[:, :]), 16)
        s_mh.inc(sync.dma_start(out=bias_sb[:, :], in_=bqkv[:, :]), 16)
        s_mh.inc(sync.dma_start(out=bo_sb[:, :], in_=bo0[:, :]), 16)
        s_mh.inc(sync.dma_start(out=bz_sb[:, :], in_=bz[:, :]), 16)

    @p.block.gpsimd
    def _(g):
        s_msw.inc(g.dma_start(out=wg_sb[:, :], in_=wg[:, :]), 16)
        s_msw.inc(g.dma_start(out=gw_sb[:, :], in_=gwt[:, :]), 16)
        s_msw.inc(g.dma_start(out=gb_sb[:, :], in_=gbt[:, :]), 16)

    k0c_v = k0c.rearrange("o (v p t) -> (o v) p t", v=KVH0, p=128)
    v0c_v = v0c.rearrange("o (v t d) -> (o v) t d", v=KVH0, t=T)
    with p.scope():
        cos_sb = p.sb("cos", [128, T], F32)
        sin_sb = p.sb("sin", [128, T], F32)
        pt_sb = p.sb("pt", [128, 128], BF16)
        s_m1 = p.sem(kind="hw")

        @p.block.sync
        def _(sync):
            sync.dma_start(out=cos_sb[:, :], in_=ctab[:, :]).then_inc(s_m1.h, 16)
            sync.dma_start(out=sin_sb[:, :], in_=stab[:, :]).then_inc(s_m1.h, 16)
            sync.dma_start(out=pt_sb[:, :], in_=cpt[:, :]).then_inc(s_m1.h, 16)
        s_m1.n = 48

        xln = p.sb("xln1", [128, NCH * T], BF16)
        s_x = emit_load_xbf(p, x.rearrange("c p t -> (c p) t"), xln)
        s_ln = emit_ln(p, xln, xln, consts, [s_x, s_const], "f1")
        rope = (cos_sb, sin_sb, pt_sb)
        sq0 = emit_proj(p, xln, wq0, NH, consts, [s_ln, s_m1, s_misc], "q0",
                        bias_sb=bias_sb, bias_off=0, mode="rope", rope=rope, out_dram=q0)
        sk0 = emit_proj(p, xln, wk0, KVH0, consts, [s_ln, sq0], "k0",
                        bias_sb=bias_sb, bias_off=NH, mode="rope", rope=rope, out_dram=k0c_v)
        sv0 = emit_proj(p, xln, wv0, KVH0, consts, [s_ln, sk0], "v0",
                        bias_sb=bias_sb, bias_off=NH + KVH0, mode="vtm", out_dram=v0c_v)

    if upto <= 1:
        emit_final_wait(p, [sq0, sk0, sv0])
        return p
    with p.scope():
        sd1 = emit_dequant(p, consts, gup, gus, wgu_mine.rearrange("m (p x) -> m p x", p=128),
                           11, 256, "gu", waits=[s_const, sv0], bank=0)
        sd2 = emit_dequant(p, consts, dnp, dns, down_mine.rearrange("m (p x) -> m p x", p=128),
                           2, 704, "dn", waits=[s_const, sv0], bank=1)
    s_agw = emit_collective(p, wgu_mine[:, :], wgu_all[:, :], ALL8, [sd1], "agw")
    s_agd = emit_collective(p, down_mine[:, :], down_all[:, :], ALL8, [sd2], "agd")

    if upto <= 2:
        emit_final_wait(p, [s_agw, s_agd])
        return p
    s_c1k = emit_collective(p, k0c[:, :], k0g[:, :], PAIRS, [sk0], "c1k")
    s_c1v = emit_collective(p, v0c[:, :], v0g[:, :], PAIRS, [sv0], "c1v")
    if upto == 25:
        emit_final_wait(p, [s_c1k, s_c1v, s_agw, s_agd])
        return p

    if upto == 26:
        with p.scope():
            s26 = emit_attn_tok(p, consts, q0,
                                k0g.rearrange("s (v p t) -> s v p t", v=KVH0, p=128),
                                v0g.rearrange("s (v t d) -> s v t d", v=KVH0, t=T),
                                att0, UPTO26_NHD, lambda h: h // 4, scale,
                                ptq_sb, cpos_sb, [s_c1k, s_c1v, sq0, s_misc, sd1, sd2], "a0")
        emit_final_wait(p, [s26])
        return p
    with p.scope():
        satt0 = emit_attn_tok(p, consts, q0,
                              k0g.rearrange("s (v p t) -> s v p t", v=KVH0, p=128),
                              v0g.rearrange("s (v t d) -> s v t d", v=KVH0, t=T),
                              att0, NH, lambda h: h // 4, scale,
                              ptq_sb, cpos_sb, [s_c1k, s_c1v, sq0, s_misc, sd1, sd2], "a0")

    if upto <= 3:
        emit_final_wait(p, [satt0])
        return p
    kshc_v = kshc.rearrange("o (p t) -> (o p) t", p=128)
    vshc_v = vshc.rearrange("o (t d) -> (o t) d", t=T)
    with p.scope():
        xln3 = p.sb("xln3", [128, NCH * T], BF16)
        with p.scope():
            s_x1e, s_st1 = emit_resadd_ln(p, x, att0, x1, xln3, consts,
                                          [satt0, s_misc, s_const], "f3",
                                          wo_dram=wo0, bias_sb=bo_sb, bias_off=0)
            s_ln3 = emit_ln(p, xln3, xln3, consts, [s_x1e, s_st1], "f3")
        ind_fm, top1row, s_ind, s_gre = emit_l0f_gating(p, xln3, wg_sb, consts,
                                                        [s_ln3, s_misc], "f3")
        s_indf = p.sem(kind="hw")

        @p.block.sync
        def _(sync):
            s_ind.wge(sync, s_ind.n)
            s_indf.inc(sync.dma_start(out=ind_dram[:, :], in_=ind_fm[0:16, :]), 16)

        ksh_acc = p.sb("ksh_acc", [128, T], BF16)
        vsh_acc = p.sb("vsh_acc", [128, T], BF16)
        vtm_sb = p.sb("vtm_sb", [128, TT], BF16)
        mask_all, s_mask = emit_mask_all(p, consts, ind_dram, s_indf, "f3")
        sqf = emit_proj(p, xln3, wqkv[0:16], 16, consts, [s_ln3, s_ind], "qf",
                        mode="plain", out_dram=qf0)
        s_kacc = emit_proj(p, xln3, wqkv[16:32], 16, consts, [s_ln3, sqf, s_mask], "kf",
                           mode="sel", sel=(mask_all, ksh_acc))
        s_vacc = emit_proj(p, xln3, wqkv[32:48], 16, consts, [s_ln3, s_kacc], "vf",
                           mode="sel", sel=(mask_all, vsh_acc))
        s_kst = p.sem(kind="sw")

        @p.block.gpsimd
        def _(g):
            s_kacc.wge(g, s_kacc.n)
            s_kst.inc(g.dma_start(out=kshc_v[:, :], in_=ksh_acc[:, :]), 16)

        t1_ps = p.pf[3]
        vt_ps = p.pb[0]
        s_t1b, s_t1m, s_vt, s_vte = (p.sem() for _ in range(4))
        s_vst = p.sem(kind="hw")
        orow_f = consts["ones_row_f"]
        identb = consts["ident"]

        @p.block.tensor
        def _(tensor):
            s_vacc.wge(tensor, s_vacc.n)
            s_gre.wge(tensor, s_gre.n)
            for tt in range(NTT):
                if tt > 0:
                    s_t1m.wge(tensor, tt)
                s_t1b.inc(tensor.matmul(t1_ps[:, :], orow_f[0:1, :],
                                        top1row[0:1, tt * TT:(tt + 1) * TT], start=True, stop=True))
                s_t1m.wge(tensor, tt + 1)
                if tt > 0:
                    s_vte.wge(tensor, 4 * tt)
                for sg in range(4):
                    i = tensor.transpose(vt_ps[:, sg * 128:(sg + 1) * 128],
                                         vsh_acc[:, tt * TT + sg * 128:tt * TT + (sg + 1) * 128],
                                         identb[:, :])
                s_vt.inc(i)

        @p.block.vector
        def _(vector):
            for tt in range(NTT):
                s_t1b.wge(vector, tt + 1)
                s_t1m.inc(vector.tensor_tensor(vsh_acc[:, tt * TT:(tt + 1) * TT],
                                               vsh_acc[:, tt * TT:(tt + 1) * TT],
                                               t1_ps[:, :], ALU.mult))
                s_vt.wge(vector, tt + 1)
                if tt > 0:
                    s_vst.wge(vector, 16 * tt)
                for sg in range(4):
                    i = vector.tensor_copy(vtm_sb[:, sg * 128:(sg + 1) * 128],
                                           vt_ps[:, sg * 128:(sg + 1) * 128])
                    s_vte.inc(i)

        @p.block.sync
        def _(g):
            for tt in range(NTT):
                s_vte.wge(g, 4 * (tt + 1))
                s_vst.inc(g.dma_start(
                    out=vshc_v[tt * TT:(tt + 1) * TT, :].rearrange("(sg pp) d -> pp sg d", pp=128),
                    in_=vtm_sb[:, :].rearrange("p (sg d) -> p sg d", sg=4)), 16)

    if upto <= 4:
        emit_final_wait(p, [sqf, s_kst, s_vst, s_st1])
        return p
    s_c2k = emit_collective(p, kshc[:, :], kshg[:, :], PAIRS, [s_kst], "c2k")
    s_c2v = emit_collective(p, vshc[:, :], vshg[:, :], PAIRS, [s_vst], "c2v")

    with p.scope():
        sattm = emit_attn_tok(p, consts, qf0,
                              kshg.rearrange("s (v p t) -> s v p t", v=1, p=128),
                              vshg.rearrange("s (v t d) -> s v t d", v=1, t=T),
                              attm, NH, lambda h: 0, scale,
                              ptq_sb, cpos_sb, [s_c2k, s_c2v, sqf, s_misc], "am")

    if upto <= 5:
        emit_final_wait(p, [sattm])
        return p
    k1c_v = k1c.rearrange("o (v p t) -> (o v) p t", v=NH, p=128)
    v1c_v = v1c.rearrange("o (v t d) -> (o v) t d", v=NH, t=T)
    with p.scope():
        xln5 = p.sb("xln5", [128, NCH * T], BF16)
        with p.scope():
            s_x2e, s_st2 = emit_resadd_ln(p, x1, attm, x2, xln5, consts,
                                          [sattm, s_st1], "f5")
            s_ln5 = emit_ln(p, xln5, xln5, consts, [s_x2e, s_st2], "f5")
        sq1 = emit_proj(p, xln5, wq1, NH, consts, [s_ln5], "q1", mode="plain", out_dram=q1)
        sk1 = emit_proj(p, xln5, wk1, NH, consts, [s_ln5, sq1], "k1", mode="plain", out_dram=k1c_v)
        sv1 = emit_proj(p, xln5, wv1, NH, consts, [s_ln5, sk1], "v1", mode="vtm", out_dram=v1c_v)

    if upto <= 6:
        emit_final_wait(p, [sq1, sk1, sv1, s_st2])
        return p
    s_c3k = emit_collective(p, k1c[:, :], k1g[:, :], PAIRS, [sk1], "c3k")
    s_c3v = emit_collective(p, v1c[:, :], v1g[:, :], PAIRS, [sv1], "c3v")

    with p.scope():
        satt1 = emit_attn_tok(p, consts, q1,
                              k1g.rearrange("s (v p t) -> s v p t", v=NH, p=128),
                              v1g.rearrange("s (v t d) -> s v t d", v=NH, t=T),
                              att1, NH, lambda h: h, scale,
                              ptq_sb, cpos_sb, [s_c3k, s_c3v, sq1, s_misc], "a1",
                              gate=(gw_sb, gb_sb))

    if upto <= 7:
        emit_final_wait(p, [satt1])
        return p
    with p.scope():
        xln7 = p.sb("xln7", [128, NCH * T], BF16)
        with p.scope():
            s_x3e, s_st3 = emit_resadd_ln(p, x2, att1, xout, xln7, consts,
                                          [satt1, s_st2, s_misc], "f7",
                                          wo_dram=ow1, bias_sb=bz_sb, bias_off=0)
            s_ln7 = emit_ln(p, xln7, xln7, consts, [s_x3e, s_st3], "f7")
        s_fst = emit_l1ffn(p, consts, xln7,
                           wgu_all.rearrange("m (p x) -> m p x", p=128),
                           down_all.rearrange("m (p x) -> m p x", p=128),
                           h_dram, xout, s_st3, s_ln7, [s_agw, s_agd])
    emit_final_wait(p, [s_fst])
    return p


def run_fused(inputs):
    sh = _prep_shared(inputs)
    xs = _x_shards(np.asarray(inputs["L0_attn_x"]))
    p = build_fused()
    wo0_t = _tile_lhsT(np.asarray(inputs["L0_attn_Wo"]))
    bo0 = np.ascontiguousarray(np.asarray(inputs["L0_attn_bo"]).reshape(NCH, 128).T.astype(np.float32))
    Wg = np.asarray(inputs["L0_ffn_W_gate"])
    wg_host = np.ascontiguousarray(Wg.reshape(16, 16, 128).transpose(2, 1, 0).reshape(128, 256).astype(np.float32))
    wqkv_t = _tile_lhsT(np.asarray(inputs["L0_ffn_W_qkv"]))
    wq1_t = _tile_lhsT(np.asarray(inputs["L1_attn_q_w"]))
    wk1_t = _tile_lhsT(np.asarray(inputs["L1_attn_k_w"]))
    wv1_t = _tile_lhsT(np.asarray(inputs["L1_attn_v_w"]))
    ow1_t = _tile_lhsT(np.asarray(inputs["L1_attn_o_w"]))
    gwt = np.ascontiguousarray(np.asarray(inputs["L1_attn_gate_w"]).T.astype(np.float32))
    gbt = np.ascontiguousarray(np.asarray(inputs["L1_attn_gate_b"]).reshape(128, 1).astype(np.float32))
    bz = np.zeros((128, NCH), np.float32)
    gup = np.asarray(inputs["L1_ffn_gate_up_packed"])
    gus = np.asarray(inputs["L1_ffn_gate_up_scales"])
    dnp = np.asarray(inputs["L1_ffn_down_packed"])
    dns = np.asarray(inputs["L1_ffn_down_scales"])
    cposr = np.ascontiguousarray(np.arange(S, dtype=np.float32).reshape(1, S))
    in_maps = []
    for c in range(NCORE):
        pos = _core_pos(c)
        r0, r1 = c * 1408, (c + 1) * 1408
        d0, d1 = c * 256, (c + 1) * 256
        ptqh = np.ascontiguousarray(pos.astype(np.float32).reshape(16, 128).T)
        in_maps.append({
            "x": np.ascontiguousarray(xs[c].reshape(NCH, 128, T)),
            "wq0": sh["wq0"], "wk0": sh["wk0"], "wv0": sh["wv0"], "bqkv": sh["bqkv0"],
            "ctab": np.ascontiguousarray(sh["rope_cos"][:, pos]),
            "stab": np.ascontiguousarray(sh["rope_sin"][:, pos]),
            "c_pt": sh["c_pt"], "c_id": sh["c_id"], "c_idf": sh["c_idf"],
            "wo0": wo0_t, "bo0": bo0, "wg": wg_host, "wqkv": wqkv_t,
            "wq1": wq1_t, "wk1": wk1_t, "wv1": wv1_t, "gwt": gwt, "gbt": gbt,
            "ow1": ow1_t, "bz": bz,
            "gup": np.ascontiguousarray(gup[r0:r1].reshape(11, 128, 256)),
            "gus": np.ascontiguousarray(gus.T[r0:r1].reshape(11, 128, 16)),
            "dnp": np.ascontiguousarray(dnp[d0:d1].reshape(2, 128, 704)),
            "dns": np.ascontiguousarray(dns.T[d0:d1].reshape(2, 128, 44)),
            "ptq": ptqh, "cpos": cposr,
        })
    res = run_bass_kernel_spmd(p.nc, in_maps, core_ids=list(range(NCORE)))
    p.close()
    return res.results


USE_FUSED = True


def kernel(**inputs):
    inputs = {kk: np.asarray(v) for kk, v in inputs.items()}
    out = np.zeros((B, S, D), np.float32)
    if USE_FUSED:
        rf = run_fused(inputs)
        for c in range(NCORE):
            out[c // 2, _core_pos(c), :] = rf[c]["xout"].reshape(D, T).T
        return out
    xs = _x_shards(inputs["L0_attn_x"])
    p1 = run_p1(inputs)
    p2 = run_p2(p1, _diag_masks())
    p3 = run_p3(inputs, p2, xs)
    p4 = run_p4(p3)
    p5 = run_p5(inputs, p4, p3)
    p6 = run_p6(inputs, p5)
    p7a = run_p7a(inputs)
    p7b = run_p7b(inputs, p6, p5, p7a)
    for c in range(NCORE):
        out[c // 2, _core_pos(c), :] = p7b[c]["xout"].reshape(D, T).T
    return out


# version 06
# speedup vs baseline: 1.1873x; 1.1873x over previous
import contextlib
import numpy as np
import ml_dtypes
import concourse.bass as bass
import concourse.mybir as mybir
from concourse.bass_utils import run_bass_kernel_spmd

F32 = mybir.dt.float32
BF16 = mybir.dt.bfloat16
U8 = mybir.dt.uint8
I32 = mybir.dt.int32
F16 = mybir.dt.float16
AF = mybir.ActivationFunctionType
ALU = mybir.AluOpType
BF = ml_dtypes.bfloat16

B, S, D = 4, 4096, 2048
NCORE = 8
T = 2048
NCH = D // 128
NH, HD = 16, 128
KVH0 = 4
TT = 512
NTT = T // TT
CHUNKS_OF = {0: [0, 3, 4, 7], 1: [1, 2, 5, 6]}
EXT = [2, 4, 6, 8]
_SLOT = {}
for _h in (0, 1):
    for _i, _g in enumerate(CHUNKS_OF[_h]):
        _SLOT[_g] = _h * 4 + _i


def _core_pos(c):
    return np.concatenate([np.arange(g * TT, (g + 1) * TT) for g in CHUNKS_OF[c % 2]])
ROPE_BASE = 10000.0
NEG = -1e30
EPS_LN = 1e-5


def _tile_lhsT(w):
    o, i = w.shape
    mo, k = o // 128, i // 128
    t = w.reshape(mo, 128, k, 128)
    return np.ascontiguousarray(t.transpose(0, 3, 2, 1)).astype(np.float32)


def _rope_tables():
    inv = 1.0 / ROPE_BASE ** (np.arange(0, HD, 2, dtype=np.float64) / HD)
    pos = np.arange(S, dtype=np.float64)
    fr = pos[:, None] * inv[None, :]
    cos = np.repeat(np.cos(fr), 2, axis=1).T
    sin = np.repeat(np.sin(fr), 2, axis=1).T
    return (np.ascontiguousarray(cos.astype(np.float32)),
            np.ascontiguousarray(sin.astype(np.float32)))


def _rope_P():
    P = np.zeros((HD, HD), np.float32)
    for i in range(HD // 2):
        P[2 * i, 2 * i + 1] = -1.0
        P[2 * i + 1, 2 * i] = 1.0
    return np.ascontiguousarray(P.T.astype(BF))


def _diag_masks():
    m = np.zeros((4, 128, 512), np.float32)
    for j in range(4):
        for r in range(128):
            m[j, r, j * 128 + r + 1:] = NEG
    return np.ascontiguousarray(m)


class Sem:
    def __init__(self, handle, base=0):
        self.h = handle
        self.base = base
        self.n = 0

    def inc(self, instr, k=1):
        instr.then_inc(self.h, k)
        self.n += k
        return instr

    def wge(self, eng, v):
        return eng.wait_ge(self.h, self.base + v)

    def pairs(self):
        return [(self.h, self.base + self.n)]


class SemGroup:
    def __init__(self, sems):
        self.sems = sems

    def pairs(self):
        return [pp for s in self.sems for pp in s.pairs()]


def _wpairs(waits):
    return [pp for w in waits for pp in w.pairs()]


class Prog:
    def __init__(self, name="mk"):
        self.nc = bass.Bass("TRN2", target_bir_lowering=False, debug=False)
        self.stack = contextlib.ExitStack()
        self.sb_stacks = [self.stack]
        self._n = 0
        self.block = None

    def sem(self, name=None, kind="eng"):
        if not hasattr(self, "_sem_free"):
            self._sem_free = {"eng": [], "sw": [], "hw": [], "col": []}
            self._sem_scopes = [[]]
        pool = self._sem_free[kind]
        if pool:
            h, total = pool.pop()
            s = Sem(h, base=total)
        else:
            name = name or f"s{self._n}"
            self._n += 1
            s = Sem(self.stack.enter_context(self.nc.semaphore(name)))
        s.kind = kind
        self._sem_scopes[-1].append(s)
        return s

    def sb(self, name, shape, dtype):
        return self.sb_stacks[-1].enter_context(self.nc.sbuf_tensor(name, shape, dtype))

    @contextlib.contextmanager
    def scope(self):
        st = contextlib.ExitStack()
        self.sb_stacks.append(st)
        if not hasattr(self, "_sem_free"):
            self._sem_free = {"eng": [], "sw": [], "hw": [], "col": []}
            self._sem_scopes = [[]]
        self._sem_scopes.append([])
        try:
            yield
        finally:
            self.sb_stacks.pop()
            st.close()
            freed = self._sem_scopes.pop()
            drain = [(s.h, s.base + s.n) for s in freed
                     if getattr(s, "kind", "eng") in ("sw", "hw", "col") and s.n > 0]
            if drain:
                @self.block.gpsimd
                def _(g):
                    for h, n in drain:
                        g.wait_ge(h, n)

                @self.block.sync
                def _(g):
                    for h, n in drain:
                        g.wait_ge(h, n)
            for s in freed:
                self._sem_free[getattr(s, "kind", "eng")].append((s.h, s.base + s.n))

    def ps(self, name, shape, dtype=F32):
        return self.stack.enter_context(self.nc.psum_tensor(name, shape, dtype))

    def din(self, name, shape, dtype):
        return self.nc.dram_tensor(name, shape, dtype, kind="ExternalInput").ap()

    def dout(self, name, shape, dtype):
        return self.nc.dram_tensor(name, shape, dtype, kind="ExternalOutput").ap()

    def dint(self, name, shape, dtype):
        return self.nc.dram_tensor(name, shape, dtype, kind="Internal").ap()

    def open(self):
        self.pf = [self.ps(f"pf{i}", [128, 512], F32) for i in range(6)]
        self.pb = [self.ps(f"pb{i}", [128, 1024], BF16) for i in range(2)]
        self.block = self.stack.enter_context(self.nc.Block())
        return self

    def close(self):
        self.stack.close()


def load_consts(p, reg_vals=()):
    cid = p.din("c_id", [128, 128], BF16)
    cidf = p.din("c_idf", [128, 128], F32)
    ident = p.sb("ident", [128, 128], BF16)
    identf = p.sb("identf", [128, 128], F32)
    ones_col = p.sb("ones_col", [128, 1], BF16)
    ones_row = p.sb("ones_row", [1, 128], BF16)
    ones_row_f = p.sb("ones_row_f", [1, 128], F32)
    ctiles = []
    for i, v in enumerate(reg_vals):
        t = p.sb(f"creg{i}", [128, 1], F32)
        p.nc.const_aps.register(F32, float(v), t[:, 0:1])
        ctiles.append((t, float(v)))
    s_hw = p.sem("s_const", kind="hw")
    s_en = p.sem()

    @p.block.sync
    def _(sync):
        s_hw.inc(sync.dma_start(out=ident[:, :], in_=cid[:, :]), 16)
        s_hw.inc(sync.dma_start(out=identf[:, :], in_=cidf[:, :]), 16)

    @p.block.vector
    def _(vector):
        vector.memset(ones_col[:, :], 1.0)
        vector.memset(ones_row[:, :], 1.0)
        for t, v in ctiles:
            vector.memset(t[:, :], v)
        s_en.inc(vector.memset(ones_row_f[:, :], 1.0))
    return {"ident": ident, "identf": identf, "ones_col": ones_col,
            "ones_row": ones_row, "ones_row_f": ones_row_f}, SemGroup([s_hw, s_en])


def emit_load_xbf(p, xsrc, xbf):
    s_x = p.sem(kind="sw")

    @p.block.gpsimd
    def _(g):
        s_x.inc(g.dma_start(out=xbf[:, :].rearrange("p (c t) -> p c t", c=NCH),
                            in_=xsrc.rearrange("(c p) t -> p c t", p=128)), 16)
    return s_x


def ensure_proj_scratch(p):
    if not hasattr(p, "proj_scratch"):
        p.proj_scratch = {
            "wbuf": [p.sb(f"pj_w{i}", [128, NCH * 128], BF16) for i in range(2)],
            "ev": [p.sb(f"pj_ev{i}", [128, TT], BF16) for i in range(2)],
            "qf": [p.sb(f"pj_qf{i}", [128, TT], BF16) for i in range(2)],
            "rt1": p.sb("pj_rt1", [128, TT], F32),
            "rt2": p.sb("pj_rt2", [128, TT], F32),
        }
    return p.proj_scratch


def emit_ln(p, xbf, xln, consts, waits, tag, ntok=T):
    ntt = ntok // TT
    oc = consts["ones_col"]
    orow = consts["ones_row"]
    xsq = p.sb(f"xsq_{tag}", [128, NCH * TT], BF16)
    srow = p.sb(f"srow_{tag}", [1, 6 * TT], F32)
    mrow = p.sb(f"mrow_{tag}", [1, 2 * TT], BF16)
    tmp = p.sb(f"lntmp_{tag}", [128, 4 * TT], BF16)
    st_sum, st_sq = p.pf[4], p.pf[5]
    bc0, bc1 = p.pf[2], p.pf[3]
    s_sq, s_stat, s_row, s_sd, s_rr, s_bc, s_va = (p.sem() for _ in range(7))
    s_ln = p.sem()
    wvals = _wpairs(waits)

    def xs(c, t0):
        return slice(c * ntok + t0, c * ntok + t0 + TT)

    @p.block.scalar
    def _(scalar):
        for tt in range(ntt):
            t0 = tt * TT
            if tt == 0:
                for h, n in wvals:
                    scalar.wait_ge(h, n)
            for c in range(NCH - 1):
                scalar.activation(xsq[:, c * TT:(c + 1) * TT], xbf[:, xs(c, t0)], AF.Square)
            s_sq.inc(scalar.activation(xsq[:, (NCH - 1) * TT:], xbf[:, xs(NCH - 1, t0)], AF.Square))
            s_row.wge(scalar, tt + 1)
            s_sd.inc(scalar.activation(srow[:, 4 * TT:5 * TT], srow[:, 3 * TT:4 * TT], AF.Sqrt))

    @p.block.tensor
    def _(tensor):
        for tt in range(ntt):
            t0 = tt * TT
            if tt == 0:
                for h, n in wvals:
                    tensor.wait_ge(h, n)
            s_sq.wge(tensor, tt + 1)
            for c in range(NCH):
                tensor.matmul(st_sum[0:1, :], oc[:, :], xbf[:, xs(c, t0)],
                              start=(c == 0), stop=(c == NCH - 1))
            for c in range(NCH):
                i = tensor.matmul(st_sq[0:1, :], oc[:, :], xsq[:, c * TT:(c + 1) * TT],
                                  start=(c == 0), stop=(c == NCH - 1))
            s_stat.inc(i)
            s_rr.wge(tensor, tt + 1)
            if tt > 0:
                s_ln.wge(tensor, tt)
            tensor.matmul(bc0[:, :], orow[:, :], mrow[0:1, 0:TT], start=True, stop=True)
            s_bc.inc(tensor.matmul(bc1[:, :], orow[:, :], mrow[0:1, TT:2 * TT], start=True, stop=True))

    @p.block.vector
    def _(vector):
        for tt in range(ntt):
            t0 = tt * TT
            s_stat.wge(vector, tt + 1)
            vector.tensor_scalar(srow[:, 0:TT], st_sum[0:1, :], 1.0 / D, None, ALU.mult)
            vector.tensor_scalar(mrow[:, 0:TT], st_sum[0:1, :], 1.0 / D, None, ALU.mult)
            s_va.inc(vector.tensor_scalar(srow[:, TT:2 * TT], st_sq[0:1, :], 1.0 / D, None, ALU.mult))
            s_va.wge(vector, s_va.n)
            s_va.inc(vector.tensor_tensor(srow[:, 2 * TT:3 * TT], srow[:, 0:TT], srow[:, 0:TT], ALU.mult))
            s_va.wge(vector, s_va.n)
            s_row.inc(vector.scalar_tensor_tensor(srow[:, 3 * TT:4 * TT], srow[:, TT:2 * TT],
                                                  EPS_LN, srow[:, 2 * TT:3 * TT],
                                                  ALU.add, ALU.subtract))
            s_sd.wge(vector, tt + 1)
            s_va.inc(vector.reciprocal(srow[:, 5 * TT:6 * TT], srow[:, 4 * TT:5 * TT]))
            s_va.wge(vector, s_va.n)
            s_rr.inc(vector.tensor_copy(mrow[:, TT:2 * TT], srow[:, 5 * TT:6 * TT]))
            s_bc.wge(vector, tt + 1)
            for g in range(NCH // 4):
                if g > 0:
                    s_va.wge(vector, s_va.n)
                elif tt > 0:
                    s_ln.wge(vector, tt)
                for c4 in range(4):
                    c = g * 4 + c4
                    i = vector.tensor_tensor(tmp[:, c4 * TT:(c4 + 1) * TT], xbf[:, xs(c, t0)],
                                             bc0[:, :], ALU.subtract)
                s_va.inc(i)
                s_va.wge(vector, s_va.n)
                for c4 in range(4):
                    c = g * 4 + c4
                    i = vector.tensor_tensor(xln[:, xs(c, t0)], tmp[:, c4 * TT:(c4 + 1) * TT],
                                             bc1[:, :], ALU.mult)
                if g < NCH // 4 - 1:
                    s_va.inc(i)
            s_ln.inc(i)
    return s_ln


def emit_proj(p, xln, wdram, nmo, consts, waits, tag,
              bias_sb=None, bias_off=0, mode="plain", rope=None, out_dram=None,
              ntok=T, ev_consumer=None, sel=None):
    ntt = ntok // TT
    sc = ensure_proj_scratch(p)
    wbuf, ev = sc["wbuf"], sc["ev"]
    pp = [p.pf[0], p.pf[1]]
    s_wld = [p.sem(kind="sw"), p.sem(kind="sw")]
    s_done2 = [p.sem(kind="hw"), p.sem(kind="hw")]
    s_mm, s_ev, s_rr2 = p.sem(), p.sem(), p.sem()
    wvals = _wpairs(waits)

    if mode in ("rope", "vtm"):
        qf = sc["qf"]
        rt1, rt2 = sc["rt1"], sc["rt2"]
        rot_ps = p.pf[2]
        tp_ps = p.pb[0]
        s_rot, s_qf = p.sem(), p.sem()
    s_free = p.sem()

    def xs(c, t0):
        return slice(c * ntok + t0, c * ntok + t0 + TT)

    @p.block.gpsimd
    def _(g):
        for m in range(nmo):
            if m == 0:
                for h, n in wvals:
                    g.wait_ge(h, n)
            if m >= 2:
                s_mm.wge(g, (m - 1) * ntt)
            s_wld[m % 2].inc(g.dma_start(out=wbuf[m % 2][:, :],
                                         in_=wdram[m, :, :, :].rearrange("p k mm -> p (k mm)")), 16)

    @p.block.tensor
    def _(tensor):
        for m in range(nmo):
            s_wld[m % 2].wge(tensor, 16 * (m // 2 + 1))
            if m == 0:
                for h, n in wvals:
                    tensor.wait_ge(h, n)
            for tt in range(ntt):
                t0 = tt * TT
                it = m * ntt + tt
                if it >= 2:
                    s_ev.wge(tensor, it - 1)
                for c in range(NCH):
                    i = tensor.matmul(pp[it % 2][:, :], wbuf[m % 2][:, c * 128:(c + 1) * 128],
                                      xln[:, xs(c, t0)], start=(c == 0), stop=(c == NCH - 1))
                s_mm.inc(i)
                if mode == "rope":
                    s_ev.wge(tensor, it + 1)
                    if it >= 1:
                        s_qf.wge(tensor, it)
                    s_rot.inc(tensor.matmul(rot_ps[:, :], rope[2][:, :], ev[it % 2][:, :],
                                            start=True, stop=True))
                elif mode == "vtm":
                    s_ev.wge(tensor, it + 1)
                    if it >= 1:
                        s_qf.wge(tensor, it)
                    for sg in range(4):
                        i = tensor.transpose(tp_ps[:, sg * 128:(sg + 1) * 128],
                                             ev[it % 2][:, sg * 128:(sg + 1) * 128], consts["ident"][:, :])
                    s_rot.inc(i)

    @p.block.vector
    def _(vector):
        for m in range(nmo):
            for tt in range(ntt):
                it = m * ntt + tt
                s_mm.wge(vector, it + 1)
                if it >= 2 and mode != "sel":
                    if mode in ("rope", "vtm"):
                        s_qf.wge(vector, it - 1)
                    elif mode == "cb":
                        s_free.wge(vector, it - 1)
                    else:
                        s_done2[it % 2].wge(vector, 16 * (it // 2))
                if mode == "sel":
                    mask_all, acc = sel
                    i = vector.copy_predicated(
                        acc[:, tt * TT:(tt + 1) * TT],
                        mask_all[:, m * T + tt * TT:m * T + (tt + 1) * TT],
                        pp[it % 2][:, :])
                elif bias_sb is not None:
                    i = vector.tensor_scalar(ev[it % 2][:, :], pp[it % 2][:, :],
                                             bias_sb[:, bias_off + m:bias_off + m + 1], None, ALU.add)
                else:
                    i = vector.tensor_copy(ev[it % 2][:, :], pp[it % 2][:, :])
                s_ev.inc(i)
                if mode == "rope":
                    s_rot.wge(vector, it + 1)
                    if it >= 2:
                        s_done2[it % 2].wge(vector, 16 * (it // 2))
                    cos_sb, sin_sb, _ = rope
                    t0 = tt * TT
                    s_ev.wge(vector, it + 1)
                    vector.tensor_tensor(rt1[:, :], ev[it % 2][:, :], cos_sb[:, t0:t0 + TT], ALU.mult)
                    s_rr2.inc(vector.tensor_tensor(rt2[:, :], rot_ps[:, :], sin_sb[:, t0:t0 + TT], ALU.mult))
                    s_rr2.wge(vector, s_rr2.n)
                    s_qf.inc(vector.tensor_tensor(qf[it % 2][:, :], rt1[:, :], rt2[:, :], ALU.add))
                elif mode == "vtm":
                    s_rot.wge(vector, it + 1)
                    if it >= 2:
                        s_done2[it % 2].wge(vector, 16 * (it // 2))
                    s_qf.inc(vector.tensor_copy(qf[it % 2][:, :], tp_ps[:, 0:512]))

    if mode == "sel":
        return s_ev
    if mode == "cb":
        ev_consumer(s_ev, s_free, ev)
        return s_ev

    @p.block.sync
    def _(g):
        for m in range(nmo):
            for tt in range(ntt):
                it = m * ntt + tt
                t0 = tt * TT
                if mode in ("rope", "vtm"):
                    s_qf.wge(g, it + 1)
                    src = qf[it % 2][:, :]
                else:
                    s_ev.wge(g, it + 1)
                    src = ev[it % 2][:, :]
                if mode == "vtm":
                    oap = out_dram[m, t0:t0 + TT, :].rearrange("(sg pp) d -> pp sg d", pp=128)
                    src = src.rearrange("p (sg d) -> p sg d", sg=4)
                else:
                    oap = out_dram[m, :, t0:t0 + TT]
                i = g.dma_start(out=oap, in_=src)
                s_done2[it % 2].inc(i, 16)
    return SemGroup(s_done2)


def emit_resadd_ln(p, x_dram, add_dram, x1_dram, xbf, consts, waits, tag,
                   wo_dram=None, bias_sb=None, bias_off=0):
    xt = p.sb(f"rx_{tag}", [128, NCH * TT], F32)
    at = [p.sb(f"ra0_{tag}", [128, NCH * TT], BF16)]
    x1c = [p.sb(f"rx1{i}_{tag}", [128, TT], F32) for i in range(2)]
    if wo_dram is not None:
        wbuf = ensure_proj_scratch(p)["wbuf"]
    pp = [p.pf[0], p.pf[1]]
    ybuf = ensure_proj_scratch(p)["rt1"]
    s_mm, s_ev, s_yb = (p.sem() for _ in range(3))
    s_xl = p.sem(kind="sw")
    s_al = p.sem(kind="sw")
    s_wl = [p.sem(kind="sw"), p.sem(kind="sw")]
    s_st = [p.sem(kind="hw"), p.sem(kind="hw")]
    wvals = _wpairs(waits)

    @p.block.gpsimd
    def _(g):
        for hh, n in wvals:
            g.wait_ge(hh, n)
        for tt in range(NTT):
            if tt >= 1:
                s_ev.wge(g, 2 * NCH * tt)
            s_xl.inc(g.dma_start(out=xt[:, :].rearrange("p (c t) -> p c t", c=NCH),
                                 in_=x_dram[:, :, tt * TT:(tt + 1) * TT].rearrange("c p t -> p c t")), 16)
            s_al.inc(g.dma_start(out=at[0][:, :].rearrange("p (c t) -> p c t", c=NCH),
                                 in_=add_dram[:, :, tt * TT:(tt + 1) * TT].rearrange("c p t -> p c t")), 16)
            if wo_dram is not None:
                for m in range(NCH):
                    it = tt * NCH + m
                    if it >= 2:
                        s_mm.wge(g, it - 1)
                    s_wl[it % 2].inc(g.dma_start(out=wbuf[it % 2][:, :],
                                                 in_=wo_dram[m, :, :, :].rearrange("p k mm -> p (k mm)")), 16)

    if wo_dram is not None:
        @p.block.tensor
        def _(tensor):
            for tt in range(NTT):
                s_al.wge(tensor, 16 * (tt + 1))
                if tt == 0:
                    for hh, n in wvals:
                        tensor.wait_ge(hh, n)
                for m in range(NCH):
                    it = tt * NCH + m
                    s_wl[it % 2].wge(tensor, 16 * (it // 2 + 1))
                    if it >= 2:
                        s_ev.wge(tensor, 2 * (it - 1))
                    for c in range(NCH):
                        i = tensor.matmul(pp[it % 2][:, :], wbuf[it % 2][:, c * 128:(c + 1) * 128],
                                          at[0][:, c * TT:(c + 1) * TT],
                                          start=(c == 0), stop=(c == NCH - 1))
                    s_mm.inc(i)

    @p.block.vector
    def _(vector):
        for tt in range(NTT):
            s_xl.wge(vector, 16 * (tt + 1))
            for m in range(NCH):
                it = tt * NCH + m
                xc = xt[:, m * TT:(m + 1) * TT]
                if wo_dram is not None:
                    s_mm.wge(vector, it + 1)
                    if it >= 2:
                        s_st[it % 2].wge(vector, 16 * (it // 2))
                    if it >= 1:
                        s_ev.wge(vector, 2 * it)
                    s_yb.inc(vector.tensor_scalar(ybuf[:, :], pp[it % 2][:, :],
                                                  bias_sb[:, bias_off + m:bias_off + m + 1],
                                                  None, ALU.add))
                    s_yb.wge(vector, s_yb.n)
                    s_ev.inc(vector.tensor_tensor(x1c[it % 2][:, :], ybuf[:, :], xc, ALU.add))
                    i = vector.tensor_tensor(xbf[:, m * T + tt * TT:m * T + tt * TT + TT],
                                             ybuf[:, :], xc, ALU.add)
                    s_ev.inc(i)
                else:
                    s_al.wge(vector, 16 * (tt + 1))
                    if it >= 2:
                        s_st[it % 2].wge(vector, 16 * (it // 2))
                    ac = at[0][:, m * TT:(m + 1) * TT]
                    s_ev.inc(vector.tensor_tensor(x1c[it % 2][:, :], xc, ac, ALU.add))
                    i = vector.tensor_tensor(xbf[:, m * T + tt * TT:m * T + tt * TT + TT],
                                             xc, ac, ALU.add)
                    s_ev.inc(i)

    @p.block.sync
    def _(g):
        for tt in range(NTT):
            for m in range(NCH):
                it = tt * NCH + m
                s_ev.wge(g, 2 * (it + 1))
                s_st[it % 2].inc(g.dma_start(out=x1_dram[m, :, tt * TT:(tt + 1) * TT],
                                             in_=x1c[it % 2][:, :]), 16)
    return s_ev, SemGroup(s_st)


def emit_l0f_gating(p, xln, wg_sb, consts, waits, tag):
    identf = consts["identf"]
    orow_f = consts["ones_row_f"]
    g_sb = p.sb(f"g_{tag}", [16, TT], F32)
    gtm = p.sb(f"gtm_{tag}", [128, 4 * 16], F32)
    gm = p.sb(f"gm_{tag}", [128, 4], F32)
    gnm = p.sb(f"gnm_{tag}", [128, 4], F32)
    ge = p.sb(f"ge_{tag}", [128, 64], F32)
    gs = p.sb(f"gs_{tag}", [128, 4], F32)
    gt1 = p.sb(f"gt1_{tag}", [128, 4], F32)
    trow = p.sb(f"trow_{tag}", [1, TT], F32)
    ind_fm = p.sb(f"ind_{tag}", [16, NCH * 128], F32)
    top1row = p.sb(f"t1r_{tag}", [1, NCH * 128], F32)
    g_ps = p.pf[0]
    t_ps = p.pf[1]
    mb_ps = p.pf[2]
    t1_ps = p.pf[3]
    (s_gmm, s_gev, s_gt, s_gst, s_gnm, s_ge, s_gr, s_grt, s_gre, s_mb,
     s_ind) = (p.sem() for _ in range(11))
    wvals = _wpairs(waits)

    @p.block.tensor
    def _(tensor):
        for tt in range(NTT):
            if tt == 0:
                for hh, n in wvals:
                    tensor.wait_ge(hh, n)
            if tt > 0:
                s_gev.wge(tensor, tt)
            for c in range(NCH):
                i = tensor.matmul(g_ps[0:16, 0:TT], wg_sb[:, c * 16:(c + 1) * 16],
                                  xln[:, c * T + tt * TT:c * T + tt * TT + TT],
                                  start=(c == 0), stop=(c == NCH - 1))
            s_gmm.inc(i)
            s_gev.wge(tensor, tt + 1)
            if tt > 0:
                s_gst.wge(tensor, tt)
                s_gre.wge(tensor, tt)
            for gr in range(4):
                i = tensor.transpose(t_ps[0:128, gr * 16:(gr + 1) * 16],
                                     g_sb[0:16, gr * 128:(gr + 1) * 128], identf[0:16, 0:16])
            s_gt.inc(i)
            s_gr.wge(tensor, tt + 1)
            for gr in range(4):
                tensor.transpose(t_ps[0:1, gr * 128:(gr + 1) * 128], gm[:, gr:gr + 1],
                                 identf[:, :])
                i = tensor.transpose(t1_ps[0:1, gr * 128:(gr + 1) * 128], gt1[:, gr:gr + 1],
                                     identf[:, :])
            s_grt.inc(i)
            s_gre.wge(tensor, tt + 1)
            if tt > 0:
                s_ind.wge(tensor, tt)
            s_mb.inc(tensor.matmul(mb_ps[0:16, 0:TT], orow_f[0:1, 0:16], trow[0:1, 0:TT],
                                   start=True, stop=True))

    @p.block.vector
    def _(vector):
        for tt in range(NTT):
            if tt == 0:
                for hh, n in wvals:
                    vector.wait_ge(hh, n)
            s_gmm.wge(vector, tt + 1)
            if tt > 0:
                s_ind.wge(vector, tt)
            s_gev.inc(vector.tensor_copy(g_sb[0:16, :], g_ps[0:16, 0:TT]))
            s_gt.wge(vector, tt + 1)
            s_gst.inc(vector.tensor_copy(gtm[:, :], t_ps[0:128, 0:64]))
            s_gst.wge(vector, tt + 1)
            if tt > 0:
                s_grt.wge(vector, tt)
            for gr in range(4):
                i = vector.reduce_max(gm[:, gr:gr + 1], gtm[:, gr * 16:(gr + 1) * 16],
                                      axis=mybir.AxisListType.X)
            s_gnm.inc(i)
            s_gnm.wge(vector, 2 * tt + 1)
            for gr in range(4):
                i = vector.tensor_scalar(gnm[:, gr:gr + 1], gm[:, gr:gr + 1], -1.0, None, ALU.mult)
            s_gnm.inc(i)
            s_ge.wge(vector, 4 * (tt + 1))
            for gr in range(4):
                i = vector.reciprocal(gt1[:, gr:gr + 1], gs[:, gr:gr + 1])
            s_gr.inc(i)
            s_grt.wge(vector, tt + 1)
            vector.tensor_copy(trow[0:1, 0:TT], t_ps[0:1, 0:TT])
            s_gre.inc(vector.tensor_copy(top1row[0:1, tt * TT:(tt + 1) * TT], t1_ps[0:1, 0:TT]))
            s_mb.wge(vector, tt + 1)
            s_ind.inc(vector.tensor_tensor(ind_fm[0:16, tt * TT:(tt + 1) * TT], g_sb[0:16, :],
                                           mb_ps[0:16, 0:TT], ALU.is_equal))

    @p.block.scalar
    def _(scalar):
        for tt in range(NTT):
            s_gnm.wge(scalar, 2 * (tt + 1))
            for gr in range(4):
                i = scalar.activation(ge[:, gr * 16:(gr + 1) * 16], gtm[:, gr * 16:(gr + 1) * 16],
                                      AF.Exp, bias=gnm[:, gr:gr + 1], accum_out=gs[:, gr:gr + 1])
                s_ge.inc(i)
    return ind_fm, top1row, s_ind, s_gre


def emit_attn_tok(p, consts, q_dram, kg_dram, vg_dram, att_dram, nhd, kvmap, scale,
                  ptk_sb, qpos_sb, waits, tag, gate=None):
    NBQ = NTT
    q_sb = [p.sb(f"aq{i}_{tag}", [128, T], BF16) for i in range(2)]
    k_sb = [p.sb(f"ak{i}_{tag}", [128, S], BF16) for i in range(2)]
    v_sb = [p.sb(f"av{i}_{tag}", [128, S], BF16) for i in range(2)]
    p_sb = p.sb(f"apb_{tag}", [128, 4 * 512], BF16)
    bsq = p.sb(f"absq_{tag}", [128, 512], BF16)
    NBK = S // 512
    brow = p.sb(f"abrow_{tag}", [1, NTT + NBK + 4], F32)
    nbrow = p.sb(f"anbr_{tag}", [1, 1], F32)
    nb_sb = p.sb(f"anb_{tag}", [128, 1], F32)
    rt_sb = p.sb(f"art_{tag}", [1, 512], F32)
    r_sb = p.sb(f"arr_{tag}", [128, 512], F32)
    att_sb = [p.sb(f"ao{i}_{tag}", [128, 512], BF16) for i in range(2)]
    gate_sb = p.sb(f"agt_{tag}", [128, 512], BF16)
    msk_sb = p.sb(f"amsk_{tag}", [128, 4 * 2 * 4 * 512], BF16)
    att_ps = p.pf[0]
    sc_ps = [p.pf[1], p.pf[2]]
    lr_ps = p.pf[3]
    b_ps = p.pf[4]
    g_ps = p.pf[5]
    oc, orow_f = consts["ones_col"], consts["ones_row_f"]
    wvals = _wpairs(waits)

    (s_bsq, s_bmm, s_bred, s_brow, s_bsd, s_bneg, s_nb, s_nbe, s_sc, s_mk,
     s_exp, s_av, s_lmm, s_rj, s_rbc, s_rse, s_gt, s_gs, s_fin,
     s_cmb, s_aux, s_mgb, s_mgen) = (p.sem() for _ in range(23))
    s_ldq = [p.sem(kind="sw"), p.sem(kind="sw")]
    s_ldkv = [p.sem(kind="sw"), p.sem(kind="sw")]
    s_stp = [p.sem(kind="hw"), p.sem(kind="hw")]

    def mcol(qc, b2, c):
        return ((qc * 2 + b2) * 4 + c) * 512

    @p.block.tensor
    def _(tensor):
        for hh, n in wvals:
            tensor.wait_ge(hh, n)
        for qc in range(NBQ):
            if qc > 0:
                s_mgen.wge(tensor, qc * 8)
            i = tensor.matmul(b_ps[:, :], orow_f[0:1, :],
                              qpos_sb[0:1, qc * 512:(qc + 1) * 512],
                              start=True, stop=True)
            s_mgb.inc(i)

    @p.block.vector
    def _(vector):
        for hh, n in wvals:
            vector.wait_ge(hh, n)
        for qc in range(NBQ):
            s_mgb.wge(vector, qc + 1)
            for b2 in range(2):
                kb = EXT[qc] - 2 + b2
                for c in range(4):
                    s_mgen.inc(vector.tensor_scalar(
                        msk_sb[:, mcol(qc, b2, c):mcol(qc, b2, c) + 512],
                        b_ps[:, :], ptk_sb[:, kb * 4 + c:kb * 4 + c + 1], NEG,
                        ALU.is_lt, ALU.mult))

    kv_fresh = {}
    slot_of = {}
    kv_tgt = {}
    nslot = 0
    slot_uses = [0, 0]
    for h in range(nhd):
        kv = kvmap(h)
        if kv not in slot_of:
            slot_of[kv] = nslot % 2
            nslot += 1
            kv_fresh[h] = kv
            slot_uses[slot_of[kv]] += 1
        kv_tgt[h] = 16 * 16 * slot_uses[slot_of[kv]]

    sc_ord, mk_ord, exp_ord, av_ord, fin_ord, gt_ord, pv_ord = ({} for _ in range(7))
    nsc = nmk = nexp = nav = nfin = ngt = 0
    slice_base = {}
    nbs = 0
    last_av_c = {}
    for h in range(nhd):
        ns = NTT + (NBK if h in kv_fresh else 0)
        slice_base[h] = (nbs, ns)
        nbs += ns
        for qc in range(NBQ):
            E = EXT[qc]
            if gate is not None:
                ngt += 1
                gt_ord[(h, qc)] = ngt
            for kb in range(E):
                for c in range(4):
                    nsc += 1
                    sc_ord[(h, qc, kb, c)] = nsc
                    if kb >= E - 2:
                        nmk += 1
                        mk_ord[(h, qc, kb, c)] = nmk
                    nexp += 1
                    exp_ord[(h, qc, kb, c)] = nexp
                    pv_ord[(h, qc, kb, c)] = last_av_c.get(c, 0)
                for c in range(4):
                    nav += 1
                    av_ord[(h, qc, kb, c)] = nav
                    last_av_c[c] = nav
            nfin += 1
            fin_ord[(h, qc)] = nfin

    @p.block.gpsimd
    def _(g):
        for h in range(nhd):
            if h == 0:
                for hh, n in wvals:
                    g.wait_ge(hh, n)
            if h >= 2:
                s_stp[0].wge(g, 16 * 2 * (h - 1))
                s_stp[1].wge(g, 16 * 2 * (h - 1))
            if h in kv_fresh:
                kv = kv_fresh[h]
                sl = slot_of[kv]
                for gch in range(8):
                    src, pos = divmod(_SLOT[gch], 4)
                    g.dma_start(out=k_sb[sl][:, gch * 512:(gch + 1) * 512],
                                in_=kg_dram[src, kv, :, pos * 512:(pos + 1) * 512]).then_inc(s_ldkv[sl].h, 16)
                    g.dma_start(
                        out=v_sb[sl][:, gch * 512:(gch + 1) * 512].rearrange("pp (cc d) -> pp cc d", d=128),
                        in_=vg_dram[src, kv, pos * 512:(pos + 1) * 512, :].rearrange("(cc pp) d -> pp cc d", pp=128)
                    ).then_inc(s_ldkv[sl].h, 16)
                s_ldkv[sl].n += 16 * 16
            s_ldq[h % 2].inc(g.dma_start(out=q_sb[h % 2][:, :], in_=q_dram[h, :, :]), 16)

    @p.block.tensor
    def _(tensor):
        for h in range(nhd):
            kv = kvmap(h)
            ksl = slot_of[kv]
            s_ldq[h % 2].wge(tensor, 16 * (h // 2 + 1))
            s_ldkv[ksl].wge(tensor, kv_tgt[h])
            sbase, ns = slice_base[h]
            if h == 0:
                s_mgen.wge(tensor, 32)
            if h >= 1:
                s_nbe.wge(tensor, h)
            for sl in range(ns):
                s_bsq.wge(tensor, sbase + sl + 1)
                if sbase + sl > 0:
                    s_bred.wge(tensor, sbase + sl)
                s_bmm.inc(tensor.matmul(b_ps[0:1, 0:512], oc[:, :], bsq[:, :],
                                        start=True, stop=True))
            s_bneg.wge(tensor, h + 1)
            s_nb.inc(tensor.matmul(b_ps[0:128, 0:1], orow_f[:, 0:128], nbrow[0:1, 0:1],
                                   start=True, stop=True))
            for qc in range(NBQ):
                E = EXT[qc]
                fo = fin_ord[(h, qc)]
                if gate is not None:
                    go = gt_ord[(h, qc)]
                    if go > 1:
                        s_gs.wge(tensor, go - 1)
                    s_gt.inc(tensor.matmul(g_ps[:, :], gate[0][:, :],
                                           q_sb[h % 2][:, qc * 512:(qc + 1) * 512],
                                           start=True, stop=True))
                for kb in range(E):
                    for c in range(4):
                        o = sc_ord[(h, qc, kb, c)]
                        if o > 2:
                            s_exp.wge(tensor, o - 2)
                        s_sc.inc(tensor.matmul(sc_ps[c % 2][:, :],
                                               k_sb[ksl][:, kb * 512 + c * 128:kb * 512 + (c + 1) * 128],
                                               q_sb[h % 2][:, qc * 512:(qc + 1) * 512],
                                               start=True, stop=True))
                    s_exp.wge(tensor, exp_ord[(h, qc, kb, 3)])
                    for c in range(4):
                        first = (kb == 0 and c == 0)
                        last = (kb == E - 1 and c == 3)
                        if first and fo > 1:
                            s_fin.wge(tensor, fo - 1)
                            s_rse.wge(tensor, fo - 1)
                        i = tensor.matmul(lr_ps[0:1, 0:512], oc[:, 0:1],
                                          p_sb[:, c * 512:(c + 1) * 512],
                                          start=first, stop=last)
                        if last:
                            s_lmm.inc(i)
                        cc = kb * 4 + c
                        s_av.inc(tensor.matmul(att_ps[:, :],
                                               v_sb[ksl][:, cc * 128:(cc + 1) * 128],
                                               p_sb[:, c * 512:(c + 1) * 512],
                                               start=first, stop=last))
                s_rj.wge(tensor, fo)
                s_rbc.inc(tensor.matmul(lr_ps[:, 0:512], orow_f[0:1, 0:128],
                                        rt_sb[0:1, 0:512], start=True, stop=True))

    @p.block.scalar
    def _(scalar):
        for h in range(nhd):
            sbase, ns = slice_base[h]
            fresh = h in kv_fresh
            if h == 0:
                for hh, n in wvals:
                    scalar.wait_ge(hh, n)
            s_ldq[h % 2].wge(scalar, 16 * (h // 2 + 1))
            s_ldkv[slot_of[kvmap(h)]].wge(scalar, kv_tgt[h])
            si = 0
            for buf, nsl in ([(q_sb[h % 2], NTT)] +
                             ([(k_sb[slot_of[kvmap(h)]], NBK)] if fresh else [])):
                for sl in range(nsl):
                    if sbase + si > 0:
                        s_bmm.wge(scalar, sbase + si)
                    s_bsq.inc(scalar.activation(bsq[:, :], buf[:, sl * 512:(sl + 1) * 512],
                                                AF.Square))
                    si += 1
            s_brow.wge(scalar, h + 1)
            s_bsd.inc(scalar.activation(brow[0:1, NTT + NBK + 1:NTT + NBK + 2],
                                        brow[0:1, NTT + NBK:NTT + NBK + 1], AF.Sqrt))
            for qc in range(NBQ):
                E = EXT[qc]
                if gate is not None:
                    go = gt_ord[(h, qc)]
                    s_gt.wge(scalar, go)
                    if go > 1:
                        s_fin.wge(scalar, go - 1)
                    s_gs.inc(scalar.activation(gate_sb[:, :], g_ps[:, :], AF.Sigmoid,
                                               bias=gate[1][:, 0:1]))
                for kb in range(E):
                    for c in range(4):
                        if kb >= E - 2:
                            s_mk.wge(scalar, mk_ord[(h, qc, kb, c)])
                        else:
                            s_sc.wge(scalar, sc_ord[(h, qc, kb, c)])
                        pav = pv_ord[(h, qc, kb, c)]
                        if pav > 0:
                            s_av.wge(scalar, pav)
                        if kb == 0 and c == 0 and qc == 0:
                            s_nbe.wge(scalar, h + 1)
                        s_exp.inc(scalar.activation(
                            p_sb[:, c * 512:(c + 1) * 512], sc_ps[c % 2][:, :], AF.Exp,
                            bias=nb_sb[:, 0:1], scale=scale))

    @p.block.vector
    def _(vector):
        for h in range(nhd):
            sbase, ns = slice_base[h]
            if h == 0:
                for hh, n in wvals:
                    vector.wait_ge(hh, n)
            for si in range(ns):
                s_bmm.wge(vector, sbase + si + 1)
                s_bred.inc(vector.reduce_max(brow[0:1, si:si + 1], b_ps[0:1, 0:512],
                                             axis=mybir.AxisListType.X))
            s_bred.wge(vector, sbase + ns)
            vector.reduce_max(brow[0:1, NTT + NBK + 2:NTT + NBK + 3], brow[0:1, 0:NTT],
                              axis=mybir.AxisListType.X)
            s_cmb.inc(vector.reduce_max(brow[0:1, NTT + NBK + 3:NTT + NBK + 4],
                                        brow[0:1, NTT:NTT + NBK], axis=mybir.AxisListType.X))
            s_cmb.wge(vector, s_cmb.n)
            s_brow.inc(vector.tensor_tensor(brow[0:1, NTT + NBK:NTT + NBK + 1],
                                            brow[0:1, NTT + NBK + 2:NTT + NBK + 3],
                                            brow[0:1, NTT + NBK + 3:NTT + NBK + 4], ALU.mult))
            s_bsd.wge(vector, h + 1)
            s_bneg.inc(vector.tensor_scalar(nbrow[0:1, 0:1],
                                            brow[0:1, NTT + NBK + 1:NTT + NBK + 2],
                                            -scale, None, ALU.mult))
            s_nb.wge(vector, h + 1)
            s_nbe.inc(vector.tensor_copy(nb_sb[:, 0:1], b_ps[0:128, 0:1]))
            for qc in range(NBQ):
                E = EXT[qc]
                fo = fin_ord[(h, qc)]
                for kb in range(E - 2, E):
                    b2 = kb - (E - 2)
                    for c in range(4):
                        s_sc.wge(vector, sc_ord[(h, qc, kb, c)])
                        s_mk.inc(vector.tensor_tensor(
                            sc_ps[c % 2][:, :], sc_ps[c % 2][:, :],
                            msk_sb[:, mcol(qc, b2, c):mcol(qc, b2, c) + 512], ALU.add))
                s_lmm.wge(vector, fo)
                s_rj.inc(vector.reciprocal(rt_sb[0:1, :], lr_ps[0:1, 0:512]))
                s_rbc.wge(vector, fo)
                if fo > 2:
                    s_stp[(fo - 1) % 2].wge(vector, 16 * ((fo - 3) // 2 + 1))
                s_rse.inc(vector.tensor_copy(r_sb[:, :], lr_ps[:, 0:512]))
                s_rse.wge(vector, fo)
                s_av.wge(vector, av_ord[(h, qc, E - 1, 3)])
                i = vector.tensor_tensor(att_sb[(fo - 1) % 2][:, :], att_ps[:, :], r_sb[:, :], ALU.mult)
                if gate is not None:
                    s_aux.inc(i)
                    s_aux.wge(vector, s_aux.n)
                    s_gs.wge(vector, gt_ord[(h, qc)])
                    i = vector.tensor_tensor(att_sb[(fo - 1) % 2][:, :], att_sb[(fo - 1) % 2][:, :],
                                             gate_sb[:, :], ALU.mult)
                s_fin.inc(i)

    @p.block.sync
    def _(g):
        for h in range(nhd):
            for qc in range(NBQ):
                fo = fin_ord[(h, qc)]
                s_fin.wge(g, fo)
                s_stp[(fo - 1) % 2].inc(g.dma_start(out=att_dram[h, :, qc * 512:(qc + 1) * 512],
                                                    in_=att_sb[(fo - 1) % 2][:, :]), 16)
    return SemGroup(s_stp)


def emit_final_wait(p, sems):
    @p.block.gpsimd
    def _(g):
        for s in sems:
            for h, nn in s.pairs():
                g.wait_ge(h, nn)


def build_p1():
    p = Prog().open()
    x = p.din("x", [D, T], F32)
    wq = p.din("wq", [NH, 128, NCH, 128], F32)
    wk = p.din("wk", [KVH0, 128, NCH, 128], F32)
    wv = p.din("wv", [KVH0, 128, NCH, 128], F32)
    bqkv = p.din("bqkv", [128, NH + 2 * KVH0], F32)
    ctab = p.din("ctab", [128, T], F32)
    stab = p.din("stab", [128, T], F32)
    cpt = p.din("c_pt", [128, 128], BF16)
    q_out = p.dout("q_out", [NH, 128, T], BF16)
    k_out = p.dout("k_out", [KVH0, 128, T], BF16)
    v_out = p.dout("v_out", [KVH0, T, 128], BF16)

    consts, s_const = load_consts(p)
    xln = p.sb("xln", [128, NCH * T], BF16)
    xbf = xln
    cos_sb = p.sb("cos", [128, T], F32)
    sin_sb = p.sb("sin", [128, T], F32)
    pt_sb = p.sb("pt", [128, 128], BF16)
    bias_sb = p.sb("bias", [128, NH + 2 * KVH0], F32)
    s_misc = p.sem(kind="hw")

    @p.block.sync
    def _(sync):
        sync.dma_start(out=cos_sb[:, :], in_=ctab[:, :]).then_inc(s_misc.h, 16)
        sync.dma_start(out=sin_sb[:, :], in_=stab[:, :]).then_inc(s_misc.h, 16)
        sync.dma_start(out=pt_sb[:, :], in_=cpt[:, :]).then_inc(s_misc.h, 16)
        sync.dma_start(out=bias_sb[:, :], in_=bqkv[:, :]).then_inc(s_misc.h, 16)
    s_misc.n = 64

    s_x = emit_load_xbf(p, x, xbf)
    s_ln = emit_ln(p, xbf, xln, consts, [s_x, s_const], "l0a")
    rope = (cos_sb, sin_sb, pt_sb)
    sq = emit_proj(p, xln, wq, NH, consts, [s_ln, s_misc], "q",
                   bias_sb=bias_sb, bias_off=0, mode="rope", rope=rope, out_dram=q_out)
    sk = emit_proj(p, xln, wk, KVH0, consts, [s_ln, s_misc, sq], "k",
                   bias_sb=bias_sb, bias_off=NH, mode="rope", rope=rope, out_dram=k_out)
    sv = emit_proj(p, xln, wv, KVH0, consts, [s_ln, s_misc, sk], "v",
                   bias_sb=bias_sb, bias_off=NH + KVH0, mode="vtm", out_dram=v_out)
    emit_final_wait(p, [sq, sk, sv])
    return p


def build_attn_tok(nkv, gate=False, tag="a"):
    p = Prog().open()
    q = p.din("q", [NH, 128, T], BF16)
    kg = p.din("kg", [2, nkv, 128, T], BF16)
    vg = p.din("vg", [2, nkv, T, 128], BF16)
    ptk = p.din("ptk", [128, (S // 512) * 4], F32)
    qpos = p.din("qpos", [1, T], F32)
    att = p.dout("att", [NH, 128, T], BF16)
    scale = HD ** -0.5
    consts, s_const = load_consts(p, reg_vals=[scale])
    ptk_sb = p.sb("ptk_sb", [128, (S // 512) * 4], F32)
    qpos_sb = p.sb("qpos_sb", [1, T], F32)
    s_mh = p.sem(kind="hw")
    s_msw = p.sem(kind="sw")
    s_misc = SemGroup([s_mh, s_msw])
    gt = None
    if gate:
        gwt = p.din("gwt", [128, 128], F32)
        gbt = p.din("gbt", [128, 1], F32)
        gw_sb = p.sb("gw", [128, 128], BF16)
        gb_sb = p.sb("gb", [128, 1], F32)
        gt = (gw_sb, gb_sb)

    @p.block.sync
    def _(sync):
        s_mh.inc(sync.dma_start(out=ptk_sb[:, :], in_=ptk[:, :]), 16)
        s_mh.inc(sync.dma_start(out=qpos_sb[:, :], in_=qpos[:, :]), 16)

    if gate:
        @p.block.gpsimd
        def _(g):
            s_msw.inc(g.dma_start(out=gw_sb[:, :], in_=gwt[:, :]), 16)
            s_msw.inc(g.dma_start(out=gb_sb[:, :], in_=gbt[:, :]), 16)

    kvmap = (lambda h: h * nkv // NH) if nkv > 1 else (lambda h: 0)
    s_done = emit_attn_tok(p, consts, q, kg, vg, att, NH, kvmap, scale,
                           ptk_sb, qpos_sb, [s_const, s_misc], tag, gate=gt)
    emit_final_wait(p, [s_done])
    return p


def _attn_host_consts(c):
    nb4 = (S // 512) * 4
    ptk = np.ascontiguousarray(np.arange(128, dtype=np.float32)[:, None]
                               + 128.0 * np.arange(nb4, dtype=np.float32)[None, :])
    qpos = np.ascontiguousarray(_core_pos(c).astype(np.float32).reshape(1, T))
    return ptk, qpos


def _run_attn(builder_args, q_key, k_key, v_key, results, extra=None):
    nkv, gate, tag = builder_args
    p = build_attn_tok(nkv, gate=gate, tag=tag)
    cid = np.ascontiguousarray(np.eye(128, dtype=BF))
    cidf = np.ascontiguousarray(np.eye(128, dtype=np.float32))
    in_maps = []
    for c in range(NCORE):
        pair = (c // 2) * 2
        kgv = np.stack([results[pair][k_key], results[pair + 1][k_key]])
        vgv = np.stack([results[pair][v_key], results[pair + 1][v_key]])
        if kgv.ndim == 3:
            kgv = kgv[:, None]
            vgv = vgv[:, None]
        ptk, qpos = _attn_host_consts(c)
        m = {"q": results[c][q_key], "kg": np.ascontiguousarray(kgv),
             "vg": np.ascontiguousarray(vgv), "ptk": ptk, "qpos": qpos,
             "c_id": cid, "c_idf": cidf}
        if extra:
            m.update(extra)
        in_maps.append(m)
    res = run_bass_kernel_spmd(p.nc, in_maps, core_ids=list(range(NCORE)))
    p.close()
    return res.results


def run_p2(p1_results, masks=None):
    return _run_attn((KVH0, False, "a"), "q_out", "k_out", "v_out", p1_results)


def emit_mask_all(p, consts, ind_dram, s_indd, tag):
    orow_f = consts["ones_row_f"]
    mk_ps = p.pf[2]
    mask_all = p.sb(f"mska_{tag}", [128, 16 * T], U8)
    ind_t = p.sb(f"indt_{tag}", [1, 2 * TT], F32)
    s_mld = [p.sem(kind="hw"), p.sem(kind="hw")]
    s_mb2, s_mev = p.sem(), p.sem()
    indd_n = s_indd.n

    @p.block.sync
    def _(sync):
        s_indd.wge(sync, indd_n)
        for m in range(16):
            for tt in range(NTT):
                it = m * NTT + tt
                if it >= 2:
                    s_mb2.wge(sync, it - 1)
                s_mld[it % 2].inc(sync.dma_start(
                    out=ind_t[0:1, (it % 2) * TT:(it % 2 + 1) * TT],
                    in_=ind_dram[m, tt * TT:(tt + 1) * TT]), 16)

    @p.block.tensor
    def _(tensor):
        for m in range(16):
            for tt in range(NTT):
                it = m * NTT + tt
                s_mld[it % 2].wge(tensor, 16 * (it // 2 + 1))
                if it >= 1:
                    s_mev.wge(tensor, it)
                s_mb2.inc(tensor.matmul(
                    mk_ps[:, :], orow_f[0:1, :],
                    ind_t[0:1, (it % 2) * TT:(it % 2 + 1) * TT],
                    start=True, stop=True))

    @p.block.vector
    def _(vector):
        for m in range(16):
            for tt in range(NTT):
                it = m * NTT + tt
                s_mb2.wge(vector, it + 1)
                s_mev.inc(vector.tensor_copy(
                    mask_all[:, m * T + tt * TT:m * T + (tt + 1) * TT], mk_ps[:, :]))
    return mask_all, s_mev


def build_p3():
    p = Prog().open()
    x = p.din("x", [NCH, 128, T], F32)
    attd = p.din("attd", [NCH, 128, T], BF16)
    wo = p.din("wo", [NCH, 128, NCH, 128], F32)
    bo = p.din("bo", [128, NCH], F32)
    wg = p.din("wg", [128, NCH * 16], F32)
    wqkv = p.din("wqkv", [48, 128, NCH, 128], F32)
    x1_out = p.dout("x1_out", [NCH, 128, T], F32)
    q_out = p.dout("q_out", [16, 128, T], BF16)
    ksh_out = p.dout("ksh_out", [1, 128, T], BF16)
    vsh_out = p.dout("vsh_out", [1, T, 128], BF16)
    ind_dram = p.dint("ind_d", [16, T], F32)

    consts, s_const = load_consts(p)
    ensure_proj_scratch(p)
    bias_sb = p.sb("bias", [128, NCH], F32)
    wg_sb = p.sb("wg_sb", [128, NCH * 16], BF16)
    xln = p.sb("xln", [128, NCH * T], BF16)
    ksh_acc = p.sb("ksh_acc", [128, T], BF16)
    vsh_acc = p.sb("vsh_acc", [128, T], BF16)
    vtm_sb = p.sb("vtm_sb", [128, TT], BF16)
    s_mh = p.sem(kind="hw")
    s_msw = p.sem(kind="sw")
    s_misc = SemGroup([s_mh, s_msw])

    @p.block.sync
    def _(sync):
        s_mh.inc(sync.dma_start(out=bias_sb[:, :], in_=bo[:, :]), 16)

    @p.block.gpsimd
    def _(g):
        s_msw.inc(g.dma_start(out=wg_sb[:, :], in_=wg[:, :]), 16)

    with p.scope():
        s_x1, s_st1 = emit_resadd_ln(p, x, attd, x1_out, xln, consts, [s_const, s_misc],
                                     "p3", wo_dram=wo, bias_sb=bias_sb, bias_off=0)
        s_ln = emit_ln(p, xln, xln, consts, [s_x1, s_st1], "p3")
    ind_fm, top1row, s_ind, s_gre = emit_l0f_gating(p, xln, wg_sb, consts,
                                                    [s_ln, s_misc], "p3")
    s_indf = p.sem(kind="hw")

    @p.block.sync
    def _(sync):
        s_ind.wge(sync, s_ind.n)
        s_indf.inc(sync.dma_start(out=ind_dram[:, :], in_=ind_fm[0:16, :]), 16)

    sq = emit_proj(p, xln, wqkv[0:16], 16, consts, [s_ln, s_ind], "q",
                   mode="plain", out_dram=q_out)
    mask_all, s_mask = emit_mask_all(p, consts, ind_dram, s_indf, "p3")
    s_kacc = emit_proj(p, xln, wqkv[16:32], 16, consts, [s_ln, sq, s_mask], "k",
                       mode="sel", sel=(mask_all, ksh_acc))
    s_vacc = emit_proj(p, xln, wqkv[32:48], 16, consts, [s_ln, s_kacc], "v",
                       mode="sel", sel=(mask_all, vsh_acc))

    s_kst = p.sem(kind="sw")

    @p.block.gpsimd
    def _(g):
        s_kacc.wge(g, s_kacc.n)
        s_kst.inc(g.dma_start(out=ksh_out[0, :, :], in_=ksh_acc[:, :]), 16)

    t1_ps = p.pf[3]
    vt_ps = p.pb[0]
    s_t1b, s_t1m, s_vt, s_vte = (p.sem() for _ in range(4))
    s_vst = p.sem(kind="hw")
    orow_f = consts["ones_row_f"]
    ident = consts["ident"]

    @p.block.tensor
    def _(tensor):
        s_vacc.wge(tensor, s_vacc.n)
        s_gre.wge(tensor, s_gre.n)
        for tt in range(NTT):
            if tt > 0:
                s_t1m.wge(tensor, tt)
            s_t1b.inc(tensor.matmul(t1_ps[:, :], orow_f[0:1, :],
                                    top1row[0:1, tt * TT:(tt + 1) * TT], start=True, stop=True))
            s_t1m.wge(tensor, tt + 1)
            if tt > 0:
                s_vte.wge(tensor, 4 * tt)
            for sg in range(4):
                i = tensor.transpose(vt_ps[:, sg * 128:(sg + 1) * 128],
                                     vsh_acc[:, tt * TT + sg * 128:tt * TT + (sg + 1) * 128],
                                     ident[:, :])
            s_vt.inc(i)

    @p.block.vector
    def _(vector):
        for tt in range(NTT):
            s_t1b.wge(vector, tt + 1)
            s_t1m.inc(vector.tensor_tensor(vsh_acc[:, tt * TT:(tt + 1) * TT],
                                           vsh_acc[:, tt * TT:(tt + 1) * TT],
                                           t1_ps[:, :], ALU.mult))
            s_vt.wge(vector, tt + 1)
            if tt > 0:
                s_vst.wge(vector, 16 * tt)
            for sg in range(4):
                i = vector.tensor_copy(vtm_sb[:, sg * 128:(sg + 1) * 128],
                                       vt_ps[:, sg * 128:(sg + 1) * 128])
                s_vte.inc(i)

    @p.block.sync
    def _(g):
        for tt in range(NTT):
            s_vte.wge(g, 4 * (tt + 1))
            s_vst.inc(g.dma_start(
                out=vsh_out[0, tt * TT:(tt + 1) * TT, :].rearrange("(sg pp) d -> pp sg d", pp=128),
                in_=vtm_sb[:, :].rearrange("p (sg d) -> p sg d", sg=4)), 16)

    emit_final_wait(p, [s_st1, sq, s_kst, s_vst])
    return p


def run_p3(inputs, p2_results, xs_list):
    p = build_p3()
    Wo = np.asarray(inputs["L0_attn_Wo"])
    bo = np.asarray(inputs["L0_attn_bo"]).reshape(NCH, 128).T
    Wg = np.asarray(inputs["L0_ffn_W_gate"])
    wg_host = np.ascontiguousarray(Wg.reshape(16, 16, 128).transpose(2, 1, 0).reshape(128, 256).astype(np.float32))
    wo_t = _tile_lhsT(Wo)
    wqkv_t = _tile_lhsT(np.asarray(inputs["L0_ffn_W_qkv"]))
    cid = np.ascontiguousarray(np.eye(128, dtype=BF))
    cidf = np.ascontiguousarray(np.eye(128, dtype=np.float32))
    in_maps = []
    for c in range(NCORE):
        attd = p2_results[c]["att"]
        in_maps.append({
            "x": np.ascontiguousarray(xs_list[c].reshape(NCH, 128, T)),
            "attd": attd, "wo": wo_t, "bo": np.ascontiguousarray(bo.astype(np.float32)),
            "wg": wg_host, "wqkv": wqkv_t, "c_id": cid, "c_idf": cidf,
        })
    res = run_bass_kernel_spmd(p.nc, in_maps, core_ids=list(range(NCORE)))
    p.close()
    return res.results



def build_p5():
    p = Prog().open()
    x = p.din("x", [NCH, 128, T], F32)
    attd = p.din("attd", [NCH, 128, T], BF16)
    wq = p.din("wq", [NH, 128, NCH, 128], F32)
    wk = p.din("wk", [NH, 128, NCH, 128], F32)
    wv = p.din("wv", [NH, 128, NCH, 128], F32)
    x2_out = p.dout("x2_out", [NCH, 128, T], F32)
    q_out = p.dout("q_out", [NH, 128, T], BF16)
    k_out = p.dout("k_out", [NH, 128, T], BF16)
    v_out = p.dout("v_out", [NH, T, 128], BF16)

    consts, s_const = load_consts(p)
    ensure_proj_scratch(p)
    xln = p.sb("xln", [128, NCH * T], BF16)
    with p.scope():
        s_x1, s_st1 = emit_resadd_ln(p, x, attd, x2_out, xln, consts, [s_const], "p5")
        s_ln = emit_ln(p, xln, xln, consts, [s_x1, s_st1], "p5")
    sq = emit_proj(p, xln, wq, NH, consts, [s_ln], "q", mode="plain", out_dram=q_out)
    sk = emit_proj(p, xln, wk, NH, consts, [s_ln, sq], "k", mode="plain", out_dram=k_out)
    sv = emit_proj(p, xln, wv, NH, consts, [s_ln, sk], "v", mode="vtm", out_dram=v_out)
    emit_final_wait(p, [s_st1, sq, sk, sv])
    return p


def run_p5(inputs, p4_results, p3_results):
    p = build_p5()
    wq_t = _tile_lhsT(np.asarray(inputs["L1_attn_q_w"]))
    wk_t = _tile_lhsT(np.asarray(inputs["L1_attn_k_w"]))
    wv_t = _tile_lhsT(np.asarray(inputs["L1_attn_v_w"]))
    cid = np.ascontiguousarray(np.eye(128, dtype=BF))
    cidf = np.ascontiguousarray(np.eye(128, dtype=np.float32))
    in_maps = []
    for c in range(NCORE):
        attd = p4_results[c]["att"]
        in_maps.append({
            "x": p3_results[c]["x1_out"], "attd": attd,
            "wq": wq_t, "wk": wk_t, "wv": wv_t, "c_id": cid, "c_idf": cidf,
        })
    res = run_bass_kernel_spmd(p.nc, in_maps, core_ids=list(range(NCORE)))
    p.close()
    return res.results


def run_p4(p3_results):
    out = []
    for c in range(NCORE):
        r = dict(p3_results[c])
        r["ksh3"] = r["ksh_out"][0]
        r["vsh3"] = r["vsh_out"][0]
        out.append(r)
    return _run_attn((1, False, "m"), "q_out", "ksh3", "vsh3", out)


def run_p6(inputs, p5_results):
    gwt = np.ascontiguousarray(np.asarray(inputs["L1_attn_gate_w"]).T.astype(np.float32))
    gbt = np.ascontiguousarray(np.asarray(inputs["L1_attn_gate_b"]).reshape(128, 1).astype(np.float32))
    return _run_attn((NH, True, "g"), "q_out", "k_out", "v_out", p5_results,
                     extra={"gwt": gwt, "gbt": gbt})


def emit_dequant(p, consts, pk_dram, sc_dram, out_dram, nlm, nw, tag, waits=(), bank=0):
    ngrp = nw * 8 // 128
    SPW = 256 if nw % 256 == 0 else 176
    assert nw % SPW == 0 and SPW % 16 == 0
    nsp = nw // SPW
    gps = SPW * 8 // 128
    identf = consts["identf"]
    pk = p.sb(f"dqp_{tag}", [128, nw], I32)
    an = p.sb(f"dqa_{tag}", [128, 8 * SPW], I32)
    b2 = p.sb(f"dqb_{tag}", [128, 8 * SPW], I32)
    f1 = p.sb(f"dqf1_{tag}", [128, 8 * SPW], F32)
    f2 = p.sb(f"dqf2_{tag}", [128, 8 * SPW], F32)
    f3 = p.sb(f"dqf3_{tag}", [128, 8 * SPW], F32)
    f4 = p.sb(f"dqf4_{tag}", [128, 8 * SPW], F32)
    valh = p.sb(f"dqv_{tag}", [128, 8 * SPW], F32)
    sc16 = p.sb(f"dqs_{tag}", [128, ngrp], F16)
    sc32 = p.sb(f"dqs3_{tag}", [128, ngrp], F32)
    asm = p.sb(f"dqo_{tag}", [128, ngrp * 128], BF16)
    tp_ps = p.pf[bank]
    s_dec, s_val, s_scl, s_tp, s_ev, s_sc32 = (p.sem() for _ in range(6))
    s_ld = p.sem(kind="hw")
    s_st = p.sem(kind="sw")
    wvals = _wpairs(waits)

    @p.block.sync
    def _(sync):
        for lm in range(nlm):
            if lm >= 1:
                s_dec.wge(sync, lm * nsp)
            s_ld.inc(sync.dma_start(out=pk[:, :], in_=pk_dram[lm, :, :]), 16)
            if lm >= 1:
                s_sc32.wge(sync, lm)
            s_ld.inc(sync.dma_start(out=sc16[:, :], in_=sc_dram[lm, :, :]), 16)

    @p.block.vector
    def _(vector):
        for lm in range(nlm):
            if lm == 0:
                for hh, n in wvals:
                    vector.wait_ge(hh, n)
            s_ld.wge(vector, 32 * (lm + 1))
            s_sc32.inc(vector.tensor_copy(sc32[:, :], sc16[:, :]))
            for sp in range(nsp):
                if lm + sp > 0:
                    s_tp.wge(vector, (lm * nsp + sp - 1) * gps + gps)
                w0 = sp * SPW
                J = SPW

                def _stage(emit1):
                    for j in range(8):
                        i = emit1(j)
                    s_val.inc(i)
                    s_val.wge(vector, s_val.n)

                _stage(lambda j: vector.tensor_scalar(an[:, j * J:(j + 1) * J], pk[:, w0:w0 + SPW],
                                                      4 * j, 15, ALU.logical_shift_right, ALU.bitwise_and))
                _stage(lambda j: vector.tensor_scalar(b2[:, j * J:(j + 1) * J], an[:, j * J:(j + 1) * J],
                                                      7, None, ALU.bitwise_and))
                _stage(lambda j: vector.tensor_copy(f1[:, j * J:(j + 1) * J], b2[:, j * J:(j + 1) * J]))
                _stage(lambda j: vector.tensor_scalar(b2[:, j * J:(j + 1) * J], an[:, j * J:(j + 1) * J],
                                                      3, None, ALU.logical_shift_right))
                _stage(lambda j: vector.tensor_copy(f2[:, j * J:(j + 1) * J], b2[:, j * J:(j + 1) * J]))
                _stage(lambda j: vector.tensor_scalar(f2[:, j * J:(j + 1) * J], f2[:, j * J:(j + 1) * J],
                                                      -2.0, 1.0, ALU.mult, ALU.add))
                _stage(lambda j: vector.tensor_scalar(f3[:, j * J:(j + 1) * J], f1[:, j * J:(j + 1) * J],
                                                      -4.0, 0.0, ALU.add, ALU.max))
                _stage(lambda j: vector.tensor_scalar(f4[:, j * J:(j + 1) * J], f1[:, j * J:(j + 1) * J],
                                                      -6.0, 0.0, ALU.add, ALU.max))
                _stage(lambda j: vector.tensor_tensor(f3[:, j * J:(j + 1) * J], f1[:, j * J:(j + 1) * J],
                                                      f3[:, j * J:(j + 1) * J], ALU.add))
                _stage(lambda j: vector.scalar_tensor_tensor(f3[:, j * J:(j + 1) * J], f3[:, j * J:(j + 1) * J],
                                                             0.5, f4[:, j * J:(j + 1) * J], ALU.mult, ALU.add))
                vh = valh[:, :].rearrange("p (w j) -> p j w", j=8)
                for j in range(8):
                    i = vector.tensor_tensor(vh[:, j, :], f3[:, j * J:(j + 1) * J],
                                             f2[:, j * J:(j + 1) * J], ALU.mult)
                s_dec.inc(i)
                s_dec.wge(vector, s_dec.n)
                for gl in range(gps):
                    g = sp * gps + gl
                    i = vector.tensor_scalar(valh[:, gl * 128:(gl + 1) * 128],
                                             valh[:, gl * 128:(gl + 1) * 128],
                                             sc32[:, g:g + 1], None, ALU.mult)
                s_scl.inc(i)
                for gl in range(gps):
                    g = sp * gps + gl
                    if lm >= 1 and sp == 0 and gl == 0:
                        s_st.wge(vector, 16 * lm)
                    s_tp.wge(vector, (lm * nsp + sp) * gps + gl + 1)
                    s_ev.inc(vector.tensor_copy(asm[:, g * 128:(g + 1) * 128], tp_ps[:, 0:128]))

    @p.block.tensor
    def _(tensor):
        for lm in range(nlm):
            if lm == 0:
                for hh, n in wvals:
                    tensor.wait_ge(hh, n)
            for sp in range(nsp):
                s_scl.wge(tensor, lm * nsp + sp + 1)
                for gl in range(gps):
                    o = (lm * nsp + sp) * gps + gl
                    if o >= 1:
                        s_ev.wge(tensor, o)
                    s_tp.inc(tensor.transpose(tp_ps[:, 0:128],
                                              valh[:, gl * 128:(gl + 1) * 128], identf[:, :]))

    @p.block.gpsimd
    def _(g):
        for lm in range(nlm):
            s_ev.wge(g, lm * ngrp + ngrp)
            s_st.inc(g.dma_start(out=out_dram[lm, :, :], in_=asm[:, :]), 16)
    return s_st


def build_p7a():
    p = Prog().open()
    gup = p.din("gup", [11, 128, 256], I32)
    gus = p.din("gus", [11, 128, 16], F16)
    dnp = p.din("dnp", [2, 128, 704], I32)
    dns = p.din("dns", [2, 128, 44], F16)
    wgu_mine = p.dout("wgu_mine", [11, 128, 16 * 128], BF16)
    down_mine = p.dout("down_mine", [2, 128, 44 * 128], BF16)
    consts, s_const = load_consts(p)

    s1 = emit_dequant(p, consts, gup, gus, wgu_mine, 11, 256, "gu", waits=[s_const], bank=0)
    s2 = emit_dequant(p, consts, dnp, dns, down_mine, 2, 704, "dn", waits=[s_const], bank=1)
    emit_final_wait(p, [s1, s2])
    return p


def run_p7a(inputs):
    p = build_p7a()
    gup = np.asarray(inputs["L1_ffn_gate_up_packed"])
    gus = np.asarray(inputs["L1_ffn_gate_up_scales"])
    dnp = np.asarray(inputs["L1_ffn_down_packed"])
    dns = np.asarray(inputs["L1_ffn_down_scales"])
    cid = np.ascontiguousarray(np.eye(128, dtype=BF))
    cidf = np.ascontiguousarray(np.eye(128, dtype=np.float32))
    in_maps = []
    for c in range(NCORE):
        r0, r1 = c * 1408, (c + 1) * 1408
        d0, d1 = c * 256, (c + 1) * 256
        in_maps.append({
            "gup": np.ascontiguousarray(gup[r0:r1].reshape(11, 128, 256)),
            "gus": np.ascontiguousarray(gus.T[r0:r1].reshape(11, 128, 16)),
            "dnp": np.ascontiguousarray(dnp[d0:d1].reshape(2, 128, 704)),
            "dns": np.ascontiguousarray(dns.T[d0:d1].reshape(2, 128, 44)),
            "c_id": cid, "c_idf": cidf,
        })
    res = run_bass_kernel_spmd(p.nc, in_maps, core_ids=list(range(NCORE)))
    p.close()
    return res.results


def emit_l1ffn(p, consts, xln, wgu, wdn, h_dram, xout, s_st1, s_ln, extra_waits):
    oc = consts["ones_col"]
    orow_f = consts["ones_row_f"]

    wbA = [p.sb(f"wA{i}", [128, NCH * 128], BF16) for i in range(4)]
    sil_sb = p.sb("sil_sb", [128, TT], F32)
    hb = [p.sb(f"hb{i}", [128, TT], BF16) for i in range(2)]
    wdb = [p.sb(f"wd{i}", [128, 5 * 128], BF16) for i in range(2)]
    hsb = [p.sb(f"hs{i}", [128, TT], BF16) for i in range(2)]
    y_sb = p.sb("y_sb", [128, NCH * TT], BF16)
    ysq = p.sb("ysq", [128, TT], BF16)
    x3t = p.sb("x3t", [128, NCH * TT], F32)
    rrow = p.sb("rrow", [1, 3 * TT], F32)
    ytmp = p.sb("ytmp", [128, TT], F32)
    xoc = [p.sb(f"xoc{i}", [128, TT], F32) for i in range(2)]
    OG = [(0, 5), (5, 10), (10, 15), (15, 16)]
    yev_cum = []
    tot = 0
    for g in range(4 * NTT):
        yev_cum.append(tot)
        tot += OG[g % 4][1] - OG[g % 4][0]

    (s_gu, s_sil, s_h, s_ymm, s_yev, s_ysq, s_rms,
     s_msr, s_sd, s_rr, s_rb, s_fin, s_aux) = (p.sem() for _ in range(13))
    s_x3 = p.sem(kind="sw")
    s_wA = [p.sem(kind="sw"), p.sem(kind="sw")]
    s_hld = [p.sem(kind="sw"), p.sem(kind="sw")]
    s_wd = [p.sem(kind="sw"), p.sem(kind="sw")]
    s_hst = [p.sem(kind="hw"), p.sem(kind="hw")]
    s_fst = [p.sem(kind="hw"), p.sem(kind="hw")]

    @p.block.gpsimd
    def _(g):
        for hh, nn in _wpairs(extra_waits):
            g.wait_ge(hh, nn)
        for tt in range(NTT):
            for ip in range(44):
                it = tt * 44 + ip
                if it >= 2:
                    s_gu.wge(g, it - 1)
                s_wA[it % 2].inc(g.dma_start(out=wbA[(it % 2) * 2][:, :], in_=wgu[ip, :, :]), 16)
                s_wA[it % 2].inc(g.dma_start(out=wbA[(it % 2) * 2 + 1][:, :], in_=wgu[44 + ip, :, :]), 16)
            for ogi, (o0, o1) in enumerate(OG):
                if ogi == 0:
                    s_hst[0].wge(g, 16 * 22 * (tt + 1))
                    s_hst[1].wge(g, 16 * 22 * (tt + 1))
                for i in range(44):
                    ib = (tt * 4 + ogi) * 44 + i
                    if ib >= 2:
                        s_ymm.wge(g, ib - 1)
                    s_hld[ib % 2].inc(g.dma_start(out=hsb[ib % 2][:, :],
                                                  in_=h_dram[i, :, tt * TT:(tt + 1) * TT]), 16)
                    s_wd[ib % 2].inc(g.dma_start(
                        out=wdb[ib % 2][:, 0:(o1 - o0) * 128].rearrange("p (o mm) -> p o mm", o=o1 - o0),
                        in_=wdn[o0:o1, :, i * 128:(i + 1) * 128].rearrange("o p mm -> p o mm")), 16)
            if tt >= 1:
                s_fin.wge(g, tt * NCH)
            for hh, nn in s_st1.pairs():
                g.wait_ge(hh, nn)
            s_x3.inc(g.dma_start(out=x3t[:, :].rearrange("p (c t) -> p c t", c=NCH),
                                 in_=xout[0:NCH, :, tt * TT:(tt + 1) * TT].rearrange("c p t -> p c t")), 16)

    @p.block.tensor
    def _(tensor):
        for tt in range(NTT):
            if tt == 0:
                s_ln.wge(tensor, s_ln.n)
            for ip in range(44):
                it = tt * 44 + ip
                s_wA[it % 2].wge(tensor, 32 * (it // 2 + 1))
                if it >= 1:
                    s_sil.wge(tensor, it)
                    s_h.wge(tensor, it)
                if ip == 0 and tt >= 1:
                    s_aux.wge(tensor, tt * NCH)
                for c in range(NCH):
                    tensor.matmul(p.pf[0][:, :], wbA[(it % 2) * 2][:, c * 128:(c + 1) * 128],
                                  xln[:, c * T + tt * TT:c * T + tt * TT + TT],
                                  start=(c == 0), stop=(c == NCH - 1))
                for c in range(NCH):
                    i = tensor.matmul(p.pf[1][:, :], wbA[(it % 2) * 2 + 1][:, c * 128:(c + 1) * 128],
                                      xln[:, c * T + tt * TT:c * T + tt * TT + TT],
                                      start=(c == 0), stop=(c == NCH - 1))
                s_gu.inc(i)
            s_h.wge(tensor, 44 * (tt + 1))
            for ogi, (o0, o1) in enumerate(OG):
                gG = tt * 4 + ogi
                for i in range(44):
                    ib = gG * 44 + i
                    s_hld[ib % 2].wge(tensor, 16 * (ib // 2 + 1))
                    s_wd[ib % 2].wge(tensor, 16 * (ib // 2 + 1))
                    if i == 0 and gG >= 1:
                        s_yev.wge(tensor, yev_cum[gG])
                    for ol in range(o1 - o0):
                        ii = tensor.matmul(p.pf[ol][:, :], wdb[ib % 2][:, ol * 128:(ol + 1) * 128],
                                           hsb[ib % 2][:, :], start=(i == 0), stop=(i == 43))
                    s_ymm.inc(ii)
                for ol in range(o1 - o0):
                    o = o0 + ol
                    s_ysq.wge(tensor, tt * NCH + o + 1)
                    i = tensor.matmul(p.pf[5][0:1, 0:TT], oc[:, :], ysq[:, :],
                                      start=(o == 0), stop=(o == NCH - 1))
                    s_rms.inc(i)
            s_rr.wge(tensor, tt + 1)
            s_rb.inc(tensor.matmul(p.pf[0][:, :], orow_f[0:1, :], rrow[0:1, 2 * TT:3 * TT],
                                   start=True, stop=True))

    @p.block.scalar
    def _(scalar):
        for tt in range(NTT):
            for ip in range(44):
                it = tt * 44 + ip
                s_gu.wge(scalar, it + 1)
                if it >= 1:
                    s_h.wge(scalar, it)
                s_sil.inc(scalar.activation(sil_sb[:, :], p.pf[0][:, :], AF.Silu))
            for o in range(NCH):
                s_yev.wge(scalar, tt * NCH + o + 1)
                if tt * NCH + o >= 1:
                    s_rms.wge(scalar, tt * NCH + o)
                s_ysq.inc(scalar.activation(ysq[:, :], y_sb[:, o * TT:(o + 1) * TT], AF.Square))
            s_msr.wge(scalar, tt + 1)
            s_sd.inc(scalar.activation(rrow[0:1, TT:2 * TT], rrow[0:1, 0:TT], AF.Sqrt))

    @p.block.vector
    def _(vector):
        for tt in range(NTT):
            for ip in range(44):
                it = tt * 44 + ip
                s_sil.wge(vector, it + 1)
                if it >= 2:
                    s_hst[it % 2].wge(vector, 16 * (it // 2))
                s_h.inc(vector.tensor_tensor(hb[it % 2][:, :], sil_sb[:, :], p.pf[1][:, :], ALU.mult))
            for ogi, (o0, o1) in enumerate(OG):
                gG = tt * 4 + ogi
                s_ymm.wge(vector, gG * 44 + 44)
                for ol in range(o1 - o0):
                    o = o0 + ol
                    if tt >= 1 and o == 0:
                        s_ysq.wge(vector, tt * NCH)
                    s_yev.inc(vector.tensor_copy(y_sb[:, o * TT:(o + 1) * TT], p.pf[ol][:, :]))
            s_rms.wge(vector, tt * NCH + NCH)
            s_msr.inc(vector.tensor_scalar(rrow[0:1, 0:TT], p.pf[5][0:1, 0:TT],
                                           1.0 / D, 1e-6, ALU.mult, ALU.add))
            s_sd.wge(vector, tt + 1)
            s_rr.inc(vector.reciprocal(rrow[0:1, 2 * TT:3 * TT], rrow[0:1, TT:2 * TT]))
            s_rb.wge(vector, tt + 1)
            s_x3.wge(vector, 16 * (tt + 1))
            for o in range(NCH):
                if tt * NCH + o >= 2:
                    s_fst[(tt * NCH + o) % 2].wge(vector, 16 * ((tt * NCH + o) // 2))
                if tt * NCH + o >= 1:
                    s_fin.wge(vector, tt * NCH + o)
                s_aux.inc(vector.tensor_tensor(ytmp[:, :], y_sb[:, o * TT:(o + 1) * TT],
                                               p.pf[0][:, :], ALU.mult))
                s_aux.wge(vector, s_aux.n)
                s_fin.inc(vector.tensor_tensor(xoc[(tt * NCH + o) % 2][:, :], ytmp[:, :],
                                               x3t[:, o * TT:(o + 1) * TT], ALU.add))

    @p.block.sync
    def _(g):
        for tt in range(NTT):
            for ip in range(44):
                it = tt * 44 + ip
                s_h.wge(g, it + 1)
                s_hst[it % 2].inc(g.dma_start(out=h_dram[ip, :, tt * TT:(tt + 1) * TT],
                                              in_=hb[it % 2][:, :]), 16)
            for o in range(NCH):
                s_fin.wge(g, tt * NCH + o + 1)
                s_fst[(tt * NCH + o) % 2].inc(g.dma_start(out=xout[o, :, tt * TT:(tt + 1) * TT],
                                                          in_=xoc[(tt * NCH + o) % 2][:, :]), 16)

    return SemGroup(s_fst)


def build_p7b():
    p = Prog().open()
    x = p.din("x", [NCH, 128, T], F32)
    attd = p.din("attd", [NCH, 128, T], BF16)
    wo = p.din("wo", [NCH, 128, NCH, 128], F32)
    bo = p.din("bo", [128, NCH], F32)
    wgu = p.din("wgu", [88, 128, 16 * 128], BF16)
    wdn = p.din("wdn", [16, 128, 44 * 128], BF16)
    xout = p.dout("xout", [NCH, 128, T], F32)
    h_dram = p.dint("h_d", [44, 128, T], BF16)

    consts, s_const = load_consts(p)
    ensure_proj_scratch(p)
    oc = consts["ones_col"]
    orow_f = consts["ones_row_f"]
    bias_sb = p.sb("bias", [128, NCH], F32)
    xln = p.sb("xln", [128, NCH * T], BF16)
    s_misc = p.sem(kind="hw")

    @p.block.sync
    def _(sync):
        s_misc.inc(sync.dma_start(out=bias_sb[:, :], in_=bo[:, :]), 16)

    with p.scope():
        s_x1, s_st1 = emit_resadd_ln(p, x, attd, xout, xln, consts, [s_const, s_misc],
                                     "p7", wo_dram=wo, bias_sb=bias_sb, bias_off=0)
        s_ln = emit_ln(p, xln, xln, consts, [s_x1, s_st1], "p7")
    s_fst = emit_l1ffn(p, consts, xln, wgu, wdn, h_dram, xout, s_st1, s_ln, [])
    emit_final_wait(p, [SemGroup(s_fst)])
    return p


def run_p7b(inputs, p6_results, p5_results, p7a_results):
    p = build_p7b()
    wo_t = _tile_lhsT(np.asarray(inputs["L1_attn_o_w"]))
    bo = np.zeros((128, NCH), np.float32)
    wgu_all = np.concatenate([p7a_results[c]["wgu_mine"] for c in range(NCORE)], axis=0)
    wdn_all = np.concatenate([p7a_results[c]["down_mine"] for c in range(NCORE)], axis=0)
    cid = np.ascontiguousarray(np.eye(128, dtype=BF))
    cidf = np.ascontiguousarray(np.eye(128, dtype=np.float32))
    in_maps = []
    for c in range(NCORE):
        attd = p6_results[c]["att"]
        in_maps.append({
            "x": p5_results[c]["x2_out"], "attd": attd, "wo": wo_t, "bo": bo,
            "wgu": wgu_all, "wdn": wdn_all, "c_id": cid, "c_idf": cidf,
        })
    res = run_bass_kernel_spmd(p.nc, in_maps, core_ids=list(range(NCORE)))
    p.close()
    return res.results


def _prep_shared(inputs):
    d = {}
    d["c_id"] = np.ascontiguousarray(np.eye(128, dtype=BF))
    d["c_idf"] = np.ascontiguousarray(np.eye(128, dtype=np.float32))
    d["c_pt"] = _rope_P()
    cos, sin = _rope_tables()
    d["rope_cos"], d["rope_sin"] = cos, sin
    d["masks"] = _diag_masks()
    d["wq0"] = _tile_lhsT(np.asarray(inputs["L0_attn_Wq"]))
    d["wk0"] = _tile_lhsT(np.asarray(inputs["L0_attn_Wk"]))
    d["wv0"] = _tile_lhsT(np.asarray(inputs["L0_attn_Wv"]))
    bq = np.asarray(inputs["L0_attn_bq"]).reshape(NH, 128).T
    bk = np.asarray(inputs["L0_attn_bk"]).reshape(KVH0, 128).T
    bv = np.asarray(inputs["L0_attn_bv"]).reshape(KVH0, 128).T
    d["bqkv0"] = np.ascontiguousarray(np.concatenate([bq, bk, bv], axis=1).astype(np.float32))
    return d


def _x_shards(x):
    out = []
    for c in range(NCORE):
        s = c // 2
        out.append(np.ascontiguousarray(x[s, _core_pos(c), :].T))
    return out


def run_p1(inputs):
    sh = _prep_shared(inputs)
    xs = _x_shards(np.asarray(inputs["L0_attn_x"]))
    p = build_p1()
    in_maps = []
    for c in range(NCORE):
        h = c % 2
        in_maps.append({
            "x": xs[c],
            "wq": sh["wq0"], "wk": sh["wk0"], "wv": sh["wv0"],
            "bqkv": sh["bqkv0"],
            "ctab": np.ascontiguousarray(sh["rope_cos"][:, _core_pos(c)]),
            "stab": np.ascontiguousarray(sh["rope_sin"][:, _core_pos(c)]),
            "c_pt": sh["c_pt"], "c_id": sh["c_id"], "c_idf": sh["c_idf"],
        })
    res = run_bass_kernel_spmd(p.nc, in_maps, core_ids=list(range(NCORE)))
    p.close()
    return res.results


PAIRS = [[0, 1], [2, 3], [4, 5], [6, 7]]
ALL8 = [list(range(NCORE))]


def emit_collective(p, ins_ap, outs_ap, groups, wait_sems, tag):
    s_c = p.sem(kind="col")
    wv = _wpairs(wait_sems)

    @p.block.gpsimd
    def _(g):
        for h, n in wv:
            g.wait_ge(h, n)
        s_c.inc(g.collective_compute("AllGather", ALU.bypass, replica_groups=groups,
                                     ins=[ins_ap], outs=[outs_ap]))
    return s_c


def build_fused(upto=8, nhd26=8):
    global UPTO26_NHD
    UPTO26_NHD = nhd26
    p = Prog().open()
    x = p.din("x", [NCH, 128, T], F32)
    wq0 = p.din("wq0", [NH, 128, NCH, 128], F32)
    wk0 = p.din("wk0", [KVH0, 128, NCH, 128], F32)
    wv0 = p.din("wv0", [KVH0, 128, NCH, 128], F32)
    bqkv = p.din("bqkv", [128, NH + 2 * KVH0], F32)
    ctab = p.din("ctab", [128, T], F32)
    stab = p.din("stab", [128, T], F32)
    cpt = p.din("c_pt", [128, 128], BF16)
    wo0 = p.din("wo0", [NCH, 128, NCH, 128], F32)
    bo0 = p.din("bo0", [128, NCH], F32)
    wg = p.din("wg", [128, NCH * 16], F32)
    wqkv = p.din("wqkv", [48, 128, NCH, 128], F32)
    wq1 = p.din("wq1", [NH, 128, NCH, 128], F32)
    wk1 = p.din("wk1", [NH, 128, NCH, 128], F32)
    wv1 = p.din("wv1", [NH, 128, NCH, 128], F32)
    gwt = p.din("gwt", [128, 128], F32)
    gbt = p.din("gbt", [128, 1], F32)
    ow1 = p.din("ow1", [NCH, 128, NCH, 128], F32)
    bz = p.din("bz", [128, NCH], F32)
    gup = p.din("gup", [11, 128, 256], I32)
    gus = p.din("gus", [11, 128, 16], F16)
    dnp = p.din("dnp", [2, 128, 704], I32)
    dns = p.din("dns", [2, 128, 44], F16)
    ptk = p.din("ptk", [128, (S // 512) * 4], F32)
    qpos = p.din("qpos", [1, T], F32)
    xout = p.dout("xout", [NCH, 128, T], F32)

    q0 = p.dint("q0", [NH, 128, T], BF16)
    k0c = p.dint("k0c", [1, KVH0 * 128 * T], BF16)
    v0c = p.dint("v0c", [1, KVH0 * T * 128], BF16)
    k0g = p.dint("k0g", [2, KVH0 * 128 * T], BF16)
    v0g = p.dint("v0g", [2, KVH0 * T * 128], BF16)
    att0 = p.dint("att0", [NH, 128, T], BF16)
    x1 = p.dint("x1", [NCH, 128, T], F32)
    ind_dram = p.dint("ind_d", [16, T], F32)
    qf0 = p.dint("qf0", [NH, 128, T], BF16)
    kshc = p.dint("kshc", [1, 128 * T], BF16)
    vshc = p.dint("vshc", [1, T * 128], BF16)
    kshg = p.dint("kshg", [2, 128 * T], BF16)
    vshg = p.dint("vshg", [2, T * 128], BF16)
    attm = p.dint("attm", [NH, 128, T], BF16)
    x2 = p.dint("x2", [NCH, 128, T], F32)
    q1 = p.dint("q1", [NH, 128, T], BF16)
    k1c = p.dint("k1c", [1, NH * 128 * T], BF16)
    v1c = p.dint("v1c", [1, NH * T * 128], BF16)
    k1g = p.dint("k1g", [2, NH * 128 * T], BF16)
    v1g = p.dint("v1g", [2, NH * T * 128], BF16)
    att1 = p.dint("att1", [NH, 128, T], BF16)
    wgu_mine = p.dint("wgu_mine", [11, 128 * 16 * 128], BF16)
    down_mine = p.dint("down_mine", [2, 128 * 44 * 128], BF16)
    wgu_all = p.dint("wgu_all", [88, 128 * 16 * 128], BF16)
    down_all = p.dint("down_all", [16, 128 * 44 * 128], BF16)
    h_dram = p.dint("h_d", [44, 128, T], BF16)

    scale = HD ** -0.5
    consts, s_const = load_consts(p, reg_vals=[scale])
    ensure_proj_scratch(p)
    ptk_sb = p.sb("ptk_sb", [128, (S // 512) * 4], F32)
    qpos_sb = p.sb("qpos_sb", [1, T], F32)
    bias_sb = p.sb("bias_all", [128, NH + 2 * KVH0], F32)
    bo_sb = p.sb("bo_sb", [128, NCH], F32)
    bz_sb = p.sb("bz_sb", [128, NCH], F32)
    wg_sb = p.sb("wg_sb", [128, NCH * 16], BF16)
    gw_sb = p.sb("gw_sb", [128, 128], BF16)
    gb_sb = p.sb("gb_sb", [128, 1], F32)
    s_mh = p.sem(kind="hw")
    s_msw = p.sem(kind="sw")
    s_misc = SemGroup([s_mh, s_msw])

    @p.block.sync
    def _(sync):
        s_mh.inc(sync.dma_start(out=ptk_sb[:, :], in_=ptk[:, :]), 16)
        s_mh.inc(sync.dma_start(out=qpos_sb[:, :], in_=qpos[:, :]), 16)
        s_mh.inc(sync.dma_start(out=bias_sb[:, :], in_=bqkv[:, :]), 16)
        s_mh.inc(sync.dma_start(out=bo_sb[:, :], in_=bo0[:, :]), 16)
        s_mh.inc(sync.dma_start(out=bz_sb[:, :], in_=bz[:, :]), 16)

    @p.block.gpsimd
    def _(g):
        s_msw.inc(g.dma_start(out=wg_sb[:, :], in_=wg[:, :]), 16)
        s_msw.inc(g.dma_start(out=gw_sb[:, :], in_=gwt[:, :]), 16)
        s_msw.inc(g.dma_start(out=gb_sb[:, :], in_=gbt[:, :]), 16)

    k0c_v = k0c.rearrange("o (v p t) -> (o v) p t", v=KVH0, p=128)
    v0c_v = v0c.rearrange("o (v t d) -> (o v) t d", v=KVH0, t=T)
    with p.scope():
        cos_sb = p.sb("cos", [128, T], F32)
        sin_sb = p.sb("sin", [128, T], F32)
        pt_sb = p.sb("pt", [128, 128], BF16)
        s_m1 = p.sem(kind="hw")

        @p.block.sync
        def _(sync):
            sync.dma_start(out=cos_sb[:, :], in_=ctab[:, :]).then_inc(s_m1.h, 16)
            sync.dma_start(out=sin_sb[:, :], in_=stab[:, :]).then_inc(s_m1.h, 16)
            sync.dma_start(out=pt_sb[:, :], in_=cpt[:, :]).then_inc(s_m1.h, 16)
        s_m1.n = 48

        xln = p.sb("xln1", [128, NCH * T], BF16)
        s_x = emit_load_xbf(p, x.rearrange("c p t -> (c p) t"), xln)
        s_ln = emit_ln(p, xln, xln, consts, [s_x, s_const], "f1")
        rope = (cos_sb, sin_sb, pt_sb)
        sq0 = emit_proj(p, xln, wq0, NH, consts, [s_ln, s_m1, s_misc], "q0",
                        bias_sb=bias_sb, bias_off=0, mode="rope", rope=rope, out_dram=q0)
        sk0 = emit_proj(p, xln, wk0, KVH0, consts, [s_ln, sq0], "k0",
                        bias_sb=bias_sb, bias_off=NH, mode="rope", rope=rope, out_dram=k0c_v)
        sv0 = emit_proj(p, xln, wv0, KVH0, consts, [s_ln, sk0], "v0",
                        bias_sb=bias_sb, bias_off=NH + KVH0, mode="vtm", out_dram=v0c_v)

    if upto <= 1:
        emit_final_wait(p, [sq0, sk0, sv0])
        return p
    with p.scope():
        sd1 = emit_dequant(p, consts, gup, gus, wgu_mine.rearrange("m (p x) -> m p x", p=128),
                           11, 256, "gu", waits=[s_const, sv0], bank=0)
        sd2 = emit_dequant(p, consts, dnp, dns, down_mine.rearrange("m (p x) -> m p x", p=128),
                           2, 704, "dn", waits=[s_const, sv0], bank=1)
    s_agw = emit_collective(p, wgu_mine[:, :], wgu_all[:, :], ALL8, [sd1], "agw")
    s_agd = emit_collective(p, down_mine[:, :], down_all[:, :], ALL8, [sd2], "agd")

    if upto <= 2:
        emit_final_wait(p, [s_agw, s_agd])
        return p
    s_c1k = emit_collective(p, k0c[:, :], k0g[:, :], PAIRS, [sk0], "c1k")
    s_c1v = emit_collective(p, v0c[:, :], v0g[:, :], PAIRS, [sv0], "c1v")
    if upto == 25:
        emit_final_wait(p, [s_c1k, s_c1v, s_agw, s_agd])
        return p

    if upto == 26:
        with p.scope():
            s26 = emit_attn_tok(p, consts, q0,
                                k0g.rearrange("s (v p t) -> s v p t", v=KVH0, p=128),
                                v0g.rearrange("s (v t d) -> s v t d", v=KVH0, t=T),
                                att0, UPTO26_NHD, lambda h: h // 4, scale,
                                ptk_sb, qpos_sb, [s_c1k, s_c1v, sq0, s_misc, sd1, sd2], "a0")
        emit_final_wait(p, [s26])
        return p
    with p.scope():
        satt0 = emit_attn_tok(p, consts, q0,
                              k0g.rearrange("s (v p t) -> s v p t", v=KVH0, p=128),
                              v0g.rearrange("s (v t d) -> s v t d", v=KVH0, t=T),
                              att0, NH, lambda h: h // 4, scale,
                              ptk_sb, qpos_sb, [s_c1k, s_c1v, sq0, s_misc, sd1, sd2], "a0")

    if upto <= 3:
        emit_final_wait(p, [satt0])
        return p
    kshc_v = kshc.rearrange("o (p t) -> (o p) t", p=128)
    vshc_v = vshc.rearrange("o (t d) -> (o t) d", t=T)
    with p.scope():
        xln3 = p.sb("xln3", [128, NCH * T], BF16)
        with p.scope():
            s_x1e, s_st1 = emit_resadd_ln(p, x, att0, x1, xln3, consts,
                                          [satt0, s_misc, s_const], "f3",
                                          wo_dram=wo0, bias_sb=bo_sb, bias_off=0)
            s_ln3 = emit_ln(p, xln3, xln3, consts, [s_x1e, s_st1], "f3")
        ind_fm, top1row, s_ind, s_gre = emit_l0f_gating(p, xln3, wg_sb, consts,
                                                        [s_ln3, s_misc], "f3")
        s_indf = p.sem(kind="hw")

        @p.block.sync
        def _(sync):
            s_ind.wge(sync, s_ind.n)
            s_indf.inc(sync.dma_start(out=ind_dram[:, :], in_=ind_fm[0:16, :]), 16)

        ksh_acc = p.sb("ksh_acc", [128, T], BF16)
        vsh_acc = p.sb("vsh_acc", [128, T], BF16)
        vtm_sb = p.sb("vtm_sb", [128, TT], BF16)
        mask_all, s_mask = emit_mask_all(p, consts, ind_dram, s_indf, "f3")
        sqf = emit_proj(p, xln3, wqkv[0:16], 16, consts, [s_ln3, s_ind], "qf",
                        mode="plain", out_dram=qf0)
        s_kacc = emit_proj(p, xln3, wqkv[16:32], 16, consts, [s_ln3, sqf, s_mask], "kf",
                           mode="sel", sel=(mask_all, ksh_acc))
        s_vacc = emit_proj(p, xln3, wqkv[32:48], 16, consts, [s_ln3, s_kacc], "vf",
                           mode="sel", sel=(mask_all, vsh_acc))
        s_kst = p.sem(kind="sw")

        @p.block.gpsimd
        def _(g):
            s_kacc.wge(g, s_kacc.n)
            s_kst.inc(g.dma_start(out=kshc_v[:, :], in_=ksh_acc[:, :]), 16)

        t1_ps = p.pf[3]
        vt_ps = p.pb[0]
        s_t1b, s_t1m, s_vt, s_vte = (p.sem() for _ in range(4))
        s_vst = p.sem(kind="hw")
        orow_f = consts["ones_row_f"]
        identb = consts["ident"]

        @p.block.tensor
        def _(tensor):
            s_vacc.wge(tensor, s_vacc.n)
            s_gre.wge(tensor, s_gre.n)
            for tt in range(NTT):
                if tt > 0:
                    s_t1m.wge(tensor, tt)
                s_t1b.inc(tensor.matmul(t1_ps[:, :], orow_f[0:1, :],
                                        top1row[0:1, tt * TT:(tt + 1) * TT], start=True, stop=True))
                s_t1m.wge(tensor, tt + 1)
                if tt > 0:
                    s_vte.wge(tensor, 4 * tt)
                for sg in range(4):
                    i = tensor.transpose(vt_ps[:, sg * 128:(sg + 1) * 128],
                                         vsh_acc[:, tt * TT + sg * 128:tt * TT + (sg + 1) * 128],
                                         identb[:, :])
                s_vt.inc(i)

        @p.block.vector
        def _(vector):
            for tt in range(NTT):
                s_t1b.wge(vector, tt + 1)
                s_t1m.inc(vector.tensor_tensor(vsh_acc[:, tt * TT:(tt + 1) * TT],
                                               vsh_acc[:, tt * TT:(tt + 1) * TT],
                                               t1_ps[:, :], ALU.mult))
                s_vt.wge(vector, tt + 1)
                if tt > 0:
                    s_vst.wge(vector, 16 * tt)
                for sg in range(4):
                    i = vector.tensor_copy(vtm_sb[:, sg * 128:(sg + 1) * 128],
                                           vt_ps[:, sg * 128:(sg + 1) * 128])
                    s_vte.inc(i)

        @p.block.sync
        def _(g):
            for tt in range(NTT):
                s_vte.wge(g, 4 * (tt + 1))
                s_vst.inc(g.dma_start(
                    out=vshc_v[tt * TT:(tt + 1) * TT, :].rearrange("(sg pp) d -> pp sg d", pp=128),
                    in_=vtm_sb[:, :].rearrange("p (sg d) -> p sg d", sg=4)), 16)

    if upto <= 4:
        emit_final_wait(p, [sqf, s_kst, s_vst, s_st1])
        return p
    s_c2k = emit_collective(p, kshc[:, :], kshg[:, :], PAIRS, [s_kst], "c2k")
    s_c2v = emit_collective(p, vshc[:, :], vshg[:, :], PAIRS, [s_vst], "c2v")

    with p.scope():
        sattm = emit_attn_tok(p, consts, qf0,
                              kshg.rearrange("s (v p t) -> s v p t", v=1, p=128),
                              vshg.rearrange("s (v t d) -> s v t d", v=1, t=T),
                              attm, NH, lambda h: 0, scale,
                              ptk_sb, qpos_sb, [s_c2k, s_c2v, sqf, s_misc], "am")

    if upto <= 5:
        emit_final_wait(p, [sattm])
        return p
    k1c_v = k1c.rearrange("o (v p t) -> (o v) p t", v=NH, p=128)
    v1c_v = v1c.rearrange("o (v t d) -> (o v) t d", v=NH, t=T)
    with p.scope():
        xln5 = p.sb("xln5", [128, NCH * T], BF16)
        with p.scope():
            s_x2e, s_st2 = emit_resadd_ln(p, x1, attm, x2, xln5, consts,
                                          [sattm, s_st1], "f5")
            s_ln5 = emit_ln(p, xln5, xln5, consts, [s_x2e, s_st2], "f5")
        sq1 = emit_proj(p, xln5, wq1, NH, consts, [s_ln5], "q1", mode="plain", out_dram=q1)
        sk1 = emit_proj(p, xln5, wk1, NH, consts, [s_ln5, sq1], "k1", mode="plain", out_dram=k1c_v)
        sv1 = emit_proj(p, xln5, wv1, NH, consts, [s_ln5, sk1], "v1", mode="vtm", out_dram=v1c_v)

    if upto <= 6:
        emit_final_wait(p, [sq1, sk1, sv1, s_st2])
        return p
    s_c3k = emit_collective(p, k1c[:, :], k1g[:, :], PAIRS, [sk1], "c3k")
    s_c3v = emit_collective(p, v1c[:, :], v1g[:, :], PAIRS, [sv1], "c3v")

    with p.scope():
        satt1 = emit_attn_tok(p, consts, q1,
                              k1g.rearrange("s (v p t) -> s v p t", v=NH, p=128),
                              v1g.rearrange("s (v t d) -> s v t d", v=NH, t=T),
                              att1, NH, lambda h: h, scale,
                              ptk_sb, qpos_sb, [s_c3k, s_c3v, sq1, s_misc], "a1",
                              gate=(gw_sb, gb_sb))

    if upto <= 7:
        emit_final_wait(p, [satt1])
        return p
    with p.scope():
        xln7 = p.sb("xln7", [128, NCH * T], BF16)
        with p.scope():
            s_x3e, s_st3 = emit_resadd_ln(p, x2, att1, xout, xln7, consts,
                                          [satt1, s_st2, s_misc], "f7",
                                          wo_dram=ow1, bias_sb=bz_sb, bias_off=0)
            s_ln7 = emit_ln(p, xln7, xln7, consts, [s_x3e, s_st3], "f7")
        s_fst = emit_l1ffn(p, consts, xln7,
                           wgu_all.rearrange("m (p x) -> m p x", p=128),
                           down_all.rearrange("m (p x) -> m p x", p=128),
                           h_dram, xout, s_st3, s_ln7, [s_agw, s_agd])
    emit_final_wait(p, [s_fst])
    return p


def run_fused(inputs):
    sh = _prep_shared(inputs)
    xs = _x_shards(np.asarray(inputs["L0_attn_x"]))
    p = build_fused()
    wo0_t = _tile_lhsT(np.asarray(inputs["L0_attn_Wo"]))
    bo0 = np.ascontiguousarray(np.asarray(inputs["L0_attn_bo"]).reshape(NCH, 128).T.astype(np.float32))
    Wg = np.asarray(inputs["L0_ffn_W_gate"])
    wg_host = np.ascontiguousarray(Wg.reshape(16, 16, 128).transpose(2, 1, 0).reshape(128, 256).astype(np.float32))
    wqkv_t = _tile_lhsT(np.asarray(inputs["L0_ffn_W_qkv"]))
    wq1_t = _tile_lhsT(np.asarray(inputs["L1_attn_q_w"]))
    wk1_t = _tile_lhsT(np.asarray(inputs["L1_attn_k_w"]))
    wv1_t = _tile_lhsT(np.asarray(inputs["L1_attn_v_w"]))
    ow1_t = _tile_lhsT(np.asarray(inputs["L1_attn_o_w"]))
    gwt = np.ascontiguousarray(np.asarray(inputs["L1_attn_gate_w"]).T.astype(np.float32))
    gbt = np.ascontiguousarray(np.asarray(inputs["L1_attn_gate_b"]).reshape(128, 1).astype(np.float32))
    bz = np.zeros((128, NCH), np.float32)
    gup = np.asarray(inputs["L1_ffn_gate_up_packed"])
    gus = np.asarray(inputs["L1_ffn_gate_up_scales"])
    dnp = np.asarray(inputs["L1_ffn_down_packed"])
    dns = np.asarray(inputs["L1_ffn_down_scales"])
    in_maps = []
    for c in range(NCORE):
        pos = _core_pos(c)
        r0, r1 = c * 1408, (c + 1) * 1408
        d0, d1 = c * 256, (c + 1) * 256
        ptkh, qposh = _attn_host_consts(c)
        in_maps.append({
            "x": np.ascontiguousarray(xs[c].reshape(NCH, 128, T)),
            "wq0": sh["wq0"], "wk0": sh["wk0"], "wv0": sh["wv0"], "bqkv": sh["bqkv0"],
            "ctab": np.ascontiguousarray(sh["rope_cos"][:, pos]),
            "stab": np.ascontiguousarray(sh["rope_sin"][:, pos]),
            "c_pt": sh["c_pt"], "c_id": sh["c_id"], "c_idf": sh["c_idf"],
            "wo0": wo0_t, "bo0": bo0, "wg": wg_host, "wqkv": wqkv_t,
            "wq1": wq1_t, "wk1": wk1_t, "wv1": wv1_t, "gwt": gwt, "gbt": gbt,
            "ow1": ow1_t, "bz": bz,
            "gup": np.ascontiguousarray(gup[r0:r1].reshape(11, 128, 256)),
            "gus": np.ascontiguousarray(gus.T[r0:r1].reshape(11, 128, 16)),
            "dnp": np.ascontiguousarray(dnp[d0:d1].reshape(2, 128, 704)),
            "dns": np.ascontiguousarray(dns.T[d0:d1].reshape(2, 128, 44)),
            "ptk": ptkh, "qpos": qposh,
        })
    res = run_bass_kernel_spmd(p.nc, in_maps, core_ids=list(range(NCORE)))
    p.close()
    return res.results


USE_FUSED = True


def kernel(**inputs):
    inputs = {kk: np.asarray(v) for kk, v in inputs.items()}
    out = np.zeros((B, S, D), np.float32)
    if USE_FUSED:
        rf = run_fused(inputs)
        for c in range(NCORE):
            out[c // 2, _core_pos(c), :] = rf[c]["xout"].reshape(D, T).T
        return out
    xs = _x_shards(inputs["L0_attn_x"])
    p1 = run_p1(inputs)
    p2 = run_p2(p1, _diag_masks())
    p3 = run_p3(inputs, p2, xs)
    p4 = run_p4(p3)
    p5 = run_p5(inputs, p4, p3)
    p6 = run_p6(inputs, p5)
    p7a = run_p7a(inputs)
    p7b = run_p7b(inputs, p6, p5, p7a)
    for c in range(NCORE):
        out[c // 2, _core_pos(c), :] = p7b[c]["xout"].reshape(D, T).T
    return out


# version 07
# speedup vs baseline: 1.2152x; 1.0236x over previous
import contextlib
import numpy as np
import ml_dtypes
import concourse.bass as bass
import concourse.mybir as mybir
from concourse.bass_utils import run_bass_kernel_spmd

F32 = mybir.dt.float32
BF16 = mybir.dt.bfloat16
U8 = mybir.dt.uint8
I32 = mybir.dt.int32
F16 = mybir.dt.float16
AF = mybir.ActivationFunctionType
ALU = mybir.AluOpType
BF = ml_dtypes.bfloat16

B, S, D = 4, 4096, 2048
NCORE = 8
T = 2048
NCH = D // 128
NH, HD = 16, 128
KVH0 = 4
TT = 512
NTT = T // TT
CHUNKS_OF = {0: [0, 3, 4, 7], 1: [1, 2, 5, 6]}
EXT = [2, 4, 6, 8]
_SLOT = {}
for _h in (0, 1):
    for _i, _g in enumerate(CHUNKS_OF[_h]):
        _SLOT[_g] = _h * 4 + _i


def _core_pos(c):
    return np.concatenate([np.arange(g * TT, (g + 1) * TT) for g in CHUNKS_OF[c % 2]])
ROPE_BASE = 10000.0
NEG = -1e30
EPS_LN = 1e-5


def _tile_lhsT(w):
    o, i = w.shape
    mo, k = o // 128, i // 128
    t = w.reshape(mo, 128, k, 128)
    return np.ascontiguousarray(t.transpose(0, 3, 2, 1)).astype(np.float32)


def _rope_tables():
    inv = 1.0 / ROPE_BASE ** (np.arange(0, HD, 2, dtype=np.float64) / HD)
    pos = np.arange(S, dtype=np.float64)
    fr = pos[:, None] * inv[None, :]
    cos = np.repeat(np.cos(fr), 2, axis=1).T
    sin = np.repeat(np.sin(fr), 2, axis=1).T
    return (np.ascontiguousarray(cos.astype(np.float32)),
            np.ascontiguousarray(sin.astype(np.float32)))


def _rope_P():
    P = np.zeros((HD, HD), np.float32)
    for i in range(HD // 2):
        P[2 * i, 2 * i + 1] = -1.0
        P[2 * i + 1, 2 * i] = 1.0
    return np.ascontiguousarray(P.T.astype(BF))


def _diag_masks():
    m = np.zeros((4, 128, 512), np.float32)
    for j in range(4):
        for r in range(128):
            m[j, r, j * 128 + r + 1:] = NEG
    return np.ascontiguousarray(m)


class Sem:
    def __init__(self, handle, base=0):
        self.h = handle
        self.base = base
        self.n = 0

    def inc(self, instr, k=1):
        instr.then_inc(self.h, k)
        self.n += k
        return instr

    def wge(self, eng, v):
        return eng.wait_ge(self.h, self.base + v)

    def pairs(self):
        return [(self.h, self.base + self.n)]


class SemGroup:
    def __init__(self, sems):
        self.sems = sems

    def pairs(self):
        return [pp for s in self.sems for pp in s.pairs()]


def _wpairs(waits):
    return [pp for w in waits for pp in w.pairs()]


class Prog:
    def __init__(self, name="mk"):
        self.nc = bass.Bass("TRN2", target_bir_lowering=False, debug=False)
        self.stack = contextlib.ExitStack()
        self.sb_stacks = [self.stack]
        self._n = 0
        self.block = None

    def sem(self, name=None, kind="eng"):
        if not hasattr(self, "_sem_free"):
            self._sem_free = {"eng": [], "sw": [], "hw": [], "col": []}
            self._sem_scopes = [[]]
        pool = self._sem_free[kind]
        if pool:
            h, total = pool.pop()
            s = Sem(h, base=total)
        else:
            name = name or f"s{self._n}"
            self._n += 1
            s = Sem(self.stack.enter_context(self.nc.semaphore(name)))
        s.kind = kind
        self._sem_scopes[-1].append(s)
        return s

    def sb(self, name, shape, dtype):
        return self.sb_stacks[-1].enter_context(self.nc.sbuf_tensor(name, shape, dtype))

    @contextlib.contextmanager
    def scope(self):
        st = contextlib.ExitStack()
        self.sb_stacks.append(st)
        if not hasattr(self, "_sem_free"):
            self._sem_free = {"eng": [], "sw": [], "hw": [], "col": []}
            self._sem_scopes = [[]]
        self._sem_scopes.append([])
        try:
            yield
        finally:
            self.sb_stacks.pop()
            st.close()
            freed = self._sem_scopes.pop()
            drain = [(s.h, s.base + s.n) for s in freed
                     if getattr(s, "kind", "eng") in ("sw", "hw", "col") and s.n > 0]
            if drain:
                @self.block.gpsimd
                def _(g):
                    for h, n in drain:
                        g.wait_ge(h, n)

                @self.block.sync
                def _(g):
                    for h, n in drain:
                        g.wait_ge(h, n)
            for s in freed:
                self._sem_free[getattr(s, "kind", "eng")].append((s.h, s.base + s.n))

    def ps(self, name, shape, dtype=F32):
        return self.stack.enter_context(self.nc.psum_tensor(name, shape, dtype))

    def din(self, name, shape, dtype):
        return self.nc.dram_tensor(name, shape, dtype, kind="ExternalInput").ap()

    def dout(self, name, shape, dtype):
        return self.nc.dram_tensor(name, shape, dtype, kind="ExternalOutput").ap()

    def dint(self, name, shape, dtype):
        return self.nc.dram_tensor(name, shape, dtype, kind="Internal").ap()

    def open(self):
        self.pf = [self.ps(f"pf{i}", [128, 512], F32) for i in range(6)]
        self.pb = [self.ps(f"pb{i}", [128, 1024], BF16) for i in range(2)]
        self.block = self.stack.enter_context(self.nc.Block())
        return self

    def close(self):
        self.stack.close()


def load_consts(p, reg_vals=()):
    cid = p.din("c_id", [128, 128], BF16)
    cidf = p.din("c_idf", [128, 128], F32)
    ident = p.sb("ident", [128, 128], BF16)
    identf = p.sb("identf", [128, 128], F32)
    ones_col = p.sb("ones_col", [128, 1], BF16)
    ones_row = p.sb("ones_row", [1, 128], BF16)
    ones_row_f = p.sb("ones_row_f", [1, 128], F32)
    ctiles = []
    for i, v in enumerate(reg_vals):
        t = p.sb(f"creg{i}", [128, 1], F32)
        p.nc.const_aps.register(F32, float(v), t[:, 0:1])
        ctiles.append((t, float(v)))
    s_hw = p.sem("s_const", kind="hw")
    s_en = p.sem()

    @p.block.sync
    def _(sync):
        s_hw.inc(sync.dma_start(out=ident[:, :], in_=cid[:, :]), 16)
        s_hw.inc(sync.dma_start(out=identf[:, :], in_=cidf[:, :]), 16)

    @p.block.vector
    def _(vector):
        vector.memset(ones_col[:, :], 1.0)
        vector.memset(ones_row[:, :], 1.0)
        for t, v in ctiles:
            vector.memset(t[:, :], v)
        s_en.inc(vector.memset(ones_row_f[:, :], 1.0))
    return {"ident": ident, "identf": identf, "ones_col": ones_col,
            "ones_row": ones_row, "ones_row_f": ones_row_f}, SemGroup([s_hw, s_en])


def emit_load_xbf(p, xsrc, xbf):
    s_x = p.sem(kind="sw")

    @p.block.gpsimd
    def _(g):
        s_x.inc(g.dma_start(out=xbf[:, :].rearrange("p (c t) -> p c t", c=NCH),
                            in_=xsrc.rearrange("(c p) t -> p c t", p=128)), 16)
    return s_x


def ensure_proj_scratch(p):
    if not hasattr(p, "proj_scratch"):
        p.proj_scratch = {
            "wbuf": [p.sb(f"pj_w{i}", [128, NCH * 128], BF16) for i in range(2)],
            "ev": [p.sb(f"pj_ev{i}", [128, TT], BF16) for i in range(2)],
            "qf": [p.sb(f"pj_qf{i}", [128, TT], BF16) for i in range(2)],
            "rt1": p.sb("pj_rt1", [128, TT], F32),
            "rt2": p.sb("pj_rt2", [128, TT], F32),
        }
    return p.proj_scratch


def emit_ln(p, xbf, xln, consts, waits, tag, ntok=T):
    ntt = ntok // TT
    oc = consts["ones_col"]
    orow = consts["ones_row"]
    xsq = p.sb(f"xsq_{tag}", [128, NCH * TT], BF16)
    srow = p.sb(f"srow_{tag}", [1, 6 * TT], F32)
    mrow = p.sb(f"mrow_{tag}", [1, 2 * TT], BF16)
    tmp = p.sb(f"lntmp_{tag}", [128, 4 * TT], BF16)
    st_sum, st_sq = p.pf[4], p.pf[5]
    bc0, bc1 = p.pf[2], p.pf[3]
    s_sq, s_stat, s_row, s_sd, s_rr, s_bc, s_va = (p.sem() for _ in range(7))
    s_ln = p.sem()
    wvals = _wpairs(waits)

    def xs(c, t0):
        return slice(c * ntok + t0, c * ntok + t0 + TT)

    @p.block.scalar
    def _(scalar):
        for tt in range(ntt):
            t0 = tt * TT
            if tt == 0:
                for h, n in wvals:
                    scalar.wait_ge(h, n)
            for c in range(NCH - 1):
                scalar.activation(xsq[:, c * TT:(c + 1) * TT], xbf[:, xs(c, t0)], AF.Square)
            s_sq.inc(scalar.activation(xsq[:, (NCH - 1) * TT:], xbf[:, xs(NCH - 1, t0)], AF.Square))
            s_row.wge(scalar, tt + 1)
            s_sd.inc(scalar.activation(srow[:, 4 * TT:5 * TT], srow[:, 3 * TT:4 * TT], AF.Sqrt))

    @p.block.tensor
    def _(tensor):
        for tt in range(ntt):
            t0 = tt * TT
            if tt == 0:
                for h, n in wvals:
                    tensor.wait_ge(h, n)
            s_sq.wge(tensor, tt + 1)
            for c in range(NCH):
                tensor.matmul(st_sum[0:1, :], oc[:, :], xbf[:, xs(c, t0)],
                              start=(c == 0), stop=(c == NCH - 1))
            for c in range(NCH):
                i = tensor.matmul(st_sq[0:1, :], oc[:, :], xsq[:, c * TT:(c + 1) * TT],
                                  start=(c == 0), stop=(c == NCH - 1))
            s_stat.inc(i)
            s_rr.wge(tensor, tt + 1)
            if tt > 0:
                s_ln.wge(tensor, tt)
            tensor.matmul(bc0[:, :], orow[:, :], mrow[0:1, 0:TT], start=True, stop=True)
            s_bc.inc(tensor.matmul(bc1[:, :], orow[:, :], mrow[0:1, TT:2 * TT], start=True, stop=True))

    @p.block.vector
    def _(vector):
        for tt in range(ntt):
            t0 = tt * TT
            s_stat.wge(vector, tt + 1)
            vector.tensor_scalar(srow[:, 0:TT], st_sum[0:1, :], 1.0 / D, None, ALU.mult)
            vector.tensor_scalar(mrow[:, 0:TT], st_sum[0:1, :], 1.0 / D, None, ALU.mult)
            s_va.inc(vector.tensor_scalar(srow[:, TT:2 * TT], st_sq[0:1, :], 1.0 / D, None, ALU.mult))
            s_va.wge(vector, s_va.n)
            s_va.inc(vector.tensor_tensor(srow[:, 2 * TT:3 * TT], srow[:, 0:TT], srow[:, 0:TT], ALU.mult))
            s_va.wge(vector, s_va.n)
            s_row.inc(vector.scalar_tensor_tensor(srow[:, 3 * TT:4 * TT], srow[:, TT:2 * TT],
                                                  EPS_LN, srow[:, 2 * TT:3 * TT],
                                                  ALU.add, ALU.subtract))
            s_sd.wge(vector, tt + 1)
            s_va.inc(vector.reciprocal(srow[:, 5 * TT:6 * TT], srow[:, 4 * TT:5 * TT]))
            s_va.wge(vector, s_va.n)
            s_rr.inc(vector.tensor_copy(mrow[:, TT:2 * TT], srow[:, 5 * TT:6 * TT]))
            s_bc.wge(vector, tt + 1)
            for g in range(NCH // 4):
                if g > 0:
                    s_va.wge(vector, s_va.n)
                elif tt > 0:
                    s_ln.wge(vector, tt)
                for c4 in range(4):
                    c = g * 4 + c4
                    i = vector.tensor_tensor(tmp[:, c4 * TT:(c4 + 1) * TT], xbf[:, xs(c, t0)],
                                             bc0[:, :], ALU.subtract)
                s_va.inc(i)
                s_va.wge(vector, s_va.n)
                for c4 in range(4):
                    c = g * 4 + c4
                    i = vector.tensor_tensor(xln[:, xs(c, t0)], tmp[:, c4 * TT:(c4 + 1) * TT],
                                             bc1[:, :], ALU.mult)
                if g < NCH // 4 - 1:
                    s_va.inc(i)
            s_ln.inc(i)
    return s_ln


def emit_proj(p, xln, wdram, nmo, consts, waits, tag,
              bias_sb=None, bias_off=0, mode="plain", rope=None, out_dram=None,
              ntok=T, ev_consumer=None, sel=None):
    ntt = ntok // TT
    sc = ensure_proj_scratch(p)
    wbuf, ev = sc["wbuf"], sc["ev"]
    pp = [p.pf[0], p.pf[1]]
    s_wld = [p.sem(kind="sw"), p.sem(kind="sw")]
    s_done2 = [p.sem(kind="hw"), p.sem(kind="hw")]
    s_mm, s_ev, s_rr2 = p.sem(), p.sem(), p.sem()
    wvals = _wpairs(waits)

    if mode in ("rope", "vtm"):
        qf = sc["qf"]
        rt1, rt2 = sc["rt1"], sc["rt2"]
        rot_ps = p.pf[2]
        tp_ps = p.pb[0]
        s_rot, s_qf = p.sem(), p.sem()
    s_free = p.sem()

    def xs(c, t0):
        return slice(c * ntok + t0, c * ntok + t0 + TT)

    @p.block.gpsimd
    def _(g):
        for m in range(nmo):
            if m == 0:
                for h, n in wvals:
                    g.wait_ge(h, n)
            if m >= 2:
                s_mm.wge(g, (m - 1) * ntt)
            s_wld[m % 2].inc(g.dma_start(out=wbuf[m % 2][:, :],
                                         in_=wdram[m, :, :, :].rearrange("p k mm -> p (k mm)")), 16)

    @p.block.tensor
    def _(tensor):
        for m in range(nmo):
            s_wld[m % 2].wge(tensor, 16 * (m // 2 + 1))
            if m == 0:
                for h, n in wvals:
                    tensor.wait_ge(h, n)
            for tt in range(ntt):
                t0 = tt * TT
                it = m * ntt + tt
                if it >= 2:
                    s_ev.wge(tensor, it - 1)
                for c in range(NCH):
                    i = tensor.matmul(pp[it % 2][:, :], wbuf[m % 2][:, c * 128:(c + 1) * 128],
                                      xln[:, xs(c, t0)], start=(c == 0), stop=(c == NCH - 1))
                s_mm.inc(i)
                if mode == "rope":
                    s_ev.wge(tensor, it + 1)
                    if it >= 1:
                        s_qf.wge(tensor, it)
                    s_rot.inc(tensor.matmul(rot_ps[:, :], rope[2][:, :], ev[it % 2][:, :],
                                            start=True, stop=True))
                elif mode == "vtm":
                    s_ev.wge(tensor, it + 1)
                    if it >= 1:
                        s_qf.wge(tensor, it)
                    for sg in range(4):
                        i = tensor.transpose(tp_ps[:, sg * 128:(sg + 1) * 128],
                                             ev[it % 2][:, sg * 128:(sg + 1) * 128], consts["ident"][:, :])
                    s_rot.inc(i)

    @p.block.vector
    def _(vector):
        for m in range(nmo):
            for tt in range(ntt):
                it = m * ntt + tt
                s_mm.wge(vector, it + 1)
                if it >= 2 and mode != "sel":
                    if mode in ("rope", "vtm"):
                        s_qf.wge(vector, it - 1)
                    elif mode == "cb":
                        s_free.wge(vector, it - 1)
                    else:
                        s_done2[it % 2].wge(vector, 16 * (it // 2))
                if mode == "sel":
                    mask_all, acc = sel
                    i = vector.copy_predicated(
                        acc[:, tt * TT:(tt + 1) * TT],
                        mask_all[:, m * T + tt * TT:m * T + (tt + 1) * TT],
                        pp[it % 2][:, :])
                elif bias_sb is not None:
                    i = vector.tensor_scalar(ev[it % 2][:, :], pp[it % 2][:, :],
                                             bias_sb[:, bias_off + m:bias_off + m + 1], None, ALU.add)
                else:
                    i = vector.tensor_copy(ev[it % 2][:, :], pp[it % 2][:, :])
                s_ev.inc(i)
                if mode == "rope":
                    s_rot.wge(vector, it + 1)
                    if it >= 2:
                        s_done2[it % 2].wge(vector, 16 * (it // 2))
                    cos_sb, sin_sb, _ = rope
                    t0 = tt * TT
                    s_ev.wge(vector, it + 1)
                    vector.tensor_tensor(rt1[:, :], ev[it % 2][:, :], cos_sb[:, t0:t0 + TT], ALU.mult)
                    s_rr2.inc(vector.tensor_tensor(rt2[:, :], rot_ps[:, :], sin_sb[:, t0:t0 + TT], ALU.mult))
                    s_rr2.wge(vector, s_rr2.n)
                    s_qf.inc(vector.tensor_tensor(qf[it % 2][:, :], rt1[:, :], rt2[:, :], ALU.add))
                elif mode == "vtm":
                    s_rot.wge(vector, it + 1)
                    if it >= 2:
                        s_done2[it % 2].wge(vector, 16 * (it // 2))
                    s_qf.inc(vector.tensor_copy(qf[it % 2][:, :], tp_ps[:, 0:512]))

    if mode == "sel":
        return s_ev
    if mode == "cb":
        ev_consumer(s_ev, s_free, ev)
        return s_ev

    @p.block.sync
    def _(g):
        for m in range(nmo):
            for tt in range(ntt):
                it = m * ntt + tt
                t0 = tt * TT
                if mode in ("rope", "vtm"):
                    s_qf.wge(g, it + 1)
                    src = qf[it % 2][:, :]
                else:
                    s_ev.wge(g, it + 1)
                    src = ev[it % 2][:, :]
                if mode == "vtm":
                    oap = out_dram[m, t0:t0 + TT, :].rearrange("(sg pp) d -> pp sg d", pp=128)
                    src = src.rearrange("p (sg d) -> p sg d", sg=4)
                else:
                    oap = out_dram[m, :, t0:t0 + TT]
                i = g.dma_start(out=oap, in_=src)
                s_done2[it % 2].inc(i, 16)
    return SemGroup(s_done2)


def emit_resadd_ln(p, x_dram, add_dram, x1_dram, xbf, consts, waits, tag,
                   wo_dram=None, bias_sb=None, bias_off=0):
    xt = p.sb(f"rx_{tag}", [128, NCH * TT], F32)
    at = [p.sb(f"ra0_{tag}", [128, NCH * TT], BF16)]
    x1c = [p.sb(f"rx1{i}_{tag}", [128, TT], F32) for i in range(2)]
    if wo_dram is not None:
        wbuf = ensure_proj_scratch(p)["wbuf"]
    pp = [p.pf[0], p.pf[1]]
    ybuf = ensure_proj_scratch(p)["rt1"]
    s_mm, s_ev, s_yb = (p.sem() for _ in range(3))
    s_xl = p.sem(kind="sw")
    s_al = p.sem(kind="sw")
    s_wl = [p.sem(kind="sw"), p.sem(kind="sw")]
    s_st = [p.sem(kind="hw"), p.sem(kind="hw")]
    wvals = _wpairs(waits)

    @p.block.gpsimd
    def _(g):
        for hh, n in wvals:
            g.wait_ge(hh, n)
        for tt in range(NTT):
            if tt >= 1:
                s_ev.wge(g, 2 * NCH * tt)
            s_xl.inc(g.dma_start(out=xt[:, :].rearrange("p (c t) -> p c t", c=NCH),
                                 in_=x_dram[:, :, tt * TT:(tt + 1) * TT].rearrange("c p t -> p c t")), 16)
            s_al.inc(g.dma_start(out=at[0][:, :].rearrange("p (c t) -> p c t", c=NCH),
                                 in_=add_dram[:, :, tt * TT:(tt + 1) * TT].rearrange("c p t -> p c t")), 16)
            if wo_dram is not None:
                for m in range(NCH):
                    it = tt * NCH + m
                    if it >= 2:
                        s_mm.wge(g, it - 1)
                    s_wl[it % 2].inc(g.dma_start(out=wbuf[it % 2][:, :],
                                                 in_=wo_dram[m, :, :, :].rearrange("p k mm -> p (k mm)")), 16)

    if wo_dram is not None:
        @p.block.tensor
        def _(tensor):
            for tt in range(NTT):
                s_al.wge(tensor, 16 * (tt + 1))
                if tt == 0:
                    for hh, n in wvals:
                        tensor.wait_ge(hh, n)
                for m in range(NCH):
                    it = tt * NCH + m
                    s_wl[it % 2].wge(tensor, 16 * (it // 2 + 1))
                    if it >= 2:
                        s_ev.wge(tensor, 2 * (it - 1))
                    for c in range(NCH):
                        i = tensor.matmul(pp[it % 2][:, :], wbuf[it % 2][:, c * 128:(c + 1) * 128],
                                          at[0][:, c * TT:(c + 1) * TT],
                                          start=(c == 0), stop=(c == NCH - 1))
                    s_mm.inc(i)

    @p.block.vector
    def _(vector):
        for tt in range(NTT):
            s_xl.wge(vector, 16 * (tt + 1))
            for m in range(NCH):
                it = tt * NCH + m
                xc = xt[:, m * TT:(m + 1) * TT]
                if wo_dram is not None:
                    s_mm.wge(vector, it + 1)
                    if it >= 2:
                        s_st[it % 2].wge(vector, 16 * (it // 2))
                    if it >= 1:
                        s_ev.wge(vector, 2 * it)
                    s_yb.inc(vector.tensor_scalar(ybuf[:, :], pp[it % 2][:, :],
                                                  bias_sb[:, bias_off + m:bias_off + m + 1],
                                                  None, ALU.add))
                    s_yb.wge(vector, s_yb.n)
                    s_ev.inc(vector.tensor_tensor(x1c[it % 2][:, :], ybuf[:, :], xc, ALU.add))
                    i = vector.tensor_tensor(xbf[:, m * T + tt * TT:m * T + tt * TT + TT],
                                             ybuf[:, :], xc, ALU.add)
                    s_ev.inc(i)
                else:
                    s_al.wge(vector, 16 * (tt + 1))
                    if it >= 2:
                        s_st[it % 2].wge(vector, 16 * (it // 2))
                    ac = at[0][:, m * TT:(m + 1) * TT]
                    s_ev.inc(vector.tensor_tensor(x1c[it % 2][:, :], xc, ac, ALU.add))
                    i = vector.tensor_tensor(xbf[:, m * T + tt * TT:m * T + tt * TT + TT],
                                             xc, ac, ALU.add)
                    s_ev.inc(i)

    @p.block.sync
    def _(g):
        for tt in range(NTT):
            for m in range(NCH):
                it = tt * NCH + m
                s_ev.wge(g, 2 * (it + 1))
                s_st[it % 2].inc(g.dma_start(out=x1_dram[m, :, tt * TT:(tt + 1) * TT],
                                             in_=x1c[it % 2][:, :]), 16)
    return s_ev, SemGroup(s_st)


def emit_l0f_gating(p, xln, wg_sb, consts, waits, tag):
    identf = consts["identf"]
    orow_f = consts["ones_row_f"]
    g_sb = p.sb(f"g_{tag}", [16, TT], F32)
    gtm = p.sb(f"gtm_{tag}", [128, 4 * 16], F32)
    gm = p.sb(f"gm_{tag}", [128, 4], F32)
    gnm = p.sb(f"gnm_{tag}", [128, 4], F32)
    ge = p.sb(f"ge_{tag}", [128, 64], F32)
    gs = p.sb(f"gs_{tag}", [128, 4], F32)
    gt1 = p.sb(f"gt1_{tag}", [128, 4], F32)
    trow = p.sb(f"trow_{tag}", [1, TT], F32)
    ind_fm = p.sb(f"ind_{tag}", [16, NCH * 128], F32)
    top1row = p.sb(f"t1r_{tag}", [1, NCH * 128], F32)
    g_ps = p.pf[0]
    t_ps = p.pf[1]
    mb_ps = p.pf[2]
    t1_ps = p.pf[3]
    (s_gmm, s_gev, s_gt, s_gst, s_gnm, s_ge, s_gr, s_grt, s_gre, s_mb,
     s_ind) = (p.sem() for _ in range(11))
    wvals = _wpairs(waits)

    @p.block.tensor
    def _(tensor):
        for tt in range(NTT):
            if tt == 0:
                for hh, n in wvals:
                    tensor.wait_ge(hh, n)
            if tt > 0:
                s_gev.wge(tensor, tt)
            for c in range(NCH):
                i = tensor.matmul(g_ps[0:16, 0:TT], wg_sb[:, c * 16:(c + 1) * 16],
                                  xln[:, c * T + tt * TT:c * T + tt * TT + TT],
                                  start=(c == 0), stop=(c == NCH - 1))
            s_gmm.inc(i)
            s_gev.wge(tensor, tt + 1)
            if tt > 0:
                s_gst.wge(tensor, tt)
                s_gre.wge(tensor, tt)
            for gr in range(4):
                i = tensor.transpose(t_ps[0:128, gr * 16:(gr + 1) * 16],
                                     g_sb[0:16, gr * 128:(gr + 1) * 128], identf[0:16, 0:16])
            s_gt.inc(i)
            s_gr.wge(tensor, tt + 1)
            for gr in range(4):
                tensor.transpose(t_ps[0:1, gr * 128:(gr + 1) * 128], gm[:, gr:gr + 1],
                                 identf[:, :])
                i = tensor.transpose(t1_ps[0:1, gr * 128:(gr + 1) * 128], gt1[:, gr:gr + 1],
                                     identf[:, :])
            s_grt.inc(i)
            s_gre.wge(tensor, tt + 1)
            if tt > 0:
                s_ind.wge(tensor, tt)
            s_mb.inc(tensor.matmul(mb_ps[0:16, 0:TT], orow_f[0:1, 0:16], trow[0:1, 0:TT],
                                   start=True, stop=True))

    @p.block.vector
    def _(vector):
        for tt in range(NTT):
            if tt == 0:
                for hh, n in wvals:
                    vector.wait_ge(hh, n)
            s_gmm.wge(vector, tt + 1)
            if tt > 0:
                s_ind.wge(vector, tt)
            s_gev.inc(vector.tensor_copy(g_sb[0:16, :], g_ps[0:16, 0:TT]))
            s_gt.wge(vector, tt + 1)
            s_gst.inc(vector.tensor_copy(gtm[:, :], t_ps[0:128, 0:64]))
            s_gst.wge(vector, tt + 1)
            if tt > 0:
                s_grt.wge(vector, tt)
            for gr in range(4):
                i = vector.reduce_max(gm[:, gr:gr + 1], gtm[:, gr * 16:(gr + 1) * 16],
                                      axis=mybir.AxisListType.X)
            s_gnm.inc(i)
            s_gnm.wge(vector, 2 * tt + 1)
            for gr in range(4):
                i = vector.tensor_scalar(gnm[:, gr:gr + 1], gm[:, gr:gr + 1], -1.0, None, ALU.mult)
            s_gnm.inc(i)
            s_ge.wge(vector, 4 * (tt + 1))
            for gr in range(4):
                i = vector.reciprocal(gt1[:, gr:gr + 1], gs[:, gr:gr + 1])
            s_gr.inc(i)
            s_grt.wge(vector, tt + 1)
            vector.tensor_copy(trow[0:1, 0:TT], t_ps[0:1, 0:TT])
            s_gre.inc(vector.tensor_copy(top1row[0:1, tt * TT:(tt + 1) * TT], t1_ps[0:1, 0:TT]))
            s_mb.wge(vector, tt + 1)
            s_ind.inc(vector.tensor_tensor(ind_fm[0:16, tt * TT:(tt + 1) * TT], g_sb[0:16, :],
                                           mb_ps[0:16, 0:TT], ALU.is_equal))

    @p.block.scalar
    def _(scalar):
        for tt in range(NTT):
            s_gnm.wge(scalar, 2 * (tt + 1))
            for gr in range(4):
                i = scalar.activation(ge[:, gr * 16:(gr + 1) * 16], gtm[:, gr * 16:(gr + 1) * 16],
                                      AF.Exp, bias=gnm[:, gr:gr + 1], accum_out=gs[:, gr:gr + 1])
                s_ge.inc(i)
    return ind_fm, top1row, s_ind, s_gre


def emit_attn_tok(p, consts, q_dram, kg_dram, vg_dram, att_dram, nhd, kvmap, scale,
                  ptk_sb, qpos_sb, waits, tag, gate=None):
    NBQ = NTT
    q_sb = [p.sb(f"aq{i}_{tag}", [128, T], BF16) for i in range(2)]
    k_sb = [p.sb(f"ak{i}_{tag}", [128, S], BF16) for i in range(2)]
    v_sb = [p.sb(f"av{i}_{tag}", [128, S], BF16) for i in range(2)]
    p_sb = p.sb(f"apb_{tag}", [128, 4 * 512], BF16)
    bsq = p.sb(f"absq_{tag}", [128, 512], BF16)
    NBK = S // 512
    brow = p.sb(f"abrow_{tag}", [1, NTT + NBK + 4], F32)
    nbrow = p.sb(f"anbr_{tag}", [1, 1], F32)
    nb_sb = p.sb(f"anb_{tag}", [128, 1], F32)
    rt_sb = p.sb(f"art_{tag}", [1, 512], F32)
    r_sb = p.sb(f"arr_{tag}", [128, 512], F32)
    att_sb = [p.sb(f"ao{i}_{tag}", [128, 512], BF16) for i in range(2)]
    gate_sb = p.sb(f"agt_{tag}", [128, 512], BF16)
    msk_sb = p.sb(f"amsk_{tag}", [128, 4 * 2 * 4 * 512], BF16)
    att_ps = p.pf[0]
    sc_ps = [p.pf[1], p.pf[2]]
    lr_ps = p.pf[3]
    b_ps = p.pf[4]
    g_ps = p.pf[5]
    oc, orow_f = consts["ones_col"], consts["ones_row_f"]
    wvals = _wpairs(waits)

    (s_bsq, s_bmm, s_bred, s_brow, s_bsd, s_bneg, s_nb, s_nbe, s_sc, s_mk,
     s_exp, s_av, s_lmm, s_rj, s_rbc, s_rse, s_gt, s_gs, s_fin,
     s_cmb, s_aux, s_mgb, s_mgen) = (p.sem() for _ in range(23))
    s_ldq = [p.sem(kind="sw"), p.sem(kind="sw")]
    s_ldkv = [p.sem(kind="sw"), p.sem(kind="sw")]
    s_stp = [p.sem(kind="hw"), p.sem(kind="hw")]

    def mcol(qc, b2, c):
        return ((qc * 2 + b2) * 4 + c) * 512

    @p.block.tensor
    def _(tensor):
        for hh, n in wvals:
            tensor.wait_ge(hh, n)
        for qc in range(NBQ):
            if qc > 0:
                s_mgen.wge(tensor, qc * 8)
            i = tensor.matmul(b_ps[:, :], orow_f[0:1, :],
                              qpos_sb[0:1, qc * 512:(qc + 1) * 512],
                              start=True, stop=True)
            s_mgb.inc(i)

    @p.block.vector
    def _(vector):
        for hh, n in wvals:
            vector.wait_ge(hh, n)
        for qc in range(NBQ):
            s_mgb.wge(vector, qc + 1)
            for b2 in range(2):
                kb = EXT[qc] - 2 + b2
                for c in range(4):
                    s_mgen.inc(vector.tensor_scalar(
                        msk_sb[:, mcol(qc, b2, c):mcol(qc, b2, c) + 512],
                        b_ps[:, :], ptk_sb[:, kb * 4 + c:kb * 4 + c + 1], NEG,
                        ALU.is_lt, ALU.mult))

    kv_fresh = {}
    slot_of = {}
    kv_tgt = {}
    nslot = 0
    slot_uses = [0, 0]
    for h in range(nhd):
        kv = kvmap(h)
        if kv not in slot_of:
            slot_of[kv] = nslot % 2
            nslot += 1
            kv_fresh[h] = kv
            slot_uses[slot_of[kv]] += 1
        kv_tgt[h] = 16 * 16 * slot_uses[slot_of[kv]]

    sc_ord, mk_ord, exp_ord, av_ord, fin_ord, gt_ord, pv_ord = ({} for _ in range(7))
    nsc = nmk = nexp = nav = nfin = ngt = 0
    slice_base = {}
    nbs = 0
    last_av_c = {}
    for h in range(nhd):
        ns = NTT + (NBK if h in kv_fresh else 0)
        slice_base[h] = (nbs, ns)
        nbs += ns
        for qc in range(NBQ):
            E = EXT[qc]
            if gate is not None:
                ngt += 1
                gt_ord[(h, qc)] = ngt
            for kb in range(E):
                for c in range(4):
                    nsc += 1
                    sc_ord[(h, qc, kb, c)] = nsc
                    if kb >= E - 2:
                        nmk += 1
                        mk_ord[(h, qc, kb, c)] = nmk
                    nexp += 1
                    exp_ord[(h, qc, kb, c)] = nexp
                    pv_ord[(h, qc, kb, c)] = last_av_c.get(c, 0)
                for c in range(4):
                    nav += 1
                    av_ord[(h, qc, kb, c)] = nav
                    last_av_c[c] = nav
            nfin += 1
            fin_ord[(h, qc)] = nfin

    @p.block.gpsimd
    def _(g):
        for h in range(nhd):
            if h == 0:
                for hh, n in wvals:
                    g.wait_ge(hh, n)
            if h >= 2:
                s_stp[0].wge(g, 16 * 2 * (h - 1))
                s_stp[1].wge(g, 16 * 2 * (h - 1))
            if h in kv_fresh:
                kv = kv_fresh[h]
                sl = slot_of[kv]
                for gch in range(8):
                    src, pos = divmod(_SLOT[gch], 4)
                    g.dma_start(out=k_sb[sl][:, gch * 512:(gch + 1) * 512],
                                in_=kg_dram[src, kv, :, pos * 512:(pos + 1) * 512]).then_inc(s_ldkv[sl].h, 16)
                    g.dma_start(
                        out=v_sb[sl][:, gch * 512:(gch + 1) * 512].rearrange("pp (cc d) -> pp cc d", d=128),
                        in_=vg_dram[src, kv, pos * 512:(pos + 1) * 512, :].rearrange("(cc pp) d -> pp cc d", pp=128)
                    ).then_inc(s_ldkv[sl].h, 16)
                s_ldkv[sl].n += 16 * 16
            s_ldq[h % 2].inc(g.dma_start(out=q_sb[h % 2][:, :], in_=q_dram[h, :, :]), 16)

    @p.block.tensor
    def _(tensor):
        for h in range(nhd):
            kv = kvmap(h)
            ksl = slot_of[kv]
            s_ldq[h % 2].wge(tensor, 16 * (h // 2 + 1))
            s_ldkv[ksl].wge(tensor, kv_tgt[h])
            sbase, ns = slice_base[h]
            if h == 0:
                s_mgen.wge(tensor, 32)
            if h >= 1:
                s_nbe.wge(tensor, h)
            for sl in range(ns):
                s_bsq.wge(tensor, sbase + sl + 1)
                if sbase + sl > 0:
                    s_bred.wge(tensor, sbase + sl)
                s_bmm.inc(tensor.matmul(b_ps[0:1, 0:512], oc[:, :], bsq[:, :],
                                        start=True, stop=True))
            s_bneg.wge(tensor, h + 1)
            s_nb.inc(tensor.matmul(b_ps[0:128, 0:1], orow_f[:, 0:128], nbrow[0:1, 0:1],
                                   start=True, stop=True))
            for qc in range(NBQ):
                E = EXT[qc]
                fo = fin_ord[(h, qc)]
                if gate is not None:
                    go = gt_ord[(h, qc)]
                    if go > 1:
                        s_gs.wge(tensor, go - 1)
                    s_gt.inc(tensor.matmul(g_ps[:, :], gate[0][:, :],
                                           q_sb[h % 2][:, qc * 512:(qc + 1) * 512],
                                           start=True, stop=True))
                for kb in range(E):
                    for c in range(4):
                        o = sc_ord[(h, qc, kb, c)]
                        if o > 2:
                            s_exp.wge(tensor, o - 2)
                        s_sc.inc(tensor.matmul(sc_ps[c % 2][:, :],
                                               k_sb[ksl][:, kb * 512 + c * 128:kb * 512 + (c + 1) * 128],
                                               q_sb[h % 2][:, qc * 512:(qc + 1) * 512],
                                               start=True, stop=True))
                    s_exp.wge(tensor, exp_ord[(h, qc, kb, 3)])
                    for c in range(4):
                        first = (kb == 0 and c == 0)
                        last = (kb == E - 1 and c == 3)
                        if first and fo > 1:
                            s_fin.wge(tensor, fo - 1)
                            s_rse.wge(tensor, fo - 1)
                        i = tensor.matmul(lr_ps[0:1, 0:512], oc[:, 0:1],
                                          p_sb[:, c * 512:(c + 1) * 512],
                                          start=first, stop=last)
                        if last:
                            s_lmm.inc(i)
                        cc = kb * 4 + c
                        s_av.inc(tensor.matmul(att_ps[:, :],
                                               v_sb[ksl][:, cc * 128:(cc + 1) * 128],
                                               p_sb[:, c * 512:(c + 1) * 512],
                                               start=first, stop=last))
                s_rj.wge(tensor, fo)
                s_rbc.inc(tensor.matmul(lr_ps[:, 0:512], orow_f[0:1, 0:128],
                                        rt_sb[0:1, 0:512], start=True, stop=True))

    @p.block.scalar
    def _(scalar):
        for h in range(nhd):
            sbase, ns = slice_base[h]
            fresh = h in kv_fresh
            if h == 0:
                for hh, n in wvals:
                    scalar.wait_ge(hh, n)
            s_ldq[h % 2].wge(scalar, 16 * (h // 2 + 1))
            s_ldkv[slot_of[kvmap(h)]].wge(scalar, kv_tgt[h])
            si = 0
            for buf, nsl in ([(q_sb[h % 2], NTT)] +
                             ([(k_sb[slot_of[kvmap(h)]], NBK)] if fresh else [])):
                for sl in range(nsl):
                    if sbase + si > 0:
                        s_bmm.wge(scalar, sbase + si)
                    s_bsq.inc(scalar.activation(bsq[:, :], buf[:, sl * 512:(sl + 1) * 512],
                                                AF.Square))
                    si += 1
            s_brow.wge(scalar, h + 1)
            s_bsd.inc(scalar.activation(brow[0:1, NTT + NBK + 1:NTT + NBK + 2],
                                        brow[0:1, NTT + NBK:NTT + NBK + 1], AF.Sqrt))
            for qc in range(NBQ):
                E = EXT[qc]
                if gate is not None:
                    go = gt_ord[(h, qc)]
                    s_gt.wge(scalar, go)
                    if go > 1:
                        s_fin.wge(scalar, go - 1)
                    s_gs.inc(scalar.activation(gate_sb[:, :], g_ps[:, :], AF.Sigmoid,
                                               bias=gate[1][:, 0:1]))
                for kb in range(E):
                    for c in range(4):
                        if kb >= E - 2:
                            s_mk.wge(scalar, mk_ord[(h, qc, kb, c)])
                        else:
                            s_sc.wge(scalar, sc_ord[(h, qc, kb, c)])
                        pav = pv_ord[(h, qc, kb, c)]
                        if pav > 0:
                            s_av.wge(scalar, pav)
                        if kb == 0 and c == 0 and qc == 0:
                            s_nbe.wge(scalar, h + 1)
                        s_exp.inc(scalar.activation(
                            p_sb[:, c * 512:(c + 1) * 512], sc_ps[c % 2][:, :], AF.Exp,
                            bias=nb_sb[:, 0:1], scale=scale))

    @p.block.vector
    def _(vector):
        for h in range(nhd):
            sbase, ns = slice_base[h]
            if h == 0:
                for hh, n in wvals:
                    vector.wait_ge(hh, n)
            for si in range(ns):
                s_bmm.wge(vector, sbase + si + 1)
                s_bred.inc(vector.reduce_max(brow[0:1, si:si + 1], b_ps[0:1, 0:512],
                                             axis=mybir.AxisListType.X))
            s_bred.wge(vector, sbase + ns)
            vector.reduce_max(brow[0:1, NTT + NBK + 2:NTT + NBK + 3], brow[0:1, 0:NTT],
                              axis=mybir.AxisListType.X)
            s_cmb.inc(vector.reduce_max(brow[0:1, NTT + NBK + 3:NTT + NBK + 4],
                                        brow[0:1, NTT:NTT + NBK], axis=mybir.AxisListType.X))
            s_cmb.wge(vector, s_cmb.n)
            s_brow.inc(vector.tensor_tensor(brow[0:1, NTT + NBK:NTT + NBK + 1],
                                            brow[0:1, NTT + NBK + 2:NTT + NBK + 3],
                                            brow[0:1, NTT + NBK + 3:NTT + NBK + 4], ALU.mult))
            s_bsd.wge(vector, h + 1)
            s_bneg.inc(vector.tensor_scalar(nbrow[0:1, 0:1],
                                            brow[0:1, NTT + NBK + 1:NTT + NBK + 2],
                                            -scale, None, ALU.mult))
            s_nb.wge(vector, h + 1)
            s_nbe.inc(vector.tensor_copy(nb_sb[:, 0:1], b_ps[0:128, 0:1]))
            for qc in range(NBQ):
                E = EXT[qc]
                fo = fin_ord[(h, qc)]
                for kb in range(E - 2, E):
                    b2 = kb - (E - 2)
                    for c in range(4):
                        s_sc.wge(vector, sc_ord[(h, qc, kb, c)])
                        s_mk.inc(vector.tensor_tensor(
                            sc_ps[c % 2][:, :], sc_ps[c % 2][:, :],
                            msk_sb[:, mcol(qc, b2, c):mcol(qc, b2, c) + 512], ALU.add))
                s_lmm.wge(vector, fo)
                s_rj.inc(vector.reciprocal(rt_sb[0:1, :], lr_ps[0:1, 0:512]))
                s_rbc.wge(vector, fo)
                if fo > 2:
                    s_stp[(fo - 1) % 2].wge(vector, 16 * ((fo - 3) // 2 + 1))
                s_rse.inc(vector.tensor_copy(r_sb[:, :], lr_ps[:, 0:512]))
                s_rse.wge(vector, fo)
                s_av.wge(vector, av_ord[(h, qc, E - 1, 3)])
                i = vector.tensor_tensor(att_sb[(fo - 1) % 2][:, :], att_ps[:, :], r_sb[:, :], ALU.mult)
                if gate is not None:
                    s_aux.inc(i)
                    s_aux.wge(vector, s_aux.n)
                    s_gs.wge(vector, gt_ord[(h, qc)])
                    i = vector.tensor_tensor(att_sb[(fo - 1) % 2][:, :], att_sb[(fo - 1) % 2][:, :],
                                             gate_sb[:, :], ALU.mult)
                s_fin.inc(i)

    @p.block.sync
    def _(g):
        for h in range(nhd):
            for qc in range(NBQ):
                fo = fin_ord[(h, qc)]
                s_fin.wge(g, fo)
                s_stp[(fo - 1) % 2].inc(g.dma_start(out=att_dram[h, :, qc * 512:(qc + 1) * 512],
                                                    in_=att_sb[(fo - 1) % 2][:, :]), 16)
    return SemGroup(s_stp)


def emit_final_wait(p, sems):
    @p.block.gpsimd
    def _(g):
        for s in sems:
            for h, nn in s.pairs():
                g.wait_ge(h, nn)


def build_p1():
    p = Prog().open()
    x = p.din("x", [D, T], F32)
    wq = p.din("wq", [NH, 128, NCH, 128], F32)
    wk = p.din("wk", [KVH0, 128, NCH, 128], F32)
    wv = p.din("wv", [KVH0, 128, NCH, 128], F32)
    bqkv = p.din("bqkv", [128, NH + 2 * KVH0], F32)
    ctab = p.din("ctab", [128, T], F32)
    stab = p.din("stab", [128, T], F32)
    cpt = p.din("c_pt", [128, 128], BF16)
    q_out = p.dout("q_out", [NH, 128, T], BF16)
    k_out = p.dout("k_out", [KVH0, 128, T], BF16)
    v_out = p.dout("v_out", [KVH0, T, 128], BF16)

    consts, s_const = load_consts(p)
    xln = p.sb("xln", [128, NCH * T], BF16)
    xbf = xln
    cos_sb = p.sb("cos", [128, T], F32)
    sin_sb = p.sb("sin", [128, T], F32)
    pt_sb = p.sb("pt", [128, 128], BF16)
    bias_sb = p.sb("bias", [128, NH + 2 * KVH0], F32)
    s_misc = p.sem(kind="hw")

    @p.block.sync
    def _(sync):
        sync.dma_start(out=cos_sb[:, :], in_=ctab[:, :]).then_inc(s_misc.h, 16)
        sync.dma_start(out=sin_sb[:, :], in_=stab[:, :]).then_inc(s_misc.h, 16)
        sync.dma_start(out=pt_sb[:, :], in_=cpt[:, :]).then_inc(s_misc.h, 16)
        sync.dma_start(out=bias_sb[:, :], in_=bqkv[:, :]).then_inc(s_misc.h, 16)
    s_misc.n = 64

    s_x = emit_load_xbf(p, x, xbf)
    s_ln = emit_ln(p, xbf, xln, consts, [s_x, s_const], "l0a")
    rope = (cos_sb, sin_sb, pt_sb)
    sq = emit_proj(p, xln, wq, NH, consts, [s_ln, s_misc], "q",
                   bias_sb=bias_sb, bias_off=0, mode="rope", rope=rope, out_dram=q_out)
    sk = emit_proj(p, xln, wk, KVH0, consts, [s_ln, s_misc, sq], "k",
                   bias_sb=bias_sb, bias_off=NH, mode="rope", rope=rope, out_dram=k_out)
    sv = emit_proj(p, xln, wv, KVH0, consts, [s_ln, s_misc, sk], "v",
                   bias_sb=bias_sb, bias_off=NH + KVH0, mode="vtm", out_dram=v_out)
    emit_final_wait(p, [sq, sk, sv])
    return p


def build_attn_tok(nkv, gate=False, tag="a"):
    p = Prog().open()
    q = p.din("q", [NH, 128, T], BF16)
    kg = p.din("kg", [2, nkv, 128, T], BF16)
    vg = p.din("vg", [2, nkv, T, 128], BF16)
    ptk = p.din("ptk", [128, (S // 512) * 4], F32)
    qpos = p.din("qpos", [1, T], F32)
    att = p.dout("att", [NH, 128, T], BF16)
    scale = HD ** -0.5
    consts, s_const = load_consts(p, reg_vals=[scale])
    ptk_sb = p.sb("ptk_sb", [128, (S // 512) * 4], F32)
    qpos_sb = p.sb("qpos_sb", [1, T], F32)
    s_mh = p.sem(kind="hw")
    s_msw = p.sem(kind="sw")
    s_misc = SemGroup([s_mh, s_msw])
    gt = None
    if gate:
        gwt = p.din("gwt", [128, 128], F32)
        gbt = p.din("gbt", [128, 1], F32)
        gw_sb = p.sb("gw", [128, 128], BF16)
        gb_sb = p.sb("gb", [128, 1], F32)
        gt = (gw_sb, gb_sb)

    @p.block.sync
    def _(sync):
        s_mh.inc(sync.dma_start(out=ptk_sb[:, :], in_=ptk[:, :]), 16)
        s_mh.inc(sync.dma_start(out=qpos_sb[:, :], in_=qpos[:, :]), 16)

    if gate:
        @p.block.gpsimd
        def _(g):
            s_msw.inc(g.dma_start(out=gw_sb[:, :], in_=gwt[:, :]), 16)
            s_msw.inc(g.dma_start(out=gb_sb[:, :], in_=gbt[:, :]), 16)

    kvmap = (lambda h: h * nkv // NH) if nkv > 1 else (lambda h: 0)
    s_done = emit_attn_tok(p, consts, q, kg, vg, att, NH, kvmap, scale,
                           ptk_sb, qpos_sb, [s_const, s_misc], tag, gate=gt)
    emit_final_wait(p, [s_done])
    return p


def _attn_host_consts(c):
    nb4 = (S // 512) * 4
    ptk = np.ascontiguousarray(np.arange(128, dtype=np.float32)[:, None]
                               + 128.0 * np.arange(nb4, dtype=np.float32)[None, :])
    qpos = np.ascontiguousarray(_core_pos(c).astype(np.float32).reshape(1, T))
    return ptk, qpos


def _run_attn(builder_args, q_key, k_key, v_key, results, extra=None):
    nkv, gate, tag = builder_args
    p = build_attn_tok(nkv, gate=gate, tag=tag)
    cid = np.ascontiguousarray(np.eye(128, dtype=BF))
    cidf = np.ascontiguousarray(np.eye(128, dtype=np.float32))
    in_maps = []
    for c in range(NCORE):
        pair = (c // 2) * 2
        kgv = np.stack([results[pair][k_key], results[pair + 1][k_key]])
        vgv = np.stack([results[pair][v_key], results[pair + 1][v_key]])
        if kgv.ndim == 3:
            kgv = kgv[:, None]
            vgv = vgv[:, None]
        ptk, qpos = _attn_host_consts(c)
        m = {"q": results[c][q_key], "kg": np.ascontiguousarray(kgv),
             "vg": np.ascontiguousarray(vgv), "ptk": ptk, "qpos": qpos,
             "c_id": cid, "c_idf": cidf}
        if extra:
            m.update(extra)
        in_maps.append(m)
    res = run_bass_kernel_spmd(p.nc, in_maps, core_ids=list(range(NCORE)))
    p.close()
    return res.results


def run_p2(p1_results, masks=None):
    return _run_attn((KVH0, False, "a"), "q_out", "k_out", "v_out", p1_results)


def emit_mask_all(p, consts, ind_dram, s_indd, tag):
    orow_f = consts["ones_row_f"]
    mk_ps = p.pf[2]
    mask_all = p.sb(f"mska_{tag}", [128, 16 * T], U8)
    ind_t = p.sb(f"indt_{tag}", [1, 2 * TT], F32)
    s_mld = [p.sem(kind="hw"), p.sem(kind="hw")]
    s_mb2, s_mev = p.sem(), p.sem()
    indd_n = s_indd.n

    @p.block.sync
    def _(sync):
        s_indd.wge(sync, indd_n)
        for m in range(16):
            for tt in range(NTT):
                it = m * NTT + tt
                if it >= 2:
                    s_mb2.wge(sync, it - 1)
                s_mld[it % 2].inc(sync.dma_start(
                    out=ind_t[0:1, (it % 2) * TT:(it % 2 + 1) * TT],
                    in_=ind_dram[m, tt * TT:(tt + 1) * TT]), 16)

    @p.block.tensor
    def _(tensor):
        for m in range(16):
            for tt in range(NTT):
                it = m * NTT + tt
                s_mld[it % 2].wge(tensor, 16 * (it // 2 + 1))
                if it >= 1:
                    s_mev.wge(tensor, it)
                s_mb2.inc(tensor.matmul(
                    mk_ps[:, :], orow_f[0:1, :],
                    ind_t[0:1, (it % 2) * TT:(it % 2 + 1) * TT],
                    start=True, stop=True))

    @p.block.vector
    def _(vector):
        for m in range(16):
            for tt in range(NTT):
                it = m * NTT + tt
                s_mb2.wge(vector, it + 1)
                s_mev.inc(vector.tensor_copy(
                    mask_all[:, m * T + tt * TT:m * T + (tt + 1) * TT], mk_ps[:, :]))
    return mask_all, s_mev


def build_p3():
    p = Prog().open()
    x = p.din("x", [NCH, 128, T], F32)
    attd = p.din("attd", [NCH, 128, T], BF16)
    wo = p.din("wo", [NCH, 128, NCH, 128], F32)
    bo = p.din("bo", [128, NCH], F32)
    wg = p.din("wg", [128, NCH * 16], F32)
    wqkv = p.din("wqkv", [48, 128, NCH, 128], F32)
    x1_out = p.dout("x1_out", [NCH, 128, T], F32)
    q_out = p.dout("q_out", [16, 128, T], BF16)
    ksh_out = p.dout("ksh_out", [1, 128, T], BF16)
    vsh_out = p.dout("vsh_out", [1, T, 128], BF16)
    ind_dram = p.dint("ind_d", [16, T], F32)

    consts, s_const = load_consts(p)
    ensure_proj_scratch(p)
    bias_sb = p.sb("bias", [128, NCH], F32)
    wg_sb = p.sb("wg_sb", [128, NCH * 16], BF16)
    xln = p.sb("xln", [128, NCH * T], BF16)
    ksh_acc = p.sb("ksh_acc", [128, T], BF16)
    vsh_acc = p.sb("vsh_acc", [128, T], BF16)
    vtm_sb = p.sb("vtm_sb", [128, TT], BF16)
    s_mh = p.sem(kind="hw")
    s_msw = p.sem(kind="sw")
    s_misc = SemGroup([s_mh, s_msw])

    @p.block.sync
    def _(sync):
        s_mh.inc(sync.dma_start(out=bias_sb[:, :], in_=bo[:, :]), 16)

    @p.block.gpsimd
    def _(g):
        s_msw.inc(g.dma_start(out=wg_sb[:, :], in_=wg[:, :]), 16)

    with p.scope():
        s_x1, s_st1 = emit_resadd_ln(p, x, attd, x1_out, xln, consts, [s_const, s_misc],
                                     "p3", wo_dram=wo, bias_sb=bias_sb, bias_off=0)
        s_ln = emit_ln(p, xln, xln, consts, [s_x1, s_st1], "p3")
    ind_fm, top1row, s_ind, s_gre = emit_l0f_gating(p, xln, wg_sb, consts,
                                                    [s_ln, s_misc], "p3")
    s_indf = p.sem(kind="hw")

    @p.block.sync
    def _(sync):
        s_ind.wge(sync, s_ind.n)
        s_indf.inc(sync.dma_start(out=ind_dram[:, :], in_=ind_fm[0:16, :]), 16)

    sq = emit_proj(p, xln, wqkv[0:16], 16, consts, [s_ln, s_ind], "q",
                   mode="plain", out_dram=q_out)
    mask_all, s_mask = emit_mask_all(p, consts, ind_dram, s_indf, "p3")
    s_kacc = emit_proj(p, xln, wqkv[16:32], 16, consts, [s_ln, sq, s_mask], "k",
                       mode="sel", sel=(mask_all, ksh_acc))
    s_vacc = emit_proj(p, xln, wqkv[32:48], 16, consts, [s_ln, s_kacc], "v",
                       mode="sel", sel=(mask_all, vsh_acc))

    s_kst = p.sem(kind="sw")

    @p.block.gpsimd
    def _(g):
        s_kacc.wge(g, s_kacc.n)
        s_kst.inc(g.dma_start(out=ksh_out[0, :, :], in_=ksh_acc[:, :]), 16)

    t1_ps = p.pf[3]
    vt_ps = p.pb[0]
    s_t1b, s_t1m, s_vt, s_vte = (p.sem() for _ in range(4))
    s_vst = p.sem(kind="hw")
    orow_f = consts["ones_row_f"]
    ident = consts["ident"]

    @p.block.tensor
    def _(tensor):
        s_vacc.wge(tensor, s_vacc.n)
        s_gre.wge(tensor, s_gre.n)
        for tt in range(NTT):
            if tt > 0:
                s_t1m.wge(tensor, tt)
            s_t1b.inc(tensor.matmul(t1_ps[:, :], orow_f[0:1, :],
                                    top1row[0:1, tt * TT:(tt + 1) * TT], start=True, stop=True))
            s_t1m.wge(tensor, tt + 1)
            if tt > 0:
                s_vte.wge(tensor, 4 * tt)
            for sg in range(4):
                i = tensor.transpose(vt_ps[:, sg * 128:(sg + 1) * 128],
                                     vsh_acc[:, tt * TT + sg * 128:tt * TT + (sg + 1) * 128],
                                     ident[:, :])
            s_vt.inc(i)

    @p.block.vector
    def _(vector):
        for tt in range(NTT):
            s_t1b.wge(vector, tt + 1)
            s_t1m.inc(vector.tensor_tensor(vsh_acc[:, tt * TT:(tt + 1) * TT],
                                           vsh_acc[:, tt * TT:(tt + 1) * TT],
                                           t1_ps[:, :], ALU.mult))
            s_vt.wge(vector, tt + 1)
            if tt > 0:
                s_vst.wge(vector, 16 * tt)
            for sg in range(4):
                i = vector.tensor_copy(vtm_sb[:, sg * 128:(sg + 1) * 128],
                                       vt_ps[:, sg * 128:(sg + 1) * 128])
                s_vte.inc(i)

    @p.block.sync
    def _(g):
        for tt in range(NTT):
            s_vte.wge(g, 4 * (tt + 1))
            s_vst.inc(g.dma_start(
                out=vsh_out[0, tt * TT:(tt + 1) * TT, :].rearrange("(sg pp) d -> pp sg d", pp=128),
                in_=vtm_sb[:, :].rearrange("p (sg d) -> p sg d", sg=4)), 16)

    emit_final_wait(p, [s_st1, sq, s_kst, s_vst])
    return p


def run_p3(inputs, p2_results, xs_list):
    p = build_p3()
    Wo = np.asarray(inputs["L0_attn_Wo"])
    bo = np.asarray(inputs["L0_attn_bo"]).reshape(NCH, 128).T
    Wg = np.asarray(inputs["L0_ffn_W_gate"])
    wg_host = np.ascontiguousarray(Wg.reshape(16, 16, 128).transpose(2, 1, 0).reshape(128, 256).astype(np.float32))
    wo_t = _tile_lhsT(Wo)
    wqkv_t = _tile_lhsT(np.asarray(inputs["L0_ffn_W_qkv"]))
    cid = np.ascontiguousarray(np.eye(128, dtype=BF))
    cidf = np.ascontiguousarray(np.eye(128, dtype=np.float32))
    in_maps = []
    for c in range(NCORE):
        attd = p2_results[c]["att"]
        in_maps.append({
            "x": np.ascontiguousarray(xs_list[c].reshape(NCH, 128, T)),
            "attd": attd, "wo": wo_t, "bo": np.ascontiguousarray(bo.astype(np.float32)),
            "wg": wg_host, "wqkv": wqkv_t, "c_id": cid, "c_idf": cidf,
        })
    res = run_bass_kernel_spmd(p.nc, in_maps, core_ids=list(range(NCORE)))
    p.close()
    return res.results



def build_p5():
    p = Prog().open()
    x = p.din("x", [NCH, 128, T], F32)
    attd = p.din("attd", [NCH, 128, T], BF16)
    wq = p.din("wq", [NH, 128, NCH, 128], F32)
    wk = p.din("wk", [NH, 128, NCH, 128], F32)
    wv = p.din("wv", [NH, 128, NCH, 128], F32)
    x2_out = p.dout("x2_out", [NCH, 128, T], F32)
    q_out = p.dout("q_out", [NH, 128, T], BF16)
    k_out = p.dout("k_out", [NH, 128, T], BF16)
    v_out = p.dout("v_out", [NH, T, 128], BF16)

    consts, s_const = load_consts(p)
    ensure_proj_scratch(p)
    xln = p.sb("xln", [128, NCH * T], BF16)
    with p.scope():
        s_x1, s_st1 = emit_resadd_ln(p, x, attd, x2_out, xln, consts, [s_const], "p5")
        s_ln = emit_ln(p, xln, xln, consts, [s_x1, s_st1], "p5")
    sq = emit_proj(p, xln, wq, NH, consts, [s_ln], "q", mode="plain", out_dram=q_out)
    sk = emit_proj(p, xln, wk, NH, consts, [s_ln, sq], "k", mode="plain", out_dram=k_out)
    sv = emit_proj(p, xln, wv, NH, consts, [s_ln, sk], "v", mode="vtm", out_dram=v_out)
    emit_final_wait(p, [s_st1, sq, sk, sv])
    return p


def run_p5(inputs, p4_results, p3_results):
    p = build_p5()
    wq_t = _tile_lhsT(np.asarray(inputs["L1_attn_q_w"]))
    wk_t = _tile_lhsT(np.asarray(inputs["L1_attn_k_w"]))
    wv_t = _tile_lhsT(np.asarray(inputs["L1_attn_v_w"]))
    cid = np.ascontiguousarray(np.eye(128, dtype=BF))
    cidf = np.ascontiguousarray(np.eye(128, dtype=np.float32))
    in_maps = []
    for c in range(NCORE):
        attd = p4_results[c]["att"]
        in_maps.append({
            "x": p3_results[c]["x1_out"], "attd": attd,
            "wq": wq_t, "wk": wk_t, "wv": wv_t, "c_id": cid, "c_idf": cidf,
        })
    res = run_bass_kernel_spmd(p.nc, in_maps, core_ids=list(range(NCORE)))
    p.close()
    return res.results


def run_p4(p3_results):
    out = []
    for c in range(NCORE):
        r = dict(p3_results[c])
        r["ksh3"] = r["ksh_out"][0]
        r["vsh3"] = r["vsh_out"][0]
        out.append(r)
    return _run_attn((1, False, "m"), "q_out", "ksh3", "vsh3", out)


def run_p6(inputs, p5_results):
    gwt = np.ascontiguousarray(np.asarray(inputs["L1_attn_gate_w"]).T.astype(np.float32))
    gbt = np.ascontiguousarray(np.asarray(inputs["L1_attn_gate_b"]).reshape(128, 1).astype(np.float32))
    return _run_attn((NH, True, "g"), "q_out", "k_out", "v_out", p5_results,
                     extra={"gwt": gwt, "gbt": gbt})


def emit_dequant(p, consts, pk_dram, sc_dram, out_dram, nlm, nw, tag, waits=(), bank=0):
    ngrp = nw * 8 // 128
    SPW = 256 if nw % 256 == 0 else 176
    assert nw % SPW == 0 and SPW % 16 == 0
    nsp = nw // SPW
    gps = SPW * 8 // 128
    identf = consts["identf"]
    pk = p.sb(f"dqp_{tag}", [128, nw], I32)
    an = p.sb(f"dqa_{tag}", [128, 8 * SPW], I32)
    b2 = p.sb(f"dqb_{tag}", [128, 8 * SPW], I32)
    f1 = p.sb(f"dqf1_{tag}", [128, 8 * SPW], F32)
    f2 = p.sb(f"dqf2_{tag}", [128, 8 * SPW], F32)
    f3 = p.sb(f"dqf3_{tag}", [128, 8 * SPW], F32)
    f4 = p.sb(f"dqf4_{tag}", [128, 8 * SPW], F32)
    valh = p.sb(f"dqv_{tag}", [128, 8 * SPW], F32)
    sc16 = p.sb(f"dqs_{tag}", [128, ngrp], F16)
    sc32 = p.sb(f"dqs3_{tag}", [128, ngrp], F32)
    asm = p.sb(f"dqo_{tag}", [128, ngrp * 128], BF16)
    tp_ps = p.pf[bank]
    s_dec, s_val, s_scl, s_tp, s_ev, s_sc32 = (p.sem() for _ in range(6))
    s_ld = p.sem(kind="hw")
    s_st = p.sem(kind="sw")
    wvals = _wpairs(waits)

    @p.block.sync
    def _(sync):
        for lm in range(nlm):
            if lm >= 1:
                s_dec.wge(sync, lm * nsp)
            s_ld.inc(sync.dma_start(out=pk[:, :], in_=pk_dram[lm, :, :]), 16)
            if lm >= 1:
                s_sc32.wge(sync, lm)
            s_ld.inc(sync.dma_start(out=sc16[:, :], in_=sc_dram[lm, :, :]), 16)

    @p.block.vector
    def _(vector):
        for lm in range(nlm):
            if lm == 0:
                for hh, n in wvals:
                    vector.wait_ge(hh, n)
            s_ld.wge(vector, 32 * (lm + 1))
            s_sc32.inc(vector.tensor_copy(sc32[:, :], sc16[:, :]))
            for sp in range(nsp):
                if lm + sp > 0:
                    s_tp.wge(vector, (lm * nsp + sp - 1) * gps + gps)
                w0 = sp * SPW
                J = SPW

                def _stage(emit1):
                    for j in range(8):
                        i = emit1(j)
                    s_val.inc(i)
                    s_val.wge(vector, s_val.n)

                _stage(lambda j: vector.tensor_scalar(an[:, j * J:(j + 1) * J], pk[:, w0:w0 + SPW],
                                                      4 * j, 15, ALU.logical_shift_right, ALU.bitwise_and))
                _stage(lambda j: vector.tensor_scalar(b2[:, j * J:(j + 1) * J], an[:, j * J:(j + 1) * J],
                                                      7, None, ALU.bitwise_and))
                _stage(lambda j: vector.tensor_copy(f1[:, j * J:(j + 1) * J], b2[:, j * J:(j + 1) * J]))
                _stage(lambda j: vector.tensor_scalar(b2[:, j * J:(j + 1) * J], an[:, j * J:(j + 1) * J],
                                                      3, None, ALU.logical_shift_right))
                _stage(lambda j: vector.tensor_copy(f2[:, j * J:(j + 1) * J], b2[:, j * J:(j + 1) * J]))
                _stage(lambda j: vector.tensor_scalar(f2[:, j * J:(j + 1) * J], f2[:, j * J:(j + 1) * J],
                                                      -2.0, 1.0, ALU.mult, ALU.add))
                _stage(lambda j: vector.tensor_scalar(f3[:, j * J:(j + 1) * J], f1[:, j * J:(j + 1) * J],
                                                      -4.0, 0.0, ALU.add, ALU.max))
                _stage(lambda j: vector.tensor_scalar(f4[:, j * J:(j + 1) * J], f1[:, j * J:(j + 1) * J],
                                                      -6.0, 0.0, ALU.add, ALU.max))
                _stage(lambda j: vector.tensor_tensor(f3[:, j * J:(j + 1) * J], f1[:, j * J:(j + 1) * J],
                                                      f3[:, j * J:(j + 1) * J], ALU.add))
                _stage(lambda j: vector.scalar_tensor_tensor(f3[:, j * J:(j + 1) * J], f3[:, j * J:(j + 1) * J],
                                                             0.5, f4[:, j * J:(j + 1) * J], ALU.mult, ALU.add))
                vh = valh[:, :].rearrange("p (w j) -> p j w", j=8)
                for j in range(8):
                    i = vector.tensor_tensor(vh[:, j, :], f3[:, j * J:(j + 1) * J],
                                             f2[:, j * J:(j + 1) * J], ALU.mult)
                s_dec.inc(i)
                s_dec.wge(vector, s_dec.n)
                for gl in range(gps):
                    g = sp * gps + gl
                    i = vector.tensor_scalar(valh[:, gl * 128:(gl + 1) * 128],
                                             valh[:, gl * 128:(gl + 1) * 128],
                                             sc32[:, g:g + 1], None, ALU.mult)
                s_scl.inc(i)
                for gl in range(gps):
                    g = sp * gps + gl
                    if lm >= 1 and sp == 0 and gl == 0:
                        s_st.wge(vector, 16 * lm)
                    s_tp.wge(vector, (lm * nsp + sp) * gps + gl + 1)
                    s_ev.inc(vector.tensor_copy(asm[:, g * 128:(g + 1) * 128], tp_ps[:, 0:128]))

    @p.block.tensor
    def _(tensor):
        for lm in range(nlm):
            if lm == 0:
                for hh, n in wvals:
                    tensor.wait_ge(hh, n)
            for sp in range(nsp):
                s_scl.wge(tensor, lm * nsp + sp + 1)
                for gl in range(gps):
                    o = (lm * nsp + sp) * gps + gl
                    if o >= 1:
                        s_ev.wge(tensor, o)
                    s_tp.inc(tensor.transpose(tp_ps[:, 0:128],
                                              valh[:, gl * 128:(gl + 1) * 128], identf[:, :]))

    @p.block.gpsimd
    def _(g):
        for lm in range(nlm):
            s_ev.wge(g, lm * ngrp + ngrp)
            s_st.inc(g.dma_start(out=out_dram[lm, :, :], in_=asm[:, :]), 16)
    return s_st


def build_p7a():
    p = Prog().open()
    gup = p.din("gup", [11, 128, 256], I32)
    gus = p.din("gus", [11, 128, 16], F16)
    dnp = p.din("dnp", [2, 128, 704], I32)
    dns = p.din("dns", [2, 128, 44], F16)
    wgu_mine = p.dout("wgu_mine", [11, 128, 16 * 128], BF16)
    down_mine = p.dout("down_mine", [2, 128, 44 * 128], BF16)
    consts, s_const = load_consts(p)

    s1 = emit_dequant(p, consts, gup, gus, wgu_mine, 11, 256, "gu", waits=[s_const], bank=0)
    s2 = emit_dequant(p, consts, dnp, dns, down_mine, 2, 704, "dn", waits=[s_const], bank=1)
    emit_final_wait(p, [s1, s2])
    return p


def run_p7a(inputs):
    p = build_p7a()
    gup = np.asarray(inputs["L1_ffn_gate_up_packed"])
    gus = np.asarray(inputs["L1_ffn_gate_up_scales"])
    dnp = np.asarray(inputs["L1_ffn_down_packed"])
    dns = np.asarray(inputs["L1_ffn_down_scales"])
    cid = np.ascontiguousarray(np.eye(128, dtype=BF))
    cidf = np.ascontiguousarray(np.eye(128, dtype=np.float32))
    in_maps = []
    for c in range(NCORE):
        r0, r1 = c * 1408, (c + 1) * 1408
        d0, d1 = c * 256, (c + 1) * 256
        in_maps.append({
            "gup": np.ascontiguousarray(gup[r0:r1].reshape(11, 128, 256)),
            "gus": np.ascontiguousarray(gus.T[r0:r1].reshape(11, 128, 16)),
            "dnp": np.ascontiguousarray(dnp[d0:d1].reshape(2, 128, 704)),
            "dns": np.ascontiguousarray(dns.T[d0:d1].reshape(2, 128, 44)),
            "c_id": cid, "c_idf": cidf,
        })
    res = run_bass_kernel_spmd(p.nc, in_maps, core_ids=list(range(NCORE)))
    p.close()
    return res.results


def emit_l1ffn(p, consts, xln, wgu, wdn, h_dram, xout, s_st1, s_ln, extra_waits):
    oc = consts["ones_col"]
    orow_f = consts["ones_row_f"]

    wbA = [p.sb(f"wA{i}", [128, NCH * 128], BF16) for i in range(4)]
    sil_sb = [p.sb(f"sil_sb{i}", [128, TT], F32) for i in range(2)]
    hb = [p.sb(f"hb{i}", [128, TT], BF16) for i in range(2)]
    wdb = [p.sb(f"wd{i}", [128, 5 * 128], BF16) for i in range(2)]
    hsb = [p.sb(f"hs{i}", [128, TT], BF16) for i in range(2)]
    y_sb = p.sb("y_sb", [128, NCH * TT], BF16)
    ysq = p.sb("ysq", [128, TT], BF16)
    x3t = p.sb("x3t", [128, NCH * TT], F32)
    rrow = p.sb("rrow", [1, 3 * TT], F32)
    ytmp = p.sb("ytmp", [128, TT], F32)
    xoc = [p.sb(f"xoc{i}", [128, TT], F32) for i in range(2)]
    OG = [(0, 5), (5, 10), (10, 15), (15, 16)]
    yev_cum = []
    tot = 0
    for g in range(4 * NTT):
        yev_cum.append(tot)
        tot += OG[g % 4][1] - OG[g % 4][0]

    (s_gu, s_sil, s_h, s_ymm, s_yev, s_ysq, s_rms,
     s_msr, s_sd, s_rr, s_rb, s_fin, s_aux) = (p.sem() for _ in range(13))
    s_x3 = p.sem(kind="sw")
    s_wA = [p.sem(kind="sw"), p.sem(kind="sw")]
    s_hld = [p.sem(kind="sw"), p.sem(kind="sw")]
    s_wd = [p.sem(kind="sw"), p.sem(kind="sw")]
    s_hst = [p.sem(kind="hw"), p.sem(kind="hw")]
    s_fst = [p.sem(kind="hw"), p.sem(kind="hw")]

    @p.block.gpsimd
    def _(g):
        for hh, nn in _wpairs(extra_waits):
            g.wait_ge(hh, nn)
        for tt in range(NTT):
            for ip in range(44):
                it = tt * 44 + ip
                if it >= 2:
                    s_gu.wge(g, it - 1)
                s_wA[it % 2].inc(g.dma_start(out=wbA[(it % 2) * 2][:, :], in_=wgu[ip, :, :]), 16)
                s_wA[it % 2].inc(g.dma_start(out=wbA[(it % 2) * 2 + 1][:, :], in_=wgu[44 + ip, :, :]), 16)
            for ogi, (o0, o1) in enumerate(OG):
                if ogi == 0:
                    s_hst[0].wge(g, 16 * 22 * (tt + 1))
                    s_hst[1].wge(g, 16 * 22 * (tt + 1))
                for i in range(44):
                    ib = (tt * 4 + ogi) * 44 + i
                    if ib >= 2:
                        s_ymm.wge(g, ib - 1)
                    s_hld[ib % 2].inc(g.dma_start(out=hsb[ib % 2][:, :],
                                                  in_=h_dram[i, :, tt * TT:(tt + 1) * TT]), 16)
                    s_wd[ib % 2].inc(g.dma_start(
                        out=wdb[ib % 2][:, 0:(o1 - o0) * 128].rearrange("p (o mm) -> p o mm", o=o1 - o0),
                        in_=wdn[o0:o1, :, i * 128:(i + 1) * 128].rearrange("o p mm -> p o mm")), 16)
            if tt >= 1:
                s_fin.wge(g, tt * NCH)
            for hh, nn in s_st1.pairs():
                g.wait_ge(hh, nn)
            s_x3.inc(g.dma_start(out=x3t[:, :].rearrange("p (c t) -> p c t", c=NCH),
                                 in_=xout[0:NCH, :, tt * TT:(tt + 1) * TT].rearrange("c p t -> p c t")), 16)

    @p.block.tensor
    def _(tensor):
        for tt in range(NTT):
            if tt == 0:
                s_ln.wge(tensor, s_ln.n)
            for ip in range(44):
                it = tt * 44 + ip
                s_wA[it % 2].wge(tensor, 32 * (it // 2 + 1))
                if it >= 2:
                    s_sil.wge(tensor, it - 1)
                    s_h.wge(tensor, it - 1)
                if ip == 0 and tt >= 1:
                    s_aux.wge(tensor, tt * NCH)
                for c in range(NCH):
                    tensor.matmul(p.pf[(it % 2) * 2][:, :], wbA[(it % 2) * 2][:, c * 128:(c + 1) * 128],
                                  xln[:, c * T + tt * TT:c * T + tt * TT + TT],
                                  start=(c == 0), stop=(c == NCH - 1))
                for c in range(NCH):
                    i = tensor.matmul(p.pf[(it % 2) * 2 + 1][:, :], wbA[(it % 2) * 2 + 1][:, c * 128:(c + 1) * 128],
                                      xln[:, c * T + tt * TT:c * T + tt * TT + TT],
                                      start=(c == 0), stop=(c == NCH - 1))
                s_gu.inc(i)
            s_h.wge(tensor, 44 * (tt + 1))
            for ogi, (o0, o1) in enumerate(OG):
                gG = tt * 4 + ogi
                for i in range(44):
                    ib = gG * 44 + i
                    s_hld[ib % 2].wge(tensor, 16 * (ib // 2 + 1))
                    s_wd[ib % 2].wge(tensor, 16 * (ib // 2 + 1))
                    if i == 0 and gG >= 1:
                        s_yev.wge(tensor, yev_cum[gG])
                    for ol in range(o1 - o0):
                        ii = tensor.matmul(p.pf[ol][:, :], wdb[ib % 2][:, ol * 128:(ol + 1) * 128],
                                           hsb[ib % 2][:, :], start=(i == 0), stop=(i == 43))
                    s_ymm.inc(ii)
                for ol in range(o1 - o0):
                    o = o0 + ol
                    s_ysq.wge(tensor, tt * NCH + o + 1)
                    i = tensor.matmul(p.pf[5][0:1, 0:TT], oc[:, :], ysq[:, :],
                                      start=(o == 0), stop=(o == NCH - 1))
                    s_rms.inc(i)
            s_rr.wge(tensor, tt + 1)
            s_rb.inc(tensor.matmul(p.pf[0][:, :], orow_f[0:1, :], rrow[0:1, 2 * TT:3 * TT],
                                   start=True, stop=True))

    @p.block.scalar
    def _(scalar):
        for tt in range(NTT):
            for ip in range(44):
                it = tt * 44 + ip
                s_gu.wge(scalar, it + 1)
                if it >= 2:
                    s_h.wge(scalar, it - 1)
                s_sil.inc(scalar.activation(sil_sb[it % 2][:, :], p.pf[(it % 2) * 2][:, :], AF.Silu))
            for o in range(NCH):
                s_yev.wge(scalar, tt * NCH + o + 1)
                if tt * NCH + o >= 1:
                    s_rms.wge(scalar, tt * NCH + o)
                s_ysq.inc(scalar.activation(ysq[:, :], y_sb[:, o * TT:(o + 1) * TT], AF.Square))
            s_msr.wge(scalar, tt + 1)
            s_sd.inc(scalar.activation(rrow[0:1, TT:2 * TT], rrow[0:1, 0:TT], AF.Sqrt))

    @p.block.vector
    def _(vector):
        for tt in range(NTT):
            for ip in range(44):
                it = tt * 44 + ip
                s_sil.wge(vector, it + 1)
                if it >= 2:
                    s_hst[it % 2].wge(vector, 16 * (it // 2))
                s_h.inc(vector.tensor_tensor(hb[it % 2][:, :], sil_sb[it % 2][:, :], p.pf[(it % 2) * 2 + 1][:, :], ALU.mult))
            for ogi, (o0, o1) in enumerate(OG):
                gG = tt * 4 + ogi
                s_ymm.wge(vector, gG * 44 + 44)
                for ol in range(o1 - o0):
                    o = o0 + ol
                    if tt >= 1 and o == 0:
                        s_ysq.wge(vector, tt * NCH)
                    s_yev.inc(vector.tensor_copy(y_sb[:, o * TT:(o + 1) * TT], p.pf[ol][:, :]))
            s_rms.wge(vector, tt * NCH + NCH)
            s_msr.inc(vector.tensor_scalar(rrow[0:1, 0:TT], p.pf[5][0:1, 0:TT],
                                           1.0 / D, 1e-6, ALU.mult, ALU.add))
            s_sd.wge(vector, tt + 1)
            s_rr.inc(vector.reciprocal(rrow[0:1, 2 * TT:3 * TT], rrow[0:1, TT:2 * TT]))
            s_rb.wge(vector, tt + 1)
            s_x3.wge(vector, 16 * (tt + 1))
            for o in range(NCH):
                if tt * NCH + o >= 2:
                    s_fst[(tt * NCH + o) % 2].wge(vector, 16 * ((tt * NCH + o) // 2))
                if tt * NCH + o >= 1:
                    s_fin.wge(vector, tt * NCH + o)
                s_aux.inc(vector.tensor_tensor(ytmp[:, :], y_sb[:, o * TT:(o + 1) * TT],
                                               p.pf[0][:, :], ALU.mult))
                s_aux.wge(vector, s_aux.n)
                s_fin.inc(vector.tensor_tensor(xoc[(tt * NCH + o) % 2][:, :], ytmp[:, :],
                                               x3t[:, o * TT:(o + 1) * TT], ALU.add))

    @p.block.sync
    def _(g):
        for tt in range(NTT):
            for ip in range(44):
                it = tt * 44 + ip
                s_h.wge(g, it + 1)
                s_hst[it % 2].inc(g.dma_start(out=h_dram[ip, :, tt * TT:(tt + 1) * TT],
                                              in_=hb[it % 2][:, :]), 16)
            for o in range(NCH):
                s_fin.wge(g, tt * NCH + o + 1)
                s_fst[(tt * NCH + o) % 2].inc(g.dma_start(out=xout[o, :, tt * TT:(tt + 1) * TT],
                                                          in_=xoc[(tt * NCH + o) % 2][:, :]), 16)

    return SemGroup(s_fst)


def build_p7b():
    p = Prog().open()
    x = p.din("x", [NCH, 128, T], F32)
    attd = p.din("attd", [NCH, 128, T], BF16)
    wo = p.din("wo", [NCH, 128, NCH, 128], F32)
    bo = p.din("bo", [128, NCH], F32)
    wgu = p.din("wgu", [88, 128, 16 * 128], BF16)
    wdn = p.din("wdn", [16, 128, 44 * 128], BF16)
    xout = p.dout("xout", [NCH, 128, T], F32)
    h_dram = p.dint("h_d", [44, 128, T], BF16)

    consts, s_const = load_consts(p)
    ensure_proj_scratch(p)
    oc = consts["ones_col"]
    orow_f = consts["ones_row_f"]
    bias_sb = p.sb("bias", [128, NCH], F32)
    xln = p.sb("xln", [128, NCH * T], BF16)
    s_misc = p.sem(kind="hw")

    @p.block.sync
    def _(sync):
        s_misc.inc(sync.dma_start(out=bias_sb[:, :], in_=bo[:, :]), 16)

    with p.scope():
        s_x1, s_st1 = emit_resadd_ln(p, x, attd, xout, xln, consts, [s_const, s_misc],
                                     "p7", wo_dram=wo, bias_sb=bias_sb, bias_off=0)
        s_ln = emit_ln(p, xln, xln, consts, [s_x1, s_st1], "p7")
    s_fst = emit_l1ffn(p, consts, xln, wgu, wdn, h_dram, xout, s_st1, s_ln, [])
    emit_final_wait(p, [SemGroup(s_fst)])
    return p


def run_p7b(inputs, p6_results, p5_results, p7a_results):
    p = build_p7b()
    wo_t = _tile_lhsT(np.asarray(inputs["L1_attn_o_w"]))
    bo = np.zeros((128, NCH), np.float32)
    wgu_all = np.concatenate([p7a_results[c]["wgu_mine"] for c in range(NCORE)], axis=0)
    wdn_all = np.concatenate([p7a_results[c]["down_mine"] for c in range(NCORE)], axis=0)
    cid = np.ascontiguousarray(np.eye(128, dtype=BF))
    cidf = np.ascontiguousarray(np.eye(128, dtype=np.float32))
    in_maps = []
    for c in range(NCORE):
        attd = p6_results[c]["att"]
        in_maps.append({
            "x": p5_results[c]["x2_out"], "attd": attd, "wo": wo_t, "bo": bo,
            "wgu": wgu_all, "wdn": wdn_all, "c_id": cid, "c_idf": cidf,
        })
    res = run_bass_kernel_spmd(p.nc, in_maps, core_ids=list(range(NCORE)))
    p.close()
    return res.results


def _prep_shared(inputs):
    d = {}
    d["c_id"] = np.ascontiguousarray(np.eye(128, dtype=BF))
    d["c_idf"] = np.ascontiguousarray(np.eye(128, dtype=np.float32))
    d["c_pt"] = _rope_P()
    cos, sin = _rope_tables()
    d["rope_cos"], d["rope_sin"] = cos, sin
    d["masks"] = _diag_masks()
    d["wq0"] = _tile_lhsT(np.asarray(inputs["L0_attn_Wq"]))
    d["wk0"] = _tile_lhsT(np.asarray(inputs["L0_attn_Wk"]))
    d["wv0"] = _tile_lhsT(np.asarray(inputs["L0_attn_Wv"]))
    bq = np.asarray(inputs["L0_attn_bq"]).reshape(NH, 128).T
    bk = np.asarray(inputs["L0_attn_bk"]).reshape(KVH0, 128).T
    bv = np.asarray(inputs["L0_attn_bv"]).reshape(KVH0, 128).T
    d["bqkv0"] = np.ascontiguousarray(np.concatenate([bq, bk, bv], axis=1).astype(np.float32))
    return d


def _x_shards(x):
    out = []
    for c in range(NCORE):
        s = c // 2
        out.append(np.ascontiguousarray(x[s, _core_pos(c), :].T))
    return out


def run_p1(inputs):
    sh = _prep_shared(inputs)
    xs = _x_shards(np.asarray(inputs["L0_attn_x"]))
    p = build_p1()
    in_maps = []
    for c in range(NCORE):
        h = c % 2
        in_maps.append({
            "x": xs[c],
            "wq": sh["wq0"], "wk": sh["wk0"], "wv": sh["wv0"],
            "bqkv": sh["bqkv0"],
            "ctab": np.ascontiguousarray(sh["rope_cos"][:, _core_pos(c)]),
            "stab": np.ascontiguousarray(sh["rope_sin"][:, _core_pos(c)]),
            "c_pt": sh["c_pt"], "c_id": sh["c_id"], "c_idf": sh["c_idf"],
        })
    res = run_bass_kernel_spmd(p.nc, in_maps, core_ids=list(range(NCORE)))
    p.close()
    return res.results


PAIRS = [[0, 1], [2, 3], [4, 5], [6, 7]]
ALL8 = [list(range(NCORE))]


def emit_collective(p, ins_ap, outs_ap, groups, wait_sems, tag):
    s_c = p.sem(kind="col")
    wv = _wpairs(wait_sems)

    @p.block.gpsimd
    def _(g):
        for h, n in wv:
            g.wait_ge(h, n)
        s_c.inc(g.collective_compute("AllGather", ALU.bypass, replica_groups=groups,
                                     ins=[ins_ap], outs=[outs_ap]))
    return s_c


def build_fused(upto=8, nhd26=8):
    global UPTO26_NHD
    UPTO26_NHD = nhd26
    p = Prog().open()
    x = p.din("x", [NCH, 128, T], F32)
    wq0 = p.din("wq0", [NH, 128, NCH, 128], F32)
    wk0 = p.din("wk0", [KVH0, 128, NCH, 128], F32)
    wv0 = p.din("wv0", [KVH0, 128, NCH, 128], F32)
    bqkv = p.din("bqkv", [128, NH + 2 * KVH0], F32)
    ctab = p.din("ctab", [128, T], F32)
    stab = p.din("stab", [128, T], F32)
    cpt = p.din("c_pt", [128, 128], BF16)
    wo0 = p.din("wo0", [NCH, 128, NCH, 128], F32)
    bo0 = p.din("bo0", [128, NCH], F32)
    wg = p.din("wg", [128, NCH * 16], F32)
    wqkv = p.din("wqkv", [48, 128, NCH, 128], F32)
    wq1 = p.din("wq1", [NH, 128, NCH, 128], F32)
    wk1 = p.din("wk1", [NH, 128, NCH, 128], F32)
    wv1 = p.din("wv1", [NH, 128, NCH, 128], F32)
    gwt = p.din("gwt", [128, 128], F32)
    gbt = p.din("gbt", [128, 1], F32)
    ow1 = p.din("ow1", [NCH, 128, NCH, 128], F32)
    bz = p.din("bz", [128, NCH], F32)
    gup = p.din("gup", [11, 128, 256], I32)
    gus = p.din("gus", [11, 128, 16], F16)
    dnp = p.din("dnp", [2, 128, 704], I32)
    dns = p.din("dns", [2, 128, 44], F16)
    ptk = p.din("ptk", [128, (S // 512) * 4], F32)
    qpos = p.din("qpos", [1, T], F32)
    xout = p.dout("xout", [NCH, 128, T], F32)

    q0 = p.dint("q0", [NH, 128, T], BF16)
    k0c = p.dint("k0c", [1, KVH0 * 128 * T], BF16)
    v0c = p.dint("v0c", [1, KVH0 * T * 128], BF16)
    k0g = p.dint("k0g", [2, KVH0 * 128 * T], BF16)
    v0g = p.dint("v0g", [2, KVH0 * T * 128], BF16)
    att0 = p.dint("att0", [NH, 128, T], BF16)
    x1 = p.dint("x1", [NCH, 128, T], F32)
    ind_dram = p.dint("ind_d", [16, T], F32)
    qf0 = p.dint("qf0", [NH, 128, T], BF16)
    kshc = p.dint("kshc", [1, 128 * T], BF16)
    vshc = p.dint("vshc", [1, T * 128], BF16)
    kshg = p.dint("kshg", [2, 128 * T], BF16)
    vshg = p.dint("vshg", [2, T * 128], BF16)
    attm = p.dint("attm", [NH, 128, T], BF16)
    x2 = p.dint("x2", [NCH, 128, T], F32)
    q1 = p.dint("q1", [NH, 128, T], BF16)
    k1c = p.dint("k1c", [1, NH * 128 * T], BF16)
    v1c = p.dint("v1c", [1, NH * T * 128], BF16)
    k1g = p.dint("k1g", [2, NH * 128 * T], BF16)
    v1g = p.dint("v1g", [2, NH * T * 128], BF16)
    att1 = p.dint("att1", [NH, 128, T], BF16)
    wgu_mine = p.dint("wgu_mine", [11, 128 * 16 * 128], BF16)
    down_mine = p.dint("down_mine", [2, 128 * 44 * 128], BF16)
    wgu_all = p.dint("wgu_all", [88, 128 * 16 * 128], BF16)
    down_all = p.dint("down_all", [16, 128 * 44 * 128], BF16)
    h_dram = p.dint("h_d", [44, 128, T], BF16)

    scale = HD ** -0.5
    consts, s_const = load_consts(p, reg_vals=[scale])
    ensure_proj_scratch(p)
    ptk_sb = p.sb("ptk_sb", [128, (S // 512) * 4], F32)
    qpos_sb = p.sb("qpos_sb", [1, T], F32)
    bias_sb = p.sb("bias_all", [128, NH + 2 * KVH0], F32)
    bo_sb = p.sb("bo_sb", [128, NCH], F32)
    bz_sb = p.sb("bz_sb", [128, NCH], F32)
    wg_sb = p.sb("wg_sb", [128, NCH * 16], BF16)
    gw_sb = p.sb("gw_sb", [128, 128], BF16)
    gb_sb = p.sb("gb_sb", [128, 1], F32)
    s_mh = p.sem(kind="hw")
    s_msw = p.sem(kind="sw")
    s_misc = SemGroup([s_mh, s_msw])

    @p.block.sync
    def _(sync):
        s_mh.inc(sync.dma_start(out=ptk_sb[:, :], in_=ptk[:, :]), 16)
        s_mh.inc(sync.dma_start(out=qpos_sb[:, :], in_=qpos[:, :]), 16)
        s_mh.inc(sync.dma_start(out=bias_sb[:, :], in_=bqkv[:, :]), 16)
        s_mh.inc(sync.dma_start(out=bo_sb[:, :], in_=bo0[:, :]), 16)
        s_mh.inc(sync.dma_start(out=bz_sb[:, :], in_=bz[:, :]), 16)

    @p.block.gpsimd
    def _(g):
        s_msw.inc(g.dma_start(out=wg_sb[:, :], in_=wg[:, :]), 16)
        s_msw.inc(g.dma_start(out=gw_sb[:, :], in_=gwt[:, :]), 16)
        s_msw.inc(g.dma_start(out=gb_sb[:, :], in_=gbt[:, :]), 16)

    k0c_v = k0c.rearrange("o (v p t) -> (o v) p t", v=KVH0, p=128)
    v0c_v = v0c.rearrange("o (v t d) -> (o v) t d", v=KVH0, t=T)
    with p.scope():
        cos_sb = p.sb("cos", [128, T], F32)
        sin_sb = p.sb("sin", [128, T], F32)
        pt_sb = p.sb("pt", [128, 128], BF16)
        s_m1 = p.sem(kind="hw")

        @p.block.sync
        def _(sync):
            sync.dma_start(out=cos_sb[:, :], in_=ctab[:, :]).then_inc(s_m1.h, 16)
            sync.dma_start(out=sin_sb[:, :], in_=stab[:, :]).then_inc(s_m1.h, 16)
            sync.dma_start(out=pt_sb[:, :], in_=cpt[:, :]).then_inc(s_m1.h, 16)
        s_m1.n = 48

        xln = p.sb("xln1", [128, NCH * T], BF16)
        s_x = emit_load_xbf(p, x.rearrange("c p t -> (c p) t"), xln)
        s_ln = emit_ln(p, xln, xln, consts, [s_x, s_const], "f1")
        rope = (cos_sb, sin_sb, pt_sb)
        sq0 = emit_proj(p, xln, wq0, NH, consts, [s_ln, s_m1, s_misc], "q0",
                        bias_sb=bias_sb, bias_off=0, mode="rope", rope=rope, out_dram=q0)
        sk0 = emit_proj(p, xln, wk0, KVH0, consts, [s_ln, sq0], "k0",
                        bias_sb=bias_sb, bias_off=NH, mode="rope", rope=rope, out_dram=k0c_v)
        sv0 = emit_proj(p, xln, wv0, KVH0, consts, [s_ln, sk0], "v0",
                        bias_sb=bias_sb, bias_off=NH + KVH0, mode="vtm", out_dram=v0c_v)

    if upto <= 1:
        emit_final_wait(p, [sq0, sk0, sv0])
        return p
    with p.scope():
        sd1 = emit_dequant(p, consts, gup, gus, wgu_mine.rearrange("m (p x) -> m p x", p=128),
                           11, 256, "gu", waits=[s_const, sv0], bank=0)
        sd2 = emit_dequant(p, consts, dnp, dns, down_mine.rearrange("m (p x) -> m p x", p=128),
                           2, 704, "dn", waits=[s_const, sv0], bank=1)
    s_agw = emit_collective(p, wgu_mine[:, :], wgu_all[:, :], ALL8, [sd1], "agw")
    s_agd = emit_collective(p, down_mine[:, :], down_all[:, :], ALL8, [sd2], "agd")

    if upto <= 2:
        emit_final_wait(p, [s_agw, s_agd])
        return p
    s_c1k = emit_collective(p, k0c[:, :], k0g[:, :], PAIRS, [sk0], "c1k")
    s_c1v = emit_collective(p, v0c[:, :], v0g[:, :], PAIRS, [sv0], "c1v")
    if upto == 25:
        emit_final_wait(p, [s_c1k, s_c1v, s_agw, s_agd])
        return p

    if upto == 26:
        with p.scope():
            s26 = emit_attn_tok(p, consts, q0,
                                k0g.rearrange("s (v p t) -> s v p t", v=KVH0, p=128),
                                v0g.rearrange("s (v t d) -> s v t d", v=KVH0, t=T),
                                att0, UPTO26_NHD, lambda h: h // 4, scale,
                                ptk_sb, qpos_sb, [s_c1k, s_c1v, sq0, s_misc, sd1, sd2], "a0")
        emit_final_wait(p, [s26])
        return p
    with p.scope():
        satt0 = emit_attn_tok(p, consts, q0,
                              k0g.rearrange("s (v p t) -> s v p t", v=KVH0, p=128),
                              v0g.rearrange("s (v t d) -> s v t d", v=KVH0, t=T),
                              att0, NH, lambda h: h // 4, scale,
                              ptk_sb, qpos_sb, [s_c1k, s_c1v, sq0, s_misc, sd1, sd2], "a0")

    if upto <= 3:
        emit_final_wait(p, [satt0])
        return p
    kshc_v = kshc.rearrange("o (p t) -> (o p) t", p=128)
    vshc_v = vshc.rearrange("o (t d) -> (o t) d", t=T)
    with p.scope():
        xln3 = p.sb("xln3", [128, NCH * T], BF16)
        with p.scope():
            s_x1e, s_st1 = emit_resadd_ln(p, x, att0, x1, xln3, consts,
                                          [satt0, s_misc, s_const], "f3",
                                          wo_dram=wo0, bias_sb=bo_sb, bias_off=0)
            s_ln3 = emit_ln(p, xln3, xln3, consts, [s_x1e, s_st1], "f3")
        ind_fm, top1row, s_ind, s_gre = emit_l0f_gating(p, xln3, wg_sb, consts,
                                                        [s_ln3, s_misc], "f3")
        s_indf = p.sem(kind="hw")

        @p.block.sync
        def _(sync):
            s_ind.wge(sync, s_ind.n)
            s_indf.inc(sync.dma_start(out=ind_dram[:, :], in_=ind_fm[0:16, :]), 16)

        ksh_acc = p.sb("ksh_acc", [128, T], BF16)
        vsh_acc = p.sb("vsh_acc", [128, T], BF16)
        vtm_sb = p.sb("vtm_sb", [128, TT], BF16)
        mask_all, s_mask = emit_mask_all(p, consts, ind_dram, s_indf, "f3")
        sqf = emit_proj(p, xln3, wqkv[0:16], 16, consts, [s_ln3, s_ind], "qf",
                        mode="plain", out_dram=qf0)
        s_kacc = emit_proj(p, xln3, wqkv[16:32], 16, consts, [s_ln3, sqf, s_mask], "kf",
                           mode="sel", sel=(mask_all, ksh_acc))
        s_vacc = emit_proj(p, xln3, wqkv[32:48], 16, consts, [s_ln3, s_kacc], "vf",
                           mode="sel", sel=(mask_all, vsh_acc))
        s_kst = p.sem(kind="sw")

        @p.block.gpsimd
        def _(g):
            s_kacc.wge(g, s_kacc.n)
            s_kst.inc(g.dma_start(out=kshc_v[:, :], in_=ksh_acc[:, :]), 16)

        t1_ps = p.pf[3]
        vt_ps = p.pb[0]
        s_t1b, s_t1m, s_vt, s_vte = (p.sem() for _ in range(4))
        s_vst = p.sem(kind="hw")
        orow_f = consts["ones_row_f"]
        identb = consts["ident"]

        @p.block.tensor
        def _(tensor):
            s_vacc.wge(tensor, s_vacc.n)
            s_gre.wge(tensor, s_gre.n)
            for tt in range(NTT):
                if tt > 0:
                    s_t1m.wge(tensor, tt)
                s_t1b.inc(tensor.matmul(t1_ps[:, :], orow_f[0:1, :],
                                        top1row[0:1, tt * TT:(tt + 1) * TT], start=True, stop=True))
                s_t1m.wge(tensor, tt + 1)
                if tt > 0:
                    s_vte.wge(tensor, 4 * tt)
                for sg in range(4):
                    i = tensor.transpose(vt_ps[:, sg * 128:(sg + 1) * 128],
                                         vsh_acc[:, tt * TT + sg * 128:tt * TT + (sg + 1) * 128],
                                         identb[:, :])
                s_vt.inc(i)

        @p.block.vector
        def _(vector):
            for tt in range(NTT):
                s_t1b.wge(vector, tt + 1)
                s_t1m.inc(vector.tensor_tensor(vsh_acc[:, tt * TT:(tt + 1) * TT],
                                               vsh_acc[:, tt * TT:(tt + 1) * TT],
                                               t1_ps[:, :], ALU.mult))
                s_vt.wge(vector, tt + 1)
                if tt > 0:
                    s_vst.wge(vector, 16 * tt)
                for sg in range(4):
                    i = vector.tensor_copy(vtm_sb[:, sg * 128:(sg + 1) * 128],
                                           vt_ps[:, sg * 128:(sg + 1) * 128])
                    s_vte.inc(i)

        @p.block.sync
        def _(g):
            for tt in range(NTT):
                s_vte.wge(g, 4 * (tt + 1))
                s_vst.inc(g.dma_start(
                    out=vshc_v[tt * TT:(tt + 1) * TT, :].rearrange("(sg pp) d -> pp sg d", pp=128),
                    in_=vtm_sb[:, :].rearrange("p (sg d) -> p sg d", sg=4)), 16)

    if upto <= 4:
        emit_final_wait(p, [sqf, s_kst, s_vst, s_st1])
        return p
    s_c2k = emit_collective(p, kshc[:, :], kshg[:, :], PAIRS, [s_kst], "c2k")
    s_c2v = emit_collective(p, vshc[:, :], vshg[:, :], PAIRS, [s_vst], "c2v")

    with p.scope():
        sattm = emit_attn_tok(p, consts, qf0,
                              kshg.rearrange("s (v p t) -> s v p t", v=1, p=128),
                              vshg.rearrange("s (v t d) -> s v t d", v=1, t=T),
                              attm, NH, lambda h: 0, scale,
                              ptk_sb, qpos_sb, [s_c2k, s_c2v, sqf, s_misc], "am")

    if upto <= 5:
        emit_final_wait(p, [sattm])
        return p
    k1c_v = k1c.rearrange("o (v p t) -> (o v) p t", v=NH, p=128)
    v1c_v = v1c.rearrange("o (v t d) -> (o v) t d", v=NH, t=T)
    with p.scope():
        xln5 = p.sb("xln5", [128, NCH * T], BF16)
        with p.scope():
            s_x2e, s_st2 = emit_resadd_ln(p, x1, attm, x2, xln5, consts,
                                          [sattm, s_st1], "f5")
            s_ln5 = emit_ln(p, xln5, xln5, consts, [s_x2e, s_st2], "f5")
        sq1 = emit_proj(p, xln5, wq1, NH, consts, [s_ln5], "q1", mode="plain", out_dram=q1)
        sk1 = emit_proj(p, xln5, wk1, NH, consts, [s_ln5, sq1], "k1", mode="plain", out_dram=k1c_v)
        sv1 = emit_proj(p, xln5, wv1, NH, consts, [s_ln5, sk1], "v1", mode="vtm", out_dram=v1c_v)

    if upto <= 6:
        emit_final_wait(p, [sq1, sk1, sv1, s_st2])
        return p
    s_c3k = emit_collective(p, k1c[:, :], k1g[:, :], PAIRS, [sk1], "c3k")
    s_c3v = emit_collective(p, v1c[:, :], v1g[:, :], PAIRS, [sv1], "c3v")

    with p.scope():
        satt1 = emit_attn_tok(p, consts, q1,
                              k1g.rearrange("s (v p t) -> s v p t", v=NH, p=128),
                              v1g.rearrange("s (v t d) -> s v t d", v=NH, t=T),
                              att1, NH, lambda h: h, scale,
                              ptk_sb, qpos_sb, [s_c3k, s_c3v, sq1, s_misc], "a1",
                              gate=(gw_sb, gb_sb))

    if upto <= 7:
        emit_final_wait(p, [satt1])
        return p
    with p.scope():
        xln7 = p.sb("xln7", [128, NCH * T], BF16)
        with p.scope():
            s_x3e, s_st3 = emit_resadd_ln(p, x2, att1, xout, xln7, consts,
                                          [satt1, s_st2, s_misc], "f7",
                                          wo_dram=ow1, bias_sb=bz_sb, bias_off=0)
            s_ln7 = emit_ln(p, xln7, xln7, consts, [s_x3e, s_st3], "f7")
        s_fst = emit_l1ffn(p, consts, xln7,
                           wgu_all.rearrange("m (p x) -> m p x", p=128),
                           down_all.rearrange("m (p x) -> m p x", p=128),
                           h_dram, xout, s_st3, s_ln7, [s_agw, s_agd])
    emit_final_wait(p, [s_fst])
    return p


def run_fused(inputs):
    sh = _prep_shared(inputs)
    xs = _x_shards(np.asarray(inputs["L0_attn_x"]))
    p = build_fused()
    wo0_t = _tile_lhsT(np.asarray(inputs["L0_attn_Wo"]))
    bo0 = np.ascontiguousarray(np.asarray(inputs["L0_attn_bo"]).reshape(NCH, 128).T.astype(np.float32))
    Wg = np.asarray(inputs["L0_ffn_W_gate"])
    wg_host = np.ascontiguousarray(Wg.reshape(16, 16, 128).transpose(2, 1, 0).reshape(128, 256).astype(np.float32))
    wqkv_t = _tile_lhsT(np.asarray(inputs["L0_ffn_W_qkv"]))
    wq1_t = _tile_lhsT(np.asarray(inputs["L1_attn_q_w"]))
    wk1_t = _tile_lhsT(np.asarray(inputs["L1_attn_k_w"]))
    wv1_t = _tile_lhsT(np.asarray(inputs["L1_attn_v_w"]))
    ow1_t = _tile_lhsT(np.asarray(inputs["L1_attn_o_w"]))
    gwt = np.ascontiguousarray(np.asarray(inputs["L1_attn_gate_w"]).T.astype(np.float32))
    gbt = np.ascontiguousarray(np.asarray(inputs["L1_attn_gate_b"]).reshape(128, 1).astype(np.float32))
    bz = np.zeros((128, NCH), np.float32)
    gup = np.asarray(inputs["L1_ffn_gate_up_packed"])
    gus = np.asarray(inputs["L1_ffn_gate_up_scales"])
    dnp = np.asarray(inputs["L1_ffn_down_packed"])
    dns = np.asarray(inputs["L1_ffn_down_scales"])
    in_maps = []
    for c in range(NCORE):
        pos = _core_pos(c)
        r0, r1 = c * 1408, (c + 1) * 1408
        d0, d1 = c * 256, (c + 1) * 256
        ptkh, qposh = _attn_host_consts(c)
        in_maps.append({
            "x": np.ascontiguousarray(xs[c].reshape(NCH, 128, T)),
            "wq0": sh["wq0"], "wk0": sh["wk0"], "wv0": sh["wv0"], "bqkv": sh["bqkv0"],
            "ctab": np.ascontiguousarray(sh["rope_cos"][:, pos]),
            "stab": np.ascontiguousarray(sh["rope_sin"][:, pos]),
            "c_pt": sh["c_pt"], "c_id": sh["c_id"], "c_idf": sh["c_idf"],
            "wo0": wo0_t, "bo0": bo0, "wg": wg_host, "wqkv": wqkv_t,
            "wq1": wq1_t, "wk1": wk1_t, "wv1": wv1_t, "gwt": gwt, "gbt": gbt,
            "ow1": ow1_t, "bz": bz,
            "gup": np.ascontiguousarray(gup[r0:r1].reshape(11, 128, 256)),
            "gus": np.ascontiguousarray(gus.T[r0:r1].reshape(11, 128, 16)),
            "dnp": np.ascontiguousarray(dnp[d0:d1].reshape(2, 128, 704)),
            "dns": np.ascontiguousarray(dns.T[d0:d1].reshape(2, 128, 44)),
            "ptk": ptkh, "qpos": qposh,
        })
    res = run_bass_kernel_spmd(p.nc, in_maps, core_ids=list(range(NCORE)))
    p.close()
    return res.results


USE_FUSED = True


def kernel(**inputs):
    inputs = {kk: np.asarray(v) for kk, v in inputs.items()}
    out = np.zeros((B, S, D), np.float32)
    if USE_FUSED:
        rf = run_fused(inputs)
        for c in range(NCORE):
            out[c // 2, _core_pos(c), :] = rf[c]["xout"].reshape(D, T).T
        return out
    xs = _x_shards(inputs["L0_attn_x"])
    p1 = run_p1(inputs)
    p2 = run_p2(p1, _diag_masks())
    p3 = run_p3(inputs, p2, xs)
    p4 = run_p4(p3)
    p5 = run_p5(inputs, p4, p3)
    p6 = run_p6(inputs, p5)
    p7a = run_p7a(inputs)
    p7b = run_p7b(inputs, p6, p5, p7a)
    for c in range(NCORE):
        out[c // 2, _core_pos(c), :] = p7b[c]["xout"].reshape(D, T).T
    return out


# version 08 (final)
# speedup vs baseline: 1.2440x; 1.0237x over previous
import contextlib
import numpy as np
import ml_dtypes
import concourse.bass as bass
import concourse.mybir as mybir
from concourse.bass_utils import run_bass_kernel_spmd

F32 = mybir.dt.float32
BF16 = mybir.dt.bfloat16
U8 = mybir.dt.uint8
I32 = mybir.dt.int32
F16 = mybir.dt.float16
AF = mybir.ActivationFunctionType
ALU = mybir.AluOpType
BF = ml_dtypes.bfloat16

B, S, D = 4, 4096, 2048
NCORE = 8
T = 2048
NCH = D // 128
NH, HD = 16, 128
KVH0 = 4
TT = 512
NTT = T // TT
CHUNKS_OF = {0: [0, 3, 4, 7], 1: [1, 2, 5, 6]}
EXT = [2, 4, 6, 8]
_SLOT = {}
for _h in (0, 1):
    for _i, _g in enumerate(CHUNKS_OF[_h]):
        _SLOT[_g] = _h * 4 + _i


def _core_pos(c):
    return np.concatenate([np.arange(g * TT, (g + 1) * TT) for g in CHUNKS_OF[c % 2]])
ROPE_BASE = 10000.0
NEG = -1e30
EPS_LN = 1e-5


def _tile_lhsT(w):
    o, i = w.shape
    mo, k = o // 128, i // 128
    t = w.reshape(mo, 128, k, 128)
    return np.ascontiguousarray(t.transpose(0, 3, 2, 1)).astype(np.float32)


def _rope_tables():
    inv = 1.0 / ROPE_BASE ** (np.arange(0, HD, 2, dtype=np.float64) / HD)
    pos = np.arange(S, dtype=np.float64)
    fr = pos[:, None] * inv[None, :]
    cos = np.repeat(np.cos(fr), 2, axis=1).T
    sin = np.repeat(np.sin(fr), 2, axis=1).T
    return (np.ascontiguousarray(cos.astype(np.float32)),
            np.ascontiguousarray(sin.astype(np.float32)))


def _rope_P():
    P = np.zeros((HD, HD), np.float32)
    for i in range(HD // 2):
        P[2 * i, 2 * i + 1] = -1.0
        P[2 * i + 1, 2 * i] = 1.0
    return np.ascontiguousarray(P.T.astype(BF))


def _diag_masks():
    m = np.zeros((4, 128, 512), np.float32)
    for j in range(4):
        for r in range(128):
            m[j, r, j * 128 + r + 1:] = NEG
    return np.ascontiguousarray(m)


class Sem:
    def __init__(self, handle, base=0):
        self.h = handle
        self.base = base
        self.n = 0

    def inc(self, instr, k=1):
        instr.then_inc(self.h, k)
        self.n += k
        return instr

    def wge(self, eng, v):
        return eng.wait_ge(self.h, self.base + v)

    def pairs(self):
        return [(self.h, self.base + self.n)]


class SemGroup:
    def __init__(self, sems):
        self.sems = sems

    def pairs(self):
        return [pp for s in self.sems for pp in s.pairs()]


def _wpairs(waits):
    return [pp for w in waits for pp in w.pairs()]


class Prog:
    def __init__(self, name="mk"):
        self.nc = bass.Bass("TRN2", target_bir_lowering=False, debug=False)
        self.stack = contextlib.ExitStack()
        self.sb_stacks = [self.stack]
        self._n = 0
        self.block = None

    def sem(self, name=None, kind="eng"):
        if not hasattr(self, "_sem_free"):
            self._sem_free = {"eng": [], "sw": [], "hw": [], "col": []}
            self._sem_scopes = [[]]
        pool = self._sem_free[kind]
        if pool:
            h, total = pool.pop()
            s = Sem(h, base=total)
        else:
            name = name or f"s{self._n}"
            self._n += 1
            s = Sem(self.stack.enter_context(self.nc.semaphore(name)))
        s.kind = kind
        self._sem_scopes[-1].append(s)
        return s

    def sb(self, name, shape, dtype):
        return self.sb_stacks[-1].enter_context(self.nc.sbuf_tensor(name, shape, dtype))

    @contextlib.contextmanager
    def scope(self):
        st = contextlib.ExitStack()
        self.sb_stacks.append(st)
        if not hasattr(self, "_sem_free"):
            self._sem_free = {"eng": [], "sw": [], "hw": [], "col": []}
            self._sem_scopes = [[]]
        self._sem_scopes.append([])
        try:
            yield
        finally:
            self.sb_stacks.pop()
            st.close()
            freed = self._sem_scopes.pop()
            drain = [(s.h, s.base + s.n) for s in freed
                     if getattr(s, "kind", "eng") in ("sw", "hw", "col") and s.n > 0]
            if drain:
                @self.block.gpsimd
                def _(g):
                    for h, n in drain:
                        g.wait_ge(h, n)

                @self.block.sync
                def _(g):
                    for h, n in drain:
                        g.wait_ge(h, n)
            for s in freed:
                self._sem_free[getattr(s, "kind", "eng")].append((s.h, s.base + s.n))

    def ps(self, name, shape, dtype=F32):
        return self.stack.enter_context(self.nc.psum_tensor(name, shape, dtype))

    def din(self, name, shape, dtype):
        return self.nc.dram_tensor(name, shape, dtype, kind="ExternalInput").ap()

    def dout(self, name, shape, dtype):
        return self.nc.dram_tensor(name, shape, dtype, kind="ExternalOutput").ap()

    def dint(self, name, shape, dtype):
        return self.nc.dram_tensor(name, shape, dtype, kind="Internal").ap()

    def open(self):
        self.pf = [self.ps(f"pf{i}", [128, 512], F32) for i in range(6)]
        self.pb = [self.ps(f"pb{i}", [128, 1024], BF16) for i in range(2)]
        self.block = self.stack.enter_context(self.nc.Block())
        return self

    def close(self):
        self.stack.close()


def load_consts(p, reg_vals=()):
    cid = p.din("c_id", [128, 128], BF16)
    cidf = p.din("c_idf", [128, 128], F32)
    ident = p.sb("ident", [128, 128], BF16)
    identf = p.sb("identf", [128, 128], F32)
    ones_col = p.sb("ones_col", [128, 1], BF16)
    ones_row = p.sb("ones_row", [1, 128], BF16)
    ones_row_f = p.sb("ones_row_f", [1, 128], F32)
    ctiles = []
    for i, v in enumerate(reg_vals):
        t = p.sb(f"creg{i}", [128, 1], F32)
        p.nc.const_aps.register(F32, float(v), t[:, 0:1])
        ctiles.append((t, float(v)))
    s_hw = p.sem("s_const", kind="hw")
    s_en = p.sem()

    @p.block.sync
    def _(sync):
        s_hw.inc(sync.dma_start(out=ident[:, :], in_=cid[:, :]), 16)
        s_hw.inc(sync.dma_start(out=identf[:, :], in_=cidf[:, :]), 16)

    @p.block.vector
    def _(vector):
        vector.memset(ones_col[:, :], 1.0)
        vector.memset(ones_row[:, :], 1.0)
        for t, v in ctiles:
            vector.memset(t[:, :], v)
        s_en.inc(vector.memset(ones_row_f[:, :], 1.0))
    return {"ident": ident, "identf": identf, "ones_col": ones_col,
            "ones_row": ones_row, "ones_row_f": ones_row_f}, SemGroup([s_hw, s_en])


def emit_load_xbf(p, xsrc, xbf):
    s_x = p.sem(kind="sw")

    @p.block.gpsimd
    def _(g):
        s_x.inc(g.dma_start(out=xbf[:, :].rearrange("p (c t) -> p c t", c=NCH),
                            in_=xsrc.rearrange("(c p) t -> p c t", p=128)), 16)
    return s_x


def ensure_proj_scratch(p):
    if not hasattr(p, "proj_scratch"):
        p.proj_scratch = {
            "wbuf": [p.sb(f"pj_w{i}", [128, NCH * 128], BF16) for i in range(2)],
            "ev": [p.sb(f"pj_ev{i}", [128, TT], BF16) for i in range(2)],
            "qf": [p.sb(f"pj_qf{i}", [128, TT], BF16) for i in range(2)],
            "rt1": p.sb("pj_rt1", [128, TT], F32),
            "rt2": p.sb("pj_rt2", [128, TT], F32),
        }
    return p.proj_scratch


def emit_ln(p, xbf, xln, consts, waits, tag, ntok=T):
    ntt = ntok // TT
    oc = consts["ones_col"]
    orow = consts["ones_row"]
    xsq = p.sb(f"xsq_{tag}", [128, NCH * TT], BF16)
    srow = p.sb(f"srow_{tag}", [1, 6 * TT], F32)
    mrow = p.sb(f"mrow_{tag}", [1, 2 * TT], BF16)
    tmp = p.sb(f"lntmp_{tag}", [128, 4 * TT], BF16)
    st_sum, st_sq = p.pf[4], p.pf[5]
    bc0, bc1 = p.pf[2], p.pf[3]
    s_sq, s_stat, s_row, s_sd, s_rr, s_bc, s_va = (p.sem() for _ in range(7))
    s_ln = p.sem()
    wvals = _wpairs(waits)

    def xs(c, t0):
        return slice(c * ntok + t0, c * ntok + t0 + TT)

    @p.block.scalar
    def _(scalar):
        for tt in range(ntt):
            t0 = tt * TT
            if tt == 0:
                for h, n in wvals:
                    scalar.wait_ge(h, n)
            for c in range(NCH - 1):
                scalar.activation(xsq[:, c * TT:(c + 1) * TT], xbf[:, xs(c, t0)], AF.Square)
            s_sq.inc(scalar.activation(xsq[:, (NCH - 1) * TT:], xbf[:, xs(NCH - 1, t0)], AF.Square))
            s_row.wge(scalar, tt + 1)
            s_sd.inc(scalar.activation(srow[:, 4 * TT:5 * TT], srow[:, 3 * TT:4 * TT], AF.Sqrt))

    @p.block.tensor
    def _(tensor):
        for tt in range(ntt):
            t0 = tt * TT
            if tt == 0:
                for h, n in wvals:
                    tensor.wait_ge(h, n)
            s_sq.wge(tensor, tt + 1)
            for c in range(NCH):
                tensor.matmul(st_sum[0:1, :], oc[:, :], xbf[:, xs(c, t0)],
                              start=(c == 0), stop=(c == NCH - 1))
            for c in range(NCH):
                i = tensor.matmul(st_sq[0:1, :], oc[:, :], xsq[:, c * TT:(c + 1) * TT],
                                  start=(c == 0), stop=(c == NCH - 1))
            s_stat.inc(i)
            s_rr.wge(tensor, tt + 1)
            if tt > 0:
                s_ln.wge(tensor, tt)
            tensor.matmul(bc0[:, :], orow[:, :], mrow[0:1, 0:TT], start=True, stop=True)
            s_bc.inc(tensor.matmul(bc1[:, :], orow[:, :], mrow[0:1, TT:2 * TT], start=True, stop=True))

    @p.block.vector
    def _(vector):
        for tt in range(ntt):
            t0 = tt * TT
            s_stat.wge(vector, tt + 1)
            vector.tensor_scalar(srow[:, 0:TT], st_sum[0:1, :], 1.0 / D, None, ALU.mult)
            vector.tensor_scalar(mrow[:, 0:TT], st_sum[0:1, :], 1.0 / D, None, ALU.mult)
            s_va.inc(vector.tensor_scalar(srow[:, TT:2 * TT], st_sq[0:1, :], 1.0 / D, None, ALU.mult))
            s_va.wge(vector, s_va.n)
            s_va.inc(vector.tensor_tensor(srow[:, 2 * TT:3 * TT], srow[:, 0:TT], srow[:, 0:TT], ALU.mult))
            s_va.wge(vector, s_va.n)
            s_row.inc(vector.scalar_tensor_tensor(srow[:, 3 * TT:4 * TT], srow[:, TT:2 * TT],
                                                  EPS_LN, srow[:, 2 * TT:3 * TT],
                                                  ALU.add, ALU.subtract))
            s_sd.wge(vector, tt + 1)
            s_va.inc(vector.reciprocal(srow[:, 5 * TT:6 * TT], srow[:, 4 * TT:5 * TT]))
            s_va.wge(vector, s_va.n)
            s_rr.inc(vector.tensor_copy(mrow[:, TT:2 * TT], srow[:, 5 * TT:6 * TT]))
            s_bc.wge(vector, tt + 1)
            for g in range(NCH // 4):
                if g > 0:
                    s_va.wge(vector, s_va.n)
                elif tt > 0:
                    s_ln.wge(vector, tt)
                for c4 in range(4):
                    c = g * 4 + c4
                    i = vector.tensor_tensor(tmp[:, c4 * TT:(c4 + 1) * TT], xbf[:, xs(c, t0)],
                                             bc0[:, :], ALU.subtract)
                s_va.inc(i)
                s_va.wge(vector, s_va.n)
                for c4 in range(4):
                    c = g * 4 + c4
                    i = vector.tensor_tensor(xln[:, xs(c, t0)], tmp[:, c4 * TT:(c4 + 1) * TT],
                                             bc1[:, :], ALU.mult)
                if g < NCH // 4 - 1:
                    s_va.inc(i)
            s_ln.inc(i)
    return s_ln


def emit_proj(p, xln, wdram, nmo, consts, waits, tag,
              bias_sb=None, bias_off=0, mode="plain", rope=None, out_dram=None,
              ntok=T, ev_consumer=None, sel=None):
    ntt = ntok // TT
    sc = ensure_proj_scratch(p)
    wbuf, ev = sc["wbuf"], sc["ev"]
    pp = [p.pf[0], p.pf[1]]
    s_wld = [p.sem(kind="sw"), p.sem(kind="sw")]
    s_done2 = [p.sem(kind="hw"), p.sem(kind="hw")]
    s_mm, s_ev, s_rr2 = p.sem(), p.sem(), p.sem()
    wvals = _wpairs(waits)

    if mode in ("rope", "vtm"):
        qf = sc["qf"]
        rt1, rt2 = sc["rt1"], sc["rt2"]
        rot_ps = p.pf[2]
        tp_ps = p.pb[0]
        s_rot, s_qf = p.sem(), p.sem()
    s_free = p.sem()

    def xs(c, t0):
        return slice(c * ntok + t0, c * ntok + t0 + TT)

    @p.block.gpsimd
    def _(g):
        for m in range(nmo):
            if m == 0:
                for h, n in wvals:
                    g.wait_ge(h, n)
            if m >= 2:
                s_mm.wge(g, (m - 1) * ntt)
            s_wld[m % 2].inc(g.dma_start(out=wbuf[m % 2][:, :],
                                         in_=wdram[m, :, :, :].rearrange("p k mm -> p (k mm)")), 16)

    @p.block.tensor
    def _(tensor):
        for m in range(nmo):
            s_wld[m % 2].wge(tensor, 16 * (m // 2 + 1))
            if m == 0:
                for h, n in wvals:
                    tensor.wait_ge(h, n)
            for tt in range(ntt):
                t0 = tt * TT
                it = m * ntt + tt
                if it >= 2:
                    s_ev.wge(tensor, it - 1)
                for c in range(NCH):
                    i = tensor.matmul(pp[it % 2][:, :], wbuf[m % 2][:, c * 128:(c + 1) * 128],
                                      xln[:, xs(c, t0)], start=(c == 0), stop=(c == NCH - 1))
                s_mm.inc(i)
                if mode == "rope":
                    s_ev.wge(tensor, it + 1)
                    if it >= 1:
                        s_qf.wge(tensor, it)
                    s_rot.inc(tensor.matmul(rot_ps[:, :], rope[2][:, :], ev[it % 2][:, :],
                                            start=True, stop=True))
                elif mode == "vtm":
                    s_ev.wge(tensor, it + 1)
                    if it >= 1:
                        s_qf.wge(tensor, it)
                    for sg in range(4):
                        i = tensor.transpose(tp_ps[:, sg * 128:(sg + 1) * 128],
                                             ev[it % 2][:, sg * 128:(sg + 1) * 128], consts["ident"][:, :])
                    s_rot.inc(i)

    @p.block.vector
    def _(vector):
        for m in range(nmo):
            for tt in range(ntt):
                it = m * ntt + tt
                s_mm.wge(vector, it + 1)
                if it >= 2 and mode != "sel":
                    if mode in ("rope", "vtm"):
                        s_qf.wge(vector, it - 1)
                    elif mode == "cb":
                        s_free.wge(vector, it - 1)
                    else:
                        s_done2[it % 2].wge(vector, 16 * (it // 2))
                if mode == "sel":
                    mask_all, acc = sel
                    i = vector.copy_predicated(
                        acc[:, tt * TT:(tt + 1) * TT],
                        mask_all[:, m * T + tt * TT:m * T + (tt + 1) * TT],
                        pp[it % 2][:, :])
                elif bias_sb is not None:
                    i = vector.tensor_scalar(ev[it % 2][:, :], pp[it % 2][:, :],
                                             bias_sb[:, bias_off + m:bias_off + m + 1], None, ALU.add)
                else:
                    i = vector.tensor_copy(ev[it % 2][:, :], pp[it % 2][:, :])
                s_ev.inc(i)
                if mode == "rope":
                    s_rot.wge(vector, it + 1)
                    if it >= 2:
                        s_done2[it % 2].wge(vector, 16 * (it // 2))
                    cos_sb, sin_sb, _ = rope
                    t0 = tt * TT
                    s_ev.wge(vector, it + 1)
                    vector.tensor_tensor(rt1[:, :], ev[it % 2][:, :], cos_sb[:, t0:t0 + TT], ALU.mult)
                    s_rr2.inc(vector.tensor_tensor(rt2[:, :], rot_ps[:, :], sin_sb[:, t0:t0 + TT], ALU.mult))
                    s_rr2.wge(vector, s_rr2.n)
                    s_qf.inc(vector.tensor_tensor(qf[it % 2][:, :], rt1[:, :], rt2[:, :], ALU.add))
                elif mode == "vtm":
                    s_rot.wge(vector, it + 1)
                    if it >= 2:
                        s_done2[it % 2].wge(vector, 16 * (it // 2))
                    s_qf.inc(vector.tensor_copy(qf[it % 2][:, :], tp_ps[:, 0:512]))

    if mode == "sel":
        return s_ev
    if mode == "cb":
        ev_consumer(s_ev, s_free, ev)
        return s_ev

    @p.block.sync
    def _(g):
        for m in range(nmo):
            for tt in range(ntt):
                it = m * ntt + tt
                t0 = tt * TT
                if mode in ("rope", "vtm"):
                    s_qf.wge(g, it + 1)
                    src = qf[it % 2][:, :]
                else:
                    s_ev.wge(g, it + 1)
                    src = ev[it % 2][:, :]
                if mode == "vtm":
                    oap = out_dram[m, t0:t0 + TT, :].rearrange("(sg pp) d -> pp sg d", pp=128)
                    src = src.rearrange("p (sg d) -> p sg d", sg=4)
                else:
                    oap = out_dram[m, :, t0:t0 + TT]
                i = g.dma_start(out=oap, in_=src)
                s_done2[it % 2].inc(i, 16)
    return SemGroup(s_done2)


def emit_resadd_ln(p, x_dram, add_dram, x1_dram, xbf, consts, waits, tag,
                   wo_dram=None, bias_sb=None, bias_off=0):
    xt = p.sb(f"rx_{tag}", [128, NCH * TT], F32)
    at = [p.sb(f"ra0_{tag}", [128, NCH * TT], BF16)]
    x1c = [p.sb(f"rx1{i}_{tag}", [128, TT], F32) for i in range(2)]
    if wo_dram is not None:
        wbuf = ensure_proj_scratch(p)["wbuf"]
    pp = [p.pf[0], p.pf[1]]
    ybuf = ensure_proj_scratch(p)["rt1"]
    s_mm, s_ev, s_yb = (p.sem() for _ in range(3))
    s_xl = p.sem(kind="sw")
    s_al = p.sem(kind="sw")
    s_wl = [p.sem(kind="sw"), p.sem(kind="sw")]
    s_st = [p.sem(kind="hw"), p.sem(kind="hw")]
    wvals = _wpairs(waits)

    @p.block.gpsimd
    def _(g):
        for hh, n in wvals:
            g.wait_ge(hh, n)
        for tt in range(NTT):
            if tt >= 1:
                s_ev.wge(g, 2 * NCH * tt)
            s_xl.inc(g.dma_start(out=xt[:, :].rearrange("p (c t) -> p c t", c=NCH),
                                 in_=x_dram[:, :, tt * TT:(tt + 1) * TT].rearrange("c p t -> p c t")), 16)
            s_al.inc(g.dma_start(out=at[0][:, :].rearrange("p (c t) -> p c t", c=NCH),
                                 in_=add_dram[:, :, tt * TT:(tt + 1) * TT].rearrange("c p t -> p c t")), 16)
            if wo_dram is not None:
                for m in range(NCH):
                    it = tt * NCH + m
                    if it >= 2:
                        s_mm.wge(g, it - 1)
                    s_wl[it % 2].inc(g.dma_start(out=wbuf[it % 2][:, :],
                                                 in_=wo_dram[m, :, :, :].rearrange("p k mm -> p (k mm)")), 16)

    if wo_dram is not None:
        @p.block.tensor
        def _(tensor):
            for tt in range(NTT):
                s_al.wge(tensor, 16 * (tt + 1))
                if tt == 0:
                    for hh, n in wvals:
                        tensor.wait_ge(hh, n)
                for m in range(NCH):
                    it = tt * NCH + m
                    s_wl[it % 2].wge(tensor, 16 * (it // 2 + 1))
                    if it >= 2:
                        s_ev.wge(tensor, 2 * (it - 1))
                    for c in range(NCH):
                        i = tensor.matmul(pp[it % 2][:, :], wbuf[it % 2][:, c * 128:(c + 1) * 128],
                                          at[0][:, c * TT:(c + 1) * TT],
                                          start=(c == 0), stop=(c == NCH - 1))
                    s_mm.inc(i)

    @p.block.vector
    def _(vector):
        for tt in range(NTT):
            s_xl.wge(vector, 16 * (tt + 1))
            for m in range(NCH):
                it = tt * NCH + m
                xc = xt[:, m * TT:(m + 1) * TT]
                if wo_dram is not None:
                    s_mm.wge(vector, it + 1)
                    if it >= 2:
                        s_st[it % 2].wge(vector, 16 * (it // 2))
                    if it >= 1:
                        s_ev.wge(vector, 2 * it)
                    s_yb.inc(vector.tensor_scalar(ybuf[:, :], pp[it % 2][:, :],
                                                  bias_sb[:, bias_off + m:bias_off + m + 1],
                                                  None, ALU.add))
                    s_yb.wge(vector, s_yb.n)
                    s_ev.inc(vector.tensor_tensor(x1c[it % 2][:, :], ybuf[:, :], xc, ALU.add))
                    i = vector.tensor_tensor(xbf[:, m * T + tt * TT:m * T + tt * TT + TT],
                                             ybuf[:, :], xc, ALU.add)
                    s_ev.inc(i)
                else:
                    s_al.wge(vector, 16 * (tt + 1))
                    if it >= 2:
                        s_st[it % 2].wge(vector, 16 * (it // 2))
                    ac = at[0][:, m * TT:(m + 1) * TT]
                    s_ev.inc(vector.tensor_tensor(x1c[it % 2][:, :], xc, ac, ALU.add))
                    i = vector.tensor_tensor(xbf[:, m * T + tt * TT:m * T + tt * TT + TT],
                                             xc, ac, ALU.add)
                    s_ev.inc(i)

    @p.block.sync
    def _(g):
        for tt in range(NTT):
            for m in range(NCH):
                it = tt * NCH + m
                s_ev.wge(g, 2 * (it + 1))
                s_st[it % 2].inc(g.dma_start(out=x1_dram[m, :, tt * TT:(tt + 1) * TT],
                                             in_=x1c[it % 2][:, :]), 16)
    return s_ev, SemGroup(s_st)


def emit_l0f_gating(p, xln, wg_sb, consts, waits, tag):
    identf = consts["identf"]
    orow_f = consts["ones_row_f"]
    g_sb = p.sb(f"g_{tag}", [16, TT], F32)
    gtm = p.sb(f"gtm_{tag}", [128, 4 * 16], F32)
    gm = p.sb(f"gm_{tag}", [128, 4], F32)
    gnm = p.sb(f"gnm_{tag}", [128, 4], F32)
    ge = p.sb(f"ge_{tag}", [128, 64], F32)
    gs = p.sb(f"gs_{tag}", [128, 4], F32)
    gt1 = p.sb(f"gt1_{tag}", [128, 4], F32)
    trow = p.sb(f"trow_{tag}", [1, TT], F32)
    ind_fm = p.sb(f"ind_{tag}", [16, NCH * 128], F32)
    top1row = p.sb(f"t1r_{tag}", [1, NCH * 128], F32)
    g_ps = p.pf[0]
    t_ps = p.pf[1]
    mb_ps = p.pf[2]
    t1_ps = p.pf[3]
    (s_gmm, s_gev, s_gt, s_gst, s_gnm, s_ge, s_gr, s_grt, s_gre, s_mb,
     s_ind) = (p.sem() for _ in range(11))
    wvals = _wpairs(waits)

    @p.block.tensor
    def _(tensor):
        for tt in range(NTT):
            if tt == 0:
                for hh, n in wvals:
                    tensor.wait_ge(hh, n)
            if tt > 0:
                s_gev.wge(tensor, tt)
            for c in range(NCH):
                i = tensor.matmul(g_ps[0:16, 0:TT], wg_sb[:, c * 16:(c + 1) * 16],
                                  xln[:, c * T + tt * TT:c * T + tt * TT + TT],
                                  start=(c == 0), stop=(c == NCH - 1))
            s_gmm.inc(i)
            s_gev.wge(tensor, tt + 1)
            if tt > 0:
                s_gst.wge(tensor, tt)
                s_gre.wge(tensor, tt)
            for gr in range(4):
                i = tensor.transpose(t_ps[0:128, gr * 16:(gr + 1) * 16],
                                     g_sb[0:16, gr * 128:(gr + 1) * 128], identf[0:16, 0:16])
            s_gt.inc(i)
            s_gr.wge(tensor, tt + 1)
            for gr in range(4):
                tensor.transpose(t_ps[0:1, gr * 128:(gr + 1) * 128], gm[:, gr:gr + 1],
                                 identf[:, :])
                i = tensor.transpose(t1_ps[0:1, gr * 128:(gr + 1) * 128], gt1[:, gr:gr + 1],
                                     identf[:, :])
            s_grt.inc(i)
            s_gre.wge(tensor, tt + 1)
            if tt > 0:
                s_ind.wge(tensor, tt)
            s_mb.inc(tensor.matmul(mb_ps[0:16, 0:TT], orow_f[0:1, 0:16], trow[0:1, 0:TT],
                                   start=True, stop=True))

    @p.block.vector
    def _(vector):
        for tt in range(NTT):
            if tt == 0:
                for hh, n in wvals:
                    vector.wait_ge(hh, n)
            s_gmm.wge(vector, tt + 1)
            if tt > 0:
                s_ind.wge(vector, tt)
            s_gev.inc(vector.tensor_copy(g_sb[0:16, :], g_ps[0:16, 0:TT]))
            s_gt.wge(vector, tt + 1)
            s_gst.inc(vector.tensor_copy(gtm[:, :], t_ps[0:128, 0:64]))
            s_gst.wge(vector, tt + 1)
            if tt > 0:
                s_grt.wge(vector, tt)
            for gr in range(4):
                i = vector.reduce_max(gm[:, gr:gr + 1], gtm[:, gr * 16:(gr + 1) * 16],
                                      axis=mybir.AxisListType.X)
            s_gnm.inc(i)
            s_gnm.wge(vector, 2 * tt + 1)
            for gr in range(4):
                i = vector.tensor_scalar(gnm[:, gr:gr + 1], gm[:, gr:gr + 1], -1.0, None, ALU.mult)
            s_gnm.inc(i)
            s_ge.wge(vector, 4 * (tt + 1))
            for gr in range(4):
                i = vector.reciprocal(gt1[:, gr:gr + 1], gs[:, gr:gr + 1])
            s_gr.inc(i)
            s_grt.wge(vector, tt + 1)
            vector.tensor_copy(trow[0:1, 0:TT], t_ps[0:1, 0:TT])
            s_gre.inc(vector.tensor_copy(top1row[0:1, tt * TT:(tt + 1) * TT], t1_ps[0:1, 0:TT]))
            s_mb.wge(vector, tt + 1)
            s_ind.inc(vector.tensor_tensor(ind_fm[0:16, tt * TT:(tt + 1) * TT], g_sb[0:16, :],
                                           mb_ps[0:16, 0:TT], ALU.is_equal))

    @p.block.scalar
    def _(scalar):
        for tt in range(NTT):
            s_gnm.wge(scalar, 2 * (tt + 1))
            for gr in range(4):
                i = scalar.activation(ge[:, gr * 16:(gr + 1) * 16], gtm[:, gr * 16:(gr + 1) * 16],
                                      AF.Exp, bias=gnm[:, gr:gr + 1], accum_out=gs[:, gr:gr + 1])
                s_ge.inc(i)
    return ind_fm, top1row, s_ind, s_gre


def emit_attn_tok(p, consts, q_dram, kg_dram, vg_dram, att_dram, nhd, kvmap, scale,
                  ptk_sb, qpos_sb, waits, tag, gate=None):
    NBQ = NTT
    q_sb = [p.sb(f"aq{i}_{tag}", [128, T], BF16) for i in range(2)]
    k_sb = [p.sb(f"ak{i}_{tag}", [128, S], BF16) for i in range(2)]
    v_sb = [p.sb(f"av{i}_{tag}", [128, S], BF16) for i in range(2)]
    p_sb = p.sb(f"apb_{tag}", [128, 4 * 512], BF16)
    bsq = p.sb(f"absq_{tag}", [128, 512], BF16)
    NBK = S // 512
    brow = p.sb(f"abrow_{tag}", [1, NTT + NBK + 4], F32)
    nbrow = p.sb(f"anbr_{tag}", [1, 1], F32)
    nb_sb = p.sb(f"anb_{tag}", [128, 1], F32)
    rt_sb = p.sb(f"art_{tag}", [1, 512], F32)
    r_sb = p.sb(f"arr_{tag}", [128, 512], F32)
    att_sb = [p.sb(f"ao{i}_{tag}", [128, 512], BF16) for i in range(2)]
    gate_sb = p.sb(f"agt_{tag}", [128, 512], BF16)
    msk_sb = p.sb(f"amsk_{tag}", [128, 4 * 2 * 4 * 512], BF16)
    att_ps = p.pf[0]
    sc_ps = [p.pf[1], p.pf[2]]
    lr_ps = p.pf[3]
    b_ps = p.pf[4]
    g_ps = p.pf[5]
    oc, orow_f = consts["ones_col"], consts["ones_row_f"]
    wvals = _wpairs(waits)

    (s_bsq, s_bmm, s_bred, s_brow, s_bsd, s_bneg, s_nb, s_nbe, s_sc, s_mk,
     s_exp, s_av, s_lmm, s_rj, s_rbc, s_rse, s_gt, s_gs, s_fin,
     s_cmb, s_aux, s_mgb, s_mgen) = (p.sem() for _ in range(23))
    s_ldq = [p.sem(kind="sw"), p.sem(kind="sw")]
    s_ldkv = [p.sem(kind="sw"), p.sem(kind="sw")]
    s_stp = [p.sem(kind="hw"), p.sem(kind="hw")]

    def mcol(qc, b2, c):
        return ((qc * 2 + b2) * 4 + c) * 512

    @p.block.tensor
    def _(tensor):
        for hh, n in wvals:
            tensor.wait_ge(hh, n)
        for qc in range(NBQ):
            if qc > 0:
                s_mgen.wge(tensor, qc * 8)
            i = tensor.matmul(b_ps[:, :], orow_f[0:1, :],
                              qpos_sb[0:1, qc * 512:(qc + 1) * 512],
                              start=True, stop=True)
            s_mgb.inc(i)

    @p.block.vector
    def _(vector):
        for hh, n in wvals:
            vector.wait_ge(hh, n)
        for qc in range(NBQ):
            s_mgb.wge(vector, qc + 1)
            for b2 in range(2):
                kb = EXT[qc] - 2 + b2
                for c in range(4):
                    s_mgen.inc(vector.tensor_scalar(
                        msk_sb[:, mcol(qc, b2, c):mcol(qc, b2, c) + 512],
                        b_ps[:, :], ptk_sb[:, kb * 4 + c:kb * 4 + c + 1], NEG,
                        ALU.is_lt, ALU.mult))

    kv_fresh = {}
    slot_of = {}
    kv_tgt = {}
    nslot = 0
    slot_uses = [0, 0]
    for h in range(nhd):
        kv = kvmap(h)
        if kv not in slot_of:
            slot_of[kv] = nslot % 2
            nslot += 1
            kv_fresh[h] = kv
            slot_uses[slot_of[kv]] += 1
        kv_tgt[h] = 16 * 16 * slot_uses[slot_of[kv]]

    sc_ord, mk_ord, exp_ord, av_ord, fin_ord, gt_ord, pv_ord = ({} for _ in range(7))
    nsc = nmk = nexp = nav = nfin = ngt = 0
    slice_base = {}
    nbs = 0
    last_av_c = {}
    for h in range(nhd):
        ns = NTT + (NBK if h in kv_fresh else 0)
        slice_base[h] = (nbs, ns)
        nbs += ns
        for qc in range(NBQ):
            E = EXT[qc]
            if gate is not None:
                ngt += 1
                gt_ord[(h, qc)] = ngt
            for kb in range(E):
                for c in range(4):
                    nsc += 1
                    sc_ord[(h, qc, kb, c)] = nsc
                    if kb >= E - 2:
                        nmk += 1
                        mk_ord[(h, qc, kb, c)] = nmk
                    nexp += 1
                    exp_ord[(h, qc, kb, c)] = nexp
                    pv_ord[(h, qc, kb, c)] = last_av_c.get(c, 0)
                for c in range(4):
                    nav += 1
                    av_ord[(h, qc, kb, c)] = nav
                    last_av_c[c] = nav
            nfin += 1
            fin_ord[(h, qc)] = nfin

    @p.block.gpsimd
    def _(g):
        for h in range(nhd):
            if h == 0:
                for hh, n in wvals:
                    g.wait_ge(hh, n)
            if h >= 2:
                s_stp[0].wge(g, 16 * 2 * (h - 1))
                s_stp[1].wge(g, 16 * 2 * (h - 1))
            if h in kv_fresh:
                kv = kv_fresh[h]
                sl = slot_of[kv]
                for gch in range(8):
                    src, pos = divmod(_SLOT[gch], 4)
                    g.dma_start(out=k_sb[sl][:, gch * 512:(gch + 1) * 512],
                                in_=kg_dram[src, kv, :, pos * 512:(pos + 1) * 512]).then_inc(s_ldkv[sl].h, 16)
                    g.dma_start(
                        out=v_sb[sl][:, gch * 512:(gch + 1) * 512].rearrange("pp (cc d) -> pp cc d", d=128),
                        in_=vg_dram[src, kv, pos * 512:(pos + 1) * 512, :].rearrange("(cc pp) d -> pp cc d", pp=128)
                    ).then_inc(s_ldkv[sl].h, 16)
                s_ldkv[sl].n += 16 * 16
            s_ldq[h % 2].inc(g.dma_start(out=q_sb[h % 2][:, :], in_=q_dram[h, :, :]), 16)

    @p.block.tensor
    def _(tensor):
        for h in range(nhd):
            kv = kvmap(h)
            ksl = slot_of[kv]
            s_ldq[h % 2].wge(tensor, 16 * (h // 2 + 1))
            s_ldkv[ksl].wge(tensor, kv_tgt[h])
            sbase, ns = slice_base[h]
            if h == 0:
                s_mgen.wge(tensor, 32)
            if h >= 1:
                s_nbe.wge(tensor, h)
            for sl in range(ns):
                s_bsq.wge(tensor, sbase + sl + 1)
                if sbase + sl > 0:
                    s_bred.wge(tensor, sbase + sl)
                s_bmm.inc(tensor.matmul(b_ps[0:1, 0:512], oc[:, :], bsq[:, :],
                                        start=True, stop=True))
            s_bneg.wge(tensor, h + 1)
            s_nb.inc(tensor.matmul(b_ps[0:128, 0:1], orow_f[:, 0:128], nbrow[0:1, 0:1],
                                   start=True, stop=True))
            for qc in range(NBQ):
                E = EXT[qc]
                fo = fin_ord[(h, qc)]
                if gate is not None:
                    go = gt_ord[(h, qc)]
                    if go > 1:
                        s_gs.wge(tensor, go - 1)
                    s_gt.inc(tensor.matmul(g_ps[:, :], gate[0][:, :],
                                           q_sb[h % 2][:, qc * 512:(qc + 1) * 512],
                                           start=True, stop=True))
                for kb in range(E):
                    for c in range(4):
                        o = sc_ord[(h, qc, kb, c)]
                        if o > 2:
                            s_exp.wge(tensor, o - 2)
                        s_sc.inc(tensor.matmul(sc_ps[c % 2][:, :],
                                               k_sb[ksl][:, kb * 512 + c * 128:kb * 512 + (c + 1) * 128],
                                               q_sb[h % 2][:, qc * 512:(qc + 1) * 512],
                                               start=True, stop=True))
                    s_exp.wge(tensor, exp_ord[(h, qc, kb, 3)])
                    for c in range(4):
                        first = (kb == 0 and c == 0)
                        last = (kb == E - 1 and c == 3)
                        if first and fo > 1:
                            s_fin.wge(tensor, fo - 1)
                            s_rse.wge(tensor, fo - 1)
                        i = tensor.matmul(lr_ps[0:1, 0:512], oc[:, 0:1],
                                          p_sb[:, c * 512:(c + 1) * 512],
                                          start=first, stop=last)
                        if last:
                            s_lmm.inc(i)
                        cc = kb * 4 + c
                        s_av.inc(tensor.matmul(att_ps[:, :],
                                               v_sb[ksl][:, cc * 128:(cc + 1) * 128],
                                               p_sb[:, c * 512:(c + 1) * 512],
                                               start=first, stop=last))
                s_rj.wge(tensor, fo)
                s_rbc.inc(tensor.matmul(lr_ps[:, 0:512], orow_f[0:1, 0:128],
                                        rt_sb[0:1, 0:512], start=True, stop=True))

    @p.block.scalar
    def _(scalar):
        for h in range(nhd):
            sbase, ns = slice_base[h]
            fresh = h in kv_fresh
            if h == 0:
                for hh, n in wvals:
                    scalar.wait_ge(hh, n)
            s_ldq[h % 2].wge(scalar, 16 * (h // 2 + 1))
            s_ldkv[slot_of[kvmap(h)]].wge(scalar, kv_tgt[h])
            si = 0
            for buf, nsl in ([(q_sb[h % 2], NTT)] +
                             ([(k_sb[slot_of[kvmap(h)]], NBK)] if fresh else [])):
                for sl in range(nsl):
                    if sbase + si > 0:
                        s_bmm.wge(scalar, sbase + si)
                    s_bsq.inc(scalar.activation(bsq[:, :], buf[:, sl * 512:(sl + 1) * 512],
                                                AF.Square))
                    si += 1
            s_brow.wge(scalar, h + 1)
            s_bsd.inc(scalar.activation(brow[0:1, NTT + NBK + 1:NTT + NBK + 2],
                                        brow[0:1, NTT + NBK:NTT + NBK + 1], AF.Sqrt))
            for qc in range(NBQ):
                E = EXT[qc]
                if gate is not None:
                    go = gt_ord[(h, qc)]
                    s_gt.wge(scalar, go)
                    if go > 1:
                        s_fin.wge(scalar, go - 1)
                    s_gs.inc(scalar.activation(gate_sb[:, :], g_ps[:, :], AF.Sigmoid,
                                               bias=gate[1][:, 0:1]))
                for kb in range(E):
                    for c in range(4):
                        if kb >= E - 2:
                            s_mk.wge(scalar, mk_ord[(h, qc, kb, c)])
                        else:
                            s_sc.wge(scalar, sc_ord[(h, qc, kb, c)])
                        pav = pv_ord[(h, qc, kb, c)]
                        if pav > 0:
                            s_av.wge(scalar, pav)
                        if kb == 0 and c == 0 and qc == 0:
                            s_nbe.wge(scalar, h + 1)
                        s_exp.inc(scalar.activation(
                            p_sb[:, c * 512:(c + 1) * 512], sc_ps[c % 2][:, :], AF.Exp,
                            bias=nb_sb[:, 0:1], scale=scale))

    @p.block.vector
    def _(vector):
        for h in range(nhd):
            sbase, ns = slice_base[h]
            if h == 0:
                for hh, n in wvals:
                    vector.wait_ge(hh, n)
            for si in range(ns):
                s_bmm.wge(vector, sbase + si + 1)
                s_bred.inc(vector.reduce_max(brow[0:1, si:si + 1], b_ps[0:1, 0:512],
                                             axis=mybir.AxisListType.X))
            s_bred.wge(vector, sbase + ns)
            vector.reduce_max(brow[0:1, NTT + NBK + 2:NTT + NBK + 3], brow[0:1, 0:NTT],
                              axis=mybir.AxisListType.X)
            s_cmb.inc(vector.reduce_max(brow[0:1, NTT + NBK + 3:NTT + NBK + 4],
                                        brow[0:1, NTT:NTT + NBK], axis=mybir.AxisListType.X))
            s_cmb.wge(vector, s_cmb.n)
            s_brow.inc(vector.tensor_tensor(brow[0:1, NTT + NBK:NTT + NBK + 1],
                                            brow[0:1, NTT + NBK + 2:NTT + NBK + 3],
                                            brow[0:1, NTT + NBK + 3:NTT + NBK + 4], ALU.mult))
            s_bsd.wge(vector, h + 1)
            s_bneg.inc(vector.tensor_scalar(nbrow[0:1, 0:1],
                                            brow[0:1, NTT + NBK + 1:NTT + NBK + 2],
                                            -scale, None, ALU.mult))
            s_nb.wge(vector, h + 1)
            s_nbe.inc(vector.tensor_copy(nb_sb[:, 0:1], b_ps[0:128, 0:1]))
            for qc in range(NBQ):
                E = EXT[qc]
                fo = fin_ord[(h, qc)]
                for kb in range(E - 2, E):
                    b2 = kb - (E - 2)
                    for c in range(4):
                        s_sc.wge(vector, sc_ord[(h, qc, kb, c)])
                        s_mk.inc(vector.tensor_tensor(
                            sc_ps[c % 2][:, :], sc_ps[c % 2][:, :],
                            msk_sb[:, mcol(qc, b2, c):mcol(qc, b2, c) + 512], ALU.add))
                s_lmm.wge(vector, fo)
                s_rj.inc(vector.reciprocal(rt_sb[0:1, :], lr_ps[0:1, 0:512]))
                s_rbc.wge(vector, fo)
                if fo > 2:
                    s_stp[(fo - 1) % 2].wge(vector, 16 * ((fo - 3) // 2 + 1))
                s_rse.inc(vector.tensor_copy(r_sb[:, :], lr_ps[:, 0:512]))
                s_rse.wge(vector, fo)
                s_av.wge(vector, av_ord[(h, qc, E - 1, 3)])
                i = vector.tensor_tensor(att_sb[(fo - 1) % 2][:, :], att_ps[:, :], r_sb[:, :], ALU.mult)
                if gate is not None:
                    s_aux.inc(i)
                    s_aux.wge(vector, s_aux.n)
                    s_gs.wge(vector, gt_ord[(h, qc)])
                    i = vector.tensor_tensor(att_sb[(fo - 1) % 2][:, :], att_sb[(fo - 1) % 2][:, :],
                                             gate_sb[:, :], ALU.mult)
                s_fin.inc(i)

    @p.block.sync
    def _(g):
        for h in range(nhd):
            for qc in range(NBQ):
                fo = fin_ord[(h, qc)]
                s_fin.wge(g, fo)
                s_stp[(fo - 1) % 2].inc(g.dma_start(out=att_dram[h, :, qc * 512:(qc + 1) * 512],
                                                    in_=att_sb[(fo - 1) % 2][:, :]), 16)
    return SemGroup(s_stp)


def emit_final_wait(p, sems):
    @p.block.gpsimd
    def _(g):
        for s in sems:
            for h, nn in s.pairs():
                g.wait_ge(h, nn)


def build_p1():
    p = Prog().open()
    x = p.din("x", [D, T], F32)
    wq = p.din("wq", [NH, 128, NCH, 128], F32)
    wk = p.din("wk", [KVH0, 128, NCH, 128], F32)
    wv = p.din("wv", [KVH0, 128, NCH, 128], F32)
    bqkv = p.din("bqkv", [128, NH + 2 * KVH0], F32)
    ctab = p.din("ctab", [128, T], F32)
    stab = p.din("stab", [128, T], F32)
    cpt = p.din("c_pt", [128, 128], BF16)
    q_out = p.dout("q_out", [NH, 128, T], BF16)
    k_out = p.dout("k_out", [KVH0, 128, T], BF16)
    v_out = p.dout("v_out", [KVH0, T, 128], BF16)

    consts, s_const = load_consts(p)
    xln = p.sb("xln", [128, NCH * T], BF16)
    xbf = xln
    cos_sb = p.sb("cos", [128, T], F32)
    sin_sb = p.sb("sin", [128, T], F32)
    pt_sb = p.sb("pt", [128, 128], BF16)
    bias_sb = p.sb("bias", [128, NH + 2 * KVH0], F32)
    s_misc = p.sem(kind="hw")

    @p.block.sync
    def _(sync):
        sync.dma_start(out=cos_sb[:, :], in_=ctab[:, :]).then_inc(s_misc.h, 16)
        sync.dma_start(out=sin_sb[:, :], in_=stab[:, :]).then_inc(s_misc.h, 16)
        sync.dma_start(out=pt_sb[:, :], in_=cpt[:, :]).then_inc(s_misc.h, 16)
        sync.dma_start(out=bias_sb[:, :], in_=bqkv[:, :]).then_inc(s_misc.h, 16)
    s_misc.n = 64

    s_x = emit_load_xbf(p, x, xbf)
    s_ln = emit_ln(p, xbf, xln, consts, [s_x, s_const], "l0a")
    rope = (cos_sb, sin_sb, pt_sb)
    sq = emit_proj(p, xln, wq, NH, consts, [s_ln, s_misc], "q",
                   bias_sb=bias_sb, bias_off=0, mode="rope", rope=rope, out_dram=q_out)
    sk = emit_proj(p, xln, wk, KVH0, consts, [s_ln, s_misc, sq], "k",
                   bias_sb=bias_sb, bias_off=NH, mode="rope", rope=rope, out_dram=k_out)
    sv = emit_proj(p, xln, wv, KVH0, consts, [s_ln, s_misc, sk], "v",
                   bias_sb=bias_sb, bias_off=NH + KVH0, mode="vtm", out_dram=v_out)
    emit_final_wait(p, [sq, sk, sv])
    return p


def build_attn_tok(nkv, gate=False, tag="a"):
    p = Prog().open()
    q = p.din("q", [NH, 128, T], BF16)
    kg = p.din("kg", [2, nkv, 128, T], BF16)
    vg = p.din("vg", [2, nkv, T, 128], BF16)
    ptk = p.din("ptk", [128, (S // 512) * 4], F32)
    qpos = p.din("qpos", [1, T], F32)
    att = p.dout("att", [NH, 128, T], BF16)
    scale = HD ** -0.5
    consts, s_const = load_consts(p, reg_vals=[scale])
    ptk_sb = p.sb("ptk_sb", [128, (S // 512) * 4], F32)
    qpos_sb = p.sb("qpos_sb", [1, T], F32)
    s_mh = p.sem(kind="hw")
    s_msw = p.sem(kind="sw")
    s_misc = SemGroup([s_mh, s_msw])
    gt = None
    if gate:
        gwt = p.din("gwt", [128, 128], F32)
        gbt = p.din("gbt", [128, 1], F32)
        gw_sb = p.sb("gw", [128, 128], BF16)
        gb_sb = p.sb("gb", [128, 1], F32)
        gt = (gw_sb, gb_sb)

    @p.block.sync
    def _(sync):
        s_mh.inc(sync.dma_start(out=ptk_sb[:, :], in_=ptk[:, :]), 16)
        s_mh.inc(sync.dma_start(out=qpos_sb[:, :], in_=qpos[:, :]), 16)

    if gate:
        @p.block.gpsimd
        def _(g):
            s_msw.inc(g.dma_start(out=gw_sb[:, :], in_=gwt[:, :]), 16)
            s_msw.inc(g.dma_start(out=gb_sb[:, :], in_=gbt[:, :]), 16)

    kvmap = (lambda h: h * nkv // NH) if nkv > 1 else (lambda h: 0)
    s_done = emit_attn_tok(p, consts, q, kg, vg, att, NH, kvmap, scale,
                           ptk_sb, qpos_sb, [s_const, s_misc], tag, gate=gt)
    emit_final_wait(p, [s_done])
    return p


def _attn_host_consts(c):
    nb4 = (S // 512) * 4
    ptk = np.ascontiguousarray(np.arange(128, dtype=np.float32)[:, None]
                               + 128.0 * np.arange(nb4, dtype=np.float32)[None, :])
    qpos = np.ascontiguousarray(_core_pos(c).astype(np.float32).reshape(1, T))
    return ptk, qpos


def _run_attn(builder_args, q_key, k_key, v_key, results, extra=None):
    nkv, gate, tag = builder_args
    p = build_attn_tok(nkv, gate=gate, tag=tag)
    cid = np.ascontiguousarray(np.eye(128, dtype=BF))
    cidf = np.ascontiguousarray(np.eye(128, dtype=np.float32))
    in_maps = []
    for c in range(NCORE):
        pair = (c // 2) * 2
        kgv = np.stack([results[pair][k_key], results[pair + 1][k_key]])
        vgv = np.stack([results[pair][v_key], results[pair + 1][v_key]])
        if kgv.ndim == 3:
            kgv = kgv[:, None]
            vgv = vgv[:, None]
        ptk, qpos = _attn_host_consts(c)
        m = {"q": results[c][q_key], "kg": np.ascontiguousarray(kgv),
             "vg": np.ascontiguousarray(vgv), "ptk": ptk, "qpos": qpos,
             "c_id": cid, "c_idf": cidf}
        if extra:
            m.update(extra)
        in_maps.append(m)
    res = run_bass_kernel_spmd(p.nc, in_maps, core_ids=list(range(NCORE)))
    p.close()
    return res.results


def run_p2(p1_results, masks=None):
    return _run_attn((KVH0, False, "a"), "q_out", "k_out", "v_out", p1_results)


def emit_mask_all(p, consts, ind_dram, s_indd, tag):
    orow_f = consts["ones_row_f"]
    mk_ps = p.pf[2]
    mask_all = p.sb(f"mska_{tag}", [128, 16 * T], U8)
    ind_t = p.sb(f"indt_{tag}", [1, 2 * TT], F32)
    s_mld = [p.sem(kind="hw"), p.sem(kind="hw")]
    s_mb2, s_mev = p.sem(), p.sem()
    indd_n = s_indd.n

    @p.block.sync
    def _(sync):
        s_indd.wge(sync, indd_n)
        for m in range(16):
            for tt in range(NTT):
                it = m * NTT + tt
                if it >= 2:
                    s_mb2.wge(sync, it - 1)
                s_mld[it % 2].inc(sync.dma_start(
                    out=ind_t[0:1, (it % 2) * TT:(it % 2 + 1) * TT],
                    in_=ind_dram[m, tt * TT:(tt + 1) * TT]), 16)

    @p.block.tensor
    def _(tensor):
        for m in range(16):
            for tt in range(NTT):
                it = m * NTT + tt
                s_mld[it % 2].wge(tensor, 16 * (it // 2 + 1))
                if it >= 1:
                    s_mev.wge(tensor, it)
                s_mb2.inc(tensor.matmul(
                    mk_ps[:, :], orow_f[0:1, :],
                    ind_t[0:1, (it % 2) * TT:(it % 2 + 1) * TT],
                    start=True, stop=True))

    @p.block.vector
    def _(vector):
        for m in range(16):
            for tt in range(NTT):
                it = m * NTT + tt
                s_mb2.wge(vector, it + 1)
                s_mev.inc(vector.tensor_copy(
                    mask_all[:, m * T + tt * TT:m * T + (tt + 1) * TT], mk_ps[:, :]))
    return mask_all, s_mev


def build_p3():
    p = Prog().open()
    x = p.din("x", [NCH, 128, T], F32)
    attd = p.din("attd", [NCH, 128, T], BF16)
    wo = p.din("wo", [NCH, 128, NCH, 128], F32)
    bo = p.din("bo", [128, NCH], F32)
    wg = p.din("wg", [128, NCH * 16], F32)
    wqkv = p.din("wqkv", [48, 128, NCH, 128], F32)
    x1_out = p.dout("x1_out", [NCH, 128, T], F32)
    q_out = p.dout("q_out", [16, 128, T], BF16)
    ksh_out = p.dout("ksh_out", [1, 128, T], BF16)
    vsh_out = p.dout("vsh_out", [1, T, 128], BF16)
    ind_dram = p.dint("ind_d", [16, T], F32)

    consts, s_const = load_consts(p)
    ensure_proj_scratch(p)
    bias_sb = p.sb("bias", [128, NCH], F32)
    wg_sb = p.sb("wg_sb", [128, NCH * 16], BF16)
    xln = p.sb("xln", [128, NCH * T], BF16)
    ksh_acc = p.sb("ksh_acc", [128, T], BF16)
    vsh_acc = p.sb("vsh_acc", [128, T], BF16)
    vtm_sb = p.sb("vtm_sb", [128, TT], BF16)
    s_mh = p.sem(kind="hw")
    s_msw = p.sem(kind="sw")
    s_misc = SemGroup([s_mh, s_msw])

    @p.block.sync
    def _(sync):
        s_mh.inc(sync.dma_start(out=bias_sb[:, :], in_=bo[:, :]), 16)

    @p.block.gpsimd
    def _(g):
        s_msw.inc(g.dma_start(out=wg_sb[:, :], in_=wg[:, :]), 16)

    with p.scope():
        s_x1, s_st1 = emit_resadd_ln(p, x, attd, x1_out, xln, consts, [s_const, s_misc],
                                     "p3", wo_dram=wo, bias_sb=bias_sb, bias_off=0)
        s_ln = emit_ln(p, xln, xln, consts, [s_x1, s_st1], "p3")
    ind_fm, top1row, s_ind, s_gre = emit_l0f_gating(p, xln, wg_sb, consts,
                                                    [s_ln, s_misc], "p3")
    s_indf = p.sem(kind="hw")

    @p.block.sync
    def _(sync):
        s_ind.wge(sync, s_ind.n)
        s_indf.inc(sync.dma_start(out=ind_dram[:, :], in_=ind_fm[0:16, :]), 16)

    sq = emit_proj(p, xln, wqkv[0:16], 16, consts, [s_ln, s_ind], "q",
                   mode="plain", out_dram=q_out)
    mask_all, s_mask = emit_mask_all(p, consts, ind_dram, s_indf, "p3")
    s_kacc = emit_proj(p, xln, wqkv[16:32], 16, consts, [s_ln, sq, s_mask], "k",
                       mode="sel", sel=(mask_all, ksh_acc))
    s_vacc = emit_proj(p, xln, wqkv[32:48], 16, consts, [s_ln, s_kacc], "v",
                       mode="sel", sel=(mask_all, vsh_acc))

    s_kst = p.sem(kind="sw")

    @p.block.gpsimd
    def _(g):
        s_kacc.wge(g, s_kacc.n)
        s_kst.inc(g.dma_start(out=ksh_out[0, :, :], in_=ksh_acc[:, :]), 16)

    t1_ps = p.pf[3]
    vt_ps = p.pb[0]
    s_t1b, s_t1m, s_vt, s_vte = (p.sem() for _ in range(4))
    s_vst = p.sem(kind="hw")
    orow_f = consts["ones_row_f"]
    ident = consts["ident"]

    @p.block.tensor
    def _(tensor):
        s_vacc.wge(tensor, s_vacc.n)
        s_gre.wge(tensor, s_gre.n)
        for tt in range(NTT):
            if tt > 0:
                s_t1m.wge(tensor, tt)
            s_t1b.inc(tensor.matmul(t1_ps[:, :], orow_f[0:1, :],
                                    top1row[0:1, tt * TT:(tt + 1) * TT], start=True, stop=True))
            s_t1m.wge(tensor, tt + 1)
            if tt > 0:
                s_vte.wge(tensor, 4 * tt)
            for sg in range(4):
                i = tensor.transpose(vt_ps[:, sg * 128:(sg + 1) * 128],
                                     vsh_acc[:, tt * TT + sg * 128:tt * TT + (sg + 1) * 128],
                                     ident[:, :])
            s_vt.inc(i)

    @p.block.vector
    def _(vector):
        for tt in range(NTT):
            s_t1b.wge(vector, tt + 1)
            s_t1m.inc(vector.tensor_tensor(vsh_acc[:, tt * TT:(tt + 1) * TT],
                                           vsh_acc[:, tt * TT:(tt + 1) * TT],
                                           t1_ps[:, :], ALU.mult))
            s_vt.wge(vector, tt + 1)
            if tt > 0:
                s_vst.wge(vector, 16 * tt)
            for sg in range(4):
                i = vector.tensor_copy(vtm_sb[:, sg * 128:(sg + 1) * 128],
                                       vt_ps[:, sg * 128:(sg + 1) * 128])
                s_vte.inc(i)

    @p.block.sync
    def _(g):
        for tt in range(NTT):
            s_vte.wge(g, 4 * (tt + 1))
            s_vst.inc(g.dma_start(
                out=vsh_out[0, tt * TT:(tt + 1) * TT, :].rearrange("(sg pp) d -> pp sg d", pp=128),
                in_=vtm_sb[:, :].rearrange("p (sg d) -> p sg d", sg=4)), 16)

    emit_final_wait(p, [s_st1, sq, s_kst, s_vst])
    return p


def run_p3(inputs, p2_results, xs_list):
    p = build_p3()
    Wo = np.asarray(inputs["L0_attn_Wo"])
    bo = np.asarray(inputs["L0_attn_bo"]).reshape(NCH, 128).T
    Wg = np.asarray(inputs["L0_ffn_W_gate"])
    wg_host = np.ascontiguousarray(Wg.reshape(16, 16, 128).transpose(2, 1, 0).reshape(128, 256).astype(np.float32))
    wo_t = _tile_lhsT(Wo)
    wqkv_t = _tile_lhsT(np.asarray(inputs["L0_ffn_W_qkv"]))
    cid = np.ascontiguousarray(np.eye(128, dtype=BF))
    cidf = np.ascontiguousarray(np.eye(128, dtype=np.float32))
    in_maps = []
    for c in range(NCORE):
        attd = p2_results[c]["att"]
        in_maps.append({
            "x": np.ascontiguousarray(xs_list[c].reshape(NCH, 128, T)),
            "attd": attd, "wo": wo_t, "bo": np.ascontiguousarray(bo.astype(np.float32)),
            "wg": wg_host, "wqkv": wqkv_t, "c_id": cid, "c_idf": cidf,
        })
    res = run_bass_kernel_spmd(p.nc, in_maps, core_ids=list(range(NCORE)))
    p.close()
    return res.results



def build_p5():
    p = Prog().open()
    x = p.din("x", [NCH, 128, T], F32)
    attd = p.din("attd", [NCH, 128, T], BF16)
    wq = p.din("wq", [NH, 128, NCH, 128], F32)
    wk = p.din("wk", [NH, 128, NCH, 128], F32)
    wv = p.din("wv", [NH, 128, NCH, 128], F32)
    x2_out = p.dout("x2_out", [NCH, 128, T], F32)
    q_out = p.dout("q_out", [NH, 128, T], BF16)
    k_out = p.dout("k_out", [NH, 128, T], BF16)
    v_out = p.dout("v_out", [NH, T, 128], BF16)

    consts, s_const = load_consts(p)
    ensure_proj_scratch(p)
    xln = p.sb("xln", [128, NCH * T], BF16)
    with p.scope():
        s_x1, s_st1 = emit_resadd_ln(p, x, attd, x2_out, xln, consts, [s_const], "p5")
        s_ln = emit_ln(p, xln, xln, consts, [s_x1, s_st1], "p5")
    sq = emit_proj(p, xln, wq, NH, consts, [s_ln], "q", mode="plain", out_dram=q_out)
    sk = emit_proj(p, xln, wk, NH, consts, [s_ln, sq], "k", mode="plain", out_dram=k_out)
    sv = emit_proj(p, xln, wv, NH, consts, [s_ln, sk], "v", mode="vtm", out_dram=v_out)
    emit_final_wait(p, [s_st1, sq, sk, sv])
    return p


def run_p5(inputs, p4_results, p3_results):
    p = build_p5()
    wq_t = _tile_lhsT(np.asarray(inputs["L1_attn_q_w"]))
    wk_t = _tile_lhsT(np.asarray(inputs["L1_attn_k_w"]))
    wv_t = _tile_lhsT(np.asarray(inputs["L1_attn_v_w"]))
    cid = np.ascontiguousarray(np.eye(128, dtype=BF))
    cidf = np.ascontiguousarray(np.eye(128, dtype=np.float32))
    in_maps = []
    for c in range(NCORE):
        attd = p4_results[c]["att"]
        in_maps.append({
            "x": p3_results[c]["x1_out"], "attd": attd,
            "wq": wq_t, "wk": wk_t, "wv": wv_t, "c_id": cid, "c_idf": cidf,
        })
    res = run_bass_kernel_spmd(p.nc, in_maps, core_ids=list(range(NCORE)))
    p.close()
    return res.results


def run_p4(p3_results):
    out = []
    for c in range(NCORE):
        r = dict(p3_results[c])
        r["ksh3"] = r["ksh_out"][0]
        r["vsh3"] = r["vsh_out"][0]
        out.append(r)
    return _run_attn((1, False, "m"), "q_out", "ksh3", "vsh3", out)


def run_p6(inputs, p5_results):
    gwt = np.ascontiguousarray(np.asarray(inputs["L1_attn_gate_w"]).T.astype(np.float32))
    gbt = np.ascontiguousarray(np.asarray(inputs["L1_attn_gate_b"]).reshape(128, 1).astype(np.float32))
    return _run_attn((NH, True, "g"), "q_out", "k_out", "v_out", p5_results,
                     extra={"gwt": gwt, "gbt": gbt})


def emit_dequant(p, consts, pk_dram, sc_dram, out_dram, nlm, nw, tag, waits=(), bank=0):
    ngrp = nw * 8 // 128
    SPW = 256 if nw % 256 == 0 else 176
    assert nw % SPW == 0 and SPW % 16 == 0
    nsp = nw // SPW
    gps = SPW * 8 // 128
    identf = consts["identf"]
    pk = p.sb(f"dqp_{tag}", [128, nw], I32)
    an = p.sb(f"dqa_{tag}", [128, 8 * SPW], I32)
    b2 = p.sb(f"dqb_{tag}", [128, 8 * SPW], I32)
    f1 = p.sb(f"dqf1_{tag}", [128, 8 * SPW], F32)
    f2 = p.sb(f"dqf2_{tag}", [128, 8 * SPW], F32)
    f3 = p.sb(f"dqf3_{tag}", [128, 8 * SPW], F32)
    f4 = p.sb(f"dqf4_{tag}", [128, 8 * SPW], F32)
    valh = p.sb(f"dqv_{tag}", [128, 8 * SPW], F32)
    sc16 = p.sb(f"dqs_{tag}", [128, ngrp], F16)
    sc32 = p.sb(f"dqs3_{tag}", [128, ngrp], F32)
    asm = p.sb(f"dqo_{tag}", [128, ngrp * 128], BF16)
    tp_ps = p.pf[bank]
    s_dec, s_val, s_scl, s_tp, s_ev, s_sc32 = (p.sem() for _ in range(6))
    s_ld = p.sem(kind="hw")
    s_st = p.sem(kind="sw")
    wvals = _wpairs(waits)

    @p.block.sync
    def _(sync):
        for lm in range(nlm):
            if lm >= 1:
                s_dec.wge(sync, lm * nsp)
            s_ld.inc(sync.dma_start(out=pk[:, :], in_=pk_dram[lm, :, :]), 16)
            if lm >= 1:
                s_sc32.wge(sync, lm)
            s_ld.inc(sync.dma_start(out=sc16[:, :], in_=sc_dram[lm, :, :]), 16)

    @p.block.vector
    def _(vector):
        for lm in range(nlm):
            if lm == 0:
                for hh, n in wvals:
                    vector.wait_ge(hh, n)
            s_ld.wge(vector, 32 * (lm + 1))
            s_sc32.inc(vector.tensor_copy(sc32[:, :], sc16[:, :]))
            for sp in range(nsp):
                if lm + sp > 0:
                    s_tp.wge(vector, (lm * nsp + sp - 1) * gps + gps)
                w0 = sp * SPW
                J = SPW

                def _stage(emit1):
                    for j in range(8):
                        i = emit1(j)
                    s_val.inc(i)
                    s_val.wge(vector, s_val.n)

                _stage(lambda j: vector.tensor_scalar(an[:, j * J:(j + 1) * J], pk[:, w0:w0 + SPW],
                                                      4 * j, 15, ALU.logical_shift_right, ALU.bitwise_and))
                _stage(lambda j: vector.tensor_scalar(b2[:, j * J:(j + 1) * J], an[:, j * J:(j + 1) * J],
                                                      7, None, ALU.bitwise_and))
                _stage(lambda j: vector.tensor_copy(f1[:, j * J:(j + 1) * J], b2[:, j * J:(j + 1) * J]))
                _stage(lambda j: vector.tensor_scalar(b2[:, j * J:(j + 1) * J], an[:, j * J:(j + 1) * J],
                                                      3, None, ALU.logical_shift_right))
                _stage(lambda j: vector.tensor_copy(f2[:, j * J:(j + 1) * J], b2[:, j * J:(j + 1) * J]))
                _stage(lambda j: vector.tensor_scalar(f2[:, j * J:(j + 1) * J], f2[:, j * J:(j + 1) * J],
                                                      -2.0, 1.0, ALU.mult, ALU.add))
                _stage(lambda j: vector.tensor_scalar(f3[:, j * J:(j + 1) * J], f1[:, j * J:(j + 1) * J],
                                                      -4.0, 0.0, ALU.add, ALU.max))
                _stage(lambda j: vector.tensor_scalar(f4[:, j * J:(j + 1) * J], f1[:, j * J:(j + 1) * J],
                                                      -6.0, 0.0, ALU.add, ALU.max))
                _stage(lambda j: vector.tensor_tensor(f3[:, j * J:(j + 1) * J], f1[:, j * J:(j + 1) * J],
                                                      f3[:, j * J:(j + 1) * J], ALU.add))
                _stage(lambda j: vector.scalar_tensor_tensor(f3[:, j * J:(j + 1) * J], f3[:, j * J:(j + 1) * J],
                                                             0.5, f4[:, j * J:(j + 1) * J], ALU.mult, ALU.add))
                vh = valh[:, :].rearrange("p (w j) -> p j w", j=8)
                for j in range(8):
                    i = vector.tensor_tensor(vh[:, j, :], f3[:, j * J:(j + 1) * J],
                                             f2[:, j * J:(j + 1) * J], ALU.mult)
                s_dec.inc(i)
                s_dec.wge(vector, s_dec.n)
                for gl in range(gps):
                    g = sp * gps + gl
                    i = vector.tensor_scalar(valh[:, gl * 128:(gl + 1) * 128],
                                             valh[:, gl * 128:(gl + 1) * 128],
                                             sc32[:, g:g + 1], None, ALU.mult)
                s_scl.inc(i)
                for gl in range(gps):
                    g = sp * gps + gl
                    if lm >= 1 and sp == 0 and gl == 0:
                        s_st.wge(vector, 16 * lm)
                    s_tp.wge(vector, (lm * nsp + sp) * gps + gl + 1)
                    s_ev.inc(vector.tensor_copy(asm[:, g * 128:(g + 1) * 128], tp_ps[:, 0:128]))

    @p.block.tensor
    def _(tensor):
        for lm in range(nlm):
            if lm == 0:
                for hh, n in wvals:
                    tensor.wait_ge(hh, n)
            for sp in range(nsp):
                s_scl.wge(tensor, lm * nsp + sp + 1)
                for gl in range(gps):
                    o = (lm * nsp + sp) * gps + gl
                    if o >= 1:
                        s_ev.wge(tensor, o)
                    s_tp.inc(tensor.transpose(tp_ps[:, 0:128],
                                              valh[:, gl * 128:(gl + 1) * 128], identf[:, :]))

    @p.block.gpsimd
    def _(g):
        for lm in range(nlm):
            s_ev.wge(g, lm * ngrp + ngrp)
            s_st.inc(g.dma_start(out=out_dram[lm, :, :], in_=asm[:, :]), 16)
    return s_st


def build_p7a():
    p = Prog().open()
    gup = p.din("gup", [11, 128, 256], I32)
    gus = p.din("gus", [11, 128, 16], F16)
    dnp = p.din("dnp", [2, 128, 704], I32)
    dns = p.din("dns", [2, 128, 44], F16)
    wgu_mine = p.dout("wgu_mine", [11, 128, 16 * 128], BF16)
    down_mine = p.dout("down_mine", [2, 128, 44 * 128], BF16)
    consts, s_const = load_consts(p)

    s1 = emit_dequant(p, consts, gup, gus, wgu_mine, 11, 256, "gu", waits=[s_const], bank=0)
    s2 = emit_dequant(p, consts, dnp, dns, down_mine, 2, 704, "dn", waits=[s_const], bank=1)
    emit_final_wait(p, [s1, s2])
    return p


def run_p7a(inputs):
    p = build_p7a()
    gup = np.asarray(inputs["L1_ffn_gate_up_packed"])
    gus = np.asarray(inputs["L1_ffn_gate_up_scales"])
    dnp = np.asarray(inputs["L1_ffn_down_packed"])
    dns = np.asarray(inputs["L1_ffn_down_scales"])
    cid = np.ascontiguousarray(np.eye(128, dtype=BF))
    cidf = np.ascontiguousarray(np.eye(128, dtype=np.float32))
    in_maps = []
    for c in range(NCORE):
        r0, r1 = c * 1408, (c + 1) * 1408
        d0, d1 = c * 256, (c + 1) * 256
        in_maps.append({
            "gup": np.ascontiguousarray(gup[r0:r1].reshape(11, 128, 256)),
            "gus": np.ascontiguousarray(gus.T[r0:r1].reshape(11, 128, 16)),
            "dnp": np.ascontiguousarray(dnp[d0:d1].reshape(2, 128, 704)),
            "dns": np.ascontiguousarray(dns.T[d0:d1].reshape(2, 128, 44)),
            "c_id": cid, "c_idf": cidf,
        })
    res = run_bass_kernel_spmd(p.nc, in_maps, core_ids=list(range(NCORE)))
    p.close()
    return res.results


def emit_l1ffn(p, consts, xln, wgu, wdn, h_dram, xout, s_st1, s_ln, extra_waits):
    oc = consts["ones_col"]
    orow_f = consts["ones_row_f"]

    wbA = [p.sb(f"wA{i}", [128, NCH * 128], BF16) for i in range(4)]
    sil_sb = [p.sb(f"sil_sb{i}", [128, TT], F32) for i in range(2)]
    hb = [p.sb(f"hb{i}", [128, TT], BF16) for i in range(2)]
    wdb = [p.sb(f"wd{i}", [128, 5 * 128], BF16) for i in range(2)]
    hsb = [p.sb(f"hs{i}", [128, TT], BF16) for i in range(2)]
    y_sb = p.sb("y_sb", [128, NCH * TT], BF16)
    ysq = p.sb("ysq", [128, TT], BF16)
    x3t = p.sb("x3t", [128, NCH * TT], F32)
    rrow = p.sb("rrow", [1, 3 * TT], F32)
    ytmp = p.sb("ytmp", [128, TT], F32)
    xoc = [p.sb(f"xoc{i}", [128, TT], F32) for i in range(2)]
    OG = [(0, 5), (5, 10), (10, 15), (15, 16)]
    yev_cum = []
    tot = 0
    for g in range(4 * NTT):
        yev_cum.append(tot)
        tot += OG[g % 4][1] - OG[g % 4][0]

    (s_gu, s_sil, s_h, s_ymm, s_yev, s_ysq, s_rms,
     s_msr, s_sd, s_rr, s_rb, s_fin, s_aux) = (p.sem() for _ in range(13))
    s_x3 = p.sem(kind="sw")
    s_wA = [p.sem(kind="sw"), p.sem(kind="sw")]
    s_hld = [p.sem(kind="sw"), p.sem(kind="sw")]
    s_wd = [p.sem(kind="sw"), p.sem(kind="sw")]
    s_hst = [p.sem(kind="hw"), p.sem(kind="hw")]
    s_fst = [p.sem(kind="hw"), p.sem(kind="hw")]

    @p.block.gpsimd
    def _(g):
        for hh, nn in _wpairs(extra_waits):
            g.wait_ge(hh, nn)
        for tt in range(NTT):
            for ip in range(44):
                it = tt * 44 + ip
                if it >= 2:
                    s_gu.wge(g, it - 1)
                s_wA[it % 2].inc(g.dma_start(out=wbA[(it % 2) * 2][:, :], in_=wgu[ip, :, :]), 16)
                s_wA[it % 2].inc(g.dma_start(out=wbA[(it % 2) * 2 + 1][:, :], in_=wgu[44 + ip, :, :]), 16)
            for ogi, (o0, o1) in enumerate(OG):
                if ogi == 0:
                    s_hst[0].wge(g, 16 * 22 * (tt + 1))
                    s_hst[1].wge(g, 16 * 22 * (tt + 1))
                for i in range(44):
                    ib = (tt * 4 + ogi) * 44 + i
                    if ib >= 2:
                        s_ymm.wge(g, ib - 1)
                    s_hld[ib % 2].inc(g.dma_start(out=hsb[ib % 2][:, :],
                                                  in_=h_dram[i, :, tt * TT:(tt + 1) * TT]), 16)
                    s_wd[ib % 2].inc(g.dma_start(
                        out=wdb[ib % 2][:, 0:(o1 - o0) * 128].rearrange("p (o mm) -> p o mm", o=o1 - o0),
                        in_=wdn[o0:o1, :, i * 128:(i + 1) * 128].rearrange("o p mm -> p o mm")), 16)
            if tt >= 1:
                s_fin.wge(g, tt * NCH)
            for hh, nn in s_st1.pairs():
                g.wait_ge(hh, nn)
            s_x3.inc(g.dma_start(out=x3t[:, :].rearrange("p (c t) -> p c t", c=NCH),
                                 in_=xout[0:NCH, :, tt * TT:(tt + 1) * TT].rearrange("c p t -> p c t")), 16)

    @p.block.tensor
    def _(tensor):
        for tt in range(NTT):
            if tt == 0:
                s_ln.wge(tensor, s_ln.n)
            for ip in range(44):
                it = tt * 44 + ip
                s_wA[it % 2].wge(tensor, 32 * (it // 2 + 1))
                if it >= 2:
                    s_sil.wge(tensor, it - 1)
                    s_h.wge(tensor, it - 1)
                if ip == 0 and tt >= 1:
                    s_aux.wge(tensor, tt * NCH)
                for c in range(NCH):
                    tensor.matmul(p.pf[(it % 2) * 2][:, :], wbA[(it % 2) * 2][:, c * 128:(c + 1) * 128],
                                  xln[:, c * T + tt * TT:c * T + tt * TT + TT],
                                  start=(c == 0), stop=(c == NCH - 1))
                for c in range(NCH):
                    i = tensor.matmul(p.pf[(it % 2) * 2 + 1][:, :], wbA[(it % 2) * 2 + 1][:, c * 128:(c + 1) * 128],
                                      xln[:, c * T + tt * TT:c * T + tt * TT + TT],
                                      start=(c == 0), stop=(c == NCH - 1))
                s_gu.inc(i)
            s_h.wge(tensor, 44 * (tt + 1))
            for ogi, (o0, o1) in enumerate(OG):
                gG = tt * 4 + ogi
                for i in range(44):
                    ib = gG * 44 + i
                    s_hld[ib % 2].wge(tensor, 16 * (ib // 2 + 1))
                    s_wd[ib % 2].wge(tensor, 16 * (ib // 2 + 1))
                    if i == 0 and gG >= 1:
                        s_yev.wge(tensor, yev_cum[gG])
                    for ol in range(o1 - o0):
                        ii = tensor.matmul(p.pf[ol][:, :], wdb[ib % 2][:, ol * 128:(ol + 1) * 128],
                                           hsb[ib % 2][:, :], start=(i == 0), stop=(i == 43))
                    s_ymm.inc(ii)
                for ol in range(o1 - o0):
                    o = o0 + ol
                    s_ysq.wge(tensor, tt * NCH + o + 1)
                    i = tensor.matmul(p.pf[5][0:1, 0:TT], oc[:, :], ysq[:, :],
                                      start=(o == 0), stop=(o == NCH - 1))
                    s_rms.inc(i)
            s_rr.wge(tensor, tt + 1)
            s_rb.inc(tensor.matmul(p.pf[0][:, :], orow_f[0:1, :], rrow[0:1, 2 * TT:3 * TT],
                                   start=True, stop=True))

    @p.block.scalar
    def _(scalar):
        for tt in range(NTT):
            for ip in range(44):
                it = tt * 44 + ip
                s_gu.wge(scalar, it + 1)
                if it >= 2:
                    s_h.wge(scalar, it - 1)
                s_sil.inc(scalar.activation(sil_sb[it % 2][:, :], p.pf[(it % 2) * 2][:, :], AF.Silu))
            for o in range(NCH):
                s_yev.wge(scalar, tt * NCH + o + 1)
                if tt * NCH + o >= 1:
                    s_rms.wge(scalar, tt * NCH + o)
                s_ysq.inc(scalar.activation(ysq[:, :], y_sb[:, o * TT:(o + 1) * TT], AF.Square))
            s_msr.wge(scalar, tt + 1)
            s_sd.inc(scalar.activation(rrow[0:1, TT:2 * TT], rrow[0:1, 0:TT], AF.Sqrt))

    @p.block.vector
    def _(vector):
        for tt in range(NTT):
            for ip in range(44):
                it = tt * 44 + ip
                s_sil.wge(vector, it + 1)
                if it >= 2:
                    s_hst[it % 2].wge(vector, 16 * (it // 2))
                s_h.inc(vector.tensor_tensor(hb[it % 2][:, :], sil_sb[it % 2][:, :], p.pf[(it % 2) * 2 + 1][:, :], ALU.mult))
            for ogi, (o0, o1) in enumerate(OG):
                gG = tt * 4 + ogi
                s_ymm.wge(vector, gG * 44 + 44)
                for ol in range(o1 - o0):
                    o = o0 + ol
                    if tt >= 1 and o == 0:
                        s_ysq.wge(vector, tt * NCH)
                    s_yev.inc(vector.tensor_copy(y_sb[:, o * TT:(o + 1) * TT], p.pf[ol][:, :]))
            s_rms.wge(vector, tt * NCH + NCH)
            s_msr.inc(vector.tensor_scalar(rrow[0:1, 0:TT], p.pf[5][0:1, 0:TT],
                                           1.0 / D, 1e-6, ALU.mult, ALU.add))
            s_sd.wge(vector, tt + 1)
            s_rr.inc(vector.reciprocal(rrow[0:1, 2 * TT:3 * TT], rrow[0:1, TT:2 * TT]))
            s_rb.wge(vector, tt + 1)
            s_x3.wge(vector, 16 * (tt + 1))
            for o in range(NCH):
                if tt * NCH + o >= 2:
                    s_fst[(tt * NCH + o) % 2].wge(vector, 16 * ((tt * NCH + o) // 2))
                if tt * NCH + o >= 1:
                    s_fin.wge(vector, tt * NCH + o)
                s_aux.inc(vector.tensor_tensor(ytmp[:, :], y_sb[:, o * TT:(o + 1) * TT],
                                               p.pf[0][:, :], ALU.mult))
                s_aux.wge(vector, s_aux.n)
                s_fin.inc(vector.tensor_tensor(xoc[(tt * NCH + o) % 2][:, :], ytmp[:, :],
                                               x3t[:, o * TT:(o + 1) * TT], ALU.add))

    @p.block.sync
    def _(g):
        for tt in range(NTT):
            for ip in range(44):
                it = tt * 44 + ip
                s_h.wge(g, it + 1)
                s_hst[it % 2].inc(g.dma_start(out=h_dram[ip, :, tt * TT:(tt + 1) * TT],
                                              in_=hb[it % 2][:, :]), 16)
            for o in range(NCH):
                s_fin.wge(g, tt * NCH + o + 1)
                s_fst[(tt * NCH + o) % 2].inc(g.dma_start(out=xout[o, :, tt * TT:(tt + 1) * TT],
                                                          in_=xoc[(tt * NCH + o) % 2][:, :]), 16)

    return SemGroup(s_fst)


def build_p7b():
    p = Prog().open()
    x = p.din("x", [NCH, 128, T], F32)
    attd = p.din("attd", [NCH, 128, T], BF16)
    wo = p.din("wo", [NCH, 128, NCH, 128], F32)
    bo = p.din("bo", [128, NCH], F32)
    wgu = p.din("wgu", [88, 128, 16 * 128], BF16)
    wdn = p.din("wdn", [16, 128, 44 * 128], BF16)
    xout = p.dout("xout", [NCH, 128, T], F32)
    h_dram = p.dint("h_d", [44, 128, T], BF16)

    consts, s_const = load_consts(p)
    ensure_proj_scratch(p)
    oc = consts["ones_col"]
    orow_f = consts["ones_row_f"]
    bias_sb = p.sb("bias", [128, NCH], F32)
    xln = p.sb("xln", [128, NCH * T], BF16)
    s_misc = p.sem(kind="hw")

    @p.block.sync
    def _(sync):
        s_misc.inc(sync.dma_start(out=bias_sb[:, :], in_=bo[:, :]), 16)

    with p.scope():
        s_x1, s_st1 = emit_resadd_ln(p, x, attd, xout, xln, consts, [s_const, s_misc],
                                     "p7", wo_dram=wo, bias_sb=bias_sb, bias_off=0)
        s_ln = emit_ln(p, xln, xln, consts, [s_x1, s_st1], "p7")
    s_fst = emit_l1ffn(p, consts, xln, wgu, wdn, h_dram, xout, s_st1, s_ln, [])
    emit_final_wait(p, [SemGroup(s_fst)])
    return p


def run_p7b(inputs, p6_results, p5_results, p7a_results):
    p = build_p7b()
    wo_t = _tile_lhsT(np.asarray(inputs["L1_attn_o_w"]))
    bo = np.zeros((128, NCH), np.float32)
    wgu_all = np.concatenate([p7a_results[c]["wgu_mine"] for c in range(NCORE)], axis=0)
    wdn_all = np.concatenate([p7a_results[c]["down_mine"] for c in range(NCORE)], axis=0)
    cid = np.ascontiguousarray(np.eye(128, dtype=BF))
    cidf = np.ascontiguousarray(np.eye(128, dtype=np.float32))
    in_maps = []
    for c in range(NCORE):
        attd = p6_results[c]["att"]
        in_maps.append({
            "x": p5_results[c]["x2_out"], "attd": attd, "wo": wo_t, "bo": bo,
            "wgu": wgu_all, "wdn": wdn_all, "c_id": cid, "c_idf": cidf,
        })
    res = run_bass_kernel_spmd(p.nc, in_maps, core_ids=list(range(NCORE)))
    p.close()
    return res.results


def _prep_shared(inputs):
    d = {}
    d["c_id"] = np.ascontiguousarray(np.eye(128, dtype=BF))
    d["c_idf"] = np.ascontiguousarray(np.eye(128, dtype=np.float32))
    d["c_pt"] = _rope_P()
    cos, sin = _rope_tables()
    d["rope_cos"], d["rope_sin"] = cos, sin
    d["masks"] = _diag_masks()
    d["wq0"] = _tile_lhsT(np.asarray(inputs["L0_attn_Wq"]))
    d["wk0"] = _tile_lhsT(np.asarray(inputs["L0_attn_Wk"]))
    d["wv0"] = _tile_lhsT(np.asarray(inputs["L0_attn_Wv"]))
    bq = np.asarray(inputs["L0_attn_bq"]).reshape(NH, 128).T
    bk = np.asarray(inputs["L0_attn_bk"]).reshape(KVH0, 128).T
    bv = np.asarray(inputs["L0_attn_bv"]).reshape(KVH0, 128).T
    d["bqkv0"] = np.ascontiguousarray(np.concatenate([bq, bk, bv], axis=1).astype(np.float32))
    return d


def _x_shards(x):
    out = []
    for c in range(NCORE):
        s = c // 2
        out.append(np.ascontiguousarray(x[s, _core_pos(c), :].T))
    return out


def run_p1(inputs):
    sh = _prep_shared(inputs)
    xs = _x_shards(np.asarray(inputs["L0_attn_x"]))
    p = build_p1()
    in_maps = []
    for c in range(NCORE):
        h = c % 2
        in_maps.append({
            "x": xs[c],
            "wq": sh["wq0"], "wk": sh["wk0"], "wv": sh["wv0"],
            "bqkv": sh["bqkv0"],
            "ctab": np.ascontiguousarray(sh["rope_cos"][:, _core_pos(c)]),
            "stab": np.ascontiguousarray(sh["rope_sin"][:, _core_pos(c)]),
            "c_pt": sh["c_pt"], "c_id": sh["c_id"], "c_idf": sh["c_idf"],
        })
    res = run_bass_kernel_spmd(p.nc, in_maps, core_ids=list(range(NCORE)))
    p.close()
    return res.results


PAIRS = [[0, 1], [2, 3], [4, 5], [6, 7]]
ALL8 = [list(range(NCORE))]


def emit_collective(p, ins_ap, outs_ap, groups, wait_sems, tag):
    s_c = p.sem(kind="col")
    wv = _wpairs(wait_sems)

    @p.block.gpsimd
    def _(g):
        for h, n in wv:
            g.wait_ge(h, n)
        s_c.inc(g.collective_compute("AllGather", ALU.bypass, replica_groups=groups,
                                     ins=[ins_ap], outs=[outs_ap]))
    return s_c


def build_fused(upto=8, nhd26=8):
    global UPTO26_NHD
    UPTO26_NHD = nhd26
    p = Prog().open()
    x = p.din("x", [NCH, 128, T], F32)
    wq0 = p.din("wq0", [NH, 128, NCH, 128], F32)
    wk0 = p.din("wk0", [KVH0, 128, NCH, 128], F32)
    wv0 = p.din("wv0", [KVH0, 128, NCH, 128], F32)
    bqkv = p.din("bqkv", [128, NH + 2 * KVH0], F32)
    ctab = p.din("ctab", [128, T], F32)
    stab = p.din("stab", [128, T], F32)
    cpt = p.din("c_pt", [128, 128], BF16)
    wo0 = p.din("wo0", [NCH, 128, NCH, 128], F32)
    bo0 = p.din("bo0", [128, NCH], F32)
    wg = p.din("wg", [128, NCH * 16], F32)
    wqkv = p.din("wqkv", [48, 128, NCH, 128], F32)
    wq1 = p.din("wq1", [NH, 128, NCH, 128], F32)
    wk1 = p.din("wk1", [NH, 128, NCH, 128], F32)
    wv1 = p.din("wv1", [NH, 128, NCH, 128], F32)
    gwt = p.din("gwt", [128, 128], F32)
    gbt = p.din("gbt", [128, 1], F32)
    ow1 = p.din("ow1", [NCH, 128, NCH, 128], F32)
    bz = p.din("bz", [128, NCH], F32)
    gup = p.din("gup", [11, 128, 256], I32)
    gus = p.din("gus", [11, 128, 16], F16)
    dnp = p.din("dnp", [2, 128, 704], I32)
    dns = p.din("dns", [2, 128, 44], F16)
    ptk = p.din("ptk", [128, (S // 512) * 4], F32)
    qpos = p.din("qpos", [1, T], F32)
    xout = p.dout("xout", [NCH, 128, T], F32)

    q0 = p.dint("q0", [NH, 128, T], BF16)
    k0c = p.dint("k0c", [1, KVH0 * 128 * T], BF16)
    v0c = p.dint("v0c", [1, KVH0 * T * 128], BF16)
    k0g = p.dint("k0g", [2, KVH0 * 128 * T], BF16)
    v0g = p.dint("v0g", [2, KVH0 * T * 128], BF16)
    att0 = p.dint("att0", [NH, 128, T], BF16)
    x1 = p.dint("x1", [NCH, 128, T], F32)
    ind_dram = p.dint("ind_d", [16, T], F32)
    qf0 = p.dint("qf0", [NH, 128, T], BF16)
    kshc = p.dint("kshc", [1, 128 * T], BF16)
    vshc = p.dint("vshc", [1, T * 128], BF16)
    kshg = p.dint("kshg", [2, 128 * T], BF16)
    vshg = p.dint("vshg", [2, T * 128], BF16)
    attm = p.dint("attm", [NH, 128, T], BF16)
    x2 = p.dint("x2", [NCH, 128, T], F32)
    q1 = p.dint("q1", [NH, 128, T], BF16)
    k1c = p.dint("k1c", [1, NH * 128 * T], BF16)
    v1c = p.dint("v1c", [1, NH * T * 128], BF16)
    k1g = p.dint("k1g", [2, NH * 128 * T], BF16)
    v1g = p.dint("v1g", [2, NH * T * 128], BF16)
    att1 = p.dint("att1", [NH, 128, T], BF16)
    wgu_mine = p.dint("wgu_mine", [11, 128 * 16 * 128], BF16)
    down_mine = p.dint("down_mine", [2, 128 * 44 * 128], BF16)
    wgu_all = p.dint("wgu_all", [88, 128 * 16 * 128], BF16)
    down_all = p.dint("down_all", [16, 128 * 44 * 128], BF16)
    h_dram = p.dint("h_d", [44, 128, T], BF16)

    scale = HD ** -0.5
    consts, s_const = load_consts(p, reg_vals=[scale])
    ensure_proj_scratch(p)
    ptk_sb = p.sb("ptk_sb", [128, (S // 512) * 4], F32)
    qpos_sb = p.sb("qpos_sb", [1, T], F32)
    bias_sb = p.sb("bias_all", [128, NH + 2 * KVH0], F32)
    bo_sb = p.sb("bo_sb", [128, NCH], F32)
    bz_sb = p.sb("bz_sb", [128, NCH], F32)
    wg_sb = p.sb("wg_sb", [128, NCH * 16], BF16)
    gw_sb = p.sb("gw_sb", [128, 128], BF16)
    gb_sb = p.sb("gb_sb", [128, 1], F32)
    s_mh = p.sem(kind="hw")
    s_msw = p.sem(kind="sw")
    s_misc = SemGroup([s_mh, s_msw])

    @p.block.sync
    def _(sync):
        s_mh.inc(sync.dma_start(out=ptk_sb[:, :], in_=ptk[:, :]), 16)
        s_mh.inc(sync.dma_start(out=qpos_sb[:, :], in_=qpos[:, :]), 16)
        s_mh.inc(sync.dma_start(out=bias_sb[:, :], in_=bqkv[:, :]), 16)
        s_mh.inc(sync.dma_start(out=bo_sb[:, :], in_=bo0[:, :]), 16)
        s_mh.inc(sync.dma_start(out=bz_sb[:, :], in_=bz[:, :]), 16)

    @p.block.gpsimd
    def _(g):
        s_msw.inc(g.dma_start(out=wg_sb[:, :], in_=wg[:, :]), 16)
        s_msw.inc(g.dma_start(out=gw_sb[:, :], in_=gwt[:, :]), 16)
        s_msw.inc(g.dma_start(out=gb_sb[:, :], in_=gbt[:, :]), 16)

    k0c_v = k0c.rearrange("o (v p t) -> (o v) p t", v=KVH0, p=128)
    v0c_v = v0c.rearrange("o (v t d) -> (o v) t d", v=KVH0, t=T)
    with p.scope():
        cos_sb = p.sb("cos", [128, T], F32)
        sin_sb = p.sb("sin", [128, T], F32)
        pt_sb = p.sb("pt", [128, 128], BF16)
        s_m1 = p.sem(kind="hw")

        @p.block.sync
        def _(sync):
            sync.dma_start(out=cos_sb[:, :], in_=ctab[:, :]).then_inc(s_m1.h, 16)
            sync.dma_start(out=sin_sb[:, :], in_=stab[:, :]).then_inc(s_m1.h, 16)
            sync.dma_start(out=pt_sb[:, :], in_=cpt[:, :]).then_inc(s_m1.h, 16)
        s_m1.n = 48

        xln = p.sb("xln1", [128, NCH * T], BF16)
        s_x = emit_load_xbf(p, x.rearrange("c p t -> (c p) t"), xln)
        s_ln = emit_ln(p, xln, xln, consts, [s_x, s_const], "f1")
        rope = (cos_sb, sin_sb, pt_sb)
        sq0 = emit_proj(p, xln, wq0, NH, consts, [s_ln, s_m1, s_misc], "q0",
                        bias_sb=bias_sb, bias_off=0, mode="rope", rope=rope, out_dram=q0)
        sk0 = emit_proj(p, xln, wk0, KVH0, consts, [s_ln, sq0], "k0",
                        bias_sb=bias_sb, bias_off=NH, mode="rope", rope=rope, out_dram=k0c_v)
        sv0 = emit_proj(p, xln, wv0, KVH0, consts, [s_ln, sk0], "v0",
                        bias_sb=bias_sb, bias_off=NH + KVH0, mode="vtm", out_dram=v0c_v)

    if upto <= 1:
        emit_final_wait(p, [sq0, sk0, sv0])
        return p
    with p.scope():
        sd1 = emit_dequant(p, consts, gup, gus, wgu_mine.rearrange("m (p x) -> m p x", p=128),
                           11, 256, "gu", waits=[s_const, sv0], bank=0)
        sd2 = emit_dequant(p, consts, dnp, dns, down_mine.rearrange("m (p x) -> m p x", p=128),
                           2, 704, "dn", waits=[s_const, sv0], bank=1)
    s_c1k = emit_collective(p, k0c[:, :], k0g[:, :], PAIRS, [sk0], "c1k")
    s_c1v = emit_collective(p, v0c[:, :], v0g[:, :], PAIRS, [sv0], "c1v")
    s_agw = emit_collective(p, wgu_mine[:, :], wgu_all[:, :], ALL8, [sd1], "agw")
    s_agd = emit_collective(p, down_mine[:, :], down_all[:, :], ALL8, [sd2], "agd")

    if upto <= 2:
        emit_final_wait(p, [s_agw, s_agd])
        return p
    if upto == 25:
        emit_final_wait(p, [s_c1k, s_c1v, s_agw, s_agd])
        return p

    if upto == 26:
        with p.scope():
            s26 = emit_attn_tok(p, consts, q0,
                                k0g.rearrange("s (v p t) -> s v p t", v=KVH0, p=128),
                                v0g.rearrange("s (v t d) -> s v t d", v=KVH0, t=T),
                                att0, UPTO26_NHD, lambda h: h // 4, scale,
                                ptk_sb, qpos_sb, [s_c1k, s_c1v, sq0, s_misc, sd1, sd2], "a0")
        emit_final_wait(p, [s26])
        return p
    with p.scope():
        satt0 = emit_attn_tok(p, consts, q0,
                              k0g.rearrange("s (v p t) -> s v p t", v=KVH0, p=128),
                              v0g.rearrange("s (v t d) -> s v t d", v=KVH0, t=T),
                              att0, NH, lambda h: h // 4, scale,
                              ptk_sb, qpos_sb, [s_c1k, s_c1v, sq0, s_misc, sd1, sd2], "a0")

    if upto <= 3:
        emit_final_wait(p, [satt0])
        return p
    kshc_v = kshc.rearrange("o (p t) -> (o p) t", p=128)
    vshc_v = vshc.rearrange("o (t d) -> (o t) d", t=T)
    with p.scope():
        xln3 = p.sb("xln3", [128, NCH * T], BF16)
        with p.scope():
            s_x1e, s_st1 = emit_resadd_ln(p, x, att0, x1, xln3, consts,
                                          [satt0, s_misc, s_const], "f3",
                                          wo_dram=wo0, bias_sb=bo_sb, bias_off=0)
            s_ln3 = emit_ln(p, xln3, xln3, consts, [s_x1e, s_st1], "f3")
        ind_fm, top1row, s_ind, s_gre = emit_l0f_gating(p, xln3, wg_sb, consts,
                                                        [s_ln3, s_misc], "f3")
        s_indf = p.sem(kind="hw")

        @p.block.sync
        def _(sync):
            s_ind.wge(sync, s_ind.n)
            s_indf.inc(sync.dma_start(out=ind_dram[:, :], in_=ind_fm[0:16, :]), 16)

        ksh_acc = p.sb("ksh_acc", [128, T], BF16)
        vsh_acc = p.sb("vsh_acc", [128, T], BF16)
        vtm_sb = p.sb("vtm_sb", [128, TT], BF16)
        mask_all, s_mask = emit_mask_all(p, consts, ind_dram, s_indf, "f3")
        sqf = emit_proj(p, xln3, wqkv[0:16], 16, consts, [s_ln3, s_ind], "qf",
                        mode="plain", out_dram=qf0)
        s_kacc = emit_proj(p, xln3, wqkv[16:32], 16, consts, [s_ln3, sqf, s_mask], "kf",
                           mode="sel", sel=(mask_all, ksh_acc))
        s_vacc = emit_proj(p, xln3, wqkv[32:48], 16, consts, [s_ln3, s_kacc], "vf",
                           mode="sel", sel=(mask_all, vsh_acc))
        s_kst = p.sem(kind="sw")

        @p.block.gpsimd
        def _(g):
            s_kacc.wge(g, s_kacc.n)
            s_kst.inc(g.dma_start(out=kshc_v[:, :], in_=ksh_acc[:, :]), 16)

        t1_ps = p.pf[3]
        vt_ps = p.pb[0]
        s_t1b, s_t1m, s_vt, s_vte = (p.sem() for _ in range(4))
        s_vst = p.sem(kind="hw")
        orow_f = consts["ones_row_f"]
        identb = consts["ident"]

        @p.block.tensor
        def _(tensor):
            s_vacc.wge(tensor, s_vacc.n)
            s_gre.wge(tensor, s_gre.n)
            for tt in range(NTT):
                if tt > 0:
                    s_t1m.wge(tensor, tt)
                s_t1b.inc(tensor.matmul(t1_ps[:, :], orow_f[0:1, :],
                                        top1row[0:1, tt * TT:(tt + 1) * TT], start=True, stop=True))
                s_t1m.wge(tensor, tt + 1)
                if tt > 0:
                    s_vte.wge(tensor, 4 * tt)
                for sg in range(4):
                    i = tensor.transpose(vt_ps[:, sg * 128:(sg + 1) * 128],
                                         vsh_acc[:, tt * TT + sg * 128:tt * TT + (sg + 1) * 128],
                                         identb[:, :])
                s_vt.inc(i)

        @p.block.vector
        def _(vector):
            for tt in range(NTT):
                s_t1b.wge(vector, tt + 1)
                s_t1m.inc(vector.tensor_tensor(vsh_acc[:, tt * TT:(tt + 1) * TT],
                                               vsh_acc[:, tt * TT:(tt + 1) * TT],
                                               t1_ps[:, :], ALU.mult))
                s_vt.wge(vector, tt + 1)
                if tt > 0:
                    s_vst.wge(vector, 16 * tt)
                for sg in range(4):
                    i = vector.tensor_copy(vtm_sb[:, sg * 128:(sg + 1) * 128],
                                           vt_ps[:, sg * 128:(sg + 1) * 128])
                    s_vte.inc(i)

        @p.block.sync
        def _(g):
            for tt in range(NTT):
                s_vte.wge(g, 4 * (tt + 1))
                s_vst.inc(g.dma_start(
                    out=vshc_v[tt * TT:(tt + 1) * TT, :].rearrange("(sg pp) d -> pp sg d", pp=128),
                    in_=vtm_sb[:, :].rearrange("p (sg d) -> p sg d", sg=4)), 16)

    if upto <= 4:
        emit_final_wait(p, [sqf, s_kst, s_vst, s_st1])
        return p
    s_c2k = emit_collective(p, kshc[:, :], kshg[:, :], PAIRS, [s_kst], "c2k")
    s_c2v = emit_collective(p, vshc[:, :], vshg[:, :], PAIRS, [s_vst], "c2v")

    with p.scope():
        sattm = emit_attn_tok(p, consts, qf0,
                              kshg.rearrange("s (v p t) -> s v p t", v=1, p=128),
                              vshg.rearrange("s (v t d) -> s v t d", v=1, t=T),
                              attm, NH, lambda h: 0, scale,
                              ptk_sb, qpos_sb, [s_c2k, s_c2v, sqf, s_misc], "am")

    if upto <= 5:
        emit_final_wait(p, [sattm])
        return p
    k1c_v = k1c.rearrange("o (v p t) -> (o v) p t", v=NH, p=128)
    v1c_v = v1c.rearrange("o (v t d) -> (o v) t d", v=NH, t=T)
    with p.scope():
        xln5 = p.sb("xln5", [128, NCH * T], BF16)
        with p.scope():
            s_x2e, s_st2 = emit_resadd_ln(p, x1, attm, x2, xln5, consts,
                                          [sattm, s_st1], "f5")
            s_ln5 = emit_ln(p, xln5, xln5, consts, [s_x2e, s_st2], "f5")
        sq1 = emit_proj(p, xln5, wq1, NH, consts, [s_ln5], "q1", mode="plain", out_dram=q1)
        sk1 = emit_proj(p, xln5, wk1, NH, consts, [s_ln5, sq1], "k1", mode="plain", out_dram=k1c_v)
        sv1 = emit_proj(p, xln5, wv1, NH, consts, [s_ln5, sk1], "v1", mode="vtm", out_dram=v1c_v)

    if upto <= 6:
        emit_final_wait(p, [sq1, sk1, sv1, s_st2])
        return p
    s_c3k = emit_collective(p, k1c[:, :], k1g[:, :], PAIRS, [sk1], "c3k")
    s_c3v = emit_collective(p, v1c[:, :], v1g[:, :], PAIRS, [sv1], "c3v")

    with p.scope():
        satt1 = emit_attn_tok(p, consts, q1,
                              k1g.rearrange("s (v p t) -> s v p t", v=NH, p=128),
                              v1g.rearrange("s (v t d) -> s v t d", v=NH, t=T),
                              att1, NH, lambda h: h, scale,
                              ptk_sb, qpos_sb, [s_c3k, s_c3v, sq1, s_misc], "a1",
                              gate=(gw_sb, gb_sb))

    if upto <= 7:
        emit_final_wait(p, [satt1])
        return p
    with p.scope():
        xln7 = p.sb("xln7", [128, NCH * T], BF16)
        with p.scope():
            s_x3e, s_st3 = emit_resadd_ln(p, x2, att1, xout, xln7, consts,
                                          [satt1, s_st2, s_misc], "f7",
                                          wo_dram=ow1, bias_sb=bz_sb, bias_off=0)
            s_ln7 = emit_ln(p, xln7, xln7, consts, [s_x3e, s_st3], "f7")
        s_fst = emit_l1ffn(p, consts, xln7,
                           wgu_all.rearrange("m (p x) -> m p x", p=128),
                           down_all.rearrange("m (p x) -> m p x", p=128),
                           h_dram, xout, s_st3, s_ln7, [s_agw, s_agd])
    emit_final_wait(p, [s_fst])
    return p


def run_fused(inputs):
    sh = _prep_shared(inputs)
    xs = _x_shards(np.asarray(inputs["L0_attn_x"]))
    p = build_fused()
    wo0_t = _tile_lhsT(np.asarray(inputs["L0_attn_Wo"]))
    bo0 = np.ascontiguousarray(np.asarray(inputs["L0_attn_bo"]).reshape(NCH, 128).T.astype(np.float32))
    Wg = np.asarray(inputs["L0_ffn_W_gate"])
    wg_host = np.ascontiguousarray(Wg.reshape(16, 16, 128).transpose(2, 1, 0).reshape(128, 256).astype(np.float32))
    wqkv_t = _tile_lhsT(np.asarray(inputs["L0_ffn_W_qkv"]))
    wq1_t = _tile_lhsT(np.asarray(inputs["L1_attn_q_w"]))
    wk1_t = _tile_lhsT(np.asarray(inputs["L1_attn_k_w"]))
    wv1_t = _tile_lhsT(np.asarray(inputs["L1_attn_v_w"]))
    ow1_t = _tile_lhsT(np.asarray(inputs["L1_attn_o_w"]))
    gwt = np.ascontiguousarray(np.asarray(inputs["L1_attn_gate_w"]).T.astype(np.float32))
    gbt = np.ascontiguousarray(np.asarray(inputs["L1_attn_gate_b"]).reshape(128, 1).astype(np.float32))
    bz = np.zeros((128, NCH), np.float32)
    gup = np.asarray(inputs["L1_ffn_gate_up_packed"])
    gus = np.asarray(inputs["L1_ffn_gate_up_scales"])
    dnp = np.asarray(inputs["L1_ffn_down_packed"])
    dns = np.asarray(inputs["L1_ffn_down_scales"])
    in_maps = []
    for c in range(NCORE):
        pos = _core_pos(c)
        r0, r1 = c * 1408, (c + 1) * 1408
        d0, d1 = c * 256, (c + 1) * 256
        ptkh, qposh = _attn_host_consts(c)
        in_maps.append({
            "x": np.ascontiguousarray(xs[c].reshape(NCH, 128, T)),
            "wq0": sh["wq0"], "wk0": sh["wk0"], "wv0": sh["wv0"], "bqkv": sh["bqkv0"],
            "ctab": np.ascontiguousarray(sh["rope_cos"][:, pos]),
            "stab": np.ascontiguousarray(sh["rope_sin"][:, pos]),
            "c_pt": sh["c_pt"], "c_id": sh["c_id"], "c_idf": sh["c_idf"],
            "wo0": wo0_t, "bo0": bo0, "wg": wg_host, "wqkv": wqkv_t,
            "wq1": wq1_t, "wk1": wk1_t, "wv1": wv1_t, "gwt": gwt, "gbt": gbt,
            "ow1": ow1_t, "bz": bz,
            "gup": np.ascontiguousarray(gup[r0:r1].reshape(11, 128, 256)),
            "gus": np.ascontiguousarray(gus.T[r0:r1].reshape(11, 128, 16)),
            "dnp": np.ascontiguousarray(dnp[d0:d1].reshape(2, 128, 704)),
            "dns": np.ascontiguousarray(dns.T[d0:d1].reshape(2, 128, 44)),
            "ptk": ptkh, "qpos": qposh,
        })
    res = run_bass_kernel_spmd(p.nc, in_maps, core_ids=list(range(NCORE)))
    p.close()
    return res.results


USE_FUSED = True


def kernel(**inputs):
    inputs = {kk: np.asarray(v) for kk, v in inputs.items()}
    out = np.zeros((B, S, D), np.float32)
    if USE_FUSED:
        rf = run_fused(inputs)
        for c in range(NCORE):
            out[c // 2, _core_pos(c), :] = rf[c]["xout"].reshape(D, T).T
        return out
    xs = _x_shards(inputs["L0_attn_x"])
    p1 = run_p1(inputs)
    p2 = run_p2(p1, _diag_masks())
    p3 = run_p3(inputs, p2, xs)
    p4 = run_p4(p3)
    p5 = run_p5(inputs, p4, p3)
    p6 = run_p6(inputs, p5)
    p7a = run_p7a(inputs)
    p7b = run_p7b(inputs, p6, p5, p7a)
    for c in range(NCORE):
        out[c // 2, _core_pos(c), :] = p7b[c]["xout"].reshape(D, T).T
    return out
